# version 12
# baseline (speedup 1.0000x reference)
import numpy as np
import concourse.bass as bass
import concourse.mybir as mybir
from concourse.bass_utils import run_bass_kernel_spmd
from contextlib import ExitStack

F32 = mybir.dt.float32
F16 = mybir.dt.float16
AF = mybir.ActivationFunctionType
ALU = mybir.AluOpType
AX = mybir.AxisListType.X

MAGIC = 12582912.0
QMAX = 32767.0
CLIP16 = 65535.0 / 65536.0
D = 1024
I = 3072
DC = D // 128
IC = I // 128
S = 512
NTT = S // 128


class P:
    ENGINES = ("pe", "act", "dve", "pool", "sp")
    DQUEUES = ("spq", "gpq", "acq", "ccq")
    NPOOL = 8

    def __init__(self, nc):
        self.nc = nc
        self.ops = {e: [] for e in self.ENGINES}
        self.cnt = {e: 0 for e in self.ENGINES}
        self.dcnt = {}
        for q in self.DQUEUES:
            for i in range(self.NPOOL):
                self.dcnt[f"{q}{i}"] = 0
        self.didx = {q: 0 for q in self.DQUEUES}
        self.sems = {}
        self.water = {e: {} for e in self.ENGINES}

    TRACE = []

    def wait(self, e, deps):
        for d in deps:
            if d is None:
                continue
            sname, val = d
            if sname == e:
                if val >= self.cnt[e]:
                    val = self.cnt[e]
            w = self.water[e]
            if w.get(sname, 0) >= val:
                continue
            w[sname] = val

            def op(eng, sem_name=sname, val=val, self=self):
                eng.wait_ge(self.sems[sem_name], val)
            self.ops[e].append(op)
            P.TRACE.append(("wait", e, sname, val))

    def emit(self, e, builder, deps=()):
        self.wait(e, deps)
        self.cnt[e] += 1
        val = self.cnt[e]

        def op(eng, builder=builder, e=e, self=self):
            builder(eng).then_inc(self.sems[e], 1)
        self.ops[e].append(op)
        import traceback
        P.TRACE.append(("emit", e, val, traceback.extract_stack()[-2].lineno))
        return (e, val)

    def dma(self, e, q, builder, deps=()):
        sq = f"{q}{self.didx[q] % self.NPOOL}"
        self.didx[q] += 1
        prev = getattr(self, "dlast", None)
        if prev is None:
            self.dlast = prev = {}
        self.wait(e, list(deps) + [prev.get(sq)])
        self.dcnt[sq] += 16
        val = self.dcnt[sq]
        prev[sq] = (sq, val)

        def op(eng, builder=builder, sq=sq, self=self):
            builder(eng).then_inc(self.sems[sq], 16)
        self.ops[e].append(op)
        import traceback
        P.TRACE.append(("dma", e, sq, val, traceback.extract_stack()[-2].lineno))
        return (sq, val)

    def fence(self):
        toks = [(e, self.cnt[e]) for e in self.ENGINES]
        toks += [(sq, self.dcnt[sq]) for sq in self.dcnt]
        for e in self.ENGINES:
            self.wait(e, toks)


def build_program(T=8192, use_cc=True, n_cores=8, dbg=False):
    NST = T // S
    nc = bass.Bass("TRN2", target_bir_lowering=False,
                   num_devices=(n_cores if use_cc else None))

    x_h = nc.dram_tensor("x", [T, D], F32, kind="ExternalInput").ap()
    win_h = nc.dram_tensor("w_in", [D, D], F32, kind="ExternalInput").ap()
    wg_h = nc.dram_tensor("w_gate", [D, D], F32, kind="ExternalInput").ap()
    bin_h = nc.dram_tensor("b_in", [1, D], F32, kind="ExternalInput").ap()
    bg_h = nc.dram_tensor("b_gate", [1, D], F32, kind="ExternalInput").ap()
    lam_h = nc.dram_tensor("lam", [1, D], F32, kind="ExternalInput").ap()
    wup_h = nc.dram_tensor("w_up", [I, D], F32, kind="ExternalInput").ap()
    wgf_h = nc.dram_tensor("w_gf", [I, D], F32, kind="ExternalInput").ap()
    wd_h = nc.dram_tensor("w_down", [D, I], F32, kind="ExternalInput").ap()
    id16_h = nc.dram_tensor("id16_h", [128, 128], F16, kind="ExternalInput").ap()
    id32_h = nc.dram_tensor("id32_h", [128, 128], F32, kind="ExternalInput").ap()
    out_h = nc.dram_tensor("out", [T, D], F32, kind="ExternalOutput").ap()

    kd = "ExternalOutput" if dbg else "Internal"
    x2_d = nc.dram_tensor("x2_d", [T, D], F32, kind=kd).ap()
    xn2_d = nc.dram_tensor("xn2_d", [T, D], F16, kind=kd).ap()
    xqt_d = nc.dram_tensor("xqt_d", [D, T], F16, kind=kd).ap()
    gate_d = nc.dram_tensor("gate_d", [I, T], F16, kind=kd).ap()
    up_d = nc.dram_tensor("up_d", [I, T], F16, kind=kd).ap()
    g2_d = nc.dram_tensor("g2_d", [I, T], F16, kind=kd).ap()
    p_d = nc.dram_tensor("p_d", [I, T], F16, kind=kd).ap()
    ccin_d = [nc.dram_tensor(f"ccin{k}", [1, 2], F32).ap() for k in range(4)]
    ccout_d = [nc.dram_tensor(f"ccout{k}", [1, 2], F32).ap() for k in range(4)]

    p = P(nc)
    es = ExitStack()
    for sname in list(p.cnt) + list(p.dcnt):
        p.sems[sname] = es.enter_context(nc.semaphore(f"s_{sname}"))

    A = nc.alloc_sbuf_tensor
    id16 = A("id16", [128, 128], F16)
    id32 = A("id32", [128, 128], F32)
    ones128 = A("ones128", [1, 128], F32)
    k2 = A("k2", [128, DC], F32)
    bin2 = A("bin2", [128, DC], F32)
    bg2 = A("bg2", [128, DC], F32)
    scal = A("scal", [128, 16], F32)
    srow = A("srow", [1, 16], F32)
    mcol = A("mcol", [128, 8], F32)
    mrow = A("mrow", [1, 128], F32)
    mcell = A("mcell", [1, 8], F32)
    lamt = A("lamt", [128, DC], F32)
    btmp = A("btmp", [128, DC], F32)
    bnbuf = A("bnbuf", [128, 12], F32)
    stats = A("stats", [128, 2], F32)
    colbuf = A("colbuf", [128, 16], F32)
    redcol = A("redcol", [128, 1], F32)
    redring = A("redring", [128, 40], F32)
    cst = A("cst", [128, 8], F32)

    SC = {"s0": 0, "i0": 1, "s2": 2, "i2": 3, "hs2": 4, "nhs2m": 5,
          "s1": 6, "i1": 7, "s3": 8, "i3": 9, "s4": 10, "i4": 11,
          "s2p16": 12, "eps4": 13, "s0b": 14}

    def scol(name):
        return scal[:, SC[name]:SC[name] + 1]

    d_id16 = p.dma("sp", "spq", lambda e: e.dma_start(out=id16[:, :], in_=id16_h))
    d_id32 = p.dma("sp", "spq", lambda e: e.dma_start(out=id32[:, :], in_=id32_h))
    d_lam = p.dma("sp", "spq", lambda e: e.dma_start(
        out=lamt[:, :], in_=lam_h.rearrange("o (c p) -> (o p) c", p=128),
        allow_slow_non_contiguous=True))
    d_bin = p.dma("sp", "spq", lambda e: e.dma_start(
        out=btmp[:, :], in_=bin_h.rearrange("o (c p) -> (o p) c", p=128),
        allow_slow_non_contiguous=True))
    t_ones = p.emit("dve", lambda e: e.memset(ones128[:, :], 1.0))
    t_c0 = p.emit("dve", lambda e: e.memset(cst[:, 0:1], 1e-5))
    t_c1 = p.emit("dve", lambda e: e.memset(cst[:, 1:2], 1.0))
    t_c2 = p.emit("dve", lambda e: e.memset(cst[:, 2:3], 0.25 * (1 + 1e-6)))
    t_c3 = p.emit("dve", lambda e: e.memset(cst[:, 3:4], MAGIC))
    t_c4 = p.emit("dve", lambda e: e.memset(cst[:, 4:5], 32768.0 + MAGIC))
    t_srz = p.emit("dve", lambda e: e.memset(srow[:, :], 0.0))
    t_mz = p.emit("dve", lambda e: e.memset(mcol[:, :], 0.0))
    t_e = p.emit("act", lambda e: e.activation(lamt[:, :], lamt[:, :], AF.Exp),
                 deps=[d_lam])
    t_l = p.emit("act", lambda e: e.activation(lamt[:, :], lamt[:, :], AF.Ln,
                                               bias=cst[:, 1:2]), deps=[t_e, t_c1])
    t_k2 = p.emit("dve", lambda e: e.tensor_scalar(
        k2[:, :], lamt[:, :], 4.0, None, ALU.mult), deps=[t_l])
    t_b1 = p.emit("dve", lambda e: e.tensor_scalar(
        bin2[:, :], btmp[:, :], 0.5, None, ALU.mult), deps=[d_bin])
    d_bg = p.dma("sp", "spq", lambda e: e.dma_start(
        out=btmp[:, :], in_=bg_h.rearrange("o (c p) -> (o p) c", p=128),
        allow_slow_non_contiguous=True), deps=[t_b1])
    t_b2 = p.emit("dve", lambda e: e.tensor_scalar(
        bg2[:, :], btmp[:, :], 0.5, None, ALU.mult), deps=[d_bg])
    pro_toks = [t_k2, t_b2, t_ones, t_mz, d_id16, d_id32, t_c0, t_c2, t_c3, t_c4]

    ln_prev = {"t5": None, "stats_r": None}
    reds = {"m0": [], "m2": [], "m1": [], "m3": [], "m4": []}
    RB = {"m0": 0, "m2": 8, "m1": 16, "m3": 24, "m4": 32}
    MC = {"m0": 0, "m2": 1, "m1": 2, "m3": 3, "m4": 4}
    last_fold = {}

    def _fold8(stream, nret):
        b = RB[stream]
        c = MC[stream]
        tr = p.emit("dve", lambda e, b=b, nret=nret: e.tensor_reduce(
            redcol[:, :], redring[:, b:b + nret], AX, ALU.max),
            deps=[reds[stream][-1], last_fold.get(stream, t_mz)])
        tm = p.emit("dve", lambda e, c=c: e.tensor_tensor(
            mcol[:, c:c + 1], mcol[:, c:c + 1], redcol[:, :], ALU.max),
            deps=[tr, t_mz])
        last_fold[stream] = tm
        return tm

    def emit_red(stream, src_ap, deps):
        k = RB[stream] + (len(reds[stream]) % 8)
        t = p.emit("dve", lambda e, k=k, src_ap=src_ap: e.tensor_reduce(
            redring[:, k:k + 1], src_ap, AX, ALU.max,
            apply_absolute_value=True),
            deps=list(deps) + [last_fold.get(stream)
                               if len(reds[stream]) % 8 == 0 else None])
        reds[stream].append(t)
        if len(reds[stream]) % 8 == 0:
            _fold8(stream, 8)
        return t

    def fold_ring(stream, col_idx):
        rem = len(reds[stream]) % 8
        if rem:
            _fold8(stream, rem)
        return last_fold[stream]
    colw = {}

    def ln_stats(src_ap_fn, negm_col, r_col, deps, extra_negmr=None,
                 eps_ap=None):
        war = [colw.get(r_col), ln_prev["t5"]]
        t1 = p.emit("dve", lambda e: e.bn_stats(bnbuf[:, 0:6], src_ap_fn(0)),
                    deps=list(deps) + [ln_prev["stats_r"]])
        t2 = p.emit("dve", lambda e: e.bn_stats(bnbuf[:, 6:12], src_ap_fn(512)),
                    deps=list(deps))
        t3 = p.emit("dve", lambda e: e.bn_aggr(stats[:, 0:2], bnbuf[:, 0:12]),
                    deps=[t1, t2])
        t4 = p.emit("dve", lambda e: e.tensor_scalar(
            colbuf[:, negm_col:negm_col + 1], stats[:, 0:1], -1.0, None,
            ALU.mult), deps=[t3])
        if eps_ap is None:
            eps_ap = cst[:, 0:1]
        t5 = p.emit("act", lambda e, eps_ap=eps_ap: e.activation(
            colbuf[:, r_col:r_col + 1], stats[:, 1:2], AF.Sqrt,
            bias=eps_ap), deps=[t3] + war)
        t6 = p.emit("dve", lambda e: e.reciprocal(
            colbuf[:, r_col:r_col + 1], colbuf[:, r_col:r_col + 1]), deps=[t5])
        ln_prev["t5"] = t5
        ln_prev["stats_r"] = t5
        toks = [t4, t6]
        if extra_negmr is not None:
            toks.append(p.emit("dve", lambda e: e.tensor_tensor(
                colbuf[:, extra_negmr:extra_negmr + 1],
                colbuf[:, negm_col:negm_col + 1],
                colbuf[:, r_col:r_col + 1], ALU.mult), deps=[t4, t6]))
        return toks

    def use_cols(tok, cols):
        for c in cols:
            colw[c] = tok

    def max_finalize(col_idx, cell_idx, ps_bc, deps):
        ttr = p.emit("pe", lambda e: e.transpose(
            ps_bc[0:1, 0:128], mcol[:, col_idx:col_idx + 1], id32[:, :]),
            deps=deps)
        tv = p.emit("dve", lambda e: e.tensor_copy(
            mrow[:, :], ps_bc[0:1, 0:128]), deps=[ttr])
        tr = p.emit("dve", lambda e: e.tensor_reduce(
            mcell[:, cell_idx:cell_idx + 1], mrow[:, :], AX, ALU.max),
            deps=[tv])
        return tr

    def barrier(kidx, cell0, n, scols, ps_bc, deps):
        toks = list(deps)
        if use_cc:
            d1 = p.dma("pool", "gpq", lambda e: e.dma_start(
                out=ccin_d[kidx][:, 0:n], in_=mcell[:, cell0:cell0 + n]),
                deps=toks)
            tcc = p.dma("pool", "ccq", lambda e: e.collective_compute(
                "AllReduce", ALU.max, replica_groups=[list(range(n_cores))],
                ins=[ccin_d[kidx][:, 0:n].opt()],
                outs=[ccout_d[kidx][:, 0:n].opt()]), deps=[d1])
            d2 = p.dma("pool", "gpq", lambda e: e.dma_start(
                out=mcell[:, cell0:cell0 + n], in_=ccout_d[kidx][:, 0:n]),
                deps=[tcc])
            toks = [d2]
        stoks = []
        for j in range(n):
            ts_ = p.emit("dve", lambda e, j=j: e.tensor_scalar(
                srow[:, 2 * j:2 * j + 1], mcell[:, cell0 + j:cell0 + j + 1],
                1.0 / QMAX, 1e-12, ALU.mult, ALU.max), deps=toks)
            ti_ = p.emit("dve", lambda e, j=j: e.reciprocal(
                srow[:, 2 * j + 1:2 * j + 2], srow[:, 2 * j:2 * j + 1]),
                deps=[ts_])
            stoks.append(ti_)
        tm = p.emit("pe", lambda e: e.matmul(
            ps_bc[0:128, 0:16], ones128[:, :], srow[:, :],
            start=True, stop=True), deps=stoks + [t_srz])
        evs = []
        for j, (sc_, ic_) in enumerate(scols):
            evs.append(p.emit("dve", lambda e, sc_=sc_, j=j: e.tensor_copy(
                scal[:, SC[sc_]:SC[sc_] + 1], ps_bc[0:128, 2 * j:2 * j + 1]),
                deps=[tm]))
            evs.append(p.emit("dve", lambda e, ic_=ic_, j=j: e.tensor_copy(
                scal[:, SC[ic_]:SC[ic_] + 1],
                ps_bc[0:128, 2 * j + 1:2 * j + 2]), deps=[tm]))
        return evs

    def prep_lpbq_chunk(w_hbm, dst, st_, ps_tr, ic, deps, ncols, prev_ev,
                        post_scale=None):
        ncc = ncols // 128
        nb = ncols // 32
        dld = p.dma("sp", "spq", lambda e: e.dma_start(
            out=st_["w32"][:, :ncols], in_=w_hbm[ic * 128:(ic + 1) * 128, :]),
            deps=deps)
        w3 = st_["w32"].ap()[:, :ncols].rearrange("p (b c) -> p b c", c=32)
        t1 = p.emit("dve", lambda e: e.tensor_reduce(
            st_["bmax"][:, :nb], w3, AX, ALU.max, apply_absolute_value=True),
            deps=[dld])
        t2 = p.emit("dve", lambda e: e.tensor_scalar(
            st_["sw"][:, :nb], st_["bmax"][:, :nb], 1.0 / 7.0, 1e-12,
            ALU.mult, ALU.max), deps=[t1])
        t3 = p.emit("dve", lambda e: e.reciprocal(
            st_["isw"][:, :nb], st_["sw"][:, :nb]), deps=[t2])
        if post_scale is not None:
            t2b = p.emit("dve", lambda e: e.tensor_scalar(
                st_["sw"][:, :nb], st_["sw"][:, :nb], post_scale, None,
                ALU.mult), deps=[t3])
            t3 = t2b
        isw3 = st_["isw"].ap()[:, :nb].rearrange("p (b o) -> p b o", o=1)
        sw3 = st_["sw"].ap()[:, :nb].rearrange("p (b o) -> p b o", o=1)
        z3 = st_["z"].ap()[:, :ncols].rearrange("p (b c) -> p b c", c=32)
        wq3 = st_["w16"].ap()[:, :ncols].rearrange("p (b c) -> p b c", c=32)
        a_, b_ = bass.broadcast_tensor_aps(w3, isw3)
        t4 = p.emit("dve", lambda e, a_=a_, b_=b_: e.tensor_tensor(
            z3, a_, b_, ALU.mult), deps=[t3])
        t5 = p.emit("dve", lambda e: e.tensor_scalar(
            st_["z"][:, :ncols], st_["z"][:, :ncols], MAGIC, -MAGIC,
            ALU.add, ALU.add), deps=[t4])
        a2_, b2_ = bass.broadcast_tensor_aps(z3, sw3)
        t6 = p.emit("dve", lambda e, a2_=a2_, b2_=b2_: e.tensor_tensor(
            wq3, a2_, b2_, ALU.mult), deps=[t5])
        ev_toks = list(prev_ev)
        tr_toks = []
        for dc in range(ncc):
            half = (dc // 4) % 2
            tr_toks.append(p.emit("pe", lambda e, dc=dc, half=half:
                e.transpose(
                    ps_tr[half][:, (dc % 4) * 128:(dc % 4 + 1) * 128],
                    st_["w16"][:, dc * 128:(dc + 1) * 128], id16[:, :]),
                deps=[t6] + (ev_toks[-2:-1] if len(ev_toks) >= 2
                             else ev_toks)))
            if dc % 4 == 3:
                ev_toks.append(p.emit("act", lambda e, dc=dc, half=half,
                                      ic=ic, ncc=ncc: e.copy(
                    dst[:, (ic * ncc + dc - 3) * 128:
                        (ic * ncc + dc + 1) * 128],
                    ps_tr[half][:, :]), deps=tr_toks[-4:]))
        return ev_toks[-1], t6, ev_toks[len(prev_ev):]

    ctx_w = ExitStack()
    winT = ctx_w.enter_context(nc.sbuf_tensor("winT", [128, DC * DC * 128], F16))
    wgT = ctx_w.enter_context(nc.sbuf_tensor("wgT", [128, DC * DC * 128], F16))

    ctx1 = ExitStack()
    wst = {k: ctx1.enter_context(nc.sbuf_tensor(f"p_{k}", [128, n], dt))
           for k, n, dt in [("w32", D, F32), ("w16", D, F16), ("z", D, F32),
                            ("bmax", D // 32, F32), ("sw", D // 32, F32),
                            ("isw", D // 32, F32)]}
    xs = [ctx1.enter_context(nc.sbuf_tensor(f"xs{i}", [128, D], F32))
          for i in range(2 * NTT)]
    xn_st = ctx1.enter_context(nc.sbuf_tensor("xn_st", [128, NTT * D], F16))
    xnT = ctx1.enter_context(nc.sbuf_tensor("xnT", [128, DC * S], F16))
    th_rg = ctx1.enter_context(nc.sbuf_tensor("th_rg", [128, DC * S], F16))
    th_ig = ctx1.enter_context(nc.sbuf_tensor("th_ig", [128, DC * S], F16))
    g_t = ctx1.enter_context(nc.sbuf_tensor("g_t", [128, DC * S], F32))
    al_t = ctx1.enter_context(nc.sbuf_tensor("al_t", [128, DC * S], F32))
    bh_t = ctx1.enter_context(nc.sbuf_tensor("bh_t", [128, DC * S], F16))
    u1_t = ctx1.enter_context(nc.sbuf_tensor("u1_t", [128, DC * S], F16))
    xb_t = ctx1.enter_context(nc.sbuf_tensor("xb_t", [128, DC * S], F16))
    h_t = [ctx1.enter_context(nc.sbuf_tensor(f"h{i}", [128, DC * S], F32))
           for i in range(2)]
    x2_tile = ctx1.enter_context(nc.sbuf_tensor("x2_tile", [128, D], F32))
    xm_tile = ctx1.enter_context(nc.sbuf_tensor("xm_tile", [128, D], F32))
    xn2_tile = ctx1.enter_context(nc.sbuf_tensor("xn2_t", [128, D], F16))

    ps1 = ExitStack()
    ps_tr = [ps1.enter_context(nc.psum_tensor(f"ps_tr{i}", [128, 512], F16))
             for i in range(2)]
    ps_g = [ps1.enter_context(nc.psum_tensor(f"ps_g{i}", [128, 512], F32))
            for i in range(2)]
    ps_h = ps1.enter_context(nc.psum_tensor("ps_h", [128, 1024], F32))
    ps_bc = ps1.enter_context(nc.psum_tensor("ps_bc", [128, 128], F32))

    def prep_square(w_hbm, dst, deps):
        toks = []
        last = None
        for dpc in range(DC):
            dld = p.dma("sp", "spq", lambda e, dpc=dpc: e.dma_start(
                out=wst["w32"][:, :], in_=w_hbm[dpc * 128:(dpc + 1) * 128, :]),
                deps=deps + ([last] if last else []))
            tc = p.emit("dve", lambda e: e.tensor_copy(
                wst["w16"][:, :], wst["w32"][:, :]), deps=[dld])
            last = tc
            tr_toks = []
            for dc in range(DC):
                half = (dc // 4) % 2
                tr_toks.append(p.emit("pe", lambda e, dc=dc, half=half:
                    e.transpose(
                        ps_tr[half][:, (dc % 4) * 128:(dc % 4 + 1) * 128],
                        wst["w16"][:, dc * 128:(dc + 1) * 128], id16[:, :]),
                    deps=[tc] + (toks[-2:-1] if len(toks) >= 2 else [])))
                if dc % 4 == 3:
                    toks.append(p.emit("act", lambda e, dc=dc, half=half,
                                       dpc=dpc: e.copy(
                        dst[:, (dpc * DC + dc - 3) * 128:
                            (dpc * DC + dc + 1) * 128],
                        ps_tr[half][:, :]), deps=tr_toks[-4:]))
            last = toks[-1]
        return toks[-1]

    tw1 = prep_square(win_h, winT, pro_toks)
    tw2 = prep_square(wg_h, wgT, [tw1])

    CB = {"negm1": 0, "r1": 4, "negm2": 8, "r2": 9, "negm2r2": 10,
          "negm3": 11, "r3": 12}

    x_dma = {}
    x2_dma = {}
    xn2_dma = {}
    t_scan = {}
    t_x_read = {}
    war = {
        "x2t_dma": None,
        "xn_st": None,
        "xnT": None,
        "xnT_dve": None,
        "th_rg": None, "th_ig": None,
        "g": {}, "al": {},
        "xb": None,
        "h_tr": [None, None],
        "x2t": None, "xn2t": None, "xn2t_pool": None,
        "proj_tail": None,
    }

    for st0 in range(min(2, NST)):
        for tt in range(NTT):
            x_dma[(st0, tt)] = p.dma(
                "sp", "spq", lambda e, st0=st0, tt=tt: e.dma_start(
                    out=xs[(st0 % 2) * NTT + tt][:, :],
                    in_=x_h[(st0 * S + tt * 128):(st0 * S + tt * 128 + 128), :]))

    def emit_tail(st2, scan_toks, hc2):
        t_ht_last = None
        prev_x2_stt = war.get("ps_h_reader")
        for tt in range(NTT):
            trh = []
            for dc in range(DC):
                trh.append(p.emit("pe", lambda e, dc=dc, tt=tt, hc2=hc2:
                    e.transpose(
                        ps_h[:, dc * 128:(dc + 1) * 128],
                        hc2[:, dc * S + tt * 128:dc * S + tt * 128 + 128],
                        id32[:, :]),
                    deps=[scan_toks[dc]] +
                         ([prev_x2_stt] if prev_x2_stt else [])))
            lt2 = ln_stats(lambda lo: ps_h[:, lo:lo + 512], CB["negm2"],
                           CB["r2"], trh, extra_negmr=CB["negm2r2"])
            xb_ = xs[(st2 % 2) * NTT + tt]
            txm = p.emit("dve", lambda e, xb_=xb_: e.tensor_scalar(
                xm_tile[:, :], xb_[:, :],
                colbuf[:, CB["negm2r2"]:CB["negm2r2"] + 1], None, ALU.add),
                deps=lt2)
            t_x_read[(st2, tt)] = txm
            tx2 = p.emit("dve", lambda e: e.scalar_tensor_tensor(
                x2_tile[:, :], ps_h[:, :],
                colbuf[:, CB["r2"]:CB["r2"] + 1], xm_tile[:, :],
                ALU.mult, ALU.add),
                deps=[txm] + lt2 + [war["x2t"], war["x2t_dma"]])
            use_cols(tx2, [CB["negm2"], CB["r2"], CB["negm2r2"]])
            prev_x2_stt = tx2
            war["ps_h_reader"] = tx2
            x2_dma[(st2, tt)] = p.dma("pool", "gpq",
                lambda e, st2=st2, tt=tt: e.dma_start(
                    out=x2_d[(st2 * S + tt * 128):
                             (st2 * S + tt * 128 + 128), :],
                    in_=x2_tile[:, :]), deps=[tx2])
            lt3 = ln_stats(lambda lo: x2_tile[:, lo:lo + 512], CB["negm3"],
                           CB["r3"], [tx2])
            txn2 = p.emit("dve", lambda e: e.tensor_scalar(
                xn2_tile[:, :], x2_tile[:, :],
                colbuf[:, CB["negm3"]:CB["negm3"] + 1],
                colbuf[:, CB["r3"]:CB["r3"] + 1], ALU.add, ALU.mult),
                deps=lt3 + [war["xn2t"], war["xn2t_pool"]])
            use_cols(txn2, [CB["negm3"], CB["r3"]])
            war["x2t"] = txn2
            war["x2t_dma"] = x2_dma[(st2, tt)]
            tred = emit_red("m0", xn2_tile[:, :], [txn2])
            xn2_dma[(st2, tt)] = p.dma("pool", "gpq",
                lambda e, st2=st2, tt=tt: e.dma_start(
                    out=xn2_d[(st2 * S + tt * 128):
                              (st2 * S + tt * 128 + 128), :],
                    in_=xn2_tile[:, :]), deps=[txn2])
            war["xn2t"] = xn2_dma[(st2, tt)]
            war["xn2t_pool"] = tred
            t_ht_last = trh[-1]
        war["h_tr"][st2 % 2] = t_ht_last

    pend_tail = None
    for st in range(NST):
        t_xn = {}
        for tt in range(NTT):
            xb_ = xs[(st % 2) * NTT + tt]
            lt = ln_stats(lambda lo, xb_=xb_: xb_[:, lo:lo + 512],
                          CB["negm1"] + tt, CB["r1"] + tt,
                          [x_dma[(st, tt)]])
            t_xn[tt] = p.emit("dve", lambda e, tt=tt, xb_=xb_: e.tensor_scalar(
                xn_st[:, tt * D:(tt + 1) * D], xb_[:, :],
                colbuf[:, CB["negm1"] + tt:CB["negm1"] + tt + 1],
                colbuf[:, CB["r1"] + tt:CB["r1"] + tt + 1],
                ALU.add, ALU.mult), deps=lt + [war["xn_st"]])
            use_cols(t_xn[tt], [CB["negm1"] + tt, CB["r1"] + tt])

        t_xnT = {}
        ev_list = []
        tr_last = None
        for dc in range(DC):
            half = dc % 2
            trk = []
            for tt in range(NTT):
                trk.append(p.emit("pe", lambda e, dc=dc, tt=tt, half=half:
                    e.transpose(
                        ps_tr[half][:, tt * 128:(tt + 1) * 128],
                        xn_st[:, tt * D + dc * 128:tt * D + dc * 128 + 128],
                        id16[:, :]),
                    deps=[t_xn[tt], tw2] +
                         (ev_list[-2:-1] if len(ev_list) >= 2 else [])))
            tr_last = trk[-1]
            ev_list.append(p.emit("act", lambda e, dc=dc, half=half: e.copy(
                xnT[:, dc * S:(dc + 1) * S], ps_tr[half][:, :]),
                deps=trk + [war["xnT"], war["xnT_dve"]]))
            t_xnT[dc] = ev_list[-1]
        war["xn_st"] = tr_last

        def proj(wT, th_dst, bias_t, prev_tail, th_war):
            evs = {}
            mm_last = None
            for c in range(DC):
                mms = []
                for dc in range(DC):
                    deps = [t_xnT[dc]]
                    if c >= 2:
                        deps.append(evs[c - 2])
                    elif prev_tail is not None:
                        deps.append(prev_tail.get(c - 2 + DC))
                    mms.append(p.emit("pe",
                        lambda e, c=c, dc=dc, wT=wT: e.matmul(
                            ps_g[c % 2][:, :],
                            wT[:, (c * DC + dc) * 128:(c * DC + dc + 1) * 128],
                            xnT[:, dc * S:(dc + 1) * S],
                            start=(dc == 0), stop=(dc == DC - 1)),
                        deps=deps))
                evs[c] = p.emit("act", lambda e, c=c, bias_t=bias_t,
                                th_dst=th_dst: e.activation(
                    th_dst[:, c * S:(c + 1) * S], ps_g[c % 2][:, :], AF.Tanh,
                    bias=bias_t[:, c:c + 1], scale=0.5),
                    deps=[mms[-1], th_war])
                mm_last = mms[-1]
            return evs, mm_last

        ev_rg, mm_l1 = proj(wgT, th_rg, bg2, war["proj_tail"], war["th_rg"])

        if pend_tail is not None:
            emit_tail(*pend_tail)
        pend_tail = None

        t_bh = {}
        t_al = {}
        t_g = {}
        for c in range(DC):
            t_g[c] = p.emit("dve", lambda e, c=c: e.tensor_scalar(
                g_t[:, c * S:(c + 1) * S], th_rg[:, c * S:(c + 1) * S],
                1.0, k2[:, c:c + 1], ALU.add, ALU.mult),
                deps=[ev_rg[c], war["g"].get(c)])
            t_al[c] = p.emit("act", lambda e, c=c: e.activation(
                al_t[:, c * S:(c + 1) * S], g_t[:, c * S:(c + 1) * S],
                AF.Exp, scale=-1.0), deps=[t_g[c], war["al"].get(c)])
            ta2 = p.emit("act", lambda e, c=c: e.activation(
                g_t[:, c * S:(c + 1) * S], g_t[:, c * S:(c + 1) * S],
                AF.Exp, scale=-2.0), deps=[t_al[c]])
            t_bh[c] = p.emit("act", lambda e, c=c: e.activation(
                bh_t[:, c * S:(c + 1) * S], g_t[:, c * S:(c + 1) * S],
                AF.Sqrt, scale=-0.25, bias=cst[:, 2:3]),
                deps=[ta2, war["xb"]])
        war["th_rg"] = t_g[DC - 1]

        ev_ig, mm_l2 = proj(winT, th_ig, bin2, ev_rg, war["th_ig"])
        war["proj_tail"] = ev_ig
        war["xnT"] = mm_l2

        hc = h_t[st % 2]
        hp = h_t[(st - 1) % 2]
        new_scan = {}
        t_u_last = None
        for c in range(DC):
            tu = p.emit("dve", lambda e, c=c: e.scalar_tensor_tensor(
                u1_t[:, c * S:(c + 1) * S], th_ig[:, c * S:(c + 1) * S],
                1.0, xnT[:, c * S:(c + 1) * S], ALU.add, ALU.mult),
                deps=[ev_ig[c], t_xnT[c]])
            tx = p.emit("dve", lambda e, c=c: e.tensor_tensor(
                xb_t[:, c * S:(c + 1) * S], u1_t[:, c * S:(c + 1) * S],
                bh_t[:, c * S:(c + 1) * S], ALU.mult), deps=[tu, t_bh[c]])
            if st == 0:
                init = 0.0
                dep_init = []
            else:
                init = hp[:, c * S + S - 1:c * S + S]
                dep_init = [t_scan[c]]
            new_scan[c] = p.emit("dve", lambda e, c=c, init=init, hc=hc:
                e.tensor_tensor_scan(
                    hc[:, c * S:(c + 1) * S], al_t[:, c * S:(c + 1) * S],
                    xb_t[:, c * S:(c + 1) * S], init, ALU.mult, ALU.add),
                deps=[tx, t_al[c], war["h_tr"][st % 2]] + dep_init)
            war["g"][c] = new_scan[c]
            war["al"][c] = new_scan[c]
            t_u_last = tu
        t_scan = new_scan
        war["th_ig"] = t_u_last
        war["xnT_dve"] = t_u_last
        war["xb"] = new_scan[DC - 1]

        pend_tail = (st, new_scan, hc)
        if st >= 1 and st + 1 < NST:
            for tt in range(NTT):
                x_dma[(st + 1, tt)] = p.dma(
                    "sp", "spq", lambda e, st=st, tt=tt: e.dma_start(
                        out=xs[((st + 1) % 2) * NTT + tt][:, :],
                        in_=x_h[((st + 1) * S + tt * 128):
                                ((st + 1) * S + tt * 128 + 128), :]),
                    deps=[t_x_read.get((st - 1, tt))])

    if pend_tail is not None:
        emit_tail(*pend_tail)
    t_fold0 = fold_ring("m0", 0)
    t_fm0 = max_finalize(0, 0, ps_bc, [t_fold0])
    b0 = barrier(0, 0, 1, [("s0", "i0")], ps_bc, [t_fm0])
    t_s0b = p.emit("dve", lambda e: e.tensor_scalar(
        scol("s0b"), scol("s0"), 65536.0, None, ALU.mult), deps=b0)
    b0 = b0 + [t_s0b]

    ps1.close()
    ctx1.close()
    ctx_w.close()
    p.fence()

    ctx_wu1 = ExitStack()
    wupT = ctx_wu1.enter_context(nc.sbuf_tensor("wupT", [128, IC * DC * 128], F16))
    ctx_wu2 = ExitStack()
    wgfT = ctx_wu2.enter_context(nc.sbuf_tensor("wgfT", [128, IC * DC * 128], F16))

    ctx2 = ExitStack()
    wst2 = {k: ctx2.enter_context(nc.sbuf_tensor(f"q_{k}", [128, n], dt))
            for k, n, dt in [("w32", D, F32), ("w16", D, F16), ("z", D, F32),
                             ("bmax", D // 32, F32), ("sw", D // 32, F32),
                             ("isw", D // 32, F32)]}
    RBD = 6
    xn2_sb = [ctx2.enter_context(nc.sbuf_tensor(f"xn2_sb{i}", [128, NTT * D],
                                                F16)) for i in range(2)]
    t0_sb = [ctx2.enter_context(nc.sbuf_tensor(f"t0_sb{i}", [128, D], F32))
             for i in range(3)]
    xq_sb = [ctx2.enter_context(nc.sbuf_tensor(f"xq_sb{i}", [128, NTT * D],
                                               F16)) for i in range(2)]
    xqT = [ctx2.enter_context(nc.sbuf_tensor(f"xqT2_{i}", [128, DC * S], F16))
           for i in range(2)]
    gw = [ctx2.enter_context(nc.sbuf_tensor(f"gw{i}", [128, S], F16))
          for i in range(RBD)]

    ps2 = ExitStack()
    ps_tr2 = [ps2.enter_context(nc.psum_tensor(f"ps2_tr{i}", [128, 512], F16))
              for i in range(2)]
    ps_tr2w = [ps2.enter_context(nc.psum_tensor(f"ps2_tw{i}", [128, 512], F16))
               for i in range(2)]
    ps_g2 = [ps2.enter_context(nc.psum_tensor(f"ps2_g{i}", [128, 512], F32))
             for i in range(2)]
    ps_bc2 = ps2.enter_context(nc.psum_tensor("ps_bc2", [128, 128], F32))

    t_wgf = {}
    t_wup = {}
    prev_ev_track = []
    prev_dep = b0
    for ic_ in range(IC):
        ev, _, new_evs = prep_lpbq_chunk(wgf_h, wgfT, wst2, ps_tr2w, ic_,
                                         prev_dep, D, prev_ev_track,
                                         post_scale=scol("s0b"))
        t_wgf[ic_] = ev
        prev_ev_track = (prev_ev_track + new_evs)[-2:]
        prev_dep = [ev]

    t_gdma = {}
    t_gev = {}
    up_sched = {}
    nup = 0
    for st in range(NST):
        k = 2 if NST >= IC // 2 else (IC + NST - 1) // NST
        for _ in range(k):
            if nup < IC:
                up_sched.setdefault(st, []).append(nup)
                nup += 1
    if nup < IC:
        up_sched.setdefault(NST - 1, []).extend(range(nup, IC))

    war2 = {"t0": [None] * 3}
    lastrd_xn2 = {}
    lastrd_xq = {}
    mml2 = {}
    gdma_f = {}
    t_xq_dma = {}
    for st in range(NST):
        par = st % 2
        dx = {}
        for tt in range(NTT):
            dx[tt] = p.dma("sp", "spq",
                lambda e, st=st, tt=tt, par=par: e.dma_start(
                    out=xn2_sb[par][:, tt * D:(tt + 1) * D],
                    in_=xn2_d[(st * S + tt * 128):(st * S + tt * 128 + 128), :]),
                deps=[xn2_dma[(st, tt)], lastrd_xn2.get(st - 2)] + b0)
        t_xq = {}
        last_tt0 = None
        for tt in range(NTT):
            kt = (st * NTT + tt) % 3
            tt0 = p.emit("act", lambda e, tt=tt, par=par, kt=kt: e.activation(
                t0_sb[kt][:, :], xn2_sb[par][:, tt * D:(tt + 1) * D], AF.Relu,
                scale=scol("i0"), bias=cst[:, 3:4]),
                deps=[dx[tt], war2["t0"][kt]])
            last_tt0 = tt0
            t_xq[tt] = p.emit("dve", lambda e, tt=tt, par=par, kt=kt:
                e.tensor_scalar(
                    xq_sb[par][:, tt * D:(tt + 1) * D], t0_sb[kt][:, :],
                    -MAGIC, None, ALU.add),
                deps=[tt0, lastrd_xq.get(st - 2)])
            war2["t0"][kt] = t_xq[tt]
        lastrd_xn2[st] = last_tt0
        t_xqT = {}
        ev_list = []
        tr_last = None
        for dc in range(DC):
            half = dc % 2
            trk = []
            for tt in range(NTT):
                trk.append(p.emit("pe",
                    lambda e, dc=dc, tt=tt, half=half, par=par: e.transpose(
                        ps_tr2[half][:, tt * 128:(tt + 1) * 128],
                        xq_sb[par][:, tt * D + dc * 128:tt * D + dc * 128 + 128],
                        id16[:, :]),
                    deps=[t_xq[tt], t_wgf[IC - 1]] +
                         (ev_list[-2:-1] if len(ev_list) >= 2 else [])))
            tr_last = trk[-1]
            ev_list.append(p.emit("act",
                lambda e, dc=dc, half=half, par=par: e.copy(
                    xqT[par][:, dc * S:(dc + 1) * S], ps_tr2[half][:, :]),
                deps=trk + [mml2.get(st - 2), t_xq_dma.get((st - 2, dc))]))
            t_xqT[dc] = ev_list[-1]
            t_xq_dma[(st, dc)] = p.dma("act", "acq",
                lambda e, dc=dc, st=st, par=par: e.dma_start(
                    out=xqt_d[dc * 128:(dc + 1) * 128, st * S:(st + 1) * S],
                    in_=xqT[par][:, dc * S:(dc + 1) * S]), deps=[ev_list[-1]])
        lastrd_xq[st] = tr_last

        mm_last = None
        for ic_ in range(IC):
            icg = st * IC + ic_
            kb = icg % RBD
            mms = []
            for dc in range(DC):
                deps = [t_xqT[dc], t_wgf[ic_]]
                prev_ev = t_gev.get((st, ic_ - 2), t_gev.get((st - 1, IC - 2 + ic_)))
                if prev_ev is not None:
                    deps.append(prev_ev)
                mms.append(p.emit("pe",
                    lambda e, ic_=ic_, dc=dc, par=par: e.matmul(
                        ps_g2[ic_ % 2][:, :],
                        wgfT[:, (ic_ * DC + dc) * 128:(ic_ * DC + dc + 1) * 128],
                        xqT[par][:, dc * S:(dc + 1) * S],
                        start=(dc == 0), stop=(dc == DC - 1)), deps=deps))
            mm_last = mms[-1]
            tev = p.emit("act", lambda e, ic_=ic_, kb=kb: e.activation(
                gw[kb][:, :], ps_g2[ic_ % 2][:, :], AF.Copy,
                scale=2.0 ** -16),
                deps=[mms[-1], gdma_f.get(icg - RBD)])
            t_gev[(st, ic_)] = tev
            tred = emit_red("m2", gw[kb][:, :], [tev])
            t_gdma[(st, ic_)] = p.dma("act", "acq",
                lambda e, ic_=ic_, st=st, kb=kb: e.dma_start(
                    out=gate_d[ic_ * 128:(ic_ + 1) * 128, st * S:(st + 1) * S],
                    in_=gw[kb][:, :]), deps=[tev, tred])
            gdma_f[icg] = t_gdma[(st, ic_)]
        mml2[st] = mm_last
        for ic_ in up_sched.get(st, []):
            ev, _, new_evs = prep_lpbq_chunk(
                wup_h, wupT, wst2, ps_tr2w, ic_,
                [t_wgf[IC - 1], t_wup.get(ic_ - 1)] + b0, D, prev_ev_track,
                post_scale=scol("s0b"))
            t_wup[ic_] = ev
            prev_ev_track = (prev_ev_track + new_evs)[-2:]

    t_fold2 = fold_ring("m2", 1)
    t_fm2 = max_finalize(1, 1, ps_bc2, [t_fold2])
    b1 = barrier(1, 1, 1, [("s2", "i2")], ps_bc2, [t_fm2])
    t_hs2 = p.emit("dve", lambda e: e.tensor_scalar(
        scol("hs2"), scol("s2"), 0.5, None, ALU.mult), deps=b1)
    t_nh = p.emit("dve", lambda e: e.tensor_scalar(
        scol("nhs2m"), scol("hs2"), -MAGIC, None, ALU.mult), deps=[t_hs2])
    t_p16 = p.emit("dve", lambda e: e.tensor_scalar(
        scol("s2p16"), scol("s2"), 2.0 ** -16, None, ALU.mult), deps=b1)
    b1_all = b1 + [t_hs2, t_nh, t_p16]

    ps2.close()
    ctx2.close()
    ctx_wu2.close()
    p.fence()

    ctx3 = ExitStack()
    RBD = 6
    RCD = 3
    xqT3 = [ctx3.enter_context(nc.sbuf_tensor(f"xqT3_{i}", [128, DC * S], F16))
            for i in range(2)]
    uw = [ctx3.enter_context(nc.sbuf_tensor(f"uw{i}", [128, S], F16))
          for i in range(RBD)]
    gld = [ctx3.enter_context(nc.sbuf_tensor(f"gld{i}", [128, S], F16))
           for i in range(RBD)]
    t2w = [ctx3.enter_context(nc.sbuf_tensor(f"t2w{i}", [128, S], F32))
           for i in range(RCD)]
    th3w = [ctx3.enter_context(nc.sbuf_tensor(f"th3w{i}", [128, S], F32))
            for i in range(RCD)]
    w1w = [ctx3.enter_context(nc.sbuf_tensor(f"w1w{i}", [128, S], F32))
           for i in range(RCD)]
    sqw = [ctx3.enter_context(nc.sbuf_tensor(f"sqw{i}", [128, S], F32))
           for i in range(RCD)]
    gqw = [ctx3.enter_context(nc.sbuf_tensor(f"gqw{i}", [128, S], F16))
           for i in range(RCD)]
    g2w = [ctx3.enter_context(nc.sbuf_tensor(f"g2w{i}", [128, S], F16))
           for i in range(RBD)]

    ps3 = ExitStack()
    ps_u = [ps3.enter_context(nc.psum_tensor(f"ps_u{i}", [128, 512], F32))
            for i in range(2)]
    ps_bc3 = ps3.enter_context(nc.psum_tensor("ps_bc3", [128, 128], F32))

    t_udma = {}
    t_uev = {}
    t_g2dma = {}
    udma_f = {}
    g2dma_f = {}
    tt2_f = {}
    mml3 = {}
    war3 = {"t2": [None] * RCD, "th3": [None] * RCD, "w1": [None] * RCD,
            "sq": [None] * RCD, "gq": [None] * RCD}
    for st in range(NST):
        par = st % 2
        dxq = {}
        for dc in range(DC):
            dxq[dc] = p.dma("sp", "spq",
                lambda e, dc=dc, st=st, par=par: e.dma_start(
                    out=xqT3[par][:, dc * S:(dc + 1) * S],
                    in_=xqt_d[dc * 128:(dc + 1) * 128, st * S:(st + 1) * S]),
                deps=[t_xq_dma[(st, dc)], mml3.get(st - 2)] + b1_all)
        mm_last = None
        for ic_ in range(IC):
            icg = st * IC + ic_
            kb = icg % RBD
            kc = icg % RCD
            mms = []
            for dc in range(DC):
                deps = [dxq[dc], t_wup[ic_]]
                prev_ev = t_uev.get((st, ic_ - 2), t_uev.get((st - 1, IC - 2 + ic_)))
                if prev_ev is not None:
                    deps.append(prev_ev)
                mms.append(p.emit("pe",
                    lambda e, ic_=ic_, dc=dc, par=par: e.matmul(
                        ps_u[ic_ % 2][:, :],
                        wupT[:, (ic_ * DC + dc) * 128:(ic_ * DC + dc + 1) * 128],
                        xqT3[par][:, dc * S:(dc + 1) * S],
                        start=(dc == 0), stop=(dc == DC - 1)), deps=deps))
            mm_last = mms[-1]
            tev = p.emit("act", lambda e, ic_=ic_, kb=kb: e.activation(
                uw[kb][:, :], ps_u[ic_ % 2][:, :], AF.Copy,
                scale=2.0 ** -16),
                deps=[mms[-1], udma_f.get(icg - RBD)])
            t_uev[(st, ic_)] = tev
            tred1 = emit_red("m1", uw[kb][:, :], [tev])
            t_udma[(st, ic_)] = p.dma("sp", "spq",
                lambda e, ic_=ic_, st=st, kb=kb: e.dma_start(
                    out=up_d[ic_ * 128:(ic_ + 1) * 128, st * S:(st + 1) * S],
                    in_=uw[kb][:, :]), deps=[tev, tred1])
            udma_f[icg] = t_udma[(st, ic_)]

            dg = p.dma("sp", "spq", lambda e, ic_=ic_, st=st, kb=kb: e.dma_start(
                out=gld[kb][:, :],
                in_=gate_d[ic_ * 128:(ic_ + 1) * 128, st * S:(st + 1) * S]),
                deps=[t_gdma[(st, ic_)], tt2_f.get(icg - RBD)])
            tt2 = p.emit("pool", lambda e, kb=kb, kc=kc: e.tensor_scalar(
                t2w[kc][:, :], gld[kb][:, :], scol("i2"), MAGIC,
                ALU.mult, ALU.add),
                deps=[dg, war3["t2"][kc]])
            tt2_f[icg] = tt2
            tth3 = p.emit("act", lambda e, kc=kc: e.activation(
                th3w[kc][:, :], t2w[kc][:, :], AF.Tanh, bias=scol("nhs2m"),
                scale=scol("hs2")), deps=[tt2, war3["th3"][kc]])
            tw1_ = p.emit("act", lambda e, kc=kc: e.activation(
                w1w[kc][:, :], th3w[kc][:, :], AF.Relu, scale=32768.0,
                bias=cst[:, 4:5]), deps=[tth3, war3["w1"][kc]])
            war3["th3"][kc] = tw1_
            tsq = p.emit("pool", lambda e, kc=kc: e.tensor_scalar(
                sqw[kc][:, :], w1w[kc][:, :], -MAGIC, 65535.0,
                ALU.add, ALU.min), deps=[tw1_, war3["sq"][kc]])
            war3["w1"][kc] = tsq
            tm2 = p.emit("pool", lambda e, kc=kc: e.tensor_scalar(
                gqw[kc][:, :], t2w[kc][:, :], -MAGIC, None, ALU.add),
                deps=[tt2, war3["gq"][kc]])
            war3["t2"][kc] = tm2
            tg2 = p.emit("dve", lambda e, kb=kb, kc=kc: e.scalar_tensor_tensor(
                g2w[kb][:, :], sqw[kc][:, :], scol("s2p16"), gqw[kc][:, :],
                ALU.mult, ALU.mult),
                deps=[tsq, tm2, g2dma_f.get(icg - RBD)])
            war3["sq"][kc] = tg2
            war3["gq"][kc] = tg2
            tred3 = emit_red("m3", g2w[kb][:, :], [tg2])
            dg2 = p.dma("act", "acq",
                lambda e, ic_=ic_, st=st, kb=kb: e.dma_start(
                    out=g2_d[ic_ * 128:(ic_ + 1) * 128, st * S:(st + 1) * S],
                    in_=g2w[kb][:, :]), deps=[tg2, tred3])
            t_g2dma[(st, ic_)] = dg2
            g2dma_f[icg] = dg2
        mml3[st] = mm_last

    t_fold1 = fold_ring("m1", 2)
    t_fm1 = max_finalize(2, 2, ps_bc3, [t_fold1])
    t_fold3 = fold_ring("m3", 3)
    t_fm3 = max_finalize(3, 3, ps_bc3, [t_fold3, t_fm1])
    b2 = barrier(2, 2, 2, [("s1", "i1"), ("s3", "i3")], ps_bc3, [t_fm3])

    ps3.close()
    ctx3.close()
    ctx_wu1.close()
    p.fence()

    ctx_wd = ExitStack()
    wdT = ctx_wd.enter_context(nc.sbuf_tensor("wdT", [128, DC * IC * 128], F16))
    ctx4 = ExitStack()
    wst4 = {k: ctx4.enter_context(nc.sbuf_tensor(f"r_{k}", [128, n], dt))
            for k, n, dt in [("w32", I, F32), ("w16", I, F16), ("z", I, F32),
                             ("bmax", I // 32, F32), ("sw", I // 32, F32),
                             ("isw", I // 32, F32)]}
    R4 = 10
    R3 = 3
    RP = 8
    u4 = [ctx4.enter_context(nc.sbuf_tensor(f"u4_{i}", [128, S], F16))
          for i in range(R4)]
    g4 = [ctx4.enter_context(nc.sbuf_tensor(f"g4_{i}", [128, S], F16))
          for i in range(R4)]
    t4w = [ctx4.enter_context(nc.sbuf_tensor(f"t4w{i}", [128, S], F32))
           for i in range(R3)]
    gate2w = [ctx4.enter_context(nc.sbuf_tensor(f"gate2w{i}", [128, S], F16))
              for i in range(R3)]
    tuw = [ctx4.enter_context(nc.sbuf_tensor(f"tuw{i}", [128, S], F32))
           for i in range(R3)]
    upqw = [ctx4.enter_context(nc.sbuf_tensor(f"upqw{i}", [128, S], F16))
            for i in range(R3)]
    pw = [ctx4.enter_context(nc.sbuf_tensor(f"pw{i}", [128, S], F16))
          for i in range(RP)]

    ps4 = ExitStack()
    ps_tr4 = [ps4.enter_context(nc.psum_tensor(f"ps4_tr{i}", [128, 512], F16))
              for i in range(2)]
    ps_bc4 = ps4.enter_context(nc.psum_tensor("ps_bc4", [128, 128], F32))

    t_wd = {}
    prev_dep = b2
    prev_ev_track = []
    for dpc in range(DC):
        ev, _, new_evs = prep_lpbq_chunk(wd_h, wdT, wst4, ps_tr4, dpc,
                                         prev_dep, I, prev_ev_track)
        t_wd[dpc] = ev
        prev_ev_track = (prev_ev_track + new_evs)[-2:]
        prev_dep = [ev]

    t_pdma = {}
    war4 = {"u4": [None] * R4, "g4": [None] * R4, "t4": [None] * R3,
            "gate2": [None] * R3, "tu": [None] * R3, "upq": [None] * R3,
            "pw": [None] * RP}
    for st in range(NST):
        for ic_ in range(IC):
            k4 = (st * IC + ic_) % R4
            k3 = (st * IC + ic_) % R3
            kp = (st * IC + ic_) % RP
            du = p.dma("sp", "spq", lambda e, ic_=ic_, st=st, k4=k4: e.dma_start(
                out=u4[k4][:, :],
                in_=up_d[ic_ * 128:(ic_ + 1) * 128, st * S:(st + 1) * S]),
                deps=[t_udma[(st, ic_)], t_wd[DC - 1], war4["u4"][k4]])
            dg = p.dma("act", "acq", lambda e, ic_=ic_, st=st, k4=k4: e.dma_start(
                out=g4[k4][:, :],
                in_=g2_d[ic_ * 128:(ic_ + 1) * 128, st * S:(st + 1) * S]),
                deps=[t_g2dma[(st, ic_)], war4["g4"][k4]])
            tt4 = p.emit("act", lambda e, k4=k4, k3=k3: e.activation(
                t4w[k3][:, :], g4[k4][:, :], AF.Relu,
                scale=scol("i3"), bias=cst[:, 3:4]), deps=[dg, war4["t4"][k3]])
            war4["g4"][k4] = tt4
            tg2_ = p.emit("pool", lambda e, k3=k3: e.tensor_scalar(
                gate2w[k3][:, :], t4w[k3][:, :], -MAGIC, scol("s3"),
                ALU.add, ALU.mult), deps=[tt4, war4["gate2"][k3]])
            war4["t4"][k3] = tg2_
            ttu = p.emit("act", lambda e, k4=k4, k3=k3: e.activation(
                tuw[k3][:, :], u4[k4][:, :], AF.Relu,
                scale=scol("i1"), bias=cst[:, 3:4]), deps=[du, war4["tu"][k3]])
            war4["u4"][k4] = ttu
            tuq = p.emit("pool", lambda e, k3=k3: e.tensor_scalar(
                upqw[k3][:, :], tuw[k3][:, :], -MAGIC, scol("s1"),
                ALU.add, ALU.mult), deps=[ttu, war4["upq"][k3]])
            war4["tu"][k3] = tuq
            tp = p.emit("dve", lambda e, k3=k3, kp=kp: e.tensor_tensor(
                pw[kp][:, :], gate2w[k3][:, :], upqw[k3][:, :], ALU.mult),
                deps=[tg2_, tuq, war4["pw"][kp]])
            war4["gate2"][k3] = tp
            war4["upq"][k3] = tp
            tred4 = emit_red("m4", pw[kp][:, :], [tp])
            t_pdma[(st, ic_)] = p.dma("sp", "spq",
                lambda e, ic_=ic_, st=st, kp=kp: e.dma_start(
                    out=p_d[ic_ * 128:(ic_ + 1) * 128, st * S:(st + 1) * S],
                    in_=pw[kp][:, :]), deps=[tp, tred4])
            war4["pw"][kp] = t_pdma[(st, ic_)]

    t_fold4 = fold_ring("m4", 4)
    t_fm4 = max_finalize(4, 4, ps_bc4, [t_fold4])
    b3 = barrier(3, 4, 1, [("s4", "i4")], ps_bc4, [t_fm4])
    t_e4 = p.emit("dve", lambda e: e.tensor_scalar(
        scol("eps4"), scol("i4"), scol("i4"), 1e-5, ALU.mult, ALU.mult),
        deps=b3)
    b3 = b3 + [t_e4]

    ps4.close()
    ctx4.close()
    p.fence()

    ctx5 = ExitStack()
    o_t = [ctx5.enter_context(nc.sbuf_tensor(f"o_t{i}", [128, IC * S], F16))
           for i in range(2)]
    t5w = [ctx5.enter_context(nc.sbuf_tensor(f"t5w{i}", [128, S], F32))
           for i in range(R3)]
    x2l = [ctx5.enter_context(nc.sbuf_tensor(f"x2l{i}", [128, D], F32))
           for i in range(2)]
    xml = ctx5.enter_context(nc.sbuf_tensor("xml", [128, D], F32))
    outw = [ctx5.enter_context(nc.sbuf_tensor(f"outw{i}", [128, D], F32))
            for i in range(2)]

    ps5 = ExitStack()
    ps_y = [ps5.enter_context(nc.psum_tensor(f"ps_y{i}", [128, 1024], F32))
            for i in range(2)]

    t_out_dma = {}
    war5 = {"o": [{} for _ in range(2)], "rd": [{} for _ in range(2)],
            "t5": [None] * R3, "x2l": [None, None],
            "xml": None, "outw": [None, None], "psy": [None, None]}
    for st in range(NST):
        par = st % 2
        t_o = {}
        for ic_ in range(IC):
            k3 = (st * IC + ic_) % R3
            dp = p.dma("sp", "spq", lambda e, ic_=ic_, st=st, par=par: e.dma_start(
                out=o_t[par][:, ic_ * S:(ic_ + 1) * S],
                in_=p_d[ic_ * 128:(ic_ + 1) * 128, st * S:(st + 1) * S]),
                deps=[t_pdma[(st, ic_)], war5["rd"][par].get(ic_),
                      war5["o"][par].get(ic_)] + b3)
            tt5 = p.emit("act", lambda e, ic_=ic_, par=par, k3=k3: e.activation(
                t5w[k3][:, :], o_t[par][:, ic_ * S:(ic_ + 1) * S], AF.Relu,
                scale=scol("i4"), bias=cst[:, 3:4]),
                deps=[dp, war5["t5"][k3]])
            t_o[ic_] = p.emit("dve", lambda e, ic_=ic_, par=par, k3=k3:
                e.tensor_scalar(
                    o_t[par][:, ic_ * S:(ic_ + 1) * S], t5w[k3][:, :], -MAGIC,
                    None, ALU.add), deps=[tt5])
            war5["t5"][k3] = t_o[ic_]
            war5["o"][par][ic_] = t_o[ic_]
        for tt in range(NTT):
            dx2 = p.dma("sp", "spq", lambda e, st=st, tt=tt: e.dma_start(
                out=x2l[tt % 2][:, :],
                in_=x2_d[(st * S + tt * 128):(st * S + tt * 128 + 128), :]),
                deps=[x2_dma[(st, tt)], war5["x2l"][tt % 2]])
            mms_stop = []
            wdap = wdT.ap().rearrange("p (d i c) -> p d i c", i=IC, c=128)
            for half in range(2):
                for ic_ in range(IC):
                    deps = [t_o[ic_]]
                    if war5["psy"][tt % 2] is not None:
                        deps.append(war5["psy"][tt % 2])
                    rh = wdap[:, half * 4:(half + 1) * 4, ic_, :]
                    mm = p.emit("pe",
                        lambda e, ic_=ic_, half=half, tt=tt, rh=rh, par=par:
                        e.matmul(
                            ps_y[tt % 2][:, half * 512:(half + 1) * 512],
                            o_t[par][:,
                                     ic_ * S + tt * 128:ic_ * S + tt * 128 + 128],
                            rh,
                            start=(ic_ == 0), stop=(ic_ == IC - 1)),
                        deps=deps)
                    war5["rd"][par][ic_] = mm
                mms_stop.append(mm)
            lt4 = ln_stats(lambda lo, tt=tt: ps_y[tt % 2][:, lo:lo + 512],
                           CB["negm2"], CB["r2"], mms_stop,
                           extra_negmr=CB["negm2r2"], eps_ap=scol("eps4"))
            txm = p.emit("dve", lambda e, tt=tt: e.tensor_scalar(
                xml[:, :], x2l[tt % 2][:, :],
                colbuf[:, CB["negm2r2"]:CB["negm2r2"] + 1], None, ALU.add),
                deps=lt4 + [dx2, war5["xml"]])
            war5["x2l"][tt % 2] = txm
            tout = p.emit("dve", lambda e, tt=tt: e.scalar_tensor_tensor(
                outw[tt % 2][:, :], ps_y[tt % 2][:, :],
                colbuf[:, CB["r2"]:CB["r2"] + 1], xml[:, :],
                ALU.mult, ALU.add),
                deps=[txm] + lt4 + [war5["outw"][tt % 2]])
            use_cols(tout, [CB["negm2"], CB["r2"], CB["negm2r2"]])
            war5["xml"] = tout
            war5["psy"][tt % 2] = tout
            t_out_dma[(st, tt)] = p.dma("act", "acq",
                lambda e, st=st, tt=tt: e.dma_start(
                    out=out_h[(st * S + tt * 128):(st * S + tt * 128 + 128), :],
                    in_=outw[tt % 2][:, :]), deps=[tout])
            war5["outw"][tt % 2] = t_out_dma[(st, tt)]

    finals = [(sq, v) for sq, v in p.dcnt.items() if v > 0]
    for e in P.ENGINES:
        p.water[e] = {}
        p.wait(e, finals)

    with nc.Block() as block:
        @block.tensor
        def _(eng):
            for op in p.ops["pe"]:
                op(eng)

        @block.scalar
        def _(eng):
            for op in p.ops["act"]:
                op(eng)

        @block.vector
        def _(eng):
            for op in p.ops["dve"]:
                op(eng)

        @block.gpsimd
        def _(eng):
            for op in p.ops["pool"]:
                op(eng)

        @block.sync
        def _(eng):
            for op in p.ops["sp"]:
                op(eng)

    es.pop_all()
    return nc


def _make_inputs(x_core, W_in, b_in, W_gate, b_gate, lam, w_gf, w_up, w_down):
    return {
        "x": np.ascontiguousarray(x_core, dtype=np.float32),
        "w_in": np.ascontiguousarray(W_in, dtype=np.float32),
        "w_gate": np.ascontiguousarray(W_gate, dtype=np.float32),
        "b_in": np.ascontiguousarray(np.reshape(b_in, (1, -1)), dtype=np.float32),
        "b_gate": np.ascontiguousarray(np.reshape(b_gate, (1, -1)), dtype=np.float32),
        "lam": np.ascontiguousarray(np.reshape(lam, (1, -1)), dtype=np.float32),
        "w_up": np.ascontiguousarray(w_up, dtype=np.float32),
        "w_gf": np.ascontiguousarray(w_gf, dtype=np.float32),
        "w_down": np.ascontiguousarray(w_down, dtype=np.float32),
        "id16_h": np.eye(128, dtype=np.float16),
        "id32_h": np.eye(128, dtype=np.float32),
    }


def kernel(L0_attn_x, L0_attn_W_in, L0_attn_b_in, L0_attn_W_gate,
           L0_attn_b_gate, L0_attn_forget_lambda, L0_ffn_w_gate,
           L0_ffn_w_up, L0_ffn_w_down):
    x = np.ascontiguousarray(np.asarray(L0_attn_x, dtype=np.float32))
    B, T, _ = x.shape
    in_maps = [
        _make_inputs(x[c], L0_attn_W_in, L0_attn_b_in, L0_attn_W_gate,
                     L0_attn_b_gate, L0_attn_forget_lambda, L0_ffn_w_gate,
                     L0_ffn_w_up, L0_ffn_w_down)
        for c in range(B)
    ]
    import os
    axon = (bool(os.environ.get("AXON_TERMINAL_JOB_NAME"))
            or os.environ.get("AXON_H4_ENABLED") == "1")
    use_cc = (os.environ.get("BASS_SIM_FP64") == "1") or not axon
    nc = build_program(T=T, use_cc=use_cc, n_cores=B)
    res = run_bass_kernel_spmd(nc, in_maps, core_ids=list(range(B)))
    return np.stack([np.asarray(res.results[c]["out"], dtype=np.float32)
                     for c in range(B)], axis=0)


# version 13
# speedup vs baseline: 1.4711x; 1.4711x over previous
import numpy as np
import concourse.bass as bass
import concourse.mybir as mybir
from concourse.bass_utils import run_bass_kernel_spmd
from contextlib import ExitStack

F32 = mybir.dt.float32
F16 = mybir.dt.float16
AF = mybir.ActivationFunctionType
ALU = mybir.AluOpType
AX = mybir.AxisListType.X

MAGIC = 12582912.0
QMAX = 32767.0
CLIP16 = 65535.0 / 65536.0
D = 1024
I = 3072
DC = D // 128
IC = I // 128
S = 512
NTT = S // 128


class P:
    ENGINES = ("pe", "act", "dve", "pool", "sp")
    DQUEUES = ("spq", "gpq", "acq", "ccq")
    NPOOL = 8

    def __init__(self, nc):
        self.nc = nc
        self.ops = {e: [] for e in self.ENGINES}
        self.cnt = {e: 0 for e in self.ENGINES}
        self.dcnt = {}
        for q in self.DQUEUES:
            for i in range(self.NPOOL):
                self.dcnt[f"{q}{i}"] = 0
        self.didx = {q: 0 for q in self.DQUEUES}
        self.sems = {}
        self.water = {e: {} for e in self.ENGINES}

    TRACE = []

    def wait(self, e, deps):
        for d in deps:
            if d is None:
                continue
            sname, val = d
            if sname == e:
                if val >= self.cnt[e]:
                    val = self.cnt[e]
            w = self.water[e]
            if w.get(sname, 0) >= val:
                continue
            w[sname] = val

            def op(eng, sem_name=sname, val=val, self=self):
                eng.wait_ge(self.sems[sem_name], val)
            self.ops[e].append(op)
            P.TRACE.append(("wait", e, sname, val))

    def emit(self, e, builder, deps=()):
        self.wait(e, deps)
        self.cnt[e] += 1
        val = self.cnt[e]

        def op(eng, builder=builder, e=e, self=self):
            builder(eng).then_inc(self.sems[e], 1)
        self.ops[e].append(op)
        import traceback
        P.TRACE.append(("emit", e, val, traceback.extract_stack()[-2].lineno))
        return (e, val)

    def dma(self, e, q, builder, deps=()):
        sq = f"{q}{self.didx[q] % self.NPOOL}"
        self.didx[q] += 1
        prev = getattr(self, "dlast", None)
        if prev is None:
            self.dlast = prev = {}
        self.wait(e, list(deps) + [prev.get(sq)])
        self.dcnt[sq] += 16
        val = self.dcnt[sq]
        prev[sq] = (sq, val)

        def op(eng, builder=builder, sq=sq, self=self):
            builder(eng).then_inc(self.sems[sq], 16)
        self.ops[e].append(op)
        import traceback
        P.TRACE.append(("dma", e, sq, val, traceback.extract_stack()[-2].lineno))
        return (sq, val)

    def fence(self):
        toks = [(e, self.cnt[e]) for e in self.ENGINES]
        toks += [(sq, self.dcnt[sq]) for sq in self.dcnt]
        for e in self.ENGINES:
            self.wait(e, toks)


def build_program(T=8192, use_cc=True, n_cores=8, dbg=False):
    NST = T // S
    nc = bass.Bass("TRN2", target_bir_lowering=False,
                   num_devices=(n_cores if use_cc else None))

    x_h = nc.dram_tensor("x", [T, D], F32, kind="ExternalInput").ap()
    win_h = nc.dram_tensor("w_in", [D, D], F32, kind="ExternalInput").ap()
    wg_h = nc.dram_tensor("w_gate", [D, D], F32, kind="ExternalInput").ap()
    bin_h = nc.dram_tensor("b_in", [1, D], F32, kind="ExternalInput").ap()
    bg_h = nc.dram_tensor("b_gate", [1, D], F32, kind="ExternalInput").ap()
    lam_h = nc.dram_tensor("lam", [1, D], F32, kind="ExternalInput").ap()
    wup_h = nc.dram_tensor("w_up", [I, D], F32, kind="ExternalInput").ap()
    wgf_h = nc.dram_tensor("w_gf", [I, D], F32, kind="ExternalInput").ap()
    wd_h = nc.dram_tensor("w_down", [D, I], F32, kind="ExternalInput").ap()
    id16_h = nc.dram_tensor("id16_h", [128, 128], F16, kind="ExternalInput").ap()
    id32_h = nc.dram_tensor("id32_h", [128, 128], F32, kind="ExternalInput").ap()
    out_h = nc.dram_tensor("out", [T, D], F32, kind="ExternalOutput").ap()

    kd = "ExternalOutput" if dbg else "Internal"
    x2_d = nc.dram_tensor("x2_d", [T, D], F32, kind=kd).ap()
    xn2_d = nc.dram_tensor("xn2_d", [T, D], F16, kind=kd).ap()
    xqt_d = nc.dram_tensor("xqt_d", [D, T], F16, kind=kd).ap()
    gate_d = nc.dram_tensor("gate_d", [I, T], F16, kind=kd).ap()
    up_d = nc.dram_tensor("up_d", [I, T], F16, kind=kd).ap()
    g2_d = nc.dram_tensor("g2_d", [I, T], F16, kind=kd).ap()
    p_d = nc.dram_tensor("p_d", [I, T], F16, kind=kd).ap()
    ccin_d = [nc.dram_tensor(f"ccin{k}", [1, 2], F32).ap() for k in range(4)]
    ccout_d = [nc.dram_tensor(f"ccout{k}", [1, 2], F32).ap() for k in range(4)]

    p = P(nc)
    es = ExitStack()
    for sname in list(p.cnt) + list(p.dcnt):
        p.sems[sname] = es.enter_context(nc.semaphore(f"s_{sname}"))

    A = nc.alloc_sbuf_tensor
    id16 = A("id16", [128, 128], F16)
    id32 = A("id32", [128, 128], F32)
    ones128 = A("ones128", [1, 128], F32)
    k2 = A("k2", [128, DC], F32)
    bin2 = A("bin2", [128, DC], F32)
    bg2 = A("bg2", [128, DC], F32)
    scal = A("scal", [128, 16], F32)
    srow = A("srow", [1, 16], F32)
    mcol = A("mcol", [128, 8], F32)
    mrow = A("mrow", [1, 128], F32)
    mcell = A("mcell", [1, 8], F32)
    lamt = A("lamt", [128, DC], F32)
    btmp = A("btmp", [128, DC], F32)
    bnbuf = A("bnbuf", [128, 12], F32)
    stats = A("stats", [128, 2], F32)
    colbuf = A("colbuf", [128, 16], F32)
    redcol = A("redcol", [128, 1], F32)
    redring = A("redring", [128, 40], F32)
    cst = A("cst", [128, 8], F32)

    SC = {"s0": 0, "i0": 1, "s2": 2, "i2": 3, "hs2": 4, "nhs2m": 5,
          "s1": 6, "i1": 7, "s3": 8, "i3": 9, "s4": 10, "i4": 11,
          "s2p16": 12, "eps4": 13, "s0b": 14}

    def scol(name):
        return scal[:, SC[name]:SC[name] + 1]

    d_id16 = p.dma("sp", "spq", lambda e: e.dma_start(out=id16[:, :], in_=id16_h))
    d_id32 = p.dma("sp", "spq", lambda e: e.dma_start(out=id32[:, :], in_=id32_h))
    d_lam = p.dma("sp", "spq", lambda e: e.dma_start(
        out=lamt[:, :], in_=lam_h.rearrange("o (c p) -> (o p) c", p=128),
        allow_slow_non_contiguous=True))
    d_bin = p.dma("sp", "spq", lambda e: e.dma_start(
        out=btmp[:, :], in_=bin_h.rearrange("o (c p) -> (o p) c", p=128),
        allow_slow_non_contiguous=True))
    t_ones = p.emit("dve", lambda e: e.memset(ones128[:, :], 1.0))
    t_c0 = p.emit("dve", lambda e: e.memset(cst[:, 0:1], 1e-5))
    t_c1 = p.emit("dve", lambda e: e.memset(cst[:, 1:2], 1.0))
    t_c2 = p.emit("dve", lambda e: e.memset(cst[:, 2:3], 0.25 * (1 + 1e-6)))
    t_c3 = p.emit("dve", lambda e: e.memset(cst[:, 3:4], MAGIC))
    t_c4 = p.emit("dve", lambda e: e.memset(cst[:, 4:5], 32768.0 + MAGIC))
    t_srz = p.emit("dve", lambda e: e.memset(srow[:, :], 0.0))
    t_mz = p.emit("dve", lambda e: e.memset(mcol[:, :], 0.0))
    t_e = p.emit("act", lambda e: e.activation(lamt[:, :], lamt[:, :], AF.Exp),
                 deps=[d_lam])
    t_l = p.emit("act", lambda e: e.activation(lamt[:, :], lamt[:, :], AF.Ln,
                                               bias=cst[:, 1:2]), deps=[t_e, t_c1])
    t_k2 = p.emit("dve", lambda e: e.tensor_scalar(
        k2[:, :], lamt[:, :], 4.0, None, ALU.mult), deps=[t_l])
    t_b1 = p.emit("dve", lambda e: e.tensor_scalar(
        bin2[:, :], btmp[:, :], 0.5, None, ALU.mult), deps=[d_bin])
    d_bg = p.dma("sp", "spq", lambda e: e.dma_start(
        out=btmp[:, :], in_=bg_h.rearrange("o (c p) -> (o p) c", p=128),
        allow_slow_non_contiguous=True), deps=[t_b1])
    t_b2 = p.emit("dve", lambda e: e.tensor_scalar(
        bg2[:, :], btmp[:, :], 0.5, None, ALU.mult), deps=[d_bg])
    pro_toks = [t_k2, t_b2, t_ones, t_mz, d_id16, d_id32, t_c0, t_c2, t_c3, t_c4]

    ln_prev = {"t5": None, "stats_r": None}
    reds = {"m0": [], "m2": [], "m1": [], "m3": [], "m4": []}
    RB = {"m0": 0, "m2": 8, "m1": 16, "m3": 24, "m4": 32}
    MC = {"m0": 0, "m2": 1, "m1": 2, "m3": 3, "m4": 4}
    last_fold = {}

    def _fold8(stream, nret):
        b = RB[stream]
        c = MC[stream]
        tr = p.emit("dve", lambda e, b=b, nret=nret: e.tensor_reduce(
            redcol[:, :], redring[:, b:b + nret], AX, ALU.max),
            deps=[reds[stream][-1], last_fold.get(stream, t_mz)])
        tm = p.emit("dve", lambda e, c=c: e.tensor_tensor(
            mcol[:, c:c + 1], mcol[:, c:c + 1], redcol[:, :], ALU.max),
            deps=[tr, t_mz])
        last_fold[stream] = tm
        return tm

    def emit_red(stream, src_ap, deps):
        k = RB[stream] + (len(reds[stream]) % 8)
        t = p.emit("dve", lambda e, k=k, src_ap=src_ap: e.tensor_reduce(
            redring[:, k:k + 1], src_ap, AX, ALU.max,
            apply_absolute_value=True),
            deps=list(deps) + [last_fold.get(stream)
                               if len(reds[stream]) % 8 == 0 else None])
        reds[stream].append(t)
        if len(reds[stream]) % 8 == 0:
            _fold8(stream, 8)
        return t

    def fold_ring(stream, col_idx):
        rem = len(reds[stream]) % 8
        if rem:
            _fold8(stream, rem)
        return last_fold[stream]
    colw = {}

    def ln_stats(src_ap_fn, negm_col, r_col, deps, extra_negmr=None,
                 eps_ap=None):
        war = [colw.get(r_col), ln_prev["t5"]]
        t1 = p.emit("dve", lambda e: e.bn_stats(bnbuf[:, 0:6], src_ap_fn(0)),
                    deps=list(deps) + [ln_prev["stats_r"]])
        t2 = p.emit("dve", lambda e: e.bn_stats(bnbuf[:, 6:12], src_ap_fn(512)),
                    deps=list(deps))
        t3 = p.emit("dve", lambda e: e.bn_aggr(stats[:, 0:2], bnbuf[:, 0:12]),
                    deps=[t1, t2])
        t4 = p.emit("dve", lambda e: e.tensor_scalar(
            colbuf[:, negm_col:negm_col + 1], stats[:, 0:1], -1.0, None,
            ALU.mult), deps=[t3])
        if eps_ap is None:
            eps_ap = cst[:, 0:1]
        t5 = p.emit("act", lambda e, eps_ap=eps_ap: e.activation(
            colbuf[:, r_col:r_col + 1], stats[:, 1:2], AF.Sqrt,
            bias=eps_ap), deps=[t3] + war)
        t6 = p.emit("dve", lambda e: e.reciprocal(
            colbuf[:, r_col:r_col + 1], colbuf[:, r_col:r_col + 1]), deps=[t5])
        ln_prev["t5"] = t5
        ln_prev["stats_r"] = t5
        toks = [t4, t6]
        if extra_negmr is not None:
            toks.append(p.emit("dve", lambda e: e.tensor_tensor(
                colbuf[:, extra_negmr:extra_negmr + 1],
                colbuf[:, negm_col:negm_col + 1],
                colbuf[:, r_col:r_col + 1], ALU.mult), deps=[t4, t6]))
        return toks

    def use_cols(tok, cols):
        for c in cols:
            colw[c] = tok

    def max_finalize(col_idx, cell_idx, ps_bc, deps):
        ttr = p.emit("pe", lambda e: e.transpose(
            ps_bc[0:1, 0:128], mcol[:, col_idx:col_idx + 1], id32[:, :]),
            deps=deps)
        tv = p.emit("dve", lambda e: e.tensor_copy(
            mrow[:, :], ps_bc[0:1, 0:128]), deps=[ttr])
        tr = p.emit("dve", lambda e: e.tensor_reduce(
            mcell[:, cell_idx:cell_idx + 1], mrow[:, :], AX, ALU.max),
            deps=[tv])
        return tr

    def barrier(kidx, cell0, n, scols, ps_bc, deps):
        toks = list(deps)
        if use_cc:
            d1 = p.dma("pool", "gpq", lambda e: e.dma_start(
                out=ccin_d[kidx][:, 0:n], in_=mcell[:, cell0:cell0 + n]),
                deps=toks)
            tcc = p.dma("pool", "ccq", lambda e: e.collective_compute(
                "AllReduce", ALU.max, replica_groups=[list(range(n_cores))],
                ins=[ccin_d[kidx][:, 0:n].opt()],
                outs=[ccout_d[kidx][:, 0:n].opt()]), deps=[d1])
            d2 = p.dma("pool", "gpq", lambda e: e.dma_start(
                out=mcell[:, cell0:cell0 + n], in_=ccout_d[kidx][:, 0:n]),
                deps=[tcc])
            toks = [d2]
        stoks = []
        for j in range(n):
            ts_ = p.emit("dve", lambda e, j=j: e.tensor_scalar(
                srow[:, 2 * j:2 * j + 1], mcell[:, cell0 + j:cell0 + j + 1],
                1.0 / QMAX, 1e-12, ALU.mult, ALU.max), deps=toks)
            ti_ = p.emit("dve", lambda e, j=j: e.reciprocal(
                srow[:, 2 * j + 1:2 * j + 2], srow[:, 2 * j:2 * j + 1]),
                deps=[ts_])
            stoks.append(ti_)
        tm = p.emit("pe", lambda e: e.matmul(
            ps_bc[0:128, 0:16], ones128[:, :], srow[:, :],
            start=True, stop=True), deps=stoks + [t_srz])
        evs = []
        for j, (sc_, ic_) in enumerate(scols):
            evs.append(p.emit("dve", lambda e, sc_=sc_, j=j: e.tensor_copy(
                scal[:, SC[sc_]:SC[sc_] + 1], ps_bc[0:128, 2 * j:2 * j + 1]),
                deps=[tm]))
            evs.append(p.emit("dve", lambda e, ic_=ic_, j=j: e.tensor_copy(
                scal[:, SC[ic_]:SC[ic_] + 1],
                ps_bc[0:128, 2 * j + 1:2 * j + 2]), deps=[tm]))
        return evs

    def prep_lpbq_chunk(w_hbm, dst, st_, ps_tr, ic, deps, ncols, prev_ev,
                        post_scale=None):
        ncc = ncols // 128
        nb = ncols // 32
        dld = p.dma("sp", "spq", lambda e: e.dma_start(
            out=st_["w32"][:, :ncols], in_=w_hbm[ic * 128:(ic + 1) * 128, :]),
            deps=deps)
        w3 = st_["w32"].ap()[:, :ncols].rearrange("p (b c) -> p b c", c=32)
        t1 = p.emit("dve", lambda e: e.tensor_reduce(
            st_["bmax"][:, :nb], w3, AX, ALU.max, apply_absolute_value=True),
            deps=[dld])
        t2 = p.emit("dve", lambda e: e.tensor_scalar(
            st_["sw"][:, :nb], st_["bmax"][:, :nb], 1.0 / 7.0, 1e-12,
            ALU.mult, ALU.max), deps=[t1])
        t3 = p.emit("dve", lambda e: e.reciprocal(
            st_["isw"][:, :nb], st_["sw"][:, :nb]), deps=[t2])
        if post_scale is not None:
            t2b = p.emit("dve", lambda e: e.tensor_scalar(
                st_["sw"][:, :nb], st_["sw"][:, :nb], post_scale, None,
                ALU.mult), deps=[t3])
            t3 = t2b
        isw3 = st_["isw"].ap()[:, :nb].rearrange("p (b o) -> p b o", o=1)
        sw3 = st_["sw"].ap()[:, :nb].rearrange("p (b o) -> p b o", o=1)
        z3 = st_["z"].ap()[:, :ncols].rearrange("p (b c) -> p b c", c=32)
        wq3 = st_["w16"].ap()[:, :ncols].rearrange("p (b c) -> p b c", c=32)
        a_, b_ = bass.broadcast_tensor_aps(w3, isw3)
        t4 = p.emit("dve", lambda e, a_=a_, b_=b_: e.tensor_tensor(
            z3, a_, b_, ALU.mult), deps=[t3])
        t5 = p.emit("dve", lambda e: e.tensor_scalar(
            st_["z"][:, :ncols], st_["z"][:, :ncols], MAGIC, -MAGIC,
            ALU.add, ALU.add), deps=[t4])
        a2_, b2_ = bass.broadcast_tensor_aps(z3, sw3)
        t6 = p.emit("dve", lambda e, a2_=a2_, b2_=b2_: e.tensor_tensor(
            wq3, a2_, b2_, ALU.mult), deps=[t5])
        ev_toks = list(prev_ev)
        tr_toks = []
        for dc in range(ncc):
            half = (dc // 4) % 2
            tr_toks.append(p.emit("pe", lambda e, dc=dc, half=half:
                e.transpose(
                    ps_tr[half][:, (dc % 4) * 128:(dc % 4 + 1) * 128],
                    st_["w16"][:, dc * 128:(dc + 1) * 128], id16[:, :]),
                deps=[t6] + (ev_toks[-2:-1] if len(ev_toks) >= 2
                             else ev_toks)))
            if dc % 4 == 3:
                ev_toks.append(p.emit("act", lambda e, dc=dc, half=half,
                                      ic=ic, ncc=ncc: e.copy(
                    dst[:, (ic * ncc + dc - 3) * 128:
                        (ic * ncc + dc + 1) * 128],
                    ps_tr[half][:, :]), deps=tr_toks[-4:]))
        return ev_toks[-1], t6, ev_toks[len(prev_ev):]

    ctx_w = ExitStack()
    winT = ctx_w.enter_context(nc.sbuf_tensor("winT", [128, DC * DC * 128], F16))
    wgT = ctx_w.enter_context(nc.sbuf_tensor("wgT", [128, DC * DC * 128], F16))

    ctx1 = ExitStack()
    wst = {k: ctx1.enter_context(nc.sbuf_tensor(f"p_{k}", [128, n], dt))
           for k, n, dt in [("w32", D, F32), ("w16", D, F16), ("z", D, F32),
                            ("bmax", D // 32, F32), ("sw", D // 32, F32),
                            ("isw", D // 32, F32)]}
    xs = [ctx1.enter_context(nc.sbuf_tensor(f"xs{i}", [128, D], F32))
          for i in range(2 * NTT)]
    xn_st = ctx1.enter_context(nc.sbuf_tensor("xn_st", [128, NTT * D], F16))
    xnT = ctx1.enter_context(nc.sbuf_tensor("xnT", [128, DC * S], F16))
    th_rg = ctx1.enter_context(nc.sbuf_tensor("th_rg", [128, DC * S], F16))
    th_ig = ctx1.enter_context(nc.sbuf_tensor("th_ig", [128, DC * S], F16))
    g_t = ctx1.enter_context(nc.sbuf_tensor("g_t", [128, DC * S], F32))
    al_t = ctx1.enter_context(nc.sbuf_tensor("al_t", [128, DC * S], F32))
    bh_t = ctx1.enter_context(nc.sbuf_tensor("bh_t", [128, DC * S], F16))
    u1_t = ctx1.enter_context(nc.sbuf_tensor("u1_t", [128, DC * S], F16))
    xb_t = ctx1.enter_context(nc.sbuf_tensor("xb_t", [128, DC * S], F16))
    h_t = [ctx1.enter_context(nc.sbuf_tensor(f"h{i}", [128, DC * S], F32))
           for i in range(2)]
    x2_tile = ctx1.enter_context(nc.sbuf_tensor("x2_tile", [128, D], F32))
    xm_tile = ctx1.enter_context(nc.sbuf_tensor("xm_tile", [128, D], F32))
    xn2_tile = ctx1.enter_context(nc.sbuf_tensor("xn2_t", [128, D], F16))

    ps1 = ExitStack()
    ps_tr = [ps1.enter_context(nc.psum_tensor(f"ps_tr{i}", [128, 512], F16))
             for i in range(2)]
    ps_g = [ps1.enter_context(nc.psum_tensor(f"ps_g{i}", [128, 512], F32))
            for i in range(2)]
    ps_h = ps1.enter_context(nc.psum_tensor("ps_h", [128, 1024], F32))
    ps_bc = ps1.enter_context(nc.psum_tensor("ps_bc", [128, 128], F32))

    def prep_square(w_hbm, dst, deps):
        toks = []
        last = None
        for dpc in range(DC):
            dld = p.dma("sp", "spq", lambda e, dpc=dpc: e.dma_start(
                out=wst["w32"][:, :], in_=w_hbm[dpc * 128:(dpc + 1) * 128, :]),
                deps=deps + ([last] if last else []))
            tc = p.emit("dve", lambda e: e.tensor_copy(
                wst["w16"][:, :], wst["w32"][:, :]), deps=[dld])
            last = tc
            tr_toks = []
            for dc in range(DC):
                half = (dc // 4) % 2
                tr_toks.append(p.emit("pe", lambda e, dc=dc, half=half:
                    e.transpose(
                        ps_tr[half][:, (dc % 4) * 128:(dc % 4 + 1) * 128],
                        wst["w16"][:, dc * 128:(dc + 1) * 128], id16[:, :]),
                    deps=[tc] + (toks[-2:-1] if len(toks) >= 2 else [])))
                if dc % 4 == 3:
                    toks.append(p.emit("act", lambda e, dc=dc, half=half,
                                       dpc=dpc: e.copy(
                        dst[:, (dpc * DC + dc - 3) * 128:
                            (dpc * DC + dc + 1) * 128],
                        ps_tr[half][:, :]), deps=tr_toks[-4:]))
            last = toks[-1]
        return toks[-1]

    tw1 = prep_square(win_h, winT, pro_toks)
    tw2 = prep_square(wg_h, wgT, [tw1])

    CB = {"negm1": 0, "r1": 4, "negm2": 8, "r2": 9, "negm2r2": 10,
          "negm3": 11, "r3": 12}

    x_dma = {}
    x2_dma = {}
    xn2_dma = {}
    t_scan = {}
    t_x_read = {}
    war = {
        "x2t_dma": None,
        "xn_st": None,
        "xnT": None,
        "xnT_dve": None,
        "th_rg": None, "th_ig": None,
        "g": {}, "al": {},
        "xb": None,
        "h_tr": [None, None],
        "x2t": None, "xn2t": None, "xn2t_pool": None,
        "proj_tail": None,
    }

    for st0 in range(min(2, NST)):
        for tt in range(NTT):
            x_dma[(st0, tt)] = p.dma(
                "sp", "spq", lambda e, st0=st0, tt=tt: e.dma_start(
                    out=xs[(st0 % 2) * NTT + tt][:, :],
                    in_=x_h[(st0 * S + tt * 128):(st0 * S + tt * 128 + 128), :]))

    def emit_tail(st2, scan_toks, hc2):
        t_ht_last = None
        prev_x2_stt = war.get("ps_h_reader")
        for tt in range(NTT):
            trh = []
            for dc in range(DC):
                trh.append(p.emit("pe", lambda e, dc=dc, tt=tt, hc2=hc2:
                    e.transpose(
                        ps_h[:, dc * 128:(dc + 1) * 128],
                        hc2[:, dc * S + tt * 128:dc * S + tt * 128 + 128],
                        id32[:, :]),
                    deps=[scan_toks[dc]] +
                         ([prev_x2_stt] if prev_x2_stt else [])))
            lt2 = ln_stats(lambda lo: ps_h[:, lo:lo + 512], CB["negm2"],
                           CB["r2"], trh, extra_negmr=CB["negm2r2"])
            xb_ = xs[(st2 % 2) * NTT + tt]
            txm = p.emit("dve", lambda e, xb_=xb_: e.tensor_scalar(
                xm_tile[:, :], xb_[:, :],
                colbuf[:, CB["negm2r2"]:CB["negm2r2"] + 1], None, ALU.add),
                deps=lt2)
            t_x_read[(st2, tt)] = txm
            tx2 = p.emit("dve", lambda e: e.scalar_tensor_tensor(
                x2_tile[:, :], ps_h[:, :],
                colbuf[:, CB["r2"]:CB["r2"] + 1], xm_tile[:, :],
                ALU.mult, ALU.add),
                deps=[txm] + lt2 + [war["x2t"], war["x2t_dma"]])
            use_cols(tx2, [CB["negm2"], CB["r2"], CB["negm2r2"]])
            prev_x2_stt = tx2
            war["ps_h_reader"] = tx2
            x2_dma[(st2, tt)] = p.dma("pool", "gpq",
                lambda e, st2=st2, tt=tt: e.dma_start(
                    out=x2_d[(st2 * S + tt * 128):
                             (st2 * S + tt * 128 + 128), :],
                    in_=x2_tile[:, :]), deps=[tx2])
            lt3 = ln_stats(lambda lo: x2_tile[:, lo:lo + 512], CB["negm3"],
                           CB["r3"], [tx2])
            txn2 = p.emit("dve", lambda e: e.tensor_scalar(
                xn2_tile[:, :], x2_tile[:, :],
                colbuf[:, CB["negm3"]:CB["negm3"] + 1],
                colbuf[:, CB["r3"]:CB["r3"] + 1], ALU.add, ALU.mult),
                deps=lt3 + [war["xn2t"], war["xn2t_pool"]])
            use_cols(txn2, [CB["negm3"], CB["r3"]])
            war["x2t"] = txn2
            war["x2t_dma"] = x2_dma[(st2, tt)]
            tred = emit_red("m0", xn2_tile[:, :], [txn2])
            xn2_dma[(st2, tt)] = p.dma("pool", "gpq",
                lambda e, st2=st2, tt=tt: e.dma_start(
                    out=xn2_d[(st2 * S + tt * 128):
                              (st2 * S + tt * 128 + 128), :],
                    in_=xn2_tile[:, :]), deps=[txn2])
            war["xn2t"] = xn2_dma[(st2, tt)]
            war["xn2t_pool"] = tred
            t_ht_last = trh[-1]
        war["h_tr"][st2 % 2] = t_ht_last

    pend_tail = None
    for st in range(NST):
        t_xn = {}
        for tt in range(NTT):
            xb_ = xs[(st % 2) * NTT + tt]
            lt = ln_stats(lambda lo, xb_=xb_: xb_[:, lo:lo + 512],
                          CB["negm1"] + tt, CB["r1"] + tt,
                          [x_dma[(st, tt)]])
            t_xn[tt] = p.emit("dve", lambda e, tt=tt, xb_=xb_: e.tensor_scalar(
                xn_st[:, tt * D:(tt + 1) * D], xb_[:, :],
                colbuf[:, CB["negm1"] + tt:CB["negm1"] + tt + 1],
                colbuf[:, CB["r1"] + tt:CB["r1"] + tt + 1],
                ALU.add, ALU.mult), deps=lt + [war["xn_st"]])
            use_cols(t_xn[tt], [CB["negm1"] + tt, CB["r1"] + tt])

        t_xnT = {}
        ev_list = []
        tr_last = None
        for dc in range(DC):
            half = dc % 2
            trk = []
            for tt in range(NTT):
                trk.append(p.emit("pe", lambda e, dc=dc, tt=tt, half=half:
                    e.transpose(
                        ps_tr[half][:, tt * 128:(tt + 1) * 128],
                        xn_st[:, tt * D + dc * 128:tt * D + dc * 128 + 128],
                        id16[:, :]),
                    deps=[t_xn[tt], tw2] +
                         (ev_list[-2:-1] if len(ev_list) >= 2 else [])))
            tr_last = trk[-1]
            ev_list.append(p.emit("act", lambda e, dc=dc, half=half: e.copy(
                xnT[:, dc * S:(dc + 1) * S], ps_tr[half][:, :]),
                deps=trk + [war["xnT"], war["xnT_dve"]]))
            t_xnT[dc] = ev_list[-1]
        war["xn_st"] = tr_last

        def proj(wT, th_dst, bias_t, prev_tail, th_war):
            evs = {}
            mm_last = None
            for c in range(DC):
                mms = []
                for dc in range(DC):
                    deps = [t_xnT[dc]]
                    if c >= 2:
                        deps.append(evs[c - 2])
                    elif prev_tail is not None:
                        deps.append(prev_tail.get(c - 2 + DC))
                    mms.append(p.emit("pe",
                        lambda e, c=c, dc=dc, wT=wT: e.matmul(
                            ps_g[c % 2][:, :],
                            wT[:, (c * DC + dc) * 128:(c * DC + dc + 1) * 128],
                            xnT[:, dc * S:(dc + 1) * S],
                            start=(dc == 0), stop=(dc == DC - 1)),
                        deps=deps))
                evs[c] = p.emit("act", lambda e, c=c, bias_t=bias_t,
                                th_dst=th_dst: e.activation(
                    th_dst[:, c * S:(c + 1) * S], ps_g[c % 2][:, :], AF.Tanh,
                    bias=bias_t[:, c:c + 1], scale=0.5),
                    deps=[mms[-1], th_war])
                mm_last = mms[-1]
            return evs, mm_last

        ev_rg, mm_l1 = proj(wgT, th_rg, bg2, war["proj_tail"], war["th_rg"])

        if pend_tail is not None:
            emit_tail(*pend_tail)
        pend_tail = None

        t_bh = {}
        t_al = {}
        t_g = {}
        for c in range(DC):
            t_g[c] = p.emit("dve", lambda e, c=c: e.tensor_scalar(
                g_t[:, c * S:(c + 1) * S], th_rg[:, c * S:(c + 1) * S],
                1.0, k2[:, c:c + 1], ALU.add, ALU.mult),
                deps=[ev_rg[c], war["g"].get(c)])
            t_al[c] = p.emit("act", lambda e, c=c: e.activation(
                al_t[:, c * S:(c + 1) * S], g_t[:, c * S:(c + 1) * S],
                AF.Exp, scale=-1.0), deps=[t_g[c], war["al"].get(c)])
            ta2 = p.emit("act", lambda e, c=c: e.activation(
                g_t[:, c * S:(c + 1) * S], g_t[:, c * S:(c + 1) * S],
                AF.Exp, scale=-2.0), deps=[t_al[c]])
            t_bh[c] = p.emit("act", lambda e, c=c: e.activation(
                bh_t[:, c * S:(c + 1) * S], g_t[:, c * S:(c + 1) * S],
                AF.Sqrt, scale=-0.25, bias=cst[:, 2:3]),
                deps=[ta2, war["xb"]])
        war["th_rg"] = t_g[DC - 1]

        ev_ig, mm_l2 = proj(winT, th_ig, bin2, ev_rg, war["th_ig"])
        war["proj_tail"] = ev_ig
        war["xnT"] = mm_l2

        hc = h_t[st % 2]
        hp = h_t[(st - 1) % 2]
        new_scan = {}
        t_u_last = None
        for c in range(DC):
            tu = p.emit("dve", lambda e, c=c: e.scalar_tensor_tensor(
                u1_t[:, c * S:(c + 1) * S], th_ig[:, c * S:(c + 1) * S],
                1.0, xnT[:, c * S:(c + 1) * S], ALU.add, ALU.mult),
                deps=[ev_ig[c], t_xnT[c]])
            tx = p.emit("dve", lambda e, c=c: e.tensor_tensor(
                xb_t[:, c * S:(c + 1) * S], u1_t[:, c * S:(c + 1) * S],
                bh_t[:, c * S:(c + 1) * S], ALU.mult), deps=[tu, t_bh[c]])
            if st == 0:
                init = 0.0
                dep_init = []
            else:
                init = hp[:, c * S + S - 1:c * S + S]
                dep_init = [t_scan[c]]
            new_scan[c] = p.emit("dve", lambda e, c=c, init=init, hc=hc:
                e.tensor_tensor_scan(
                    hc[:, c * S:(c + 1) * S], al_t[:, c * S:(c + 1) * S],
                    xb_t[:, c * S:(c + 1) * S], init, ALU.mult, ALU.add),
                deps=[tx, t_al[c], war["h_tr"][st % 2]] + dep_init)
            war["g"][c] = new_scan[c]
            war["al"][c] = new_scan[c]
            t_u_last = tu
        t_scan = new_scan
        war["th_ig"] = t_u_last
        war["xnT_dve"] = t_u_last
        war["xb"] = new_scan[DC - 1]

        pend_tail = (st, new_scan, hc)
        if st >= 1 and st + 1 < NST:
            for tt in range(NTT):
                x_dma[(st + 1, tt)] = p.dma(
                    "sp", "spq", lambda e, st=st, tt=tt: e.dma_start(
                        out=xs[((st + 1) % 2) * NTT + tt][:, :],
                        in_=x_h[((st + 1) * S + tt * 128):
                                ((st + 1) * S + tt * 128 + 128), :]),
                    deps=[t_x_read.get((st - 1, tt))])

    if pend_tail is not None:
        emit_tail(*pend_tail)
    t_fold0 = fold_ring("m0", 0)
    t_fm0 = max_finalize(0, 0, ps_bc, [t_fold0])
    b0 = barrier(0, 0, 1, [("s0", "i0")], ps_bc, [t_fm0])
    t_s0b = p.emit("dve", lambda e: e.tensor_scalar(
        scol("s0b"), scol("s0"), 65536.0, None, ALU.mult), deps=b0)
    b0 = b0 + [t_s0b]

    ps1.close()
    ctx1.close()
    ctx_w.close()
    p.fence()

    ctx_wu1 = ExitStack()
    wupT = ctx_wu1.enter_context(nc.sbuf_tensor("wupT", [128, IC * DC * 128], F16))
    ctx_wu2 = ExitStack()
    wgfT = ctx_wu2.enter_context(nc.sbuf_tensor("wgfT", [128, IC * DC * 128], F16))

    ctx2 = ExitStack()
    wst2 = {k: ctx2.enter_context(nc.sbuf_tensor(f"q_{k}", [128, n], dt))
            for k, n, dt in [("w32", D, F32), ("w16", D, F16), ("z", D, F32),
                             ("bmax", D // 32, F32), ("sw", D // 32, F32),
                             ("isw", D // 32, F32)]}
    RBD = 6
    xn2_sb = [ctx2.enter_context(nc.sbuf_tensor(f"xn2_sb{i}", [128, NTT * D],
                                                F16)) for i in range(2)]
    t0_sb = [ctx2.enter_context(nc.sbuf_tensor(f"t0_sb{i}", [128, D], F32))
             for i in range(3)]
    xq_sb = [ctx2.enter_context(nc.sbuf_tensor(f"xq_sb{i}", [128, NTT * D],
                                               F16)) for i in range(2)]
    xqT = [ctx2.enter_context(nc.sbuf_tensor(f"xqT2_{i}", [128, DC * S], F16))
           for i in range(2)]
    gw = [ctx2.enter_context(nc.sbuf_tensor(f"gw{i}", [128, S], F16))
          for i in range(RBD)]

    ps2 = ExitStack()
    ps_tr2 = [ps2.enter_context(nc.psum_tensor(f"ps2_tr{i}", [128, 512], F16))
              for i in range(2)]
    ps_tr2w = [ps2.enter_context(nc.psum_tensor(f"ps2_tw{i}", [128, 512], F16))
               for i in range(2)]
    ps_g2 = [ps2.enter_context(nc.psum_tensor(f"ps2_g{i}", [128, 512], F32))
             for i in range(2)]
    ps_bc2 = ps2.enter_context(nc.psum_tensor("ps_bc2", [128, 128], F32))

    t_wgf = {}
    t_wup = {}
    prev_ev_track = []
    prev_dep = b0
    for ic_ in range(IC):
        ev, _, new_evs = prep_lpbq_chunk(wgf_h, wgfT, wst2, ps_tr2w, ic_,
                                         prev_dep, D, prev_ev_track,
                                         post_scale=scol("s0b"))
        t_wgf[ic_] = ev
        prev_ev_track = (prev_ev_track + new_evs)[-2:]
        prev_dep = [ev]

    t_gdma = {}
    t_gev = {}
    up_sched = {}
    nup = 0
    for st in range(NST):
        k = 2 if NST >= IC // 2 else (IC + NST - 1) // NST
        for _ in range(k):
            if nup < IC:
                up_sched.setdefault(st, []).append(nup)
                nup += 1
    if nup < IC:
        up_sched.setdefault(NST - 1, []).extend(range(nup, IC))

    war2 = {"t0": [None] * 3}
    lastrd_xn2 = {}
    lastrd_xq = {}
    mml2 = {}
    gdma_f = {}
    t_xq_dma = {}
    for st in range(NST):
        par = st % 2
        dx = {}
        for tt in range(NTT):
            dx[tt] = p.dma("sp", "spq",
                lambda e, st=st, tt=tt, par=par: e.dma_start(
                    out=xn2_sb[par][:, tt * D:(tt + 1) * D],
                    in_=xn2_d[(st * S + tt * 128):(st * S + tt * 128 + 128), :]),
                deps=[xn2_dma[(st, tt)], lastrd_xn2.get(st - 2)] + b0)
        t_xq = {}
        last_tt0 = None
        for tt in range(NTT):
            kt = (st * NTT + tt) % 3
            tt0 = p.emit("act", lambda e, tt=tt, par=par, kt=kt: e.activation(
                t0_sb[kt][:, :], xn2_sb[par][:, tt * D:(tt + 1) * D], AF.Relu,
                scale=scol("i0"), bias=cst[:, 3:4]),
                deps=[dx[tt], war2["t0"][kt]])
            last_tt0 = tt0
            t_xq[tt] = p.emit("dve", lambda e, tt=tt, par=par, kt=kt:
                e.tensor_scalar(
                    xq_sb[par][:, tt * D:(tt + 1) * D], t0_sb[kt][:, :],
                    -MAGIC, None, ALU.add),
                deps=[tt0, lastrd_xq.get(st - 2)])
            war2["t0"][kt] = t_xq[tt]
        lastrd_xn2[st] = last_tt0
        t_xqT = {}
        ev_list = []
        tr_last = None
        for dc in range(DC):
            half = dc % 2
            trk = []
            for tt in range(NTT):
                trk.append(p.emit("pe",
                    lambda e, dc=dc, tt=tt, half=half, par=par: e.transpose(
                        ps_tr2[half][:, tt * 128:(tt + 1) * 128],
                        xq_sb[par][:, tt * D + dc * 128:tt * D + dc * 128 + 128],
                        id16[:, :]),
                    deps=[t_xq[tt], t_wgf[IC - 1]] +
                         (ev_list[-2:-1] if len(ev_list) >= 2 else [])))
            tr_last = trk[-1]
            ev_list.append(p.emit("act",
                lambda e, dc=dc, half=half, par=par: e.copy(
                    xqT[par][:, dc * S:(dc + 1) * S], ps_tr2[half][:, :]),
                deps=trk + [mml2.get(st - 2), t_xq_dma.get((st - 2, dc))]))
            t_xqT[dc] = ev_list[-1]
            t_xq_dma[(st, dc)] = p.dma("act", "acq",
                lambda e, dc=dc, st=st, par=par: e.dma_start(
                    out=xqt_d[dc * 128:(dc + 1) * 128, st * S:(st + 1) * S],
                    in_=xqT[par][:, dc * S:(dc + 1) * S]), deps=[ev_list[-1]])
        lastrd_xq[st] = tr_last

        mm_last = None
        for ic_ in range(IC):
            icg = st * IC + ic_
            kb = icg % RBD
            mms = []
            for dc in range(DC):
                deps = [t_xqT[dc], t_wgf[ic_]]
                prev_ev = t_gev.get((st, ic_ - 2), t_gev.get((st - 1, IC - 2 + ic_)))
                if prev_ev is not None:
                    deps.append(prev_ev)
                mms.append(p.emit("pe",
                    lambda e, ic_=ic_, dc=dc, par=par: e.matmul(
                        ps_g2[ic_ % 2][:, :],
                        wgfT[:, (ic_ * DC + dc) * 128:(ic_ * DC + dc + 1) * 128],
                        xqT[par][:, dc * S:(dc + 1) * S],
                        start=(dc == 0), stop=(dc == DC - 1)), deps=deps))
            mm_last = mms[-1]
            tev = p.emit("act", lambda e, ic_=ic_, kb=kb: e.activation(
                gw[kb][:, :], ps_g2[ic_ % 2][:, :], AF.Copy,
                scale=2.0 ** -16),
                deps=[mms[-1], gdma_f.get(icg - RBD)])
            t_gev[(st, ic_)] = tev
            tred = emit_red("m2", gw[kb][:, :], [tev])
            t_gdma[(st, ic_)] = p.dma("act", "acq",
                lambda e, ic_=ic_, st=st, kb=kb: e.dma_start(
                    out=gate_d[ic_ * 128:(ic_ + 1) * 128, st * S:(st + 1) * S],
                    in_=gw[kb][:, :]), deps=[tev, tred])
            gdma_f[icg] = t_gdma[(st, ic_)]
        mml2[st] = mm_last
        for ic_ in up_sched.get(st, []):
            ev, _, new_evs = prep_lpbq_chunk(
                wup_h, wupT, wst2, ps_tr2w, ic_,
                [t_wgf[IC - 1], t_wup.get(ic_ - 1)] + b0, D, prev_ev_track,
                post_scale=scol("s0b"))
            t_wup[ic_] = ev
            prev_ev_track = (prev_ev_track + new_evs)[-2:]

    t_fold2 = fold_ring("m2", 1)
    t_fm2 = max_finalize(1, 1, ps_bc2, [t_fold2])
    b1 = barrier(1, 1, 1, [("s2", "i2")], ps_bc2, [t_fm2])
    t_hs2 = p.emit("dve", lambda e: e.tensor_scalar(
        scol("hs2"), scol("s2"), 0.5, None, ALU.mult), deps=b1)
    t_nh = p.emit("dve", lambda e: e.tensor_scalar(
        scol("nhs2m"), scol("hs2"), -MAGIC, None, ALU.mult), deps=[t_hs2])
    t_p16 = p.emit("dve", lambda e: e.tensor_scalar(
        scol("s2p16"), scol("s2"), 2.0 ** -16, None, ALU.mult), deps=b1)
    b1_all = b1 + [t_hs2, t_nh, t_p16]

    ps2.close()
    ctx2.close()
    ctx_wu2.close()
    p.fence()

    ctx3 = ExitStack()
    RBD = 6
    RCD = 3
    xqT3 = [ctx3.enter_context(nc.sbuf_tensor(f"xqT3_{i}", [128, DC * S], F16))
            for i in range(2)]
    uw = [ctx3.enter_context(nc.sbuf_tensor(f"uw{i}", [128, S], F16))
          for i in range(RBD)]
    gld = [ctx3.enter_context(nc.sbuf_tensor(f"gld{i}", [128, S], F16))
           for i in range(RBD)]
    t2w = [ctx3.enter_context(nc.sbuf_tensor(f"t2w{i}", [128, S], F32))
           for i in range(RCD)]
    th3w = [ctx3.enter_context(nc.sbuf_tensor(f"th3w{i}", [128, S], F32))
            for i in range(RCD)]
    w1w = [ctx3.enter_context(nc.sbuf_tensor(f"w1w{i}", [128, S], F32))
           for i in range(RCD)]
    sqw = [ctx3.enter_context(nc.sbuf_tensor(f"sqw{i}", [128, S], F32))
           for i in range(RCD)]
    gqw = [ctx3.enter_context(nc.sbuf_tensor(f"gqw{i}", [128, S], F16))
           for i in range(RCD)]
    g2w = [ctx3.enter_context(nc.sbuf_tensor(f"g2w{i}", [128, S], F16))
           for i in range(RBD)]

    ps3 = ExitStack()
    ps_u = [ps3.enter_context(nc.psum_tensor(f"ps_u{i}", [128, 512], F32))
            for i in range(2)]
    ps_bc3 = ps3.enter_context(nc.psum_tensor("ps_bc3", [128, 128], F32))

    t_udma = {}
    t_uev = {}
    t_g2dma = {}
    udma_f = {}
    g2dma_f = {}
    tt2_f = {}
    mml3 = {}
    war3 = {"t2": [None] * RCD, "th3": [None] * RCD, "w1": [None] * RCD,
            "sq": [None] * RCD, "gq": [None] * RCD}
    for st in range(NST):
        par = st % 2
        dxq = {}
        for dc in range(DC):
            dxq[dc] = p.dma("sp", "spq",
                lambda e, dc=dc, st=st, par=par: e.dma_start(
                    out=xqT3[par][:, dc * S:(dc + 1) * S],
                    in_=xqt_d[dc * 128:(dc + 1) * 128, st * S:(st + 1) * S]),
                deps=[t_xq_dma[(st, dc)], mml3.get(st - 2)] + b1_all)
        mm_last = None
        for ic_ in range(IC):
            icg = st * IC + ic_
            kb = icg % RBD
            kc = icg % RCD
            mms = []
            for dc in range(DC):
                deps = [dxq[dc], t_wup[ic_]]
                prev_ev = t_uev.get((st, ic_ - 2), t_uev.get((st - 1, IC - 2 + ic_)))
                if prev_ev is not None:
                    deps.append(prev_ev)
                mms.append(p.emit("pe",
                    lambda e, ic_=ic_, dc=dc, par=par: e.matmul(
                        ps_u[ic_ % 2][:, :],
                        wupT[:, (ic_ * DC + dc) * 128:(ic_ * DC + dc + 1) * 128],
                        xqT3[par][:, dc * S:(dc + 1) * S],
                        start=(dc == 0), stop=(dc == DC - 1)), deps=deps))
            mm_last = mms[-1]
            tev = p.emit("act", lambda e, ic_=ic_, kb=kb: e.activation(
                uw[kb][:, :], ps_u[ic_ % 2][:, :], AF.Copy,
                scale=2.0 ** -16),
                deps=[mms[-1], udma_f.get(icg - RBD)])
            t_uev[(st, ic_)] = tev
            tred1 = emit_red("m1", uw[kb][:, :], [tev])
            t_udma[(st, ic_)] = p.dma("sp", "spq",
                lambda e, ic_=ic_, st=st, kb=kb: e.dma_start(
                    out=up_d[ic_ * 128:(ic_ + 1) * 128, st * S:(st + 1) * S],
                    in_=uw[kb][:, :]), deps=[tev, tred1])
            udma_f[icg] = t_udma[(st, ic_)]

            dg = p.dma("sp", "spq", lambda e, ic_=ic_, st=st, kb=kb: e.dma_start(
                out=gld[kb][:, :],
                in_=gate_d[ic_ * 128:(ic_ + 1) * 128, st * S:(st + 1) * S]),
                deps=[t_gdma[(st, ic_)], tt2_f.get(icg - RBD)])
            tt2 = p.emit("pool", lambda e, kb=kb, kc=kc: e.tensor_scalar(
                t2w[kc][:, :], gld[kb][:, :], scol("i2"), MAGIC,
                ALU.mult, ALU.add),
                deps=[dg, war3["t2"][kc]])
            tt2_f[icg] = tt2
            tth3 = p.emit("act", lambda e, kc=kc: e.activation(
                th3w[kc][:, :], t2w[kc][:, :], AF.Tanh, bias=scol("nhs2m"),
                scale=scol("hs2")), deps=[tt2, war3["th3"][kc]])
            tw1_ = p.emit("act", lambda e, kc=kc: e.activation(
                w1w[kc][:, :], th3w[kc][:, :], AF.Relu, scale=32768.0,
                bias=cst[:, 4:5]), deps=[tth3, war3["w1"][kc]])
            war3["th3"][kc] = tw1_
            tsq = p.emit("dve", lambda e, kc=kc: e.tensor_scalar(
                sqw[kc][:, :], w1w[kc][:, :], -MAGIC, 65535.0,
                ALU.add, ALU.min), deps=[tw1_, war3["sq"][kc]])
            war3["w1"][kc] = tsq
            tm2 = p.emit("pool", lambda e, kc=kc: e.tensor_scalar(
                gqw[kc][:, :], t2w[kc][:, :], 1.0, -MAGIC,
                ALU.mult, ALU.add),
                deps=[tt2, war3["gq"][kc]])
            war3["t2"][kc] = tm2
            tg2 = p.emit("dve", lambda e, kb=kb, kc=kc: e.scalar_tensor_tensor(
                g2w[kb][:, :], sqw[kc][:, :], scol("s2p16"), gqw[kc][:, :],
                ALU.mult, ALU.mult),
                deps=[tsq, tm2, g2dma_f.get(icg - RBD)])
            war3["sq"][kc] = tg2
            war3["gq"][kc] = tg2
            tred3 = emit_red("m3", g2w[kb][:, :], [tg2])
            dg2 = p.dma("act", "acq",
                lambda e, ic_=ic_, st=st, kb=kb: e.dma_start(
                    out=g2_d[ic_ * 128:(ic_ + 1) * 128, st * S:(st + 1) * S],
                    in_=g2w[kb][:, :]), deps=[tg2, tred3])
            t_g2dma[(st, ic_)] = dg2
            g2dma_f[icg] = dg2
        mml3[st] = mm_last

    t_fold1 = fold_ring("m1", 2)
    t_fm1 = max_finalize(2, 2, ps_bc3, [t_fold1])
    t_fold3 = fold_ring("m3", 3)
    t_fm3 = max_finalize(3, 3, ps_bc3, [t_fold3, t_fm1])
    b2 = barrier(2, 2, 2, [("s1", "i1"), ("s3", "i3")], ps_bc3, [t_fm3])

    ps3.close()
    ctx3.close()
    ctx_wu1.close()
    p.fence()

    ctx_wd = ExitStack()
    wdT = ctx_wd.enter_context(nc.sbuf_tensor("wdT", [128, DC * IC * 128], F16))
    ctx4 = ExitStack()
    wst4 = {k: ctx4.enter_context(nc.sbuf_tensor(f"r_{k}", [128, n], dt))
            for k, n, dt in [("w32", I, F32), ("w16", I, F16), ("z", I, F32),
                             ("bmax", I // 32, F32), ("sw", I // 32, F32),
                             ("isw", I // 32, F32)]}
    R4 = 10
    R3 = 3
    RP = 8
    u4 = [ctx4.enter_context(nc.sbuf_tensor(f"u4_{i}", [128, S], F16))
          for i in range(R4)]
    g4 = [ctx4.enter_context(nc.sbuf_tensor(f"g4_{i}", [128, S], F16))
          for i in range(R4)]
    t4w = [ctx4.enter_context(nc.sbuf_tensor(f"t4w{i}", [128, S], F32))
           for i in range(R3)]
    gate2w = [ctx4.enter_context(nc.sbuf_tensor(f"gate2w{i}", [128, S], F16))
              for i in range(R3)]
    tuw = [ctx4.enter_context(nc.sbuf_tensor(f"tuw{i}", [128, S], F32))
           for i in range(R3)]
    upqw = [ctx4.enter_context(nc.sbuf_tensor(f"upqw{i}", [128, S], F16))
            for i in range(R3)]
    pw = [ctx4.enter_context(nc.sbuf_tensor(f"pw{i}", [128, S], F16))
          for i in range(RP)]

    ps4 = ExitStack()
    ps_tr4 = [ps4.enter_context(nc.psum_tensor(f"ps4_tr{i}", [128, 512], F16))
              for i in range(2)]
    ps_bc4 = ps4.enter_context(nc.psum_tensor("ps_bc4", [128, 128], F32))

    t_wd = {}
    prev_dep = b2
    prev_ev_track = []
    for dpc in range(DC):
        ev, _, new_evs = prep_lpbq_chunk(wd_h, wdT, wst4, ps_tr4, dpc,
                                         prev_dep, I, prev_ev_track)
        t_wd[dpc] = ev
        prev_ev_track = (prev_ev_track + new_evs)[-2:]
        prev_dep = [ev]

    t_pdma = {}
    war4 = {"u4": [None] * R4, "g4": [None] * R4, "t4": [None] * R3,
            "gate2": [None] * R3, "tu": [None] * R3, "upq": [None] * R3,
            "pw": [None] * RP}
    for st in range(NST):
        for ic_ in range(IC):
            k4 = (st * IC + ic_) % R4
            k3 = (st * IC + ic_) % R3
            kp = (st * IC + ic_) % RP
            du = p.dma("sp", "spq", lambda e, ic_=ic_, st=st, k4=k4: e.dma_start(
                out=u4[k4][:, :],
                in_=up_d[ic_ * 128:(ic_ + 1) * 128, st * S:(st + 1) * S]),
                deps=[t_udma[(st, ic_)], t_wd[DC - 1], war4["u4"][k4]])
            dg = p.dma("act", "acq", lambda e, ic_=ic_, st=st, k4=k4: e.dma_start(
                out=g4[k4][:, :],
                in_=g2_d[ic_ * 128:(ic_ + 1) * 128, st * S:(st + 1) * S]),
                deps=[t_g2dma[(st, ic_)], war4["g4"][k4]])
            tt4 = p.emit("act", lambda e, k4=k4, k3=k3: e.activation(
                t4w[k3][:, :], g4[k4][:, :], AF.Relu,
                scale=scol("i3"), bias=cst[:, 3:4]), deps=[dg, war4["t4"][k3]])
            war4["g4"][k4] = tt4
            tg2_ = p.emit("pool", lambda e, k3=k3: e.tensor_scalar(
                gate2w[k3][:, :], t4w[k3][:, :], -MAGIC, scol("s3"),
                ALU.add, ALU.mult), deps=[tt4, war4["gate2"][k3]])
            war4["t4"][k3] = tg2_
            ttu = p.emit("act", lambda e, k4=k4, k3=k3: e.activation(
                tuw[k3][:, :], u4[k4][:, :], AF.Relu,
                scale=scol("i1"), bias=cst[:, 3:4]), deps=[du, war4["tu"][k3]])
            war4["u4"][k4] = ttu
            tuq = p.emit("pool", lambda e, k3=k3: e.tensor_scalar(
                upqw[k3][:, :], tuw[k3][:, :], -MAGIC, scol("s1"),
                ALU.add, ALU.mult), deps=[ttu, war4["upq"][k3]])
            war4["tu"][k3] = tuq
            tp = p.emit("dve", lambda e, k3=k3, kp=kp: e.tensor_tensor(
                pw[kp][:, :], gate2w[k3][:, :], upqw[k3][:, :], ALU.mult),
                deps=[tg2_, tuq, war4["pw"][kp]])
            war4["gate2"][k3] = tp
            war4["upq"][k3] = tp
            tred4 = emit_red("m4", pw[kp][:, :], [tp])
            t_pdma[(st, ic_)] = p.dma("sp", "spq",
                lambda e, ic_=ic_, st=st, kp=kp: e.dma_start(
                    out=p_d[ic_ * 128:(ic_ + 1) * 128, st * S:(st + 1) * S],
                    in_=pw[kp][:, :]), deps=[tp, tred4])
            war4["pw"][kp] = t_pdma[(st, ic_)]

    t_fold4 = fold_ring("m4", 4)
    t_fm4 = max_finalize(4, 4, ps_bc4, [t_fold4])
    b3 = barrier(3, 4, 1, [("s4", "i4")], ps_bc4, [t_fm4])
    t_e4 = p.emit("dve", lambda e: e.tensor_scalar(
        scol("eps4"), scol("i4"), scol("i4"), 1e-5, ALU.mult, ALU.mult),
        deps=b3)
    b3 = b3 + [t_e4]

    ps4.close()
    ctx4.close()
    p.fence()

    ctx5 = ExitStack()
    o_t = [ctx5.enter_context(nc.sbuf_tensor(f"o_t{i}", [128, IC * S], F16))
           for i in range(2)]
    t5w = [ctx5.enter_context(nc.sbuf_tensor(f"t5w{i}", [128, S], F32))
           for i in range(R3)]
    x2l = [ctx5.enter_context(nc.sbuf_tensor(f"x2l{i}", [128, D], F32))
           for i in range(2)]
    xml = ctx5.enter_context(nc.sbuf_tensor("xml", [128, D], F32))
    outw = [ctx5.enter_context(nc.sbuf_tensor(f"outw{i}", [128, D], F32))
            for i in range(2)]

    ps5 = ExitStack()
    ps_y = [ps5.enter_context(nc.psum_tensor(f"ps_y{i}", [128, 1024], F32))
            for i in range(2)]

    t_out_dma = {}
    war5 = {"o": [{} for _ in range(2)], "rd": [{} for _ in range(2)],
            "t5": [None] * R3, "x2l": [None, None],
            "xml": None, "outw": [None, None], "psy": [None, None]}
    for st in range(NST):
        par = st % 2
        t_o = {}
        for ic_ in range(IC):
            k3 = (st * IC + ic_) % R3
            dp = p.dma("sp", "spq", lambda e, ic_=ic_, st=st, par=par: e.dma_start(
                out=o_t[par][:, ic_ * S:(ic_ + 1) * S],
                in_=p_d[ic_ * 128:(ic_ + 1) * 128, st * S:(st + 1) * S]),
                deps=[t_pdma[(st, ic_)], war5["rd"][par].get(ic_),
                      war5["o"][par].get(ic_)] + b3)
            tt5 = p.emit("act", lambda e, ic_=ic_, par=par, k3=k3: e.activation(
                t5w[k3][:, :], o_t[par][:, ic_ * S:(ic_ + 1) * S], AF.Relu,
                scale=scol("i4"), bias=cst[:, 3:4]),
                deps=[dp, war5["t5"][k3]])
            t_o[ic_] = p.emit("dve", lambda e, ic_=ic_, par=par, k3=k3:
                e.tensor_scalar(
                    o_t[par][:, ic_ * S:(ic_ + 1) * S], t5w[k3][:, :], -MAGIC,
                    None, ALU.add), deps=[tt5])
            war5["t5"][k3] = t_o[ic_]
            war5["o"][par][ic_] = t_o[ic_]
        for tt in range(NTT):
            dx2 = p.dma("sp", "spq", lambda e, st=st, tt=tt: e.dma_start(
                out=x2l[tt % 2][:, :],
                in_=x2_d[(st * S + tt * 128):(st * S + tt * 128 + 128), :]),
                deps=[x2_dma[(st, tt)], war5["x2l"][tt % 2]])
            mms_stop = []
            wdap = wdT.ap().rearrange("p (d i c) -> p d i c", i=IC, c=128)
            for half in range(2):
                for ic_ in range(IC):
                    deps = [t_o[ic_]]
                    if war5["psy"][tt % 2] is not None:
                        deps.append(war5["psy"][tt % 2])
                    rh = wdap[:, half * 4:(half + 1) * 4, ic_, :]
                    mm = p.emit("pe",
                        lambda e, ic_=ic_, half=half, tt=tt, rh=rh, par=par:
                        e.matmul(
                            ps_y[tt % 2][:, half * 512:(half + 1) * 512],
                            o_t[par][:,
                                     ic_ * S + tt * 128:ic_ * S + tt * 128 + 128],
                            rh,
                            start=(ic_ == 0), stop=(ic_ == IC - 1)),
                        deps=deps)
                    war5["rd"][par][ic_] = mm
                mms_stop.append(mm)
            lt4 = ln_stats(lambda lo, tt=tt: ps_y[tt % 2][:, lo:lo + 512],
                           CB["negm2"], CB["r2"], mms_stop,
                           extra_negmr=CB["negm2r2"], eps_ap=scol("eps4"))
            txm = p.emit("dve", lambda e, tt=tt: e.tensor_scalar(
                xml[:, :], x2l[tt % 2][:, :],
                colbuf[:, CB["negm2r2"]:CB["negm2r2"] + 1], None, ALU.add),
                deps=lt4 + [dx2, war5["xml"]])
            war5["x2l"][tt % 2] = txm
            tout = p.emit("dve", lambda e, tt=tt: e.scalar_tensor_tensor(
                outw[tt % 2][:, :], ps_y[tt % 2][:, :],
                colbuf[:, CB["r2"]:CB["r2"] + 1], xml[:, :],
                ALU.mult, ALU.add),
                deps=[txm] + lt4 + [war5["outw"][tt % 2]])
            use_cols(tout, [CB["negm2"], CB["r2"], CB["negm2r2"]])
            war5["xml"] = tout
            war5["psy"][tt % 2] = tout
            t_out_dma[(st, tt)] = p.dma("act", "acq",
                lambda e, st=st, tt=tt: e.dma_start(
                    out=out_h[(st * S + tt * 128):(st * S + tt * 128 + 128), :],
                    in_=outw[tt % 2][:, :]), deps=[tout])
            war5["outw"][tt % 2] = t_out_dma[(st, tt)]

    finals = [(sq, v) for sq, v in p.dcnt.items() if v > 0]
    for e in P.ENGINES:
        p.water[e] = {}
        p.wait(e, finals)

    with nc.Block() as block:
        @block.tensor
        def _(eng):
            for op in p.ops["pe"]:
                op(eng)

        @block.scalar
        def _(eng):
            for op in p.ops["act"]:
                op(eng)

        @block.vector
        def _(eng):
            for op in p.ops["dve"]:
                op(eng)

        @block.gpsimd
        def _(eng):
            for op in p.ops["pool"]:
                op(eng)

        @block.sync
        def _(eng):
            for op in p.ops["sp"]:
                op(eng)

    es.pop_all()
    return nc


def _make_inputs(x_core, W_in, b_in, W_gate, b_gate, lam, w_gf, w_up, w_down):
    return {
        "x": np.ascontiguousarray(x_core, dtype=np.float32),
        "w_in": np.ascontiguousarray(W_in, dtype=np.float32),
        "w_gate": np.ascontiguousarray(W_gate, dtype=np.float32),
        "b_in": np.ascontiguousarray(np.reshape(b_in, (1, -1)), dtype=np.float32),
        "b_gate": np.ascontiguousarray(np.reshape(b_gate, (1, -1)), dtype=np.float32),
        "lam": np.ascontiguousarray(np.reshape(lam, (1, -1)), dtype=np.float32),
        "w_up": np.ascontiguousarray(w_up, dtype=np.float32),
        "w_gf": np.ascontiguousarray(w_gf, dtype=np.float32),
        "w_down": np.ascontiguousarray(w_down, dtype=np.float32),
        "id16_h": np.eye(128, dtype=np.float16),
        "id32_h": np.eye(128, dtype=np.float32),
    }


def kernel(L0_attn_x, L0_attn_W_in, L0_attn_b_in, L0_attn_W_gate,
           L0_attn_b_gate, L0_attn_forget_lambda, L0_ffn_w_gate,
           L0_ffn_w_up, L0_ffn_w_down):
    x = np.ascontiguousarray(np.asarray(L0_attn_x, dtype=np.float32))
    B, T, _ = x.shape
    in_maps = [
        _make_inputs(x[c], L0_attn_W_in, L0_attn_b_in, L0_attn_W_gate,
                     L0_attn_b_gate, L0_attn_forget_lambda, L0_ffn_w_gate,
                     L0_ffn_w_up, L0_ffn_w_down)
        for c in range(B)
    ]
    import os
    axon = (bool(os.environ.get("AXON_TERMINAL_JOB_NAME"))
            or os.environ.get("AXON_H4_ENABLED") == "1")
    use_cc = (os.environ.get("BASS_SIM_FP64") == "1") or not axon
    nc = build_program(T=T, use_cc=use_cc, n_cores=B)
    res = run_bass_kernel_spmd(nc, in_maps, core_ids=list(range(B)))
    return np.stack([np.asarray(res.results[c]["out"], dtype=np.float32)
                     for c in range(B)], axis=0)


# version 14
# speedup vs baseline: 1.7737x; 1.2056x over previous
import numpy as np
import concourse.bass as bass
import concourse.mybir as mybir
from concourse.bass_utils import run_bass_kernel_spmd
from contextlib import ExitStack

F32 = mybir.dt.float32
F16 = mybir.dt.float16
AF = mybir.ActivationFunctionType
ALU = mybir.AluOpType
AX = mybir.AxisListType.X

MAGIC = 12582912.0
QMAX = 32767.0
CLIP16 = 65535.0 / 65536.0
D = 1024
I = 3072
DC = D // 128
IC = I // 128
S = 512
NTT = S // 128


class P:
    ENGINES = ("pe", "act", "dve", "pool", "sp")
    DQUEUES = ("spq", "gpq", "acq", "ccq")
    NPOOL = 8

    def __init__(self, nc):
        self.nc = nc
        self.ops = {e: [] for e in self.ENGINES}
        self.cnt = {e: 0 for e in self.ENGINES}
        self.dcnt = {}
        for q in self.DQUEUES:
            for i in range(self.NPOOL):
                self.dcnt[f"{q}{i}"] = 0
        self.didx = {q: 0 for q in self.DQUEUES}
        self.sems = {}
        self.water = {e: {} for e in self.ENGINES}

    TRACE = []

    def wait(self, e, deps):
        for d in deps:
            if d is None:
                continue
            sname, val = d
            if sname == e:
                if val >= self.cnt[e]:
                    val = self.cnt[e]
            w = self.water[e]
            if w.get(sname, 0) >= val:
                continue
            w[sname] = val

            def op(eng, sem_name=sname, val=val, self=self):
                eng.wait_ge(self.sems[sem_name], val)
            self.ops[e].append(op)
            P.TRACE.append(("wait", e, sname, val))

    def emit(self, e, builder, deps=()):
        self.wait(e, deps)
        self.cnt[e] += 1
        val = self.cnt[e]

        def op(eng, builder=builder, e=e, self=self):
            builder(eng).then_inc(self.sems[e], 1)
        self.ops[e].append(op)
        import traceback
        P.TRACE.append(("emit", e, val, traceback.extract_stack()[-2].lineno))
        return (e, val)

    def dma(self, e, q, builder, deps=()):
        sq = f"{q}{self.didx[q] % self.NPOOL}"
        self.didx[q] += 1
        prev = getattr(self, "dlast", None)
        if prev is None:
            self.dlast = prev = {}
        self.wait(e, list(deps) + [prev.get(sq)])
        self.dcnt[sq] += 16
        val = self.dcnt[sq]
        prev[sq] = (sq, val)

        def op(eng, builder=builder, sq=sq, self=self):
            builder(eng).then_inc(self.sems[sq], 16)
        self.ops[e].append(op)
        import traceback
        P.TRACE.append(("dma", e, sq, val, traceback.extract_stack()[-2].lineno))
        return (sq, val)

    def fence(self):
        toks = [(e, self.cnt[e]) for e in self.ENGINES]
        toks += [(sq, self.dcnt[sq]) for sq in self.dcnt]
        for e in self.ENGINES:
            self.wait(e, toks)


def build_program(T=8192, use_cc=True, n_cores=8, dbg=False):
    NST = T // S
    nc = bass.Bass("TRN2", target_bir_lowering=False,
                   num_devices=(n_cores if use_cc else None))

    x_h = nc.dram_tensor("x", [T, D], F32, kind="ExternalInput").ap()
    win_h = nc.dram_tensor("w_in", [D, D], F32, kind="ExternalInput").ap()
    wg_h = nc.dram_tensor("w_gate", [D, D], F32, kind="ExternalInput").ap()
    bin_h = nc.dram_tensor("b_in", [1, D], F32, kind="ExternalInput").ap()
    bg_h = nc.dram_tensor("b_gate", [1, D], F32, kind="ExternalInput").ap()
    lam_h = nc.dram_tensor("lam", [1, D], F32, kind="ExternalInput").ap()
    wup_h = nc.dram_tensor("w_up", [I, D], F32, kind="ExternalInput").ap()
    wgf_h = nc.dram_tensor("w_gf", [I, D], F32, kind="ExternalInput").ap()
    wd_h = nc.dram_tensor("w_down", [D, I], F32, kind="ExternalInput").ap()
    id16_h = nc.dram_tensor("id16_h", [128, 128], F16, kind="ExternalInput").ap()
    id32_h = nc.dram_tensor("id32_h", [128, 128], F32, kind="ExternalInput").ap()
    out_h = nc.dram_tensor("out", [T, D], F32, kind="ExternalOutput").ap()

    kd = "ExternalOutput" if dbg else "Internal"
    x2_d = nc.dram_tensor("x2_d", [T, D], F32, kind=kd).ap()
    xn2_d = nc.dram_tensor("xn2_d", [T, D], F16, kind=kd).ap()
    xqt_d = nc.dram_tensor("xqt_d", [D, T], F16, kind=kd).ap()
    gate_d = nc.dram_tensor("gate_d", [I, T], F16, kind=kd).ap()
    up_d = nc.dram_tensor("up_d", [I, T], F16, kind=kd).ap()
    g2_d = nc.dram_tensor("g2_d", [I, T], F16, kind=kd).ap()
    p_d = nc.dram_tensor("p_d", [I, T], F16, kind=kd).ap()
    ccin_d = [nc.dram_tensor(f"ccin{k}", [1, 2], F32).ap() for k in range(4)]
    ccout_d = [nc.dram_tensor(f"ccout{k}", [1, 2], F32).ap() for k in range(4)]

    p = P(nc)
    es = ExitStack()
    for sname in list(p.cnt) + list(p.dcnt):
        p.sems[sname] = es.enter_context(nc.semaphore(f"s_{sname}"))

    A = nc.alloc_sbuf_tensor
    id16 = A("id16", [128, 128], F16)
    id32 = A("id32", [128, 128], F32)
    ones128 = A("ones128", [1, 128], F32)
    k2 = A("k2", [128, DC], F32)
    bin2 = A("bin2", [128, DC], F32)
    bg2 = A("bg2", [128, DC], F32)
    scal = A("scal", [128, 16], F32)
    srow = A("srow", [1, 16], F32)
    mcol = A("mcol", [128, 8], F32)
    mrow = A("mrow", [1, 128], F32)
    mcell = A("mcell", [1, 8], F32)
    lamt = A("lamt", [128, DC], F32)
    btmp = A("btmp", [128, DC], F32)
    bnbuf = A("bnbuf", [128, 12], F32)
    stats = A("stats", [128, 2], F32)
    colbuf = A("colbuf", [128, 16], F32)
    redcol = A("redcol", [128, 1], F32)
    redring = A("redring", [128, 40], F32)
    cst = A("cst", [128, 8], F32)

    SC = {"s0": 0, "i0": 1, "s2": 2, "i2": 3, "hs2": 4, "nhs2m": 5,
          "s1": 6, "i1": 7, "s3": 8, "i3": 9, "s4": 10, "i4": 11,
          "s2p16": 12, "eps4": 13, "s0b": 14}

    def scol(name):
        return scal[:, SC[name]:SC[name] + 1]

    d_id16 = p.dma("sp", "spq", lambda e: e.dma_start(out=id16[:, :], in_=id16_h))
    d_id32 = p.dma("sp", "spq", lambda e: e.dma_start(out=id32[:, :], in_=id32_h))
    d_lam = p.dma("sp", "spq", lambda e: e.dma_start(
        out=lamt[:, :], in_=lam_h.rearrange("o (c p) -> (o p) c", p=128),
        allow_slow_non_contiguous=True))
    d_bin = p.dma("sp", "spq", lambda e: e.dma_start(
        out=btmp[:, :], in_=bin_h.rearrange("o (c p) -> (o p) c", p=128),
        allow_slow_non_contiguous=True))
    t_ones = p.emit("dve", lambda e: e.memset(ones128[:, :], 1.0))
    t_c0 = p.emit("dve", lambda e: e.memset(cst[:, 0:1], 1e-5))
    t_c1 = p.emit("dve", lambda e: e.memset(cst[:, 1:2], 1.0))
    t_c2 = p.emit("dve", lambda e: e.memset(cst[:, 2:3], 0.25 * (1 + 1e-6)))
    t_c3 = p.emit("dve", lambda e: e.memset(cst[:, 3:4], MAGIC))
    t_c4 = p.emit("dve", lambda e: e.memset(cst[:, 4:5], 32768.0 + MAGIC))
    t_srz = p.emit("dve", lambda e: e.memset(srow[:, :], 0.0))
    t_mz = p.emit("dve", lambda e: e.memset(mcol[:, :], 0.0))
    t_e = p.emit("act", lambda e: e.activation(lamt[:, :], lamt[:, :], AF.Exp),
                 deps=[d_lam])
    t_l = p.emit("act", lambda e: e.activation(lamt[:, :], lamt[:, :], AF.Ln,
                                               bias=cst[:, 1:2]), deps=[t_e, t_c1])
    t_k2 = p.emit("dve", lambda e: e.tensor_scalar(
        k2[:, :], lamt[:, :], 4.0, None, ALU.mult), deps=[t_l])
    t_b1 = p.emit("dve", lambda e: e.tensor_scalar(
        bin2[:, :], btmp[:, :], 0.5, None, ALU.mult), deps=[d_bin])
    d_bg = p.dma("sp", "spq", lambda e: e.dma_start(
        out=btmp[:, :], in_=bg_h.rearrange("o (c p) -> (o p) c", p=128),
        allow_slow_non_contiguous=True), deps=[t_b1])
    t_b2 = p.emit("dve", lambda e: e.tensor_scalar(
        bg2[:, :], btmp[:, :], 0.5, None, ALU.mult), deps=[d_bg])
    pro_toks = [t_k2, t_b2, t_ones, t_mz, d_id16, d_id32, t_c0, t_c2, t_c3, t_c4]

    ln_prev = {"t5": None, "stats_r": None}
    reds = {"m0": [], "m2": [], "m1": [], "m3": [], "m4": []}
    RB = {"m0": 0, "m2": 8, "m1": 16, "m3": 24, "m4": 32}
    MC = {"m0": 0, "m2": 1, "m1": 2, "m3": 3, "m4": 4}
    last_fold = {}

    def _fold8(stream, nret):
        b = RB[stream]
        c = MC[stream]
        tr = p.emit("dve", lambda e, b=b, nret=nret: e.tensor_reduce(
            redcol[:, :], redring[:, b:b + nret], AX, ALU.max),
            deps=[reds[stream][-1], last_fold.get(stream, t_mz)])
        tm = p.emit("dve", lambda e, c=c: e.tensor_tensor(
            mcol[:, c:c + 1], mcol[:, c:c + 1], redcol[:, :], ALU.max),
            deps=[tr, t_mz])
        last_fold[stream] = tm
        return tm

    def emit_red(stream, src_ap, deps):
        k = RB[stream] + (len(reds[stream]) % 8)
        t = p.emit("dve", lambda e, k=k, src_ap=src_ap: e.tensor_reduce(
            redring[:, k:k + 1], src_ap, AX, ALU.max,
            apply_absolute_value=True),
            deps=list(deps) + [last_fold.get(stream)
                               if len(reds[stream]) % 8 == 0 else None])
        reds[stream].append(t)
        if len(reds[stream]) % 8 == 0:
            _fold8(stream, 8)
        return t

    def fold_ring(stream, col_idx):
        rem = len(reds[stream]) % 8
        if rem:
            _fold8(stream, rem)
        return last_fold[stream]
    colw = {}

    def ln_stats(src_ap_fn, negm_col, r_col, deps, extra_negmr=None,
                 eps_ap=None):
        war = [colw.get(r_col), ln_prev["t5"]]
        t1 = p.emit("dve", lambda e: e.bn_stats(bnbuf[:, 0:6], src_ap_fn(0)),
                    deps=list(deps) + [ln_prev["stats_r"]])
        t2 = p.emit("dve", lambda e: e.bn_stats(bnbuf[:, 6:12], src_ap_fn(512)),
                    deps=list(deps))
        t3 = p.emit("dve", lambda e: e.bn_aggr(stats[:, 0:2], bnbuf[:, 0:12]),
                    deps=[t1, t2])
        t4 = p.emit("dve", lambda e: e.tensor_scalar(
            colbuf[:, negm_col:negm_col + 1], stats[:, 0:1], -1.0, None,
            ALU.mult), deps=[t3])
        if eps_ap is None:
            eps_ap = cst[:, 0:1]
        t5 = p.emit("act", lambda e, eps_ap=eps_ap: e.activation(
            colbuf[:, r_col:r_col + 1], stats[:, 1:2], AF.Sqrt,
            bias=eps_ap), deps=[t3] + war)
        t6 = p.emit("dve", lambda e: e.reciprocal(
            colbuf[:, r_col:r_col + 1], colbuf[:, r_col:r_col + 1]), deps=[t5])
        ln_prev["t5"] = t5
        ln_prev["stats_r"] = t5
        toks = [t4, t6]
        if extra_negmr is not None:
            toks.append(p.emit("dve", lambda e: e.tensor_tensor(
                colbuf[:, extra_negmr:extra_negmr + 1],
                colbuf[:, negm_col:negm_col + 1],
                colbuf[:, r_col:r_col + 1], ALU.mult), deps=[t4, t6]))
        return toks

    def use_cols(tok, cols):
        for c in cols:
            colw[c] = tok

    def max_finalize(col_idx, cell_idx, ps_bc, deps):
        ttr = p.emit("pe", lambda e: e.transpose(
            ps_bc[0:1, 0:128], mcol[:, col_idx:col_idx + 1], id32[:, :]),
            deps=deps)
        tv = p.emit("dve", lambda e: e.tensor_copy(
            mrow[:, :], ps_bc[0:1, 0:128]), deps=[ttr])
        tr = p.emit("dve", lambda e: e.tensor_reduce(
            mcell[:, cell_idx:cell_idx + 1], mrow[:, :], AX, ALU.max),
            deps=[tv])
        return tr

    def barrier(kidx, cell0, n, scols, ps_bc, deps):
        toks = list(deps)
        if use_cc:
            d1 = p.dma("pool", "gpq", lambda e: e.dma_start(
                out=ccin_d[kidx][:, 0:n], in_=mcell[:, cell0:cell0 + n]),
                deps=toks)
            tcc = p.dma("pool", "ccq", lambda e: e.collective_compute(
                "AllReduce", ALU.max, replica_groups=[list(range(n_cores))],
                ins=[ccin_d[kidx][:, 0:n].opt()],
                outs=[ccout_d[kidx][:, 0:n].opt()]), deps=[d1])
            d2 = p.dma("pool", "gpq", lambda e: e.dma_start(
                out=mcell[:, cell0:cell0 + n], in_=ccout_d[kidx][:, 0:n]),
                deps=[tcc])
            toks = [d2]
        stoks = []
        for j in range(n):
            ts_ = p.emit("dve", lambda e, j=j: e.tensor_scalar(
                srow[:, 2 * j:2 * j + 1], mcell[:, cell0 + j:cell0 + j + 1],
                1.0 / QMAX, 1e-12, ALU.mult, ALU.max), deps=toks)
            ti_ = p.emit("dve", lambda e, j=j: e.reciprocal(
                srow[:, 2 * j + 1:2 * j + 2], srow[:, 2 * j:2 * j + 1]),
                deps=[ts_])
            stoks.append(ti_)
        tm = p.emit("pe", lambda e: e.matmul(
            ps_bc[0:128, 0:16], ones128[:, :], srow[:, :],
            start=True, stop=True), deps=stoks + [t_srz])
        evs = []
        for j, (sc_, ic_) in enumerate(scols):
            evs.append(p.emit("dve", lambda e, sc_=sc_, j=j: e.tensor_copy(
                scal[:, SC[sc_]:SC[sc_] + 1], ps_bc[0:128, 2 * j:2 * j + 1]),
                deps=[tm]))
            evs.append(p.emit("dve", lambda e, ic_=ic_, j=j: e.tensor_copy(
                scal[:, SC[ic_]:SC[ic_] + 1],
                ps_bc[0:128, 2 * j + 1:2 * j + 2]), deps=[tm]))
        return evs

    def prep_lpbq_chunk(w_hbm, dst, st_, ps_tr, ic, deps, ncols, prev_ev,
                        post_scale=None):
        ncc = ncols // 128
        nb = ncols // 32
        dld = p.dma("sp", "spq", lambda e: e.dma_start(
            out=st_["w32"][:, :ncols], in_=w_hbm[ic * 128:(ic + 1) * 128, :]),
            deps=deps)
        w3 = st_["w32"].ap()[:, :ncols].rearrange("p (b c) -> p b c", c=32)
        t1 = p.emit("dve", lambda e: e.tensor_reduce(
            st_["bmax"][:, :nb], w3, AX, ALU.max, apply_absolute_value=True),
            deps=[dld])
        t2 = p.emit("dve", lambda e: e.tensor_scalar(
            st_["sw"][:, :nb], st_["bmax"][:, :nb], 1.0 / 7.0, 1e-12,
            ALU.mult, ALU.max), deps=[t1])
        t3 = p.emit("dve", lambda e: e.reciprocal(
            st_["isw"][:, :nb], st_["sw"][:, :nb]), deps=[t2])
        if post_scale is not None:
            t2b = p.emit("dve", lambda e: e.tensor_scalar(
                st_["sw"][:, :nb], st_["sw"][:, :nb], post_scale, None,
                ALU.mult), deps=[t3])
            t3 = t2b
        isw3 = st_["isw"].ap()[:, :nb].rearrange("p (b o) -> p b o", o=1)
        sw3 = st_["sw"].ap()[:, :nb].rearrange("p (b o) -> p b o", o=1)
        z3 = st_["z"].ap()[:, :ncols].rearrange("p (b c) -> p b c", c=32)
        wq3 = st_["w16"].ap()[:, :ncols].rearrange("p (b c) -> p b c", c=32)
        a_, b_ = bass.broadcast_tensor_aps(w3, isw3)
        t4 = p.emit("dve", lambda e, a_=a_, b_=b_: e.tensor_tensor(
            z3, a_, b_, ALU.mult), deps=[t3])
        t5 = p.emit("dve", lambda e: e.tensor_scalar(
            st_["z"][:, :ncols], st_["z"][:, :ncols], MAGIC, -MAGIC,
            ALU.add, ALU.add), deps=[t4])
        a2_, b2_ = bass.broadcast_tensor_aps(z3, sw3)
        t6 = p.emit("dve", lambda e, a2_=a2_, b2_=b2_: e.tensor_tensor(
            wq3, a2_, b2_, ALU.mult), deps=[t5])
        ev_toks = list(prev_ev)
        tr_toks = []
        for dc in range(ncc):
            half = (dc // 4) % 2
            tr_toks.append(p.emit("pe", lambda e, dc=dc, half=half:
                e.transpose(
                    ps_tr[half][:, (dc % 4) * 128:(dc % 4 + 1) * 128],
                    st_["w16"][:, dc * 128:(dc + 1) * 128], id16[:, :]),
                deps=[t6] + (ev_toks[-2:-1] if len(ev_toks) >= 2
                             else ev_toks)))
            if dc % 4 == 3:
                ev_toks.append(p.emit("act", lambda e, dc=dc, half=half,
                                      ic=ic, ncc=ncc: e.copy(
                    dst[:, (ic * ncc + dc - 3) * 128:
                        (ic * ncc + dc + 1) * 128],
                    ps_tr[half][:, :]), deps=tr_toks[-4:]))
        return ev_toks[-1], t6, ev_toks[len(prev_ev):]

    ctx_w = ExitStack()
    winT = ctx_w.enter_context(nc.sbuf_tensor("winT", [128, DC * DC * 128], F16))
    wgT = ctx_w.enter_context(nc.sbuf_tensor("wgT", [128, DC * DC * 128], F16))

    ctx1 = ExitStack()
    wst = {k: ctx1.enter_context(nc.sbuf_tensor(f"p_{k}", [128, n], dt))
           for k, n, dt in [("w32", D, F32), ("w16", D, F16), ("z", D, F32),
                            ("bmax", D // 32, F32), ("sw", D // 32, F32),
                            ("isw", D // 32, F32)]}
    xs = [ctx1.enter_context(nc.sbuf_tensor(f"xs{i}", [128, D], F32))
          for i in range(2 * NTT)]
    xn_st = ctx1.enter_context(nc.sbuf_tensor("xn_st", [128, NTT * D], F16))
    xnT = ctx1.enter_context(nc.sbuf_tensor("xnT", [128, DC * S], F16))
    th_rg = ctx1.enter_context(nc.sbuf_tensor("th_rg", [128, DC * S], F16))
    th_ig = ctx1.enter_context(nc.sbuf_tensor("th_ig", [128, DC * S], F16))
    g_t = ctx1.enter_context(nc.sbuf_tensor("g_t", [128, DC * S], F32))
    al_t = ctx1.enter_context(nc.sbuf_tensor("al_t", [128, DC * S], F32))
    bh_t = ctx1.enter_context(nc.sbuf_tensor("bh_t", [128, DC * S], F16))
    u1_t = ctx1.enter_context(nc.sbuf_tensor("u1_t", [128, DC * S], F16))
    xb_t = ctx1.enter_context(nc.sbuf_tensor("xb_t", [128, DC * S], F16))
    h_t = [ctx1.enter_context(nc.sbuf_tensor(f"h{i}", [128, DC * S], F32))
           for i in range(2)]
    x2_tile = ctx1.enter_context(nc.sbuf_tensor("x2_tile", [128, D], F32))
    xm_tile = ctx1.enter_context(nc.sbuf_tensor("xm_tile", [128, D], F32))
    xn2_tile = ctx1.enter_context(nc.sbuf_tensor("xn2_t", [128, D], F16))

    ps1 = ExitStack()
    ps_tr = [ps1.enter_context(nc.psum_tensor(f"ps_tr{i}", [128, 512], F16))
             for i in range(2)]
    ps_g = [ps1.enter_context(nc.psum_tensor(f"ps_g{i}", [128, 512], F32))
            for i in range(2)]
    ps_h = ps1.enter_context(nc.psum_tensor("ps_h", [128, 1024], F32))
    ps_bc = ps1.enter_context(nc.psum_tensor("ps_bc", [128, 128], F32))

    def prep_square(w_hbm, dst, deps):
        toks = []
        last = None
        for dpc in range(DC):
            dld = p.dma("sp", "spq", lambda e, dpc=dpc: e.dma_start(
                out=wst["w32"][:, :], in_=w_hbm[dpc * 128:(dpc + 1) * 128, :]),
                deps=deps + ([last] if last else []))
            tc = p.emit("dve", lambda e: e.tensor_copy(
                wst["w16"][:, :], wst["w32"][:, :]), deps=[dld])
            last = tc
            tr_toks = []
            for dc in range(DC):
                half = (dc // 4) % 2
                tr_toks.append(p.emit("pe", lambda e, dc=dc, half=half:
                    e.transpose(
                        ps_tr[half][:, (dc % 4) * 128:(dc % 4 + 1) * 128],
                        wst["w16"][:, dc * 128:(dc + 1) * 128], id16[:, :]),
                    deps=[tc] + (toks[-2:-1] if len(toks) >= 2 else [])))
                if dc % 4 == 3:
                    toks.append(p.emit("act", lambda e, dc=dc, half=half,
                                       dpc=dpc: e.copy(
                        dst[:, (dpc * DC + dc - 3) * 128:
                            (dpc * DC + dc + 1) * 128],
                        ps_tr[half][:, :]), deps=tr_toks[-4:]))
            last = toks[-1]
        return toks[-1]

    tw1 = prep_square(win_h, winT, pro_toks)
    tw2 = prep_square(wg_h, wgT, [tw1])

    CB = {"negm1": 0, "r1": 4, "negm2": 8, "r2": 9, "negm2r2": 10,
          "negm3": 11, "r3": 12}

    x_dma = {}
    x2_dma = {}
    xn2_dma = {}
    t_scan = {}
    t_x_read = {}
    war = {
        "x2t_dma": None,
        "xn_st": None,
        "xnT": None,
        "xnT_dve": None,
        "th_rg": None, "th_ig": None,
        "g": {}, "al": {},
        "xb": None,
        "h_tr": [None, None],
        "x2t": None, "xn2t": None, "xn2t_pool": None,
        "proj_tail": None,
    }

    for st0 in range(min(2, NST)):
        for tt in range(NTT):
            x_dma[(st0, tt)] = p.dma(
                "sp", "spq", lambda e, st0=st0, tt=tt: e.dma_start(
                    out=xs[(st0 % 2) * NTT + tt][:, :],
                    in_=x_h[(st0 * S + tt * 128):(st0 * S + tt * 128 + 128), :]))

    def emit_tail(st2, scan_toks, hc2):
        t_ht_last = None
        prev_x2_stt = war.get("ps_h_reader")
        for tt in range(NTT):
            trh = []
            for dc in range(DC):
                trh.append(p.emit("pe", lambda e, dc=dc, tt=tt, hc2=hc2:
                    e.transpose(
                        ps_h[:, dc * 128:(dc + 1) * 128],
                        hc2[:, dc * S + tt * 128:dc * S + tt * 128 + 128],
                        id32[:, :]),
                    deps=[scan_toks[dc]] +
                         ([prev_x2_stt] if prev_x2_stt else [])))
            lt2 = ln_stats(lambda lo: ps_h[:, lo:lo + 512], CB["negm2"],
                           CB["r2"], trh, extra_negmr=CB["negm2r2"])
            xb_ = xs[(st2 % 2) * NTT + tt]
            txm = p.emit("dve", lambda e, xb_=xb_: e.tensor_scalar(
                xm_tile[:, :], xb_[:, :],
                colbuf[:, CB["negm2r2"]:CB["negm2r2"] + 1], None, ALU.add),
                deps=lt2)
            t_x_read[(st2, tt)] = txm
            tx2 = p.emit("dve", lambda e: e.scalar_tensor_tensor(
                x2_tile[:, :], ps_h[:, :],
                colbuf[:, CB["r2"]:CB["r2"] + 1], xm_tile[:, :],
                ALU.mult, ALU.add),
                deps=[txm] + lt2 + [war["x2t"], war["x2t_dma"]])
            use_cols(tx2, [CB["negm2"], CB["r2"], CB["negm2r2"]])
            prev_x2_stt = tx2
            war["ps_h_reader"] = tx2
            x2_dma[(st2, tt)] = p.dma("pool", "gpq",
                lambda e, st2=st2, tt=tt: e.dma_start(
                    out=x2_d[(st2 * S + tt * 128):
                             (st2 * S + tt * 128 + 128), :],
                    in_=x2_tile[:, :]), deps=[tx2])
            lt3 = ln_stats(lambda lo: x2_tile[:, lo:lo + 512], CB["negm3"],
                           CB["r3"], [tx2])
            txn2 = p.emit("dve", lambda e: e.tensor_scalar(
                xn2_tile[:, :], x2_tile[:, :],
                colbuf[:, CB["negm3"]:CB["negm3"] + 1],
                colbuf[:, CB["r3"]:CB["r3"] + 1], ALU.add, ALU.mult),
                deps=lt3 + [war["xn2t"], war["xn2t_pool"]])
            use_cols(txn2, [CB["negm3"], CB["r3"]])
            war["x2t"] = txn2
            war["x2t_dma"] = x2_dma[(st2, tt)]
            tred = emit_red("m0", xn2_tile[:, :], [txn2])
            xn2_dma[(st2, tt)] = p.dma("pool", "gpq",
                lambda e, st2=st2, tt=tt: e.dma_start(
                    out=xn2_d[(st2 * S + tt * 128):
                              (st2 * S + tt * 128 + 128), :],
                    in_=xn2_tile[:, :]), deps=[txn2])
            war["xn2t"] = xn2_dma[(st2, tt)]
            war["xn2t_pool"] = tred
            t_ht_last = trh[-1]
        war["h_tr"][st2 % 2] = t_ht_last

    pend_tail = None
    for st in range(NST):
        t_xn = {}
        for tt in range(NTT):
            xb_ = xs[(st % 2) * NTT + tt]
            lt = ln_stats(lambda lo, xb_=xb_: xb_[:, lo:lo + 512],
                          CB["negm1"] + tt, CB["r1"] + tt,
                          [x_dma[(st, tt)]])
            t_xn[tt] = p.emit("dve", lambda e, tt=tt, xb_=xb_: e.tensor_scalar(
                xn_st[:, tt * D:(tt + 1) * D], xb_[:, :],
                colbuf[:, CB["negm1"] + tt:CB["negm1"] + tt + 1],
                colbuf[:, CB["r1"] + tt:CB["r1"] + tt + 1],
                ALU.add, ALU.mult), deps=lt + [war["xn_st"]])
            use_cols(t_xn[tt], [CB["negm1"] + tt, CB["r1"] + tt])

        t_xnT = {}
        ev_list = []
        tr_last = None
        for dc in range(DC):
            half = dc % 2
            trk = []
            for tt in range(NTT):
                trk.append(p.emit("pe", lambda e, dc=dc, tt=tt, half=half:
                    e.transpose(
                        ps_tr[half][:, tt * 128:(tt + 1) * 128],
                        xn_st[:, tt * D + dc * 128:tt * D + dc * 128 + 128],
                        id16[:, :]),
                    deps=[t_xn[tt], tw2] +
                         (ev_list[-2:-1] if len(ev_list) >= 2 else [])))
            tr_last = trk[-1]
            ev_list.append(p.emit("act", lambda e, dc=dc, half=half: e.copy(
                xnT[:, dc * S:(dc + 1) * S], ps_tr[half][:, :]),
                deps=trk + [war["xnT"], war["xnT_dve"]]))
            t_xnT[dc] = ev_list[-1]
        war["xn_st"] = tr_last

        def proj(wT, th_dst, bias_t, prev_tail, th_war):
            evs = {}
            mm_last = None
            for c in range(DC):
                mms = []
                for dc in range(DC):
                    deps = [t_xnT[dc]]
                    if c >= 2:
                        deps.append(evs[c - 2])
                    elif prev_tail is not None:
                        deps.append(prev_tail.get(c - 2 + DC))
                    mms.append(p.emit("pe",
                        lambda e, c=c, dc=dc, wT=wT: e.matmul(
                            ps_g[c % 2][:, :],
                            wT[:, (c * DC + dc) * 128:(c * DC + dc + 1) * 128],
                            xnT[:, dc * S:(dc + 1) * S],
                            start=(dc == 0), stop=(dc == DC - 1)),
                        deps=deps))
                evs[c] = p.emit("act", lambda e, c=c, bias_t=bias_t,
                                th_dst=th_dst: e.activation(
                    th_dst[:, c * S:(c + 1) * S], ps_g[c % 2][:, :], AF.Tanh,
                    bias=bias_t[:, c:c + 1], scale=0.5),
                    deps=[mms[-1], th_war])
                mm_last = mms[-1]
            return evs, mm_last

        ev_rg, mm_l1 = proj(wgT, th_rg, bg2, war["proj_tail"], war["th_rg"])

        if pend_tail is not None:
            emit_tail(*pend_tail)
        pend_tail = None

        t_bh = {}
        t_al = {}
        t_g = {}
        for c in range(DC):
            t_g[c] = p.emit("dve", lambda e, c=c: e.tensor_scalar(
                g_t[:, c * S:(c + 1) * S], th_rg[:, c * S:(c + 1) * S],
                1.0, k2[:, c:c + 1], ALU.add, ALU.mult),
                deps=[ev_rg[c], war["g"].get(c)])
            t_al[c] = p.emit("act", lambda e, c=c: e.activation(
                al_t[:, c * S:(c + 1) * S], g_t[:, c * S:(c + 1) * S],
                AF.Exp, scale=-1.0), deps=[t_g[c], war["al"].get(c)])
            ta2 = p.emit("act", lambda e, c=c: e.activation(
                g_t[:, c * S:(c + 1) * S], g_t[:, c * S:(c + 1) * S],
                AF.Exp, scale=-2.0), deps=[t_al[c]])
            t_bh[c] = p.emit("act", lambda e, c=c: e.activation(
                bh_t[:, c * S:(c + 1) * S], g_t[:, c * S:(c + 1) * S],
                AF.Sqrt, scale=-0.25, bias=cst[:, 2:3]),
                deps=[ta2, war["xb"]])
        war["th_rg"] = t_g[DC - 1]

        ev_ig, mm_l2 = proj(winT, th_ig, bin2, ev_rg, war["th_ig"])
        war["proj_tail"] = ev_ig
        war["xnT"] = mm_l2

        hc = h_t[st % 2]
        hp = h_t[(st - 1) % 2]
        new_scan = {}
        t_u_last = None
        for c in range(DC):
            tu = p.emit("dve", lambda e, c=c: e.scalar_tensor_tensor(
                u1_t[:, c * S:(c + 1) * S], th_ig[:, c * S:(c + 1) * S],
                1.0, xnT[:, c * S:(c + 1) * S], ALU.add, ALU.mult),
                deps=[ev_ig[c], t_xnT[c]])
            tx = p.emit("dve", lambda e, c=c: e.tensor_tensor(
                xb_t[:, c * S:(c + 1) * S], u1_t[:, c * S:(c + 1) * S],
                bh_t[:, c * S:(c + 1) * S], ALU.mult), deps=[tu, t_bh[c]])
            if st == 0:
                init = 0.0
                dep_init = []
            else:
                init = hp[:, c * S + S - 1:c * S + S]
                dep_init = [t_scan[c]]
            new_scan[c] = p.emit("dve", lambda e, c=c, init=init, hc=hc:
                e.tensor_tensor_scan(
                    hc[:, c * S:(c + 1) * S], al_t[:, c * S:(c + 1) * S],
                    xb_t[:, c * S:(c + 1) * S], init, ALU.mult, ALU.add),
                deps=[tx, t_al[c], war["h_tr"][st % 2]] + dep_init)
            war["g"][c] = new_scan[c]
            war["al"][c] = new_scan[c]
            t_u_last = tu
        t_scan = new_scan
        war["th_ig"] = t_u_last
        war["xnT_dve"] = t_u_last
        war["xb"] = new_scan[DC - 1]

        pend_tail = (st, new_scan, hc)
        if st >= 1 and st + 1 < NST:
            for tt in range(NTT):
                x_dma[(st + 1, tt)] = p.dma(
                    "sp", "spq", lambda e, st=st, tt=tt: e.dma_start(
                        out=xs[((st + 1) % 2) * NTT + tt][:, :],
                        in_=x_h[((st + 1) * S + tt * 128):
                                ((st + 1) * S + tt * 128 + 128), :]),
                    deps=[t_x_read.get((st - 1, tt))])

    if pend_tail is not None:
        emit_tail(*pend_tail)
    t_fold0 = fold_ring("m0", 0)
    t_fm0 = max_finalize(0, 0, ps_bc, [t_fold0])
    b0 = barrier(0, 0, 1, [("s0", "i0")], ps_bc, [t_fm0])
    t_s0b = p.emit("dve", lambda e: e.tensor_scalar(
        scol("s0b"), scol("s0"), 65536.0, None, ALU.mult), deps=b0)
    b0 = b0 + [t_s0b]

    ps1.close()
    ctx1.close()
    ctx_w.close()
    p.fence()

    ctx_wu1 = ExitStack()
    wupT = ctx_wu1.enter_context(nc.sbuf_tensor("wupT", [128, IC * DC * 128], F16))
    ctx_wu2 = ExitStack()
    wgfT = ctx_wu2.enter_context(nc.sbuf_tensor("wgfT", [128, IC * DC * 128], F16))

    ctx2 = ExitStack()
    wst2 = {k: ctx2.enter_context(nc.sbuf_tensor(f"q_{k}", [128, n], dt))
            for k, n, dt in [("w32", D, F32), ("w16", D, F16), ("z", D, F32),
                             ("bmax", D // 32, F32), ("sw", D // 32, F32),
                             ("isw", D // 32, F32)]}
    RBD = 6
    xn2_sb = [ctx2.enter_context(nc.sbuf_tensor(f"xn2_sb{i}", [128, NTT * D],
                                                F16)) for i in range(2)]
    t0_sb = [ctx2.enter_context(nc.sbuf_tensor(f"t0_sb{i}", [128, D], F32))
             for i in range(3)]
    xq_sb = [ctx2.enter_context(nc.sbuf_tensor(f"xq_sb{i}", [128, NTT * D],
                                               F16)) for i in range(2)]
    xqT = [ctx2.enter_context(nc.sbuf_tensor(f"xqT2_{i}", [128, DC * S], F16))
           for i in range(2)]
    gw = [ctx2.enter_context(nc.sbuf_tensor(f"gw{i}", [128, S], F16))
          for i in range(RBD)]

    ps2 = ExitStack()
    ps_tr2 = [ps2.enter_context(nc.psum_tensor(f"ps2_tr{i}", [128, 512], F16))
              for i in range(2)]
    ps_tr2w = [ps2.enter_context(nc.psum_tensor(f"ps2_tw{i}", [128, 512], F16))
               for i in range(2)]
    ps_g2 = [ps2.enter_context(nc.psum_tensor(f"ps2_g{i}", [128, 512], F32))
             for i in range(2)]
    ps_bc2 = ps2.enter_context(nc.psum_tensor("ps_bc2", [128, 128], F32))

    t_wgf = {}
    t_wup = {}
    prev_ev_track = []
    prev_dep = b0
    for ic_ in range(IC):
        ev, _, new_evs = prep_lpbq_chunk(wgf_h, wgfT, wst2, ps_tr2w, ic_,
                                         prev_dep, D, prev_ev_track,
                                         post_scale=scol("s0b"))
        t_wgf[ic_] = ev
        prev_ev_track = (prev_ev_track + new_evs)[-2:]
        prev_dep = [ev]

    t_gdma = {}
    t_gev = {}
    up_sched = {}
    nup = 0
    for st in range(NST):
        k = 2 if NST >= IC // 2 else (IC + NST - 1) // NST
        for _ in range(k):
            if nup < IC:
                up_sched.setdefault(st, []).append(nup)
                nup += 1
    if nup < IC:
        up_sched.setdefault(NST - 1, []).extend(range(nup, IC))

    war2 = {"t0": [None] * 3}
    lastrd_xn2 = {}
    lastrd_xq = {}
    mml2 = {}
    gdma_f = {}
    t_xq_dma = {}
    for st in range(NST):
        par = st % 2
        dx = {}
        for tt in range(NTT):
            dx[tt] = p.dma("sp", "spq",
                lambda e, st=st, tt=tt, par=par: e.dma_start(
                    out=xn2_sb[par][:, tt * D:(tt + 1) * D],
                    in_=xn2_d[(st * S + tt * 128):(st * S + tt * 128 + 128), :]),
                deps=[xn2_dma[(st, tt)], lastrd_xn2.get(st - 2)] + b0)
        t_xq = {}
        last_tt0 = None
        for tt in range(NTT):
            kt = (st * NTT + tt) % 3
            tt0 = p.emit("act", lambda e, tt=tt, par=par, kt=kt: e.activation(
                t0_sb[kt][:, :], xn2_sb[par][:, tt * D:(tt + 1) * D], AF.Relu,
                scale=scol("i0"), bias=cst[:, 3:4]),
                deps=[dx[tt], war2["t0"][kt]])
            last_tt0 = tt0
            t_xq[tt] = p.emit("dve", lambda e, tt=tt, par=par, kt=kt:
                e.tensor_scalar(
                    xq_sb[par][:, tt * D:(tt + 1) * D], t0_sb[kt][:, :],
                    -MAGIC, None, ALU.add),
                deps=[tt0, lastrd_xq.get(st - 2)])
            war2["t0"][kt] = t_xq[tt]
        lastrd_xn2[st] = last_tt0
        t_xqT = {}
        ev_list = []
        tr_last = None
        for dc in range(DC):
            half = dc % 2
            trk = []
            for tt in range(NTT):
                trk.append(p.emit("pe",
                    lambda e, dc=dc, tt=tt, half=half, par=par: e.transpose(
                        ps_tr2[half][:, tt * 128:(tt + 1) * 128],
                        xq_sb[par][:, tt * D + dc * 128:tt * D + dc * 128 + 128],
                        id16[:, :]),
                    deps=[t_xq[tt], t_wgf[IC - 1]] +
                         (ev_list[-2:-1] if len(ev_list) >= 2 else [])))
            tr_last = trk[-1]
            ev_list.append(p.emit("act",
                lambda e, dc=dc, half=half, par=par: e.copy(
                    xqT[par][:, dc * S:(dc + 1) * S], ps_tr2[half][:, :]),
                deps=trk + [mml2.get(st - 2), t_xq_dma.get((st - 2, dc))]))
            t_xqT[dc] = ev_list[-1]
            t_xq_dma[(st, dc)] = p.dma("act", "acq",
                lambda e, dc=dc, st=st, par=par: e.dma_start(
                    out=xqt_d[dc * 128:(dc + 1) * 128, st * S:(st + 1) * S],
                    in_=xqT[par][:, dc * S:(dc + 1) * S]), deps=[ev_list[-1]])
        lastrd_xq[st] = tr_last

        mm_last = None
        for ic_ in range(IC):
            icg = st * IC + ic_
            kb = icg % RBD
            mms = []
            for dc in range(DC):
                deps = [t_xqT[dc], t_wgf[ic_]]
                prev_ev = t_gev.get((st, ic_ - 2), t_gev.get((st - 1, IC - 2 + ic_)))
                if prev_ev is not None:
                    deps.append(prev_ev)
                mms.append(p.emit("pe",
                    lambda e, ic_=ic_, dc=dc, par=par: e.matmul(
                        ps_g2[ic_ % 2][:, :],
                        wgfT[:, (ic_ * DC + dc) * 128:(ic_ * DC + dc + 1) * 128],
                        xqT[par][:, dc * S:(dc + 1) * S],
                        start=(dc == 0), stop=(dc == DC - 1)), deps=deps))
            mm_last = mms[-1]
            tev = p.emit("act", lambda e, ic_=ic_, kb=kb: e.activation(
                gw[kb][:, :], ps_g2[ic_ % 2][:, :], AF.Copy,
                scale=2.0 ** -16),
                deps=[mms[-1], gdma_f.get(icg - RBD)])
            t_gev[(st, ic_)] = tev
            tred = emit_red("m2", gw[kb][:, :], [tev])
            t_gdma[(st, ic_)] = p.dma("act", "acq",
                lambda e, ic_=ic_, st=st, kb=kb: e.dma_start(
                    out=gate_d[ic_ * 128:(ic_ + 1) * 128, st * S:(st + 1) * S],
                    in_=gw[kb][:, :]), deps=[tev, tred])
            gdma_f[icg] = t_gdma[(st, ic_)]
        mml2[st] = mm_last
        for ic_ in up_sched.get(st, []):
            ev, _, new_evs = prep_lpbq_chunk(
                wup_h, wupT, wst2, ps_tr2w, ic_,
                [t_wgf[IC - 1], t_wup.get(ic_ - 1)] + b0, D, prev_ev_track,
                post_scale=scol("s0b"))
            t_wup[ic_] = ev
            prev_ev_track = (prev_ev_track + new_evs)[-2:]

    t_fold2 = fold_ring("m2", 1)
    t_fm2 = max_finalize(1, 1, ps_bc2, [t_fold2])
    b1 = barrier(1, 1, 1, [("s2", "i2")], ps_bc2, [t_fm2])
    t_hs2 = p.emit("dve", lambda e: e.tensor_scalar(
        scol("hs2"), scol("s2"), 0.5, None, ALU.mult), deps=b1)
    t_nh = p.emit("dve", lambda e: e.tensor_scalar(
        scol("nhs2m"), scol("hs2"), -MAGIC, None, ALU.mult), deps=[t_hs2])
    t_p16 = p.emit("dve", lambda e: e.tensor_scalar(
        scol("s2p16"), scol("s2"), 2.0 ** -16, None, ALU.mult), deps=b1)
    b1_all = b1 + [t_hs2, t_nh, t_p16]

    ps2.close()
    ctx2.close()
    ctx_wu2.close()
    p.fence()

    ctx3 = ExitStack()
    RBD = 6
    RCD = 3
    xqT3 = [ctx3.enter_context(nc.sbuf_tensor(f"xqT3_{i}", [128, DC * S], F16))
            for i in range(2)]
    uw = [ctx3.enter_context(nc.sbuf_tensor(f"uw{i}", [128, S], F16))
          for i in range(RBD)]
    gld = [ctx3.enter_context(nc.sbuf_tensor(f"gld{i}", [128, S], F16))
           for i in range(RBD)]
    t2w = [ctx3.enter_context(nc.sbuf_tensor(f"t2w{i}", [128, S], F32))
           for i in range(RCD)]
    th3w = [ctx3.enter_context(nc.sbuf_tensor(f"th3w{i}", [128, S], F32))
            for i in range(RCD)]
    w1w = [ctx3.enter_context(nc.sbuf_tensor(f"w1w{i}", [128, S], F32))
           for i in range(RCD)]
    sqw = [ctx3.enter_context(nc.sbuf_tensor(f"sqw{i}", [128, S], F32))
           for i in range(RCD)]
    gqw = [ctx3.enter_context(nc.sbuf_tensor(f"gqw{i}", [128, S], F16))
           for i in range(RCD)]
    g2w = [ctx3.enter_context(nc.sbuf_tensor(f"g2w{i}", [128, S], F16))
           for i in range(RBD)]

    ps3 = ExitStack()
    ps_u = [ps3.enter_context(nc.psum_tensor(f"ps_u{i}", [128, 512], F32))
            for i in range(2)]
    ps_bc3 = ps3.enter_context(nc.psum_tensor("ps_bc3", [128, 128], F32))

    t_udma = {}
    t_uev = {}
    t_g2dma = {}
    udma_f = {}
    g2dma_f = {}
    tt2_f = {}
    mml3 = {}
    war3 = {"t2": [None] * RCD, "t2r": [None] * RCD, "th3": [None] * RCD,
            "w1": [None] * RCD, "sq": [None] * RCD, "gq": [None] * RCD}
    for st in range(NST):
        par = st % 2
        dxq = {}
        for dc in range(DC):
            dxq[dc] = p.dma("sp", "spq",
                lambda e, dc=dc, st=st, par=par: e.dma_start(
                    out=xqT3[par][:, dc * S:(dc + 1) * S],
                    in_=xqt_d[dc * 128:(dc + 1) * 128, st * S:(st + 1) * S]),
                deps=[t_xq_dma[(st, dc)], mml3.get(st - 2)] + b1_all)
        mm_last = None
        for ic_ in range(IC):
            icg = st * IC + ic_
            kb = icg % RBD
            kc = icg % RCD
            mms = []
            for dc in range(DC):
                deps = [dxq[dc], t_wup[ic_]]
                prev_ev = t_uev.get((st, ic_ - 2), t_uev.get((st - 1, IC - 2 + ic_)))
                if prev_ev is not None:
                    deps.append(prev_ev)
                mms.append(p.emit("pe",
                    lambda e, ic_=ic_, dc=dc, par=par: e.matmul(
                        ps_u[ic_ % 2][:, :],
                        wupT[:, (ic_ * DC + dc) * 128:(ic_ * DC + dc + 1) * 128],
                        xqT3[par][:, dc * S:(dc + 1) * S],
                        start=(dc == 0), stop=(dc == DC - 1)), deps=deps))
            mm_last = mms[-1]
            tev = p.emit("act", lambda e, ic_=ic_, kb=kb: e.activation(
                uw[kb][:, :], ps_u[ic_ % 2][:, :], AF.Copy,
                scale=2.0 ** -16),
                deps=[mms[-1], udma_f.get(icg - RBD)])
            t_uev[(st, ic_)] = tev
            tred1 = emit_red("m1", uw[kb][:, :], [tev])
            t_udma[(st, ic_)] = p.dma("act", "acq",
                lambda e, ic_=ic_, st=st, kb=kb: e.dma_start(
                    out=up_d[ic_ * 128:(ic_ + 1) * 128, st * S:(st + 1) * S],
                    in_=uw[kb][:, :]), deps=[tev, tred1])
            udma_f[icg] = t_udma[(st, ic_)]

            dg = p.dma("sp", "spq", lambda e, ic_=ic_, st=st, kb=kb: e.dma_start(
                out=gld[kb][:, :],
                in_=gate_d[ic_ * 128:(ic_ + 1) * 128, st * S:(st + 1) * S]),
                deps=[t_gdma[(st, ic_)], tt2_f.get(icg - RBD)])
            tt2 = p.emit("pool", lambda e, kb=kb, kc=kc: e.tensor_scalar(
                t2w[kc][:, :], gld[kb][:, :], scol("i2"), MAGIC,
                ALU.mult, ALU.add),
                deps=[dg, war3["t2"][kc], war3["t2r"][kc]])
            tt2_f[icg] = tt2
            tth3 = p.emit("act", lambda e, kc=kc: e.activation(
                th3w[kc][:, :], t2w[kc][:, :], AF.Tanh, bias=scol("nhs2m"),
                scale=scol("hs2")), deps=[tt2, war3["th3"][kc]])
            war3["t2r"][kc] = tth3
            tw1_ = p.emit("act", lambda e, kc=kc: e.activation(
                w1w[kc][:, :], th3w[kc][:, :], AF.Relu, scale=32768.0,
                bias=cst[:, 4:5]), deps=[tth3, war3["w1"][kc]])
            war3["th3"][kc] = tw1_
            tsq = p.emit("dve", lambda e, kc=kc: e.tensor_scalar(
                sqw[kc][:, :], w1w[kc][:, :], -MAGIC, 65535.0,
                ALU.add, ALU.min), deps=[tw1_, war3["sq"][kc]])
            war3["w1"][kc] = tsq
            tm2 = p.emit("pool", lambda e, kc=kc: e.tensor_scalar(
                gqw[kc][:, :], t2w[kc][:, :], 1.0, -MAGIC,
                ALU.mult, ALU.add),
                deps=[tt2, war3["gq"][kc]])
            war3["t2"][kc] = tm2
            tg2 = p.emit("dve", lambda e, kb=kb, kc=kc: e.scalar_tensor_tensor(
                g2w[kb][:, :], sqw[kc][:, :], scol("s2p16"), gqw[kc][:, :],
                ALU.mult, ALU.mult),
                deps=[tsq, tm2, g2dma_f.get(icg - RBD)])
            war3["sq"][kc] = tg2
            war3["gq"][kc] = tg2
            tred3 = emit_red("m3", g2w[kb][:, :], [tg2])
            dg2 = p.dma("act", "acq",
                lambda e, ic_=ic_, st=st, kb=kb: e.dma_start(
                    out=g2_d[ic_ * 128:(ic_ + 1) * 128, st * S:(st + 1) * S],
                    in_=g2w[kb][:, :]), deps=[tg2, tred3])
            t_g2dma[(st, ic_)] = dg2
            g2dma_f[icg] = dg2
        mml3[st] = mm_last

    t_fold1 = fold_ring("m1", 2)
    t_fm1 = max_finalize(2, 2, ps_bc3, [t_fold1])
    t_fold3 = fold_ring("m3", 3)
    t_fm3 = max_finalize(3, 3, ps_bc3, [t_fold3, t_fm1])
    b2 = barrier(2, 2, 2, [("s1", "i1"), ("s3", "i3")], ps_bc3, [t_fm3])

    ps3.close()
    ctx3.close()
    ctx_wu1.close()
    p.fence()

    ctx_wd = ExitStack()
    wdT = ctx_wd.enter_context(nc.sbuf_tensor("wdT", [128, DC * IC * 128], F16))
    ctx4 = ExitStack()
    wst4 = {k: ctx4.enter_context(nc.sbuf_tensor(f"r_{k}", [128, n], dt))
            for k, n, dt in [("w32", I, F32), ("w16", I, F16), ("z", I, F32),
                             ("bmax", I // 32, F32), ("sw", I // 32, F32),
                             ("isw", I // 32, F32)]}
    R4 = 10
    R3 = 3
    RP = 8
    u4 = [ctx4.enter_context(nc.sbuf_tensor(f"u4_{i}", [128, S], F16))
          for i in range(R4)]
    g4 = [ctx4.enter_context(nc.sbuf_tensor(f"g4_{i}", [128, S], F16))
          for i in range(R4)]
    t4w = [ctx4.enter_context(nc.sbuf_tensor(f"t4w{i}", [128, S], F32))
           for i in range(R3)]
    gate2w = [ctx4.enter_context(nc.sbuf_tensor(f"gate2w{i}", [128, S], F16))
              for i in range(R3)]
    tuw = [ctx4.enter_context(nc.sbuf_tensor(f"tuw{i}", [128, S], F32))
           for i in range(R3)]
    upqw = [ctx4.enter_context(nc.sbuf_tensor(f"upqw{i}", [128, S], F16))
            for i in range(R3)]
    pw = [ctx4.enter_context(nc.sbuf_tensor(f"pw{i}", [128, S], F16))
          for i in range(RP)]

    ps4 = ExitStack()
    ps_tr4 = [ps4.enter_context(nc.psum_tensor(f"ps4_tr{i}", [128, 512], F16))
              for i in range(2)]
    ps_bc4 = ps4.enter_context(nc.psum_tensor("ps_bc4", [128, 128], F32))

    t_wd = {}
    prev_dep = b2
    prev_ev_track = []
    for dpc in range(DC):
        ev, _, new_evs = prep_lpbq_chunk(wd_h, wdT, wst4, ps_tr4, dpc,
                                         prev_dep, I, prev_ev_track)
        t_wd[dpc] = ev
        prev_ev_track = (prev_ev_track + new_evs)[-2:]
        prev_dep = [ev]

    t_pdma = {}
    war4 = {"u4": [None] * R4, "g4": [None] * R4, "t4": [None] * R3,
            "gate2": [None] * R3, "tu": [None] * R3, "upq": [None] * R3,
            "pw": [None] * RP}
    for st in range(NST):
        for ic_ in range(IC):
            k4 = (st * IC + ic_) % R4
            k3 = (st * IC + ic_) % R3
            kp = (st * IC + ic_) % RP
            du = p.dma("sp", "spq", lambda e, ic_=ic_, st=st, k4=k4: e.dma_start(
                out=u4[k4][:, :],
                in_=up_d[ic_ * 128:(ic_ + 1) * 128, st * S:(st + 1) * S]),
                deps=[t_udma[(st, ic_)], t_wd[DC - 1], war4["u4"][k4]])
            dg = p.dma("sp", "spq", lambda e, ic_=ic_, st=st, k4=k4: e.dma_start(
                out=g4[k4][:, :],
                in_=g2_d[ic_ * 128:(ic_ + 1) * 128, st * S:(st + 1) * S]),
                deps=[t_g2dma[(st, ic_)], war4["g4"][k4]])
            tt4 = p.emit("act", lambda e, k4=k4, k3=k3: e.activation(
                t4w[k3][:, :], g4[k4][:, :], AF.Relu,
                scale=scol("i3"), bias=cst[:, 3:4]), deps=[dg, war4["t4"][k3]])
            war4["g4"][k4] = tt4
            tg2_ = p.emit("pool", lambda e, k3=k3: e.tensor_scalar(
                gate2w[k3][:, :], t4w[k3][:, :], -MAGIC, scol("s3"),
                ALU.add, ALU.mult), deps=[tt4, war4["gate2"][k3]])
            war4["t4"][k3] = tg2_
            ttu = p.emit("act", lambda e, k4=k4, k3=k3: e.activation(
                tuw[k3][:, :], u4[k4][:, :], AF.Relu,
                scale=scol("i1"), bias=cst[:, 3:4]), deps=[du, war4["tu"][k3]])
            war4["u4"][k4] = ttu
            tuq = p.emit("pool", lambda e, k3=k3: e.tensor_scalar(
                upqw[k3][:, :], tuw[k3][:, :], -MAGIC, scol("s1"),
                ALU.add, ALU.mult), deps=[ttu, war4["upq"][k3]])
            war4["tu"][k3] = tuq
            tp = p.emit("dve", lambda e, k3=k3, kp=kp: e.tensor_tensor(
                pw[kp][:, :], gate2w[k3][:, :], upqw[k3][:, :], ALU.mult),
                deps=[tg2_, tuq, war4["pw"][kp]])
            war4["gate2"][k3] = tp
            war4["upq"][k3] = tp
            tred4 = emit_red("m4", pw[kp][:, :], [tp])
            t_pdma[(st, ic_)] = p.dma("act", "acq",
                lambda e, ic_=ic_, st=st, kp=kp: e.dma_start(
                    out=p_d[ic_ * 128:(ic_ + 1) * 128, st * S:(st + 1) * S],
                    in_=pw[kp][:, :]), deps=[tp, tred4])
            war4["pw"][kp] = t_pdma[(st, ic_)]

    t_fold4 = fold_ring("m4", 4)
    t_fm4 = max_finalize(4, 4, ps_bc4, [t_fold4])
    b3 = barrier(3, 4, 1, [("s4", "i4")], ps_bc4, [t_fm4])
    t_e4 = p.emit("dve", lambda e: e.tensor_scalar(
        scol("eps4"), scol("i4"), scol("i4"), 1e-5, ALU.mult, ALU.mult),
        deps=b3)
    b3 = b3 + [t_e4]

    ps4.close()
    ctx4.close()
    p.fence()

    ctx5 = ExitStack()
    o_t = [ctx5.enter_context(nc.sbuf_tensor(f"o_t{i}", [128, IC * S], F16))
           for i in range(2)]
    t5w = [ctx5.enter_context(nc.sbuf_tensor(f"t5w{i}", [128, S], F32))
           for i in range(R3)]
    x2l = [ctx5.enter_context(nc.sbuf_tensor(f"x2l{i}", [128, D], F32))
           for i in range(2)]
    xml = ctx5.enter_context(nc.sbuf_tensor("xml", [128, D], F32))
    outw = [ctx5.enter_context(nc.sbuf_tensor(f"outw{i}", [128, D], F32))
            for i in range(2)]

    ps5 = ExitStack()
    ps_y = [ps5.enter_context(nc.psum_tensor(f"ps_y{i}", [128, 1024], F32))
            for i in range(2)]

    t_out_dma = {}
    war5 = {"o": [{} for _ in range(2)], "rd": [{} for _ in range(2)],
            "t5": [None] * R3, "x2l": [None, None],
            "xml": None, "outw": [None, None], "psy": [None, None]}
    for st in range(NST):
        par = st % 2
        t_o = {}
        for ic_ in range(IC):
            k3 = (st * IC + ic_) % R3
            dp = p.dma("sp", "spq", lambda e, ic_=ic_, st=st, par=par: e.dma_start(
                out=o_t[par][:, ic_ * S:(ic_ + 1) * S],
                in_=p_d[ic_ * 128:(ic_ + 1) * 128, st * S:(st + 1) * S]),
                deps=[t_pdma[(st, ic_)], war5["rd"][par].get(ic_),
                      war5["o"][par].get(ic_)] + b3)
            tt5 = p.emit("act", lambda e, ic_=ic_, par=par, k3=k3: e.activation(
                t5w[k3][:, :], o_t[par][:, ic_ * S:(ic_ + 1) * S], AF.Relu,
                scale=scol("i4"), bias=cst[:, 3:4]),
                deps=[dp, war5["t5"][k3]])
            t_o[ic_] = p.emit("dve", lambda e, ic_=ic_, par=par, k3=k3:
                e.tensor_scalar(
                    o_t[par][:, ic_ * S:(ic_ + 1) * S], t5w[k3][:, :], -MAGIC,
                    None, ALU.add), deps=[tt5])
            war5["t5"][k3] = t_o[ic_]
            war5["o"][par][ic_] = t_o[ic_]
        for tt in range(NTT):
            dx2 = p.dma("sp", "spq", lambda e, st=st, tt=tt: e.dma_start(
                out=x2l[tt % 2][:, :],
                in_=x2_d[(st * S + tt * 128):(st * S + tt * 128 + 128), :]),
                deps=[x2_dma[(st, tt)], war5["x2l"][tt % 2]])
            mms_stop = []
            wdap = wdT.ap().rearrange("p (d i c) -> p d i c", i=IC, c=128)
            for half in range(2):
                for ic_ in range(IC):
                    deps = [t_o[ic_]]
                    if war5["psy"][tt % 2] is not None:
                        deps.append(war5["psy"][tt % 2])
                    rh = wdap[:, half * 4:(half + 1) * 4, ic_, :]
                    mm = p.emit("pe",
                        lambda e, ic_=ic_, half=half, tt=tt, rh=rh, par=par:
                        e.matmul(
                            ps_y[tt % 2][:, half * 512:(half + 1) * 512],
                            o_t[par][:,
                                     ic_ * S + tt * 128:ic_ * S + tt * 128 + 128],
                            rh,
                            start=(ic_ == 0), stop=(ic_ == IC - 1)),
                        deps=deps)
                    war5["rd"][par][ic_] = mm
                mms_stop.append(mm)
            lt4 = ln_stats(lambda lo, tt=tt: ps_y[tt % 2][:, lo:lo + 512],
                           CB["negm2"], CB["r2"], mms_stop,
                           extra_negmr=CB["negm2r2"], eps_ap=scol("eps4"))
            txm = p.emit("dve", lambda e, tt=tt: e.tensor_scalar(
                xml[:, :], x2l[tt % 2][:, :],
                colbuf[:, CB["negm2r2"]:CB["negm2r2"] + 1], None, ALU.add),
                deps=lt4 + [dx2, war5["xml"]])
            war5["x2l"][tt % 2] = txm
            tout = p.emit("dve", lambda e, tt=tt: e.scalar_tensor_tensor(
                outw[tt % 2][:, :], ps_y[tt % 2][:, :],
                colbuf[:, CB["r2"]:CB["r2"] + 1], xml[:, :],
                ALU.mult, ALU.add),
                deps=[txm] + lt4 + [war5["outw"][tt % 2]])
            use_cols(tout, [CB["negm2"], CB["r2"], CB["negm2r2"]])
            war5["xml"] = tout
            war5["psy"][tt % 2] = tout
            t_out_dma[(st, tt)] = p.dma("act", "acq",
                lambda e, st=st, tt=tt: e.dma_start(
                    out=out_h[(st * S + tt * 128):(st * S + tt * 128 + 128), :],
                    in_=outw[tt % 2][:, :]), deps=[tout])
            war5["outw"][tt % 2] = t_out_dma[(st, tt)]

    finals = [(sq, v) for sq, v in p.dcnt.items() if v > 0]
    for e in P.ENGINES:
        p.water[e] = {}
        p.wait(e, finals)

    with nc.Block() as block:
        @block.tensor
        def _(eng):
            for op in p.ops["pe"]:
                op(eng)

        @block.scalar
        def _(eng):
            for op in p.ops["act"]:
                op(eng)

        @block.vector
        def _(eng):
            for op in p.ops["dve"]:
                op(eng)

        @block.gpsimd
        def _(eng):
            for op in p.ops["pool"]:
                op(eng)

        @block.sync
        def _(eng):
            for op in p.ops["sp"]:
                op(eng)

    es.pop_all()
    return nc


def _make_inputs(x_core, W_in, b_in, W_gate, b_gate, lam, w_gf, w_up, w_down):
    return {
        "x": np.ascontiguousarray(x_core, dtype=np.float32),
        "w_in": np.ascontiguousarray(W_in, dtype=np.float32),
        "w_gate": np.ascontiguousarray(W_gate, dtype=np.float32),
        "b_in": np.ascontiguousarray(np.reshape(b_in, (1, -1)), dtype=np.float32),
        "b_gate": np.ascontiguousarray(np.reshape(b_gate, (1, -1)), dtype=np.float32),
        "lam": np.ascontiguousarray(np.reshape(lam, (1, -1)), dtype=np.float32),
        "w_up": np.ascontiguousarray(w_up, dtype=np.float32),
        "w_gf": np.ascontiguousarray(w_gf, dtype=np.float32),
        "w_down": np.ascontiguousarray(w_down, dtype=np.float32),
        "id16_h": np.eye(128, dtype=np.float16),
        "id32_h": np.eye(128, dtype=np.float32),
    }


def kernel(L0_attn_x, L0_attn_W_in, L0_attn_b_in, L0_attn_W_gate,
           L0_attn_b_gate, L0_attn_forget_lambda, L0_ffn_w_gate,
           L0_ffn_w_up, L0_ffn_w_down):
    x = np.ascontiguousarray(np.asarray(L0_attn_x, dtype=np.float32))
    B, T, _ = x.shape
    in_maps = [
        _make_inputs(x[c], L0_attn_W_in, L0_attn_b_in, L0_attn_W_gate,
                     L0_attn_b_gate, L0_attn_forget_lambda, L0_ffn_w_gate,
                     L0_ffn_w_up, L0_ffn_w_down)
        for c in range(B)
    ]
    import os
    axon = (bool(os.environ.get("AXON_TERMINAL_JOB_NAME"))
            or os.environ.get("AXON_H4_ENABLED") == "1")
    use_cc = (os.environ.get("BASS_SIM_FP64") == "1") or not axon
    nc = build_program(T=T, use_cc=use_cc, n_cores=B)
    res = run_bass_kernel_spmd(nc, in_maps, core_ids=list(range(B)))
    return np.stack([np.asarray(res.results[c]["out"], dtype=np.float32)
                     for c in range(B)], axis=0)


# version 15
# speedup vs baseline: 2.4423x; 1.3770x over previous
import numpy as np
import concourse.bass as bass
import concourse.mybir as mybir
from concourse.bass_utils import run_bass_kernel_spmd
from contextlib import ExitStack

F32 = mybir.dt.float32
F16 = mybir.dt.float16
AF = mybir.ActivationFunctionType
ALU = mybir.AluOpType
AX = mybir.AxisListType.X

MAGIC = 12582912.0
QMAX = 32767.0
CLIP16 = 65535.0 / 65536.0
D = 1024
I = 3072
DC = D // 128
IC = I // 128
S = 512
NTT = S // 128


class P:
    ENGINES = ("pe", "act", "dve", "pool", "sp")
    DQUEUES = ("spq", "gpq", "acq", "ccq")
    NPOOL = 8

    def __init__(self, nc):
        self.nc = nc
        self.ops = {e: [] for e in self.ENGINES}
        self.cnt = {e: 0 for e in self.ENGINES}
        self.dcnt = {}
        for q in self.DQUEUES:
            for i in range(self.NPOOL):
                self.dcnt[f"{q}{i}"] = 0
        self.didx = {q: 0 for q in self.DQUEUES}
        self.sems = {}
        self.water = {e: {} for e in self.ENGINES}

    TRACE = []

    def wait(self, e, deps):
        for d in deps:
            if d is None:
                continue
            sname, val = d
            if sname == e:
                if val >= self.cnt[e]:
                    val = self.cnt[e]
            w = self.water[e]
            if w.get(sname, 0) >= val:
                continue
            w[sname] = val

            def op(eng, sem_name=sname, val=val, self=self):
                eng.wait_ge(self.sems[sem_name], val)
            self.ops[e].append(op)
            P.TRACE.append(("wait", e, sname, val))

    def emit(self, e, builder, deps=()):
        self.wait(e, deps)
        self.cnt[e] += 1
        val = self.cnt[e]

        def op(eng, builder=builder, e=e, self=self):
            builder(eng).then_inc(self.sems[e], 1)
        self.ops[e].append(op)
        import traceback
        P.TRACE.append(("emit", e, val, traceback.extract_stack()[-2].lineno))
        return (e, val)

    def dma(self, e, q, builder, deps=()):
        sq = f"{q}{self.didx[q] % self.NPOOL}"
        self.didx[q] += 1
        prev = getattr(self, "dlast", None)
        if prev is None:
            self.dlast = prev = {}
        self.wait(e, list(deps) + [prev.get(sq)])
        self.dcnt[sq] += 16
        val = self.dcnt[sq]
        prev[sq] = (sq, val)

        def op(eng, builder=builder, sq=sq, self=self):
            builder(eng).then_inc(self.sems[sq], 16)
        self.ops[e].append(op)
        import traceback
        P.TRACE.append(("dma", e, sq, val, traceback.extract_stack()[-2].lineno))
        return (sq, val)

    def fence(self):
        toks = [(e, self.cnt[e]) for e in self.ENGINES]
        toks += [(sq, self.dcnt[sq]) for sq in self.dcnt]
        for e in self.ENGINES:
            self.wait(e, toks)


def mk_defer(lag):
    q = []

    def push(idx, fn):
        q.append((idx, fn))

    def flush(now=None):
        while q and (now is None or q[0][0] + lag <= now):
            q.pop(0)[1]()
    return push, flush


def build_program(T=8192, use_cc=True, n_cores=8, dbg=False):
    NST = T // S
    nc = bass.Bass("TRN2", target_bir_lowering=False,
                   num_devices=(n_cores if use_cc else None))

    x_h = nc.dram_tensor("x", [T, D], F32, kind="ExternalInput").ap()
    win_h = nc.dram_tensor("w_in", [D, D], F32, kind="ExternalInput").ap()
    wg_h = nc.dram_tensor("w_gate", [D, D], F32, kind="ExternalInput").ap()
    bin_h = nc.dram_tensor("b_in", [1, D], F32, kind="ExternalInput").ap()
    bg_h = nc.dram_tensor("b_gate", [1, D], F32, kind="ExternalInput").ap()
    lam_h = nc.dram_tensor("lam", [1, D], F32, kind="ExternalInput").ap()
    wup_h = nc.dram_tensor("w_up", [I, D], F32, kind="ExternalInput").ap()
    wgf_h = nc.dram_tensor("w_gf", [I, D], F32, kind="ExternalInput").ap()
    wd_h = nc.dram_tensor("w_down", [D, I], F32, kind="ExternalInput").ap()
    id16_h = nc.dram_tensor("id16_h", [128, 128], F16, kind="ExternalInput").ap()
    id32_h = nc.dram_tensor("id32_h", [128, 128], F32, kind="ExternalInput").ap()
    out_h = nc.dram_tensor("out", [T, D], F32, kind="ExternalOutput").ap()

    kd = "ExternalOutput" if dbg else "Internal"
    x2_d = nc.dram_tensor("x2_d", [T, D], F32, kind=kd).ap()
    xn2_d = nc.dram_tensor("xn2_d", [T, D], F16, kind=kd).ap()
    xqt_d = nc.dram_tensor("xqt_d", [D, T], F16, kind=kd).ap()
    gate_d = nc.dram_tensor("gate_d", [I, T], F16, kind=kd).ap()
    up_d = nc.dram_tensor("up_d", [I, T], F16, kind=kd).ap()
    g2_d = nc.dram_tensor("g2_d", [I, T], F16, kind=kd).ap()
    p_d = nc.dram_tensor("p_d", [I, T], F16, kind=kd).ap()
    ccin_d = [nc.dram_tensor(f"ccin{k}", [1, 2], F32).ap() for k in range(4)]
    ccout_d = [nc.dram_tensor(f"ccout{k}", [1, 2], F32).ap() for k in range(4)]

    p = P(nc)
    es = ExitStack()
    for sname in list(p.cnt) + list(p.dcnt):
        p.sems[sname] = es.enter_context(nc.semaphore(f"s_{sname}"))

    A = nc.alloc_sbuf_tensor
    id16 = A("id16", [128, 128], F16)
    id32 = A("id32", [128, 128], F32)
    ones128 = A("ones128", [1, 128], F32)
    k2 = A("k2", [128, DC], F32)
    bin2 = A("bin2", [128, DC], F32)
    bg2 = A("bg2", [128, DC], F32)
    scal = A("scal", [128, 16], F32)
    srow = A("srow", [1, 16], F32)
    mcol = A("mcol", [128, 8], F32)
    mrow = A("mrow", [1, 128], F32)
    mcell = A("mcell", [1, 8], F32)
    lamt = A("lamt", [128, DC], F32)
    btmp = A("btmp", [128, DC], F32)
    bnbuf = A("bnbuf", [128, 12], F32)
    stats = A("stats", [128, 2], F32)
    colbuf = A("colbuf", [128, 16], F32)
    redcol = A("redcol", [128, 1], F32)
    redring = A("redring", [128, 40], F32)
    cst = A("cst", [128, 8], F32)

    SC = {"s0": 0, "i0": 1, "s2": 2, "i2": 3, "hs2": 4, "nhs2m": 5,
          "s1": 6, "i1": 7, "s3": 8, "i3": 9, "s4": 10, "i4": 11,
          "s2p16": 12, "eps4": 13, "s0b": 14}

    def scol(name):
        return scal[:, SC[name]:SC[name] + 1]

    d_id16 = p.dma("sp", "spq", lambda e: e.dma_start(out=id16[:, :], in_=id16_h))
    d_id32 = p.dma("sp", "spq", lambda e: e.dma_start(out=id32[:, :], in_=id32_h))
    d_lam = p.dma("sp", "spq", lambda e: e.dma_start(
        out=lamt[:, :], in_=lam_h.rearrange("o (c p) -> (o p) c", p=128),
        allow_slow_non_contiguous=True))
    d_bin = p.dma("sp", "spq", lambda e: e.dma_start(
        out=btmp[:, :], in_=bin_h.rearrange("o (c p) -> (o p) c", p=128),
        allow_slow_non_contiguous=True))
    t_ones = p.emit("dve", lambda e: e.memset(ones128[:, :], 1.0))
    t_c0 = p.emit("dve", lambda e: e.memset(cst[:, 0:1], 1e-5))
    t_c1 = p.emit("dve", lambda e: e.memset(cst[:, 1:2], 1.0))
    t_c2 = p.emit("dve", lambda e: e.memset(cst[:, 2:3], 0.25 * (1 + 1e-6)))
    t_c3 = p.emit("dve", lambda e: e.memset(cst[:, 3:4], MAGIC))
    t_c4 = p.emit("dve", lambda e: e.memset(cst[:, 4:5], 32768.0 + MAGIC))
    t_srz = p.emit("dve", lambda e: e.memset(srow[:, :], 0.0))
    t_mz = p.emit("dve", lambda e: e.memset(mcol[:, :], 0.0))
    t_e = p.emit("act", lambda e: e.activation(lamt[:, :], lamt[:, :], AF.Exp),
                 deps=[d_lam])
    t_l = p.emit("act", lambda e: e.activation(lamt[:, :], lamt[:, :], AF.Ln,
                                               bias=cst[:, 1:2]), deps=[t_e, t_c1])
    t_k2 = p.emit("dve", lambda e: e.tensor_scalar(
        k2[:, :], lamt[:, :], 4.0, None, ALU.mult), deps=[t_l])
    t_b1 = p.emit("dve", lambda e: e.tensor_scalar(
        bin2[:, :], btmp[:, :], 0.5, None, ALU.mult), deps=[d_bin])
    d_bg = p.dma("sp", "spq", lambda e: e.dma_start(
        out=btmp[:, :], in_=bg_h.rearrange("o (c p) -> (o p) c", p=128),
        allow_slow_non_contiguous=True), deps=[t_b1])
    t_b2 = p.emit("dve", lambda e: e.tensor_scalar(
        bg2[:, :], btmp[:, :], 0.5, None, ALU.mult), deps=[d_bg])
    pro_toks = [t_k2, t_b2, t_ones, t_mz, d_id16, d_id32, t_c0, t_c2, t_c3, t_c4]

    ln_prev = {"t5": None, "stats_r": None}
    reds = {"m0": [], "m2": [], "m1": [], "m3": [], "m4": []}
    RB = {"m0": 0, "m2": 8, "m1": 16, "m3": 24, "m4": 32}
    MC = {"m0": 0, "m2": 1, "m1": 2, "m3": 3, "m4": 4}
    last_fold = {}

    def _fold8(stream, nret):
        b = RB[stream]
        c = MC[stream]
        tr = p.emit("dve", lambda e, b=b, nret=nret: e.tensor_reduce(
            redcol[:, :], redring[:, b:b + nret], AX, ALU.max),
            deps=[reds[stream][-1], last_fold.get(stream, t_mz)])
        tm = p.emit("dve", lambda e, c=c: e.tensor_tensor(
            mcol[:, c:c + 1], mcol[:, c:c + 1], redcol[:, :], ALU.max),
            deps=[tr, t_mz])
        last_fold[stream] = tm
        return tm

    def emit_red(stream, src_ap, deps):
        k = RB[stream] + (len(reds[stream]) % 8)
        t = p.emit("dve", lambda e, k=k, src_ap=src_ap: e.tensor_reduce(
            redring[:, k:k + 1], src_ap, AX, ALU.max,
            apply_absolute_value=True),
            deps=list(deps) + [last_fold.get(stream)
                               if len(reds[stream]) % 8 == 0 else None])
        reds[stream].append(t)
        if len(reds[stream]) % 8 == 0:
            _fold8(stream, 8)
        return t

    def fold_ring(stream, col_idx):
        rem = len(reds[stream]) % 8
        if rem:
            _fold8(stream, rem)
        return last_fold[stream]
    colw = {}

    def ln_stats(src_ap_fn, negm_col, r_col, deps, extra_negmr=None,
                 eps_ap=None):
        war = [colw.get(r_col), ln_prev["t5"]]
        t1 = p.emit("dve", lambda e: e.bn_stats(bnbuf[:, 0:6], src_ap_fn(0)),
                    deps=list(deps) + [ln_prev["stats_r"]])
        t2 = p.emit("dve", lambda e: e.bn_stats(bnbuf[:, 6:12], src_ap_fn(512)),
                    deps=list(deps))
        t3 = p.emit("dve", lambda e: e.bn_aggr(stats[:, 0:2], bnbuf[:, 0:12]),
                    deps=[t1, t2])
        t4 = p.emit("dve", lambda e: e.tensor_scalar(
            colbuf[:, negm_col:negm_col + 1], stats[:, 0:1], -1.0, None,
            ALU.mult), deps=[t3])
        if eps_ap is None:
            eps_ap = cst[:, 0:1]
        t5 = p.emit("act", lambda e, eps_ap=eps_ap: e.activation(
            colbuf[:, r_col:r_col + 1], stats[:, 1:2], AF.Sqrt,
            bias=eps_ap), deps=[t3] + war)
        t6 = p.emit("dve", lambda e: e.reciprocal(
            colbuf[:, r_col:r_col + 1], colbuf[:, r_col:r_col + 1]), deps=[t5])
        ln_prev["t5"] = t5
        ln_prev["stats_r"] = t5
        toks = [t4, t6]
        if extra_negmr is not None:
            toks.append(p.emit("dve", lambda e: e.tensor_tensor(
                colbuf[:, extra_negmr:extra_negmr + 1],
                colbuf[:, negm_col:negm_col + 1],
                colbuf[:, r_col:r_col + 1], ALU.mult), deps=[t4, t6]))
        return toks

    def use_cols(tok, cols):
        for c in cols:
            colw[c] = tok

    def max_finalize(col_idx, cell_idx, ps_bc, deps):
        ttr = p.emit("pe", lambda e: e.transpose(
            ps_bc[0:1, 0:128], mcol[:, col_idx:col_idx + 1], id32[:, :]),
            deps=deps)
        tv = p.emit("dve", lambda e: e.tensor_copy(
            mrow[:, :], ps_bc[0:1, 0:128]), deps=[ttr])
        tr = p.emit("dve", lambda e: e.tensor_reduce(
            mcell[:, cell_idx:cell_idx + 1], mrow[:, :], AX, ALU.max),
            deps=[tv])
        return tr

    def barrier(kidx, cell0, n, scols, ps_bc, deps):
        toks = list(deps)
        if use_cc:
            d1 = p.dma("pool", "gpq", lambda e: e.dma_start(
                out=ccin_d[kidx][:, 0:n], in_=mcell[:, cell0:cell0 + n]),
                deps=toks)
            tcc = p.dma("pool", "ccq", lambda e: e.collective_compute(
                "AllReduce", ALU.max, replica_groups=[list(range(n_cores))],
                ins=[ccin_d[kidx][:, 0:n].opt()],
                outs=[ccout_d[kidx][:, 0:n].opt()]), deps=[d1])
            d2 = p.dma("pool", "gpq", lambda e: e.dma_start(
                out=mcell[:, cell0:cell0 + n], in_=ccout_d[kidx][:, 0:n]),
                deps=[tcc])
            toks = [d2]
        stoks = []
        for j in range(n):
            ts_ = p.emit("dve", lambda e, j=j: e.tensor_scalar(
                srow[:, 2 * j:2 * j + 1], mcell[:, cell0 + j:cell0 + j + 1],
                1.0 / QMAX, 1e-12, ALU.mult, ALU.max), deps=toks)
            ti_ = p.emit("dve", lambda e, j=j: e.reciprocal(
                srow[:, 2 * j + 1:2 * j + 2], srow[:, 2 * j:2 * j + 1]),
                deps=[ts_])
            stoks.append(ti_)
        tm = p.emit("pe", lambda e: e.matmul(
            ps_bc[0:128, 0:16], ones128[:, :], srow[:, :],
            start=True, stop=True), deps=stoks + [t_srz])
        evs = []
        for j, (sc_, ic_) in enumerate(scols):
            evs.append(p.emit("dve", lambda e, sc_=sc_, j=j: e.tensor_copy(
                scal[:, SC[sc_]:SC[sc_] + 1], ps_bc[0:128, 2 * j:2 * j + 1]),
                deps=[tm]))
            evs.append(p.emit("dve", lambda e, ic_=ic_, j=j: e.tensor_copy(
                scal[:, SC[ic_]:SC[ic_] + 1],
                ps_bc[0:128, 2 * j + 1:2 * j + 2]), deps=[tm]))
        return evs

    def prep_lpbq_chunk(w_hbm, dst, st_, ps_tr, ic, deps, ncols, prev_ev,
                        post_scale=None):
        ncc = ncols // 128
        nb = ncols // 32
        dld = p.dma("sp", "spq", lambda e: e.dma_start(
            out=st_["w32"][:, :ncols], in_=w_hbm[ic * 128:(ic + 1) * 128, :]),
            deps=deps)
        w3 = st_["w32"].ap()[:, :ncols].rearrange("p (b c) -> p b c", c=32)
        t1 = p.emit("dve", lambda e: e.tensor_reduce(
            st_["bmax"][:, :nb], w3, AX, ALU.max, apply_absolute_value=True),
            deps=[dld])
        t2 = p.emit("dve", lambda e: e.tensor_scalar(
            st_["sw"][:, :nb], st_["bmax"][:, :nb], 1.0 / 7.0, 1e-12,
            ALU.mult, ALU.max), deps=[t1])
        t3 = p.emit("dve", lambda e: e.reciprocal(
            st_["isw"][:, :nb], st_["sw"][:, :nb]), deps=[t2])
        if post_scale is not None:
            t2b = p.emit("dve", lambda e: e.tensor_scalar(
                st_["sw"][:, :nb], st_["sw"][:, :nb], post_scale, None,
                ALU.mult), deps=[t3])
            t3 = t2b
        isw3 = st_["isw"].ap()[:, :nb].rearrange("p (b o) -> p b o", o=1)
        sw3 = st_["sw"].ap()[:, :nb].rearrange("p (b o) -> p b o", o=1)
        z3 = st_["z"].ap()[:, :ncols].rearrange("p (b c) -> p b c", c=32)
        wq3 = st_["w16"].ap()[:, :ncols].rearrange("p (b c) -> p b c", c=32)
        a_, b_ = bass.broadcast_tensor_aps(w3, isw3)
        t4 = p.emit("dve", lambda e, a_=a_, b_=b_: e.tensor_tensor(
            z3, a_, b_, ALU.mult), deps=[t3])
        t5 = p.emit("dve", lambda e: e.tensor_scalar(
            st_["z"][:, :ncols], st_["z"][:, :ncols], MAGIC, -MAGIC,
            ALU.add, ALU.add), deps=[t4])
        a2_, b2_ = bass.broadcast_tensor_aps(z3, sw3)
        t6 = p.emit("dve", lambda e, a2_=a2_, b2_=b2_: e.tensor_tensor(
            wq3, a2_, b2_, ALU.mult), deps=[t5])
        ev_toks = list(prev_ev)
        tr_toks = []
        for dc in range(ncc):
            half = (dc // 4) % 2
            tr_toks.append(p.emit("pe", lambda e, dc=dc, half=half:
                e.transpose(
                    ps_tr[half][:, (dc % 4) * 128:(dc % 4 + 1) * 128],
                    st_["w16"][:, dc * 128:(dc + 1) * 128], id16[:, :]),
                deps=[t6] + (ev_toks[-2:-1] if len(ev_toks) >= 2
                             else ev_toks)))
            if dc % 4 == 3:
                ev_toks.append(p.emit("act", lambda e, dc=dc, half=half,
                                      ic=ic, ncc=ncc: e.copy(
                    dst[:, (ic * ncc + dc - 3) * 128:
                        (ic * ncc + dc + 1) * 128],
                    ps_tr[half][:, :]), deps=tr_toks[-4:]))
        return ev_toks[-1], t6, ev_toks[len(prev_ev):]

    ctx_w = ExitStack()
    winT = ctx_w.enter_context(nc.sbuf_tensor("winT", [128, DC * DC * 128], F16))
    wgT = ctx_w.enter_context(nc.sbuf_tensor("wgT", [128, DC * DC * 128], F16))

    ctx1 = ExitStack()
    wst = {k: ctx1.enter_context(nc.sbuf_tensor(f"p_{k}", [128, n], dt))
           for k, n, dt in [("w32", D, F32), ("w16", D, F16), ("z", D, F32),
                            ("bmax", D // 32, F32), ("sw", D // 32, F32),
                            ("isw", D // 32, F32)]}
    xs = [ctx1.enter_context(nc.sbuf_tensor(f"xs{i}", [128, D], F32))
          for i in range(2 * NTT)]
    xn_st = ctx1.enter_context(nc.sbuf_tensor("xn_st", [128, NTT * D], F16))
    xnT = ctx1.enter_context(nc.sbuf_tensor("xnT", [128, DC * S], F16))
    th_rg = ctx1.enter_context(nc.sbuf_tensor("th_rg", [128, DC * S], F16))
    th_ig = ctx1.enter_context(nc.sbuf_tensor("th_ig", [128, DC * S], F16))
    g_t = ctx1.enter_context(nc.sbuf_tensor("g_t", [128, DC * S], F32))
    al_t = ctx1.enter_context(nc.sbuf_tensor("al_t", [128, DC * S], F32))
    bh_t = ctx1.enter_context(nc.sbuf_tensor("bh_t", [128, DC * S], F16))
    u1_t = ctx1.enter_context(nc.sbuf_tensor("u1_t", [128, DC * S], F16))
    xb_t = ctx1.enter_context(nc.sbuf_tensor("xb_t", [128, DC * S], F16))
    h_t = [ctx1.enter_context(nc.sbuf_tensor(f"h{i}", [128, DC * S], F32))
           for i in range(2)]
    x2_tile = ctx1.enter_context(nc.sbuf_tensor("x2_tile", [128, D], F32))
    xm_tile = ctx1.enter_context(nc.sbuf_tensor("xm_tile", [128, D], F32))
    xn2_tile = ctx1.enter_context(nc.sbuf_tensor("xn2_t", [128, D], F16))

    ps1 = ExitStack()
    ps_tr = [ps1.enter_context(nc.psum_tensor(f"ps_tr{i}", [128, 512], F16))
             for i in range(2)]
    ps_g = [ps1.enter_context(nc.psum_tensor(f"ps_g{i}", [128, 512], F32))
            for i in range(2)]
    ps_h = ps1.enter_context(nc.psum_tensor("ps_h", [128, 1024], F32))
    ps_bc = ps1.enter_context(nc.psum_tensor("ps_bc", [128, 128], F32))

    def prep_square(w_hbm, dst, deps):
        toks = []
        last = None
        for dpc in range(DC):
            dld = p.dma("sp", "spq", lambda e, dpc=dpc: e.dma_start(
                out=wst["w32"][:, :], in_=w_hbm[dpc * 128:(dpc + 1) * 128, :]),
                deps=deps + ([last] if last else []))
            tc = p.emit("dve", lambda e: e.tensor_copy(
                wst["w16"][:, :], wst["w32"][:, :]), deps=[dld])
            last = tc
            tr_toks = []
            for dc in range(DC):
                half = (dc // 4) % 2
                tr_toks.append(p.emit("pe", lambda e, dc=dc, half=half:
                    e.transpose(
                        ps_tr[half][:, (dc % 4) * 128:(dc % 4 + 1) * 128],
                        wst["w16"][:, dc * 128:(dc + 1) * 128], id16[:, :]),
                    deps=[tc] + (toks[-2:-1] if len(toks) >= 2 else [])))
                if dc % 4 == 3:
                    toks.append(p.emit("act", lambda e, dc=dc, half=half,
                                       dpc=dpc: e.copy(
                        dst[:, (dpc * DC + dc - 3) * 128:
                            (dpc * DC + dc + 1) * 128],
                        ps_tr[half][:, :]), deps=tr_toks[-4:]))
            last = toks[-1]
        return toks[-1]

    tw1 = prep_square(win_h, winT, pro_toks)
    tw2 = prep_square(wg_h, wgT, [tw1])

    CB = {"negm1": 0, "r1": 4, "negm2": 8, "r2": 9, "negm2r2": 10,
          "negm3": 11, "r3": 12}

    x_dma = {}
    x2_dma = {}
    xn2_dma = {}
    t_scan = {}
    t_x_read = {}
    war = {
        "x2t_dma": None,
        "xn_st": None,
        "xnT": None,
        "xnT_dve": None,
        "th_rg": None, "th_ig": None,
        "g": {}, "al": {},
        "xb": None,
        "h_tr": [None, None],
        "x2t": None, "xn2t": None, "xn2t_pool": None,
        "proj_tail": None,
    }

    for st0 in range(min(2, NST)):
        for tt in range(NTT):
            x_dma[(st0, tt)] = p.dma(
                "sp", "spq", lambda e, st0=st0, tt=tt: e.dma_start(
                    out=xs[(st0 % 2) * NTT + tt][:, :],
                    in_=x_h[(st0 * S + tt * 128):(st0 * S + tt * 128 + 128), :]))

    def emit_tail(st2, scan_toks, hc2):
        t_ht_last = None
        prev_x2_stt = war.get("ps_h_reader")
        for tt in range(NTT):
            trh = []
            for dc in range(DC):
                trh.append(p.emit("pe", lambda e, dc=dc, tt=tt, hc2=hc2:
                    e.transpose(
                        ps_h[:, dc * 128:(dc + 1) * 128],
                        hc2[:, dc * S + tt * 128:dc * S + tt * 128 + 128],
                        id32[:, :]),
                    deps=[scan_toks[dc]] +
                         ([prev_x2_stt] if prev_x2_stt else [])))
            lt2 = ln_stats(lambda lo: ps_h[:, lo:lo + 512], CB["negm2"],
                           CB["r2"], trh, extra_negmr=CB["negm2r2"])
            xb_ = xs[(st2 % 2) * NTT + tt]
            txm = p.emit("dve", lambda e, xb_=xb_: e.tensor_scalar(
                xm_tile[:, :], xb_[:, :],
                colbuf[:, CB["negm2r2"]:CB["negm2r2"] + 1], None, ALU.add),
                deps=lt2)
            t_x_read[(st2, tt)] = txm
            tx2 = p.emit("dve", lambda e: e.scalar_tensor_tensor(
                x2_tile[:, :], ps_h[:, :],
                colbuf[:, CB["r2"]:CB["r2"] + 1], xm_tile[:, :],
                ALU.mult, ALU.add),
                deps=[txm] + lt2 + [war["x2t"], war["x2t_dma"]])
            use_cols(tx2, [CB["negm2"], CB["r2"], CB["negm2r2"]])
            prev_x2_stt = tx2
            war["ps_h_reader"] = tx2
            x2_dma[(st2, tt)] = p.dma("pool", "gpq",
                lambda e, st2=st2, tt=tt: e.dma_start(
                    out=x2_d[(st2 * S + tt * 128):
                             (st2 * S + tt * 128 + 128), :],
                    in_=x2_tile[:, :]), deps=[tx2])
            lt3 = ln_stats(lambda lo: x2_tile[:, lo:lo + 512], CB["negm3"],
                           CB["r3"], [tx2])
            txn2 = p.emit("dve", lambda e: e.tensor_scalar(
                xn2_tile[:, :], x2_tile[:, :],
                colbuf[:, CB["negm3"]:CB["negm3"] + 1],
                colbuf[:, CB["r3"]:CB["r3"] + 1], ALU.add, ALU.mult),
                deps=lt3 + [war["xn2t"], war["xn2t_pool"]])
            use_cols(txn2, [CB["negm3"], CB["r3"]])
            war["x2t"] = txn2
            war["x2t_dma"] = x2_dma[(st2, tt)]
            tred = emit_red("m0", xn2_tile[:, :], [txn2])
            xn2_dma[(st2, tt)] = p.dma("pool", "gpq",
                lambda e, st2=st2, tt=tt: e.dma_start(
                    out=xn2_d[(st2 * S + tt * 128):
                              (st2 * S + tt * 128 + 128), :],
                    in_=xn2_tile[:, :]), deps=[txn2])
            war["xn2t"] = xn2_dma[(st2, tt)]
            war["xn2t_pool"] = tred
            t_ht_last = trh[-1]
        war["h_tr"][st2 % 2] = t_ht_last

    pend_tail = None
    for st in range(NST):
        t_xn = {}
        for tt in range(NTT):
            xb_ = xs[(st % 2) * NTT + tt]
            lt = ln_stats(lambda lo, xb_=xb_: xb_[:, lo:lo + 512],
                          CB["negm1"] + tt, CB["r1"] + tt,
                          [x_dma[(st, tt)]])
            t_xn[tt] = p.emit("dve", lambda e, tt=tt, xb_=xb_: e.tensor_scalar(
                xn_st[:, tt * D:(tt + 1) * D], xb_[:, :],
                colbuf[:, CB["negm1"] + tt:CB["negm1"] + tt + 1],
                colbuf[:, CB["r1"] + tt:CB["r1"] + tt + 1],
                ALU.add, ALU.mult), deps=lt + [war["xn_st"]])
            use_cols(t_xn[tt], [CB["negm1"] + tt, CB["r1"] + tt])

        t_xnT = {}
        ev_list = []
        tr_last = None
        for dc in range(DC):
            half = dc % 2
            trk = []
            for tt in range(NTT):
                trk.append(p.emit("pe", lambda e, dc=dc, tt=tt, half=half:
                    e.transpose(
                        ps_tr[half][:, tt * 128:(tt + 1) * 128],
                        xn_st[:, tt * D + dc * 128:tt * D + dc * 128 + 128],
                        id16[:, :]),
                    deps=[t_xn[tt], tw2] +
                         (ev_list[-2:-1] if len(ev_list) >= 2 else [])))
            tr_last = trk[-1]
            ev_list.append(p.emit("act", lambda e, dc=dc, half=half: e.copy(
                xnT[:, dc * S:(dc + 1) * S], ps_tr[half][:, :]),
                deps=trk + [war["xnT"], war["xnT_dve"]]))
            t_xnT[dc] = ev_list[-1]
        war["xn_st"] = tr_last

        def proj(wT, th_dst, bias_t, prev_tail, th_war):
            evs = {}
            mm_last = None
            for c in range(DC):
                mms = []
                for dc in range(DC):
                    deps = [t_xnT[dc]]
                    if c >= 2:
                        deps.append(evs[c - 2])
                    elif prev_tail is not None:
                        deps.append(prev_tail.get(c - 2 + DC))
                    mms.append(p.emit("pe",
                        lambda e, c=c, dc=dc, wT=wT: e.matmul(
                            ps_g[c % 2][:, :],
                            wT[:, (c * DC + dc) * 128:(c * DC + dc + 1) * 128],
                            xnT[:, dc * S:(dc + 1) * S],
                            start=(dc == 0), stop=(dc == DC - 1)),
                        deps=deps))
                evs[c] = p.emit("act", lambda e, c=c, bias_t=bias_t,
                                th_dst=th_dst: e.activation(
                    th_dst[:, c * S:(c + 1) * S], ps_g[c % 2][:, :], AF.Tanh,
                    bias=bias_t[:, c:c + 1], scale=0.5),
                    deps=[mms[-1], th_war])
                mm_last = mms[-1]
            return evs, mm_last

        ev_rg, mm_l1 = proj(wgT, th_rg, bg2, war["proj_tail"], war["th_rg"])

        if pend_tail is not None:
            emit_tail(*pend_tail)
        pend_tail = None

        t_bh = {}
        t_al = {}
        t_g = {}
        for c in range(DC):
            t_g[c] = p.emit("dve", lambda e, c=c: e.tensor_scalar(
                g_t[:, c * S:(c + 1) * S], th_rg[:, c * S:(c + 1) * S],
                1.0, k2[:, c:c + 1], ALU.add, ALU.mult),
                deps=[ev_rg[c], war["g"].get(c)])
            t_al[c] = p.emit("act", lambda e, c=c: e.activation(
                al_t[:, c * S:(c + 1) * S], g_t[:, c * S:(c + 1) * S],
                AF.Exp, scale=-1.0), deps=[t_g[c], war["al"].get(c)])
            ta2 = p.emit("act", lambda e, c=c: e.activation(
                g_t[:, c * S:(c + 1) * S], g_t[:, c * S:(c + 1) * S],
                AF.Exp, scale=-2.0), deps=[t_al[c]])
            t_bh[c] = p.emit("act", lambda e, c=c: e.activation(
                bh_t[:, c * S:(c + 1) * S], g_t[:, c * S:(c + 1) * S],
                AF.Sqrt, scale=-0.25, bias=cst[:, 2:3]),
                deps=[ta2, war["xb"]])
        war["th_rg"] = t_g[DC - 1]

        ev_ig, mm_l2 = proj(winT, th_ig, bin2, ev_rg, war["th_ig"])
        war["proj_tail"] = ev_ig
        war["xnT"] = mm_l2

        hc = h_t[st % 2]
        hp = h_t[(st - 1) % 2]
        new_scan = {}
        t_u_last = None
        for c in range(DC):
            tu = p.emit("dve", lambda e, c=c: e.scalar_tensor_tensor(
                u1_t[:, c * S:(c + 1) * S], th_ig[:, c * S:(c + 1) * S],
                1.0, xnT[:, c * S:(c + 1) * S], ALU.add, ALU.mult),
                deps=[ev_ig[c], t_xnT[c]])
            tx = p.emit("dve", lambda e, c=c: e.tensor_tensor(
                xb_t[:, c * S:(c + 1) * S], u1_t[:, c * S:(c + 1) * S],
                bh_t[:, c * S:(c + 1) * S], ALU.mult), deps=[tu, t_bh[c]])
            if st == 0:
                init = 0.0
                dep_init = []
            else:
                init = hp[:, c * S + S - 1:c * S + S]
                dep_init = [t_scan[c]]
            new_scan[c] = p.emit("dve", lambda e, c=c, init=init, hc=hc:
                e.tensor_tensor_scan(
                    hc[:, c * S:(c + 1) * S], al_t[:, c * S:(c + 1) * S],
                    xb_t[:, c * S:(c + 1) * S], init, ALU.mult, ALU.add),
                deps=[tx, t_al[c], war["h_tr"][st % 2]] + dep_init)
            war["g"][c] = new_scan[c]
            war["al"][c] = new_scan[c]
            t_u_last = tu
        t_scan = new_scan
        war["th_ig"] = t_u_last
        war["xnT_dve"] = t_u_last
        war["xb"] = new_scan[DC - 1]

        pend_tail = (st, new_scan, hc)
        if st >= 1 and st + 1 < NST:
            for tt in range(NTT):
                x_dma[(st + 1, tt)] = p.dma(
                    "sp", "spq", lambda e, st=st, tt=tt: e.dma_start(
                        out=xs[((st + 1) % 2) * NTT + tt][:, :],
                        in_=x_h[((st + 1) * S + tt * 128):
                                ((st + 1) * S + tt * 128 + 128), :]),
                    deps=[t_x_read.get((st - 1, tt))])

    if pend_tail is not None:
        emit_tail(*pend_tail)
    t_fold0 = fold_ring("m0", 0)
    t_fm0 = max_finalize(0, 0, ps_bc, [t_fold0])
    b0 = barrier(0, 0, 1, [("s0", "i0")], ps_bc, [t_fm0])
    t_s0b = p.emit("dve", lambda e: e.tensor_scalar(
        scol("s0b"), scol("s0"), 65536.0, None, ALU.mult), deps=b0)
    b0 = b0 + [t_s0b]

    ps1.close()
    ctx1.close()
    ctx_w.close()
    p.fence()

    ctx_wu1 = ExitStack()
    wupT = ctx_wu1.enter_context(nc.sbuf_tensor("wupT", [128, IC * DC * 128], F16))
    ctx_wu2 = ExitStack()
    wgfT = ctx_wu2.enter_context(nc.sbuf_tensor("wgfT", [128, IC * DC * 128], F16))

    ctx2 = ExitStack()
    wst2 = {k: ctx2.enter_context(nc.sbuf_tensor(f"q_{k}", [128, n], dt))
            for k, n, dt in [("w32", D, F32), ("w16", D, F16), ("z", D, F32),
                             ("bmax", D // 32, F32), ("sw", D // 32, F32),
                             ("isw", D // 32, F32)]}
    RBD = 6
    xn2_sb = [ctx2.enter_context(nc.sbuf_tensor(f"xn2_sb{i}", [128, NTT * D],
                                                F16)) for i in range(2)]
    t0_sb = [ctx2.enter_context(nc.sbuf_tensor(f"t0_sb{i}", [128, D], F32))
             for i in range(3)]
    xq_sb = [ctx2.enter_context(nc.sbuf_tensor(f"xq_sb{i}", [128, NTT * D],
                                               F16)) for i in range(2)]
    xqT = [ctx2.enter_context(nc.sbuf_tensor(f"xqT2_{i}", [128, DC * S], F16))
           for i in range(2)]
    gw = [ctx2.enter_context(nc.sbuf_tensor(f"gw{i}", [128, S], F16))
          for i in range(RBD)]

    ps2 = ExitStack()
    ps_tr2 = [ps2.enter_context(nc.psum_tensor(f"ps2_tr{i}", [128, 512], F16))
              for i in range(2)]
    ps_tr2w = [ps2.enter_context(nc.psum_tensor(f"ps2_tw{i}", [128, 512], F16))
               for i in range(2)]
    ps_g2 = [ps2.enter_context(nc.psum_tensor(f"ps2_g{i}", [128, 512], F32))
             for i in range(2)]
    ps_bc2 = ps2.enter_context(nc.psum_tensor("ps_bc2", [128, 128], F32))

    t_wgf = {}
    t_wup = {}
    prev_ev_track = []
    prev_dep = b0
    for ic_ in range(IC):
        ev, _, new_evs = prep_lpbq_chunk(wgf_h, wgfT, wst2, ps_tr2w, ic_,
                                         prev_dep, D, prev_ev_track,
                                         post_scale=scol("s0b"))
        t_wgf[ic_] = ev
        prev_ev_track = (prev_ev_track + new_evs)[-2:]
        prev_dep = [ev]

    t_gdma = {}
    t_gev = {}
    up_sched = {}
    nup = 0
    for st in range(NST):
        k = 2 if NST >= IC // 2 else (IC + NST - 1) // NST
        for _ in range(k):
            if nup < IC:
                up_sched.setdefault(st, []).append(nup)
                nup += 1
    if nup < IC:
        up_sched.setdefault(NST - 1, []).extend(range(nup, IC))

    war2 = {"t0": [None] * 3}
    lastrd_xn2 = {}
    lastrd_xq = {}
    mml2 = {}
    gdma_f = {}
    t_xq_dma = {}
    push2a, flush2a = mk_defer(2)
    for st in range(NST):
        par = st % 2
        dx = {}
        for tt in range(NTT):
            dx[tt] = p.dma("sp", "spq",
                lambda e, st=st, tt=tt, par=par: e.dma_start(
                    out=xn2_sb[par][:, tt * D:(tt + 1) * D],
                    in_=xn2_d[(st * S + tt * 128):(st * S + tt * 128 + 128), :]),
                deps=[xn2_dma[(st, tt)], lastrd_xn2.get(st - 2)] + b0)
        t_xq = {}
        last_tt0 = None
        for tt in range(NTT):
            kt = (st * NTT + tt) % 3
            tt0 = p.emit("act", lambda e, tt=tt, par=par, kt=kt: e.activation(
                t0_sb[kt][:, :], xn2_sb[par][:, tt * D:(tt + 1) * D], AF.Relu,
                scale=scol("i0"), bias=cst[:, 3:4]),
                deps=[dx[tt], war2["t0"][kt]])
            last_tt0 = tt0
            t_xq[tt] = p.emit("dve", lambda e, tt=tt, par=par, kt=kt:
                e.tensor_scalar(
                    xq_sb[par][:, tt * D:(tt + 1) * D], t0_sb[kt][:, :],
                    -MAGIC, None, ALU.add),
                deps=[tt0, lastrd_xq.get(st - 2)])
            war2["t0"][kt] = t_xq[tt]
        lastrd_xn2[st] = last_tt0
        t_xqT = {}
        ev_list = []
        tr_last = None
        for dc in range(DC):
            half = dc % 2
            trk = []
            for tt in range(NTT):
                trk.append(p.emit("pe",
                    lambda e, dc=dc, tt=tt, half=half, par=par: e.transpose(
                        ps_tr2[half][:, tt * 128:(tt + 1) * 128],
                        xq_sb[par][:, tt * D + dc * 128:tt * D + dc * 128 + 128],
                        id16[:, :]),
                    deps=[t_xq[tt], t_wgf[IC - 1]] +
                         (ev_list[-2:-1] if len(ev_list) >= 2 else [])))
            tr_last = trk[-1]
            ev_list.append(p.emit("act",
                lambda e, dc=dc, half=half, par=par: e.copy(
                    xqT[par][:, dc * S:(dc + 1) * S], ps_tr2[half][:, :]),
                deps=trk + [mml2.get(st - 2), t_xq_dma.get((st - 2, dc))]))
            t_xqT[dc] = ev_list[-1]
            t_xq_dma[(st, dc)] = p.dma("act", "acq",
                lambda e, dc=dc, st=st, par=par: e.dma_start(
                    out=xqt_d[dc * 128:(dc + 1) * 128, st * S:(st + 1) * S],
                    in_=xqT[par][:, dc * S:(dc + 1) * S]), deps=[ev_list[-1]])
        lastrd_xq[st] = tr_last

        mm_last = None
        for ic_ in range(IC):
            icg = st * IC + ic_
            kb = icg % RBD
            flush2a(icg)
            mms = []
            for dc in range(DC):
                deps = [t_xqT[dc], t_wgf[ic_]]
                prev_ev = t_gev.get((st, ic_ - 2), t_gev.get((st - 1, IC - 2 + ic_)))
                if prev_ev is not None:
                    deps.append(prev_ev)
                mms.append(p.emit("pe",
                    lambda e, ic_=ic_, dc=dc, par=par: e.matmul(
                        ps_g2[ic_ % 2][:, :],
                        wgfT[:, (ic_ * DC + dc) * 128:(ic_ * DC + dc + 1) * 128],
                        xqT[par][:, dc * S:(dc + 1) * S],
                        start=(dc == 0), stop=(dc == DC - 1)), deps=deps))
            mm_last = mms[-1]
            tev = p.emit("act", lambda e, ic_=ic_, kb=kb: e.activation(
                gw[kb][:, :], ps_g2[ic_ % 2][:, :], AF.Copy,
                scale=2.0 ** -16),
                deps=[mms[-1], gdma_f.get(icg - RBD)])
            t_gev[(st, ic_)] = tev
            tred = emit_red("m2", gw[kb][:, :], [tev])

            def _sg(ic_=ic_, st=st, kb=kb, icg=icg, tev=tev, tred=tred):
                t_gdma[(st, ic_)] = p.dma("act", "acq",
                    lambda e, ic_=ic_, st=st, kb=kb: e.dma_start(
                        out=gate_d[ic_ * 128:(ic_ + 1) * 128,
                                   st * S:(st + 1) * S],
                        in_=gw[kb][:, :]), deps=[tev, tred])
                gdma_f[icg] = t_gdma[(st, ic_)]
            push2a(icg, _sg)
        mml2[st] = mm_last
        for ic_ in up_sched.get(st, []):
            ev, _, new_evs = prep_lpbq_chunk(
                wup_h, wupT, wst2, ps_tr2w, ic_,
                [t_wgf[IC - 1], t_wup.get(ic_ - 1)] + b0, D, prev_ev_track,
                post_scale=scol("s0b"))
            t_wup[ic_] = ev
            prev_ev_track = (prev_ev_track + new_evs)[-2:]

    flush2a(None)
    t_fold2 = fold_ring("m2", 1)
    t_fm2 = max_finalize(1, 1, ps_bc2, [t_fold2])
    b1 = barrier(1, 1, 1, [("s2", "i2")], ps_bc2, [t_fm2])
    t_hs2 = p.emit("dve", lambda e: e.tensor_scalar(
        scol("hs2"), scol("s2"), 0.5, None, ALU.mult), deps=b1)
    t_nh = p.emit("dve", lambda e: e.tensor_scalar(
        scol("nhs2m"), scol("hs2"), -MAGIC, None, ALU.mult), deps=[t_hs2])
    t_p16 = p.emit("dve", lambda e: e.tensor_scalar(
        scol("s2p16"), scol("s2"), 2.0 ** -16, None, ALU.mult), deps=b1)
    b1_all = b1 + [t_hs2, t_nh, t_p16]

    ps2.close()
    ctx2.close()
    ctx_wu2.close()
    p.fence()

    ctx3 = ExitStack()
    RBD = 6
    RCD = 3
    xqT3 = [ctx3.enter_context(nc.sbuf_tensor(f"xqT3_{i}", [128, DC * S], F16))
            for i in range(2)]
    uw = [ctx3.enter_context(nc.sbuf_tensor(f"uw{i}", [128, S], F16))
          for i in range(RBD)]
    gld = [ctx3.enter_context(nc.sbuf_tensor(f"gld{i}", [128, S], F16))
           for i in range(RBD)]
    t2w = [ctx3.enter_context(nc.sbuf_tensor(f"t2w{i}", [128, S], F32))
           for i in range(RCD)]
    th3w = [ctx3.enter_context(nc.sbuf_tensor(f"th3w{i}", [128, S], F32))
            for i in range(RCD)]
    w1w = [ctx3.enter_context(nc.sbuf_tensor(f"w1w{i}", [128, S], F32))
           for i in range(RCD)]
    sqw = [ctx3.enter_context(nc.sbuf_tensor(f"sqw{i}", [128, S], F32))
           for i in range(RCD)]
    gqw = [ctx3.enter_context(nc.sbuf_tensor(f"gqw{i}", [128, S], F16))
           for i in range(RCD)]
    g2w = [ctx3.enter_context(nc.sbuf_tensor(f"g2w{i}", [128, S], F16))
           for i in range(RBD)]

    ps3 = ExitStack()
    ps_u = [ps3.enter_context(nc.psum_tensor(f"ps_u{i}", [128, 512], F32))
            for i in range(2)]
    ps_bc3 = ps3.enter_context(nc.psum_tensor("ps_bc3", [128, 128], F32))

    t_udma = {}
    t_uev = {}
    t_g2dma = {}
    udma_f = {}
    g2dma_f = {}
    tt2_f = {}
    mml3 = {}
    war3 = {"t2": [None] * RCD, "t2r": [None] * RCD, "th3": [None] * RCD,
            "w1": [None] * RCD, "sq": [None] * RCD, "gq": [None] * RCD}
    push2b, flush2b = mk_defer(2)
    for st in range(NST):
        par = st % 2
        dxq = {}
        for dc in range(DC):
            dxq[dc] = p.dma("sp", "spq",
                lambda e, dc=dc, st=st, par=par: e.dma_start(
                    out=xqT3[par][:, dc * S:(dc + 1) * S],
                    in_=xqt_d[dc * 128:(dc + 1) * 128, st * S:(st + 1) * S]),
                deps=[t_xq_dma[(st, dc)], mml3.get(st - 2)] + b1_all)
        mm_last = None
        for ic_ in range(IC):
            icg = st * IC + ic_
            kb = icg % RBD
            kc = icg % RCD
            flush2b(icg)
            mms = []
            for dc in range(DC):
                deps = [dxq[dc], t_wup[ic_]]
                prev_ev = t_uev.get((st, ic_ - 2), t_uev.get((st - 1, IC - 2 + ic_)))
                if prev_ev is not None:
                    deps.append(prev_ev)
                mms.append(p.emit("pe",
                    lambda e, ic_=ic_, dc=dc, par=par: e.matmul(
                        ps_u[ic_ % 2][:, :],
                        wupT[:, (ic_ * DC + dc) * 128:(ic_ * DC + dc + 1) * 128],
                        xqT3[par][:, dc * S:(dc + 1) * S],
                        start=(dc == 0), stop=(dc == DC - 1)), deps=deps))
            mm_last = mms[-1]
            tev = p.emit("act", lambda e, ic_=ic_, kb=kb: e.activation(
                uw[kb][:, :], ps_u[ic_ % 2][:, :], AF.Copy,
                scale=2.0 ** -16),
                deps=[mms[-1], udma_f.get(icg - RBD)])
            t_uev[(st, ic_)] = tev
            tred1 = emit_red("m1", uw[kb][:, :], [tev])

            def _su(ic_=ic_, st=st, kb=kb, icg=icg, tev=tev, tred1=tred1):
                t_udma[(st, ic_)] = p.dma("act", "acq",
                    lambda e, ic_=ic_, st=st, kb=kb: e.dma_start(
                        out=up_d[ic_ * 128:(ic_ + 1) * 128,
                                 st * S:(st + 1) * S],
                        in_=uw[kb][:, :]), deps=[tev, tred1])
                udma_f[icg] = t_udma[(st, ic_)]
            push2b(icg, _su)

            dg = p.dma("sp", "spq", lambda e, ic_=ic_, st=st, kb=kb: e.dma_start(
                out=gld[kb][:, :],
                in_=gate_d[ic_ * 128:(ic_ + 1) * 128, st * S:(st + 1) * S]),
                deps=[t_gdma[(st, ic_)], tt2_f.get(icg - RBD)])
            tt2 = p.emit("pool", lambda e, kb=kb, kc=kc: e.tensor_scalar(
                t2w[kc][:, :], gld[kb][:, :], scol("i2"), MAGIC,
                ALU.mult, ALU.add),
                deps=[dg, war3["t2"][kc], war3["t2r"][kc]])
            tt2_f[icg] = tt2
            tth3 = p.emit("act", lambda e, kc=kc: e.activation(
                th3w[kc][:, :], t2w[kc][:, :], AF.Tanh, bias=scol("nhs2m"),
                scale=scol("hs2")), deps=[tt2, war3["th3"][kc]])
            war3["t2r"][kc] = tth3
            tw1_ = p.emit("act", lambda e, kc=kc: e.activation(
                w1w[kc][:, :], th3w[kc][:, :], AF.Relu, scale=32768.0,
                bias=cst[:, 4:5]), deps=[tth3, war3["w1"][kc]])
            war3["th3"][kc] = tw1_
            tsq = p.emit("dve", lambda e, kc=kc: e.tensor_scalar(
                sqw[kc][:, :], w1w[kc][:, :], -MAGIC, 65535.0,
                ALU.add, ALU.min), deps=[tw1_, war3["sq"][kc]])
            war3["w1"][kc] = tsq
            tm2 = p.emit("pool", lambda e, kc=kc: e.tensor_scalar(
                gqw[kc][:, :], t2w[kc][:, :], 1.0, -MAGIC,
                ALU.mult, ALU.add),
                deps=[tt2, war3["gq"][kc]])
            war3["t2"][kc] = tm2
            tg2 = p.emit("dve", lambda e, kb=kb, kc=kc: e.scalar_tensor_tensor(
                g2w[kb][:, :], sqw[kc][:, :], scol("s2p16"), gqw[kc][:, :],
                ALU.mult, ALU.mult),
                deps=[tsq, tm2, g2dma_f.get(icg - RBD)])
            war3["sq"][kc] = tg2
            war3["gq"][kc] = tg2
            tred3 = emit_red("m3", g2w[kb][:, :], [tg2])

            def _sg2(ic_=ic_, st=st, kb=kb, icg=icg, tg2=tg2, tred3=tred3):
                dg2 = p.dma("act", "acq",
                    lambda e, ic_=ic_, st=st, kb=kb: e.dma_start(
                        out=g2_d[ic_ * 128:(ic_ + 1) * 128,
                                 st * S:(st + 1) * S],
                        in_=g2w[kb][:, :]), deps=[tg2, tred3])
                t_g2dma[(st, ic_)] = dg2
                g2dma_f[icg] = dg2
            push2b(icg, _sg2)
        mml3[st] = mm_last

    flush2b(None)
    t_fold1 = fold_ring("m1", 2)
    t_fm1 = max_finalize(2, 2, ps_bc3, [t_fold1])
    t_fold3 = fold_ring("m3", 3)
    t_fm3 = max_finalize(3, 3, ps_bc3, [t_fold3, t_fm1])
    b2 = barrier(2, 2, 2, [("s1", "i1"), ("s3", "i3")], ps_bc3, [t_fm3])

    ps3.close()
    ctx3.close()
    ctx_wu1.close()
    p.fence()

    ctx_wd = ExitStack()
    wdT = ctx_wd.enter_context(nc.sbuf_tensor("wdT", [128, DC * IC * 128], F16))
    ctx4 = ExitStack()
    wst4 = {k: ctx4.enter_context(nc.sbuf_tensor(f"r_{k}", [128, n], dt))
            for k, n, dt in [("w32", I, F32), ("w16", I, F16), ("z", I, F32),
                             ("bmax", I // 32, F32), ("sw", I // 32, F32),
                             ("isw", I // 32, F32)]}
    R4 = 10
    R3 = 3
    RP = 8
    u4 = [ctx4.enter_context(nc.sbuf_tensor(f"u4_{i}", [128, S], F16))
          for i in range(R4)]
    g4 = [ctx4.enter_context(nc.sbuf_tensor(f"g4_{i}", [128, S], F16))
          for i in range(R4)]
    t4w = [ctx4.enter_context(nc.sbuf_tensor(f"t4w{i}", [128, S], F32))
           for i in range(R3)]
    gate2w = [ctx4.enter_context(nc.sbuf_tensor(f"gate2w{i}", [128, S], F16))
              for i in range(R3)]
    tuw = [ctx4.enter_context(nc.sbuf_tensor(f"tuw{i}", [128, S], F32))
           for i in range(R3)]
    upqw = [ctx4.enter_context(nc.sbuf_tensor(f"upqw{i}", [128, S], F16))
            for i in range(R3)]
    pw = [ctx4.enter_context(nc.sbuf_tensor(f"pw{i}", [128, S], F16))
          for i in range(RP)]

    ps4 = ExitStack()
    ps_tr4 = [ps4.enter_context(nc.psum_tensor(f"ps4_tr{i}", [128, 512], F16))
              for i in range(2)]
    ps_bc4 = ps4.enter_context(nc.psum_tensor("ps_bc4", [128, 128], F32))

    t_wd = {}
    prev_dep = b2
    prev_ev_track = []
    for dpc in range(DC):
        ev, _, new_evs = prep_lpbq_chunk(wd_h, wdT, wst4, ps_tr4, dpc,
                                         prev_dep, I, prev_ev_track)
        t_wd[dpc] = ev
        prev_ev_track = (prev_ev_track + new_evs)[-2:]
        prev_dep = [ev]

    t_pdma = {}
    war4 = {"u4": [None] * R4, "g4": [None] * R4, "t4": [None] * R3,
            "gate2": [None] * R3, "tu": [None] * R3, "upq": [None] * R3,
            "pw": [None] * RP}
    push4, flush4 = mk_defer(2)
    for st in range(NST):
        for ic_ in range(IC):
            k4 = (st * IC + ic_) % R4
            k3 = (st * IC + ic_) % R3
            kp = (st * IC + ic_) % RP
            flush4(st * IC + ic_)
            du = p.dma("sp", "spq", lambda e, ic_=ic_, st=st, k4=k4: e.dma_start(
                out=u4[k4][:, :],
                in_=up_d[ic_ * 128:(ic_ + 1) * 128, st * S:(st + 1) * S]),
                deps=[t_udma[(st, ic_)], t_wd[DC - 1], war4["u4"][k4]])
            dg = p.dma("sp", "spq", lambda e, ic_=ic_, st=st, k4=k4: e.dma_start(
                out=g4[k4][:, :],
                in_=g2_d[ic_ * 128:(ic_ + 1) * 128, st * S:(st + 1) * S]),
                deps=[t_g2dma[(st, ic_)], war4["g4"][k4]])
            tt4 = p.emit("act", lambda e, k4=k4, k3=k3: e.activation(
                t4w[k3][:, :], g4[k4][:, :], AF.Relu,
                scale=scol("i3"), bias=cst[:, 3:4]), deps=[dg, war4["t4"][k3]])
            war4["g4"][k4] = tt4
            tg2_ = p.emit("pool", lambda e, k3=k3: e.tensor_scalar(
                gate2w[k3][:, :], t4w[k3][:, :], -MAGIC, scol("s3"),
                ALU.add, ALU.mult), deps=[tt4, war4["gate2"][k3]])
            war4["t4"][k3] = tg2_
            ttu = p.emit("act", lambda e, k4=k4, k3=k3: e.activation(
                tuw[k3][:, :], u4[k4][:, :], AF.Relu,
                scale=scol("i1"), bias=cst[:, 3:4]), deps=[du, war4["tu"][k3]])
            war4["u4"][k4] = ttu
            tuq = p.emit("pool", lambda e, k3=k3: e.tensor_scalar(
                upqw[k3][:, :], tuw[k3][:, :], -MAGIC, scol("s1"),
                ALU.add, ALU.mult), deps=[ttu, war4["upq"][k3]])
            war4["tu"][k3] = tuq
            tp = p.emit("dve", lambda e, k3=k3, kp=kp: e.tensor_tensor(
                pw[kp][:, :], gate2w[k3][:, :], upqw[k3][:, :], ALU.mult),
                deps=[tg2_, tuq, war4["pw"][kp]])
            war4["gate2"][k3] = tp
            war4["upq"][k3] = tp
            tred4 = emit_red("m4", pw[kp][:, :], [tp])

            def _sp4(ic_=ic_, st=st, kp=kp, tp=tp, tred4=tred4):
                t_pdma[(st, ic_)] = p.dma("act", "acq",
                    lambda e, ic_=ic_, st=st, kp=kp: e.dma_start(
                        out=p_d[ic_ * 128:(ic_ + 1) * 128,
                                st * S:(st + 1) * S],
                        in_=pw[kp][:, :]), deps=[tp, tred4])
                war4["pw"][kp] = t_pdma[(st, ic_)]
            push4(st * IC + ic_, _sp4)

    flush4(None)
    t_fold4 = fold_ring("m4", 4)
    t_fm4 = max_finalize(4, 4, ps_bc4, [t_fold4])
    b3 = barrier(3, 4, 1, [("s4", "i4")], ps_bc4, [t_fm4])
    t_e4 = p.emit("dve", lambda e: e.tensor_scalar(
        scol("eps4"), scol("i4"), scol("i4"), 1e-5, ALU.mult, ALU.mult),
        deps=b3)
    b3 = b3 + [t_e4]

    ps4.close()
    ctx4.close()
    p.fence()

    ctx5 = ExitStack()
    o_t = [ctx5.enter_context(nc.sbuf_tensor(f"o_t{i}", [128, IC * S], F16))
           for i in range(2)]
    t5w = [ctx5.enter_context(nc.sbuf_tensor(f"t5w{i}", [128, S], F32))
           for i in range(R3)]
    x2l = [ctx5.enter_context(nc.sbuf_tensor(f"x2l{i}", [128, D], F32))
           for i in range(2)]
    xml = ctx5.enter_context(nc.sbuf_tensor("xml", [128, D], F32))
    outw = [ctx5.enter_context(nc.sbuf_tensor(f"outw{i}", [128, D], F32))
            for i in range(2)]

    ps5 = ExitStack()
    ps_y = [ps5.enter_context(nc.psum_tensor(f"ps_y{i}", [128, 1024], F32))
            for i in range(2)]

    t_out_dma = {}
    war5 = {"o": [{} for _ in range(2)], "rd": [{} for _ in range(2)],
            "t5": [None] * R3, "x2l": [None, None],
            "xml": None, "outw": [None, None], "psy": [None, None]}
    push5, flush5 = mk_defer(2)
    for st in range(NST):
        par = st % 2
        t_o = {}
        for ic_ in range(IC):
            k3 = (st * IC + ic_) % R3
            dp = p.dma("sp", "spq", lambda e, ic_=ic_, st=st, par=par: e.dma_start(
                out=o_t[par][:, ic_ * S:(ic_ + 1) * S],
                in_=p_d[ic_ * 128:(ic_ + 1) * 128, st * S:(st + 1) * S]),
                deps=[t_pdma[(st, ic_)], war5["rd"][par].get(ic_),
                      war5["o"][par].get(ic_)] + b3)
            tt5 = p.emit("act", lambda e, ic_=ic_, par=par, k3=k3: e.activation(
                t5w[k3][:, :], o_t[par][:, ic_ * S:(ic_ + 1) * S], AF.Relu,
                scale=scol("i4"), bias=cst[:, 3:4]),
                deps=[dp, war5["t5"][k3]])
            t_o[ic_] = p.emit("dve", lambda e, ic_=ic_, par=par, k3=k3:
                e.tensor_scalar(
                    o_t[par][:, ic_ * S:(ic_ + 1) * S], t5w[k3][:, :], -MAGIC,
                    None, ALU.add), deps=[tt5])
            war5["t5"][k3] = t_o[ic_]
            war5["o"][par][ic_] = t_o[ic_]
        for tt in range(NTT):
            flush5(st * NTT + tt)
            dx2 = p.dma("sp", "spq", lambda e, st=st, tt=tt: e.dma_start(
                out=x2l[tt % 2][:, :],
                in_=x2_d[(st * S + tt * 128):(st * S + tt * 128 + 128), :]),
                deps=[x2_dma[(st, tt)], war5["x2l"][tt % 2]])
            mms_stop = []
            wdap = wdT.ap().rearrange("p (d i c) -> p d i c", i=IC, c=128)
            for half in range(2):
                for ic_ in range(IC):
                    deps = [t_o[ic_]]
                    if war5["psy"][tt % 2] is not None:
                        deps.append(war5["psy"][tt % 2])
                    rh = wdap[:, half * 4:(half + 1) * 4, ic_, :]
                    mm = p.emit("pe",
                        lambda e, ic_=ic_, half=half, tt=tt, rh=rh, par=par:
                        e.matmul(
                            ps_y[tt % 2][:, half * 512:(half + 1) * 512],
                            o_t[par][:,
                                     ic_ * S + tt * 128:ic_ * S + tt * 128 + 128],
                            rh,
                            start=(ic_ == 0), stop=(ic_ == IC - 1)),
                        deps=deps)
                    war5["rd"][par][ic_] = mm
                mms_stop.append(mm)
            lt4 = ln_stats(lambda lo, tt=tt: ps_y[tt % 2][:, lo:lo + 512],
                           CB["negm2"], CB["r2"], mms_stop,
                           extra_negmr=CB["negm2r2"], eps_ap=scol("eps4"))
            txm = p.emit("dve", lambda e, tt=tt: e.tensor_scalar(
                xml[:, :], x2l[tt % 2][:, :],
                colbuf[:, CB["negm2r2"]:CB["negm2r2"] + 1], None, ALU.add),
                deps=lt4 + [dx2, war5["xml"]])
            war5["x2l"][tt % 2] = txm
            tout = p.emit("dve", lambda e, tt=tt: e.scalar_tensor_tensor(
                outw[tt % 2][:, :], ps_y[tt % 2][:, :],
                colbuf[:, CB["r2"]:CB["r2"] + 1], xml[:, :],
                ALU.mult, ALU.add),
                deps=[txm] + lt4 + [war5["outw"][tt % 2]])
            use_cols(tout, [CB["negm2"], CB["r2"], CB["negm2r2"]])
            war5["xml"] = tout
            war5["psy"][tt % 2] = tout

            def _so5(st=st, tt=tt, tout=tout):
                t_out_dma[(st, tt)] = p.dma("act", "acq",
                    lambda e, st=st, tt=tt: e.dma_start(
                        out=out_h[(st * S + tt * 128):
                                  (st * S + tt * 128 + 128), :],
                        in_=outw[tt % 2][:, :]), deps=[tout])
                war5["outw"][tt % 2] = t_out_dma[(st, tt)]
            push5(st * NTT + tt, _so5)

    flush5(None)
    finals = [(sq, v) for sq, v in p.dcnt.items() if v > 0]
    for e in P.ENGINES:
        p.water[e] = {}
        p.wait(e, finals)

    with nc.Block() as block:
        @block.tensor
        def _(eng):
            for op in p.ops["pe"]:
                op(eng)

        @block.scalar
        def _(eng):
            for op in p.ops["act"]:
                op(eng)

        @block.vector
        def _(eng):
            for op in p.ops["dve"]:
                op(eng)

        @block.gpsimd
        def _(eng):
            for op in p.ops["pool"]:
                op(eng)

        @block.sync
        def _(eng):
            for op in p.ops["sp"]:
                op(eng)

    es.pop_all()
    return nc


def _make_inputs(x_core, W_in, b_in, W_gate, b_gate, lam, w_gf, w_up, w_down):
    return {
        "x": np.ascontiguousarray(x_core, dtype=np.float32),
        "w_in": np.ascontiguousarray(W_in, dtype=np.float32),
        "w_gate": np.ascontiguousarray(W_gate, dtype=np.float32),
        "b_in": np.ascontiguousarray(np.reshape(b_in, (1, -1)), dtype=np.float32),
        "b_gate": np.ascontiguousarray(np.reshape(b_gate, (1, -1)), dtype=np.float32),
        "lam": np.ascontiguousarray(np.reshape(lam, (1, -1)), dtype=np.float32),
        "w_up": np.ascontiguousarray(w_up, dtype=np.float32),
        "w_gf": np.ascontiguousarray(w_gf, dtype=np.float32),
        "w_down": np.ascontiguousarray(w_down, dtype=np.float32),
        "id16_h": np.eye(128, dtype=np.float16),
        "id32_h": np.eye(128, dtype=np.float32),
    }


def kernel(L0_attn_x, L0_attn_W_in, L0_attn_b_in, L0_attn_W_gate,
           L0_attn_b_gate, L0_attn_forget_lambda, L0_ffn_w_gate,
           L0_ffn_w_up, L0_ffn_w_down):
    x = np.ascontiguousarray(np.asarray(L0_attn_x, dtype=np.float32))
    B, T, _ = x.shape
    in_maps = [
        _make_inputs(x[c], L0_attn_W_in, L0_attn_b_in, L0_attn_W_gate,
                     L0_attn_b_gate, L0_attn_forget_lambda, L0_ffn_w_gate,
                     L0_ffn_w_up, L0_ffn_w_down)
        for c in range(B)
    ]
    import os
    axon = (bool(os.environ.get("AXON_TERMINAL_JOB_NAME"))
            or os.environ.get("AXON_H4_ENABLED") == "1")
    use_cc = (os.environ.get("BASS_SIM_FP64") == "1") or not axon
    nc = build_program(T=T, use_cc=use_cc, n_cores=B)
    res = run_bass_kernel_spmd(nc, in_maps, core_ids=list(range(B)))
    return np.stack([np.asarray(res.results[c]["out"], dtype=np.float32)
                     for c in range(B)], axis=0)


# version 16
# speedup vs baseline: 2.5265x; 1.0345x over previous
import numpy as np
import concourse.bass as bass
import concourse.mybir as mybir
from concourse.bass_utils import run_bass_kernel_spmd
from contextlib import ExitStack

F32 = mybir.dt.float32
F16 = mybir.dt.float16
AF = mybir.ActivationFunctionType
ALU = mybir.AluOpType
AX = mybir.AxisListType.X

MAGIC = 12582912.0
QMAX = 32767.0
CLIP16 = 65535.0 / 65536.0
D = 1024
I = 3072
DC = D // 128
IC = I // 128
S = 512
NTT = S // 128


class P:
    ENGINES = ("pe", "act", "dve", "pool", "sp")
    DQUEUES = ("spq", "gpq", "acq", "ccq")
    NPOOL = 8

    def __init__(self, nc):
        self.nc = nc
        self.ops = {e: [] for e in self.ENGINES}
        self.cnt = {e: 0 for e in self.ENGINES}
        self.dcnt = {}
        for q in self.DQUEUES:
            for i in range(self.NPOOL):
                self.dcnt[f"{q}{i}"] = 0
        self.didx = {q: 0 for q in self.DQUEUES}
        self.sems = {}
        self.water = {e: {} for e in self.ENGINES}

    TRACE = []

    def wait(self, e, deps):
        for d in deps:
            if d is None:
                continue
            sname, val = d
            if sname == e:
                if val >= self.cnt[e]:
                    val = self.cnt[e]
            w = self.water[e]
            if w.get(sname, 0) >= val:
                continue
            w[sname] = val

            def op(eng, sem_name=sname, val=val, self=self):
                eng.wait_ge(self.sems[sem_name], val)
            self.ops[e].append(op)
            P.TRACE.append(("wait", e, sname, val))

    def emit(self, e, builder, deps=()):
        self.wait(e, deps)
        self.cnt[e] += 1
        val = self.cnt[e]

        def op(eng, builder=builder, e=e, self=self):
            builder(eng).then_inc(self.sems[e], 1)
        self.ops[e].append(op)
        import traceback
        P.TRACE.append(("emit", e, val, traceback.extract_stack()[-2].lineno))
        return (e, val)

    def dma(self, e, q, builder, deps=()):
        sq = f"{q}{self.didx[q] % self.NPOOL}"
        self.didx[q] += 1
        prev = getattr(self, "dlast", None)
        if prev is None:
            self.dlast = prev = {}
        self.wait(e, list(deps) + [prev.get(sq)])
        self.dcnt[sq] += 16
        val = self.dcnt[sq]
        prev[sq] = (sq, val)

        def op(eng, builder=builder, sq=sq, self=self):
            builder(eng).then_inc(self.sems[sq], 16)
        self.ops[e].append(op)
        import traceback
        P.TRACE.append(("dma", e, sq, val, traceback.extract_stack()[-2].lineno))
        return (sq, val)

    def fence(self):
        toks = [(e, self.cnt[e]) for e in self.ENGINES]
        toks += [(sq, self.dcnt[sq]) for sq in self.dcnt]
        for e in self.ENGINES:
            self.wait(e, toks)


def mk_defer(lag):
    q = []

    def push(idx, fn):
        q.append((idx, fn))

    def flush(now=None):
        while q and (now is None or q[0][0] + lag <= now):
            q.pop(0)[1]()
    return push, flush


def build_program(T=8192, use_cc=True, n_cores=8, dbg=False):
    NST = T // S
    nc = bass.Bass("TRN2", target_bir_lowering=False,
                   num_devices=(n_cores if use_cc else None))

    x_h = nc.dram_tensor("x", [T, D], F32, kind="ExternalInput").ap()
    win_h = nc.dram_tensor("w_in", [D, D], F32, kind="ExternalInput").ap()
    wg_h = nc.dram_tensor("w_gate", [D, D], F32, kind="ExternalInput").ap()
    bin_h = nc.dram_tensor("b_in", [1, D], F32, kind="ExternalInput").ap()
    bg_h = nc.dram_tensor("b_gate", [1, D], F32, kind="ExternalInput").ap()
    lam_h = nc.dram_tensor("lam", [1, D], F32, kind="ExternalInput").ap()
    wup_h = nc.dram_tensor("w_up", [I, D], F32, kind="ExternalInput").ap()
    wgf_h = nc.dram_tensor("w_gf", [I, D], F32, kind="ExternalInput").ap()
    wd_h = nc.dram_tensor("w_down", [D, I], F32, kind="ExternalInput").ap()
    id16_h = nc.dram_tensor("id16_h", [128, 128], F16, kind="ExternalInput").ap()
    id32_h = nc.dram_tensor("id32_h", [128, 128], F32, kind="ExternalInput").ap()
    out_h = nc.dram_tensor("out", [T, D], F32, kind="ExternalOutput").ap()

    kd = "ExternalOutput" if dbg else "Internal"
    x2_d = nc.dram_tensor("x2_d", [T, D], F32, kind=kd).ap()
    xn2_d = nc.dram_tensor("xn2_d", [T, D], F16, kind=kd).ap()
    xqt_d = nc.dram_tensor("xqt_d", [D, T], F16, kind=kd).ap()
    gate_d = nc.dram_tensor("gate_d", [I, T], F16, kind=kd).ap()
    up_d = nc.dram_tensor("up_d", [I, T], F16, kind=kd).ap()
    g2_d = nc.dram_tensor("g2_d", [I, T], F16, kind=kd).ap()
    p_d = nc.dram_tensor("p_d", [I, T], F16, kind=kd).ap()
    ccin_d = [nc.dram_tensor(f"ccin{k}", [1, 2], F32).ap() for k in range(4)]
    ccout_d = [nc.dram_tensor(f"ccout{k}", [1, 2], F32).ap() for k in range(4)]

    p = P(nc)
    es = ExitStack()
    for sname in list(p.cnt) + list(p.dcnt):
        p.sems[sname] = es.enter_context(nc.semaphore(f"s_{sname}"))

    A = nc.alloc_sbuf_tensor
    id16 = A("id16", [128, 128], F16)
    id32 = A("id32", [128, 128], F32)
    ones128 = A("ones128", [1, 128], F32)
    k2 = A("k2", [128, DC], F32)
    bin2 = A("bin2", [128, DC], F32)
    bg2 = A("bg2", [128, DC], F32)
    scal = A("scal", [128, 16], F32)
    srow = A("srow", [1, 16], F32)
    mcol = A("mcol", [128, 8], F32)
    mrow = A("mrow", [1, 128], F32)
    mcell = A("mcell", [1, 8], F32)
    lamt = A("lamt", [128, DC], F32)
    btmp = A("btmp", [128, DC], F32)
    bnbuf = A("bnbuf", [128, 12], F32)
    stats = A("stats", [128, 2], F32)
    colbuf = A("colbuf", [128, 16], F32)
    redcol = A("redcol", [128, 1], F32)
    redring = A("redring", [128, 40], F32)
    cst = A("cst", [128, 8], F32)

    SC = {"s0": 0, "i0": 1, "s2": 2, "i2": 3, "hs2": 4, "nhs2m": 5,
          "s1": 6, "i1": 7, "s3": 8, "i3": 9, "s4": 10, "i4": 11,
          "s2p16": 12, "eps4": 13, "s0b": 14}

    def scol(name):
        return scal[:, SC[name]:SC[name] + 1]

    d_id16 = p.dma("sp", "spq", lambda e: e.dma_start(out=id16[:, :], in_=id16_h))
    d_id32 = p.dma("sp", "spq", lambda e: e.dma_start(out=id32[:, :], in_=id32_h))
    d_lam = p.dma("sp", "spq", lambda e: e.dma_start(
        out=lamt[:, :], in_=lam_h.rearrange("o (c p) -> (o p) c", p=128),
        allow_slow_non_contiguous=True))
    d_bin = p.dma("sp", "spq", lambda e: e.dma_start(
        out=btmp[:, :], in_=bin_h.rearrange("o (c p) -> (o p) c", p=128),
        allow_slow_non_contiguous=True))
    t_ones = p.emit("dve", lambda e: e.memset(ones128[:, :], 1.0))
    t_c0 = p.emit("dve", lambda e: e.memset(cst[:, 0:1], 1e-5))
    t_c1 = p.emit("dve", lambda e: e.memset(cst[:, 1:2], 1.0))
    t_c2 = p.emit("dve", lambda e: e.memset(cst[:, 2:3], 0.25 * (1 + 1e-6)))
    t_c3 = p.emit("dve", lambda e: e.memset(cst[:, 3:4], MAGIC))
    t_c4 = p.emit("dve", lambda e: e.memset(cst[:, 4:5], 32768.0 + MAGIC))
    t_srz = p.emit("dve", lambda e: e.memset(srow[:, :], 0.0))
    t_mz = p.emit("dve", lambda e: e.memset(mcol[:, :], 0.0))
    t_e = p.emit("act", lambda e: e.activation(lamt[:, :], lamt[:, :], AF.Exp),
                 deps=[d_lam])
    t_l = p.emit("act", lambda e: e.activation(lamt[:, :], lamt[:, :], AF.Ln,
                                               bias=cst[:, 1:2]), deps=[t_e, t_c1])
    t_k2 = p.emit("dve", lambda e: e.tensor_scalar(
        k2[:, :], lamt[:, :], 4.0, None, ALU.mult), deps=[t_l])
    t_b1 = p.emit("dve", lambda e: e.tensor_scalar(
        bin2[:, :], btmp[:, :], 0.5, None, ALU.mult), deps=[d_bin])
    d_bg = p.dma("sp", "spq", lambda e: e.dma_start(
        out=btmp[:, :], in_=bg_h.rearrange("o (c p) -> (o p) c", p=128),
        allow_slow_non_contiguous=True), deps=[t_b1])
    t_b2 = p.emit("dve", lambda e: e.tensor_scalar(
        bg2[:, :], btmp[:, :], 0.5, None, ALU.mult), deps=[d_bg])
    pro_toks = [t_k2, t_b2, t_ones, t_mz, d_id16, d_id32, t_c0, t_c2, t_c3, t_c4]

    ln_prev = {"t5": None, "stats_r": None}
    reds = {"m0": [], "m2": [], "m1": [], "m3": [], "m4": []}
    RB = {"m0": 0, "m2": 8, "m1": 16, "m3": 24, "m4": 32}
    MC = {"m0": 0, "m2": 1, "m1": 2, "m3": 3, "m4": 4}
    last_fold = {}

    def _fold8(stream, nret):
        b = RB[stream]
        c = MC[stream]
        tr = p.emit("dve", lambda e, b=b, nret=nret: e.tensor_reduce(
            redcol[:, :], redring[:, b:b + nret], AX, ALU.max),
            deps=[reds[stream][-1], last_fold.get(stream, t_mz)])
        tm = p.emit("dve", lambda e, c=c: e.tensor_tensor(
            mcol[:, c:c + 1], mcol[:, c:c + 1], redcol[:, :], ALU.max),
            deps=[tr, t_mz])
        last_fold[stream] = tm
        return tm

    def emit_red(stream, src_ap, deps):
        k = RB[stream] + (len(reds[stream]) % 8)
        t = p.emit("dve", lambda e, k=k, src_ap=src_ap: e.tensor_reduce(
            redring[:, k:k + 1], src_ap, AX, ALU.max,
            apply_absolute_value=True),
            deps=list(deps) + [last_fold.get(stream)
                               if len(reds[stream]) % 8 == 0 else None])
        reds[stream].append(t)
        if len(reds[stream]) % 8 == 0:
            _fold8(stream, 8)
        return t

    def fold_ring(stream, col_idx):
        rem = len(reds[stream]) % 8
        if rem:
            _fold8(stream, rem)
        return last_fold[stream]
    colw = {}

    def ln_stats(src_ap_fn, negm_col, r_col, deps, extra_negmr=None,
                 eps_ap=None):
        war = [colw.get(r_col), ln_prev["t5"]]
        t1 = p.emit("dve", lambda e: e.bn_stats(bnbuf[:, 0:6], src_ap_fn(0)),
                    deps=list(deps) + [ln_prev["stats_r"]])
        t2 = p.emit("dve", lambda e: e.bn_stats(bnbuf[:, 6:12], src_ap_fn(512)),
                    deps=list(deps))
        t3 = p.emit("dve", lambda e: e.bn_aggr(stats[:, 0:2], bnbuf[:, 0:12]),
                    deps=[t1, t2])
        t4 = p.emit("dve", lambda e: e.tensor_scalar(
            colbuf[:, negm_col:negm_col + 1], stats[:, 0:1], -1.0, None,
            ALU.mult), deps=[t3])
        if eps_ap is None:
            eps_ap = cst[:, 0:1]
        t5 = p.emit("act", lambda e, eps_ap=eps_ap: e.activation(
            colbuf[:, r_col:r_col + 1], stats[:, 1:2], AF.Sqrt,
            bias=eps_ap), deps=[t3] + war)
        t6 = p.emit("dve", lambda e: e.reciprocal(
            colbuf[:, r_col:r_col + 1], colbuf[:, r_col:r_col + 1]), deps=[t5])
        ln_prev["t5"] = t5
        ln_prev["stats_r"] = t5
        toks = [t4, t6]
        if extra_negmr is not None:
            toks.append(p.emit("dve", lambda e: e.tensor_tensor(
                colbuf[:, extra_negmr:extra_negmr + 1],
                colbuf[:, negm_col:negm_col + 1],
                colbuf[:, r_col:r_col + 1], ALU.mult), deps=[t4, t6]))
        return toks

    def use_cols(tok, cols):
        for c in cols:
            colw[c] = tok

    def max_finalize(col_idx, cell_idx, ps_bc, deps):
        ttr = p.emit("pe", lambda e: e.transpose(
            ps_bc[0:1, 0:128], mcol[:, col_idx:col_idx + 1], id32[:, :]),
            deps=deps)
        tv = p.emit("dve", lambda e: e.tensor_copy(
            mrow[:, :], ps_bc[0:1, 0:128]), deps=[ttr])
        tr = p.emit("dve", lambda e: e.tensor_reduce(
            mcell[:, cell_idx:cell_idx + 1], mrow[:, :], AX, ALU.max),
            deps=[tv])
        return tr

    def barrier(kidx, cell0, n, scols, ps_bc, deps):
        toks = list(deps)
        if use_cc:
            d1 = p.dma("pool", "gpq", lambda e: e.dma_start(
                out=ccin_d[kidx][:, 0:n], in_=mcell[:, cell0:cell0 + n]),
                deps=toks)
            tcc = p.dma("pool", "ccq", lambda e: e.collective_compute(
                "AllReduce", ALU.max, replica_groups=[list(range(n_cores))],
                ins=[ccin_d[kidx][:, 0:n].opt()],
                outs=[ccout_d[kidx][:, 0:n].opt()]), deps=[d1])
            d2 = p.dma("pool", "gpq", lambda e: e.dma_start(
                out=mcell[:, cell0:cell0 + n], in_=ccout_d[kidx][:, 0:n]),
                deps=[tcc])
            toks = [d2]
        stoks = []
        for j in range(n):
            ts_ = p.emit("dve", lambda e, j=j: e.tensor_scalar(
                srow[:, 2 * j:2 * j + 1], mcell[:, cell0 + j:cell0 + j + 1],
                1.0 / QMAX, 1e-12, ALU.mult, ALU.max), deps=toks)
            ti_ = p.emit("dve", lambda e, j=j: e.reciprocal(
                srow[:, 2 * j + 1:2 * j + 2], srow[:, 2 * j:2 * j + 1]),
                deps=[ts_])
            stoks.append(ti_)
        tm = p.emit("pe", lambda e: e.matmul(
            ps_bc[0:128, 0:16], ones128[:, :], srow[:, :],
            start=True, stop=True), deps=stoks + [t_srz])
        evs = []
        for j, (sc_, ic_) in enumerate(scols):
            evs.append(p.emit("dve", lambda e, sc_=sc_, j=j: e.tensor_copy(
                scal[:, SC[sc_]:SC[sc_] + 1], ps_bc[0:128, 2 * j:2 * j + 1]),
                deps=[tm]))
            evs.append(p.emit("dve", lambda e, ic_=ic_, j=j: e.tensor_copy(
                scal[:, SC[ic_]:SC[ic_] + 1],
                ps_bc[0:128, 2 * j + 1:2 * j + 2]), deps=[tm]))
        return evs

    def prep_lpbq_chunk(w_hbm, dst, st_, ps_tr, ic, deps, ncols, prev_ev,
                        post_scale=None):
        ncc = ncols // 128
        nb = ncols // 32
        dld = p.dma("sp", "spq", lambda e: e.dma_start(
            out=st_["w32"][:, :ncols], in_=w_hbm[ic * 128:(ic + 1) * 128, :]),
            deps=deps)
        w3 = st_["w32"].ap()[:, :ncols].rearrange("p (b c) -> p b c", c=32)
        t1 = p.emit("dve", lambda e: e.tensor_reduce(
            st_["bmax"][:, :nb], w3, AX, ALU.max, apply_absolute_value=True),
            deps=[dld])
        t2 = p.emit("dve", lambda e: e.tensor_scalar(
            st_["sw"][:, :nb], st_["bmax"][:, :nb], 1.0 / 7.0, 1e-12,
            ALU.mult, ALU.max), deps=[t1])
        t3 = p.emit("dve", lambda e: e.reciprocal(
            st_["isw"][:, :nb], st_["sw"][:, :nb]), deps=[t2])
        if post_scale is not None:
            t2b = p.emit("dve", lambda e: e.tensor_scalar(
                st_["sw"][:, :nb], st_["sw"][:, :nb], post_scale, None,
                ALU.mult), deps=[t3])
            t3 = t2b
        isw3 = st_["isw"].ap()[:, :nb].rearrange("p (b o) -> p b o", o=1)
        sw3 = st_["sw"].ap()[:, :nb].rearrange("p (b o) -> p b o", o=1)
        z3 = st_["z"].ap()[:, :ncols].rearrange("p (b c) -> p b c", c=32)
        wq3 = st_["w16"].ap()[:, :ncols].rearrange("p (b c) -> p b c", c=32)
        a_, b_ = bass.broadcast_tensor_aps(w3, isw3)
        t4 = p.emit("dve", lambda e, a_=a_, b_=b_: e.tensor_tensor(
            z3, a_, b_, ALU.mult), deps=[t3])
        t5 = p.emit("dve", lambda e: e.tensor_scalar(
            st_["z"][:, :ncols], st_["z"][:, :ncols], MAGIC, -MAGIC,
            ALU.add, ALU.add), deps=[t4])
        a2_, b2_ = bass.broadcast_tensor_aps(z3, sw3)
        t6 = p.emit("dve", lambda e, a2_=a2_, b2_=b2_: e.tensor_tensor(
            wq3, a2_, b2_, ALU.mult), deps=[t5])
        ev_toks = list(prev_ev)
        tr_toks = []
        for dc in range(ncc):
            half = (dc // 4) % 2
            tr_toks.append(p.emit("pe", lambda e, dc=dc, half=half:
                e.transpose(
                    ps_tr[half][:, (dc % 4) * 128:(dc % 4 + 1) * 128],
                    st_["w16"][:, dc * 128:(dc + 1) * 128], id16[:, :]),
                deps=[t6] + (ev_toks[-2:-1] if len(ev_toks) >= 2
                             else ev_toks)))
            if dc % 4 == 3:
                ev_toks.append(p.emit("act", lambda e, dc=dc, half=half,
                                      ic=ic, ncc=ncc: e.copy(
                    dst[:, (ic * ncc + dc - 3) * 128:
                        (ic * ncc + dc + 1) * 128],
                    ps_tr[half][:, :]), deps=tr_toks[-4:]))
        return ev_toks[-1], t6, ev_toks[len(prev_ev):]

    ctx_w = ExitStack()
    winT = ctx_w.enter_context(nc.sbuf_tensor("winT", [128, DC * DC * 128], F16))
    wgT = ctx_w.enter_context(nc.sbuf_tensor("wgT", [128, DC * DC * 128], F16))

    ctx1 = ExitStack()
    wst = {k: ctx1.enter_context(nc.sbuf_tensor(f"p_{k}", [128, n], dt))
           for k, n, dt in [("w32", D, F32), ("w16", D, F16), ("z", D, F32),
                            ("bmax", D // 32, F32), ("sw", D // 32, F32),
                            ("isw", D // 32, F32)]}
    xs = [ctx1.enter_context(nc.sbuf_tensor(f"xs{i}", [128, D], F32))
          for i in range(2 * NTT)]
    xn_st = ctx1.enter_context(nc.sbuf_tensor("xn_st", [128, NTT * D], F16))
    xnT = ctx1.enter_context(nc.sbuf_tensor("xnT", [128, DC * S], F16))
    th_rg = ctx1.enter_context(nc.sbuf_tensor("th_rg", [128, DC * S], F16))
    th_ig = ctx1.enter_context(nc.sbuf_tensor("th_ig", [128, DC * S], F16))
    g_t = ctx1.enter_context(nc.sbuf_tensor("g_t", [128, DC * S], F32))
    al_t = ctx1.enter_context(nc.sbuf_tensor("al_t", [128, DC * S], F32))
    u1_t = ctx1.enter_context(nc.sbuf_tensor("u1_t", [128, DC * S], F16))
    xb_t = ctx1.enter_context(nc.sbuf_tensor("xb_t", [128, DC * S], F16))
    h_t = [ctx1.enter_context(nc.sbuf_tensor(f"h{i}", [128, DC * S], F32))
           for i in range(2)]
    x2_tile = [ctx1.enter_context(nc.sbuf_tensor(f"x2_tile{i}", [128, D], F32))
               for i in range(2)]
    xm_tile = ctx1.enter_context(nc.sbuf_tensor("xm_tile", [128, D], F32))
    xn2_tile = [ctx1.enter_context(nc.sbuf_tensor(f"xn2_t{i}", [128, D], F16))
                for i in range(2)]

    ps1 = ExitStack()
    ps_tr = [ps1.enter_context(nc.psum_tensor(f"ps_tr{i}", [128, 512], F16))
             for i in range(2)]
    ps_g = [ps1.enter_context(nc.psum_tensor(f"ps_g{i}", [128, 512], F32))
            for i in range(2)]
    ps_h = [ps1.enter_context(nc.psum_tensor(f"ps_h{i}", [128, 1024], F32))
            for i in range(2)]
    ps_bc = ps_g[0]

    def prep_square(w_hbm, dst, deps):
        toks = []
        last = None
        for dpc in range(DC):
            dld = p.dma("sp", "spq", lambda e, dpc=dpc: e.dma_start(
                out=wst["w32"][:, :], in_=w_hbm[dpc * 128:(dpc + 1) * 128, :]),
                deps=deps + ([last] if last else []))
            tc = p.emit("dve", lambda e: e.tensor_copy(
                wst["w16"][:, :], wst["w32"][:, :]), deps=[dld])
            last = tc
            tr_toks = []
            for dc in range(DC):
                half = (dc // 4) % 2
                tr_toks.append(p.emit("pe", lambda e, dc=dc, half=half:
                    e.transpose(
                        ps_tr[half][:, (dc % 4) * 128:(dc % 4 + 1) * 128],
                        wst["w16"][:, dc * 128:(dc + 1) * 128], id16[:, :]),
                    deps=[tc] + (toks[-2:-1] if len(toks) >= 2 else [])))
                if dc % 4 == 3:
                    toks.append(p.emit("act", lambda e, dc=dc, half=half,
                                       dpc=dpc: e.copy(
                        dst[:, (dpc * DC + dc - 3) * 128:
                            (dpc * DC + dc + 1) * 128],
                        ps_tr[half][:, :]), deps=tr_toks[-4:]))
            last = toks[-1]
        return toks[-1]

    tw1 = prep_square(win_h, winT, pro_toks)
    tw2 = prep_square(wg_h, wgT, [tw1])

    CB = {"negm1": 0, "r1": 4, "negm2": 8, "r2": 9, "negm2r2": 10,
          "negm3": 11, "r3": 12}

    x_dma = {}
    x2_dma = {}
    xn2_dma = {}
    t_scan = {}
    t_x_read = {}
    war = {
        "x2t_dma": [None, None],
        "xn_st": None,
        "xnT": None,
        "xnT_dve": None,
        "th_rg": None, "th_ig": None,
        "g": {}, "al": {},
        "xb": None, "xm": None,
        "h_tr": [None, None],
        "ps_h_rd": [None, None],
        "x2t": [None, None], "xn2t": [None, None],
        "xn2t_pool": [None, None],
        "proj_tail": None,
    }

    for st0 in range(min(2, NST)):
        for tt in range(NTT):
            x_dma[(st0, tt)] = p.dma(
                "sp", "spq", lambda e, st0=st0, tt=tt: e.dma_start(
                    out=xs[(st0 % 2) * NTT + tt][:, :],
                    in_=x_h[(st0 * S + tt * 128):(st0 * S + tt * 128 + 128), :]))

    pushP1, flushP1 = mk_defer(2)

    def emit_tail(st2, scan_toks, hc2):
        t_ht_last = None
        for tt in range(NTT):
            pp = tt % 2
            flushP1(st2 * NTT + tt)
            trh = []
            for dc in range(DC):
                trh.append(p.emit("pe", lambda e, dc=dc, tt=tt, hc2=hc2,
                                  pp=pp: e.transpose(
                        ps_h[pp][:, dc * 128:(dc + 1) * 128],
                        hc2[:, dc * S + tt * 128:dc * S + tt * 128 + 128],
                        id32[:, :]),
                    deps=[scan_toks[dc], war["ps_h_rd"][pp]]))
            lt2 = ln_stats(lambda lo, pp=pp: ps_h[pp][:, lo:lo + 512],
                           CB["negm2"], CB["r2"], trh,
                           extra_negmr=CB["negm2r2"])
            xb_ = xs[(st2 % 2) * NTT + tt]
            txm = p.emit("pool", lambda e, xb_=xb_: e.tensor_scalar(
                xm_tile[:, :], xb_[:, :], 1.0,
                colbuf[:, CB["negm2r2"]:CB["negm2r2"] + 1],
                ALU.mult, ALU.add),
                deps=lt2 + [war["xm"]])
            use_cols(txm, [CB["negm2r2"]])
            t_x_read[(st2, tt)] = txm
            tx2 = p.emit("dve", lambda e, pp=pp: e.scalar_tensor_tensor(
                x2_tile[pp][:, :], ps_h[pp][:, :],
                colbuf[:, CB["r2"]:CB["r2"] + 1], xm_tile[:, :],
                ALU.mult, ALU.add),
                deps=[txm] + lt2 + [war["x2t"][pp], war["x2t_dma"][pp]])
            use_cols(tx2, [CB["negm2"], CB["r2"], CB["negm2r2"]])
            war["xm"] = tx2
            war["ps_h_rd"][pp] = tx2

            def _sx2(st2=st2, tt=tt, pp=pp, tx2=tx2):
                x2_dma[(st2, tt)] = p.dma("act", "acq",
                    lambda e, st2=st2, tt=tt, pp=pp: e.dma_start(
                        out=x2_d[(st2 * S + tt * 128):
                                 (st2 * S + tt * 128 + 128), :],
                        in_=x2_tile[pp][:, :]), deps=[tx2])
                war["x2t_dma"][pp] = x2_dma[(st2, tt)]
            pushP1(st2 * NTT + tt, _sx2)
            lt3 = ln_stats(lambda lo, pp=pp: x2_tile[pp][:, lo:lo + 512],
                           CB["negm3"], CB["r3"], [tx2])
            txn2 = p.emit("pool", lambda e, pp=pp: e.tensor_scalar(
                xn2_tile[pp][:, :], x2_tile[pp][:, :],
                colbuf[:, CB["negm3"]:CB["negm3"] + 1],
                colbuf[:, CB["r3"]:CB["r3"] + 1], ALU.add, ALU.mult),
                deps=lt3 + [war["xn2t"][pp], war["xn2t_pool"][pp]])
            use_cols(txn2, [CB["negm3"], CB["r3"]])
            war["x2t"][pp] = txn2
            tred = emit_red("m0", xn2_tile[pp][:, :], [txn2])

            def _sxn2(st2=st2, tt=tt, pp=pp, txn2=txn2, tred=tred):
                xn2_dma[(st2, tt)] = p.dma("act", "acq",
                    lambda e, st2=st2, tt=tt, pp=pp: e.dma_start(
                        out=xn2_d[(st2 * S + tt * 128):
                                  (st2 * S + tt * 128 + 128), :],
                        in_=xn2_tile[pp][:, :]), deps=[txn2, tred])
                war["xn2t"][pp] = xn2_dma[(st2, tt)]
                war["xn2t_pool"][pp] = tred
            pushP1(st2 * NTT + tt, _sxn2)
            t_ht_last = trh[-1]
        war["h_tr"][st2 % 2] = t_ht_last

    pend_tail = None
    for st in range(NST):
        t_xn = {}
        for tt in range(NTT):
            xb_ = xs[(st % 2) * NTT + tt]
            lt = ln_stats(lambda lo, xb_=xb_: xb_[:, lo:lo + 512],
                          CB["negm1"] + tt, CB["r1"] + tt,
                          [x_dma[(st, tt)]])
            t_xn[tt] = p.emit("dve", lambda e, tt=tt, xb_=xb_: e.tensor_scalar(
                xn_st[:, tt * D:(tt + 1) * D], xb_[:, :],
                colbuf[:, CB["negm1"] + tt:CB["negm1"] + tt + 1],
                colbuf[:, CB["r1"] + tt:CB["r1"] + tt + 1],
                ALU.add, ALU.mult), deps=lt + [war["xn_st"]])
            use_cols(t_xn[tt], [CB["negm1"] + tt, CB["r1"] + tt])

        t_xnT = {}
        ev_list = []
        tr_last = None
        for dc in range(DC):
            half = dc % 2
            trk = []
            for tt in range(NTT):
                trk.append(p.emit("pe", lambda e, dc=dc, tt=tt, half=half:
                    e.transpose(
                        ps_tr[half][:, tt * 128:(tt + 1) * 128],
                        xn_st[:, tt * D + dc * 128:tt * D + dc * 128 + 128],
                        id16[:, :]),
                    deps=[t_xn[tt], tw2] +
                         (ev_list[-2:-1] if len(ev_list) >= 2 else [])))
            tr_last = trk[-1]
            ev_list.append(p.emit("act", lambda e, dc=dc, half=half: e.copy(
                xnT[:, dc * S:(dc + 1) * S], ps_tr[half][:, :]),
                deps=trk + [war["xnT"], war["xnT_dve"]]))
            t_xnT[dc] = ev_list[-1]
        war["xn_st"] = tr_last

        def proj(wT, th_dst, bias_t, prev_tail, th_war):
            evs = {}
            mm_last = None
            for c in range(DC):
                mms = []
                for dc in range(DC):
                    deps = [t_xnT[dc]]
                    if c >= 2:
                        deps.append(evs[c - 2])
                    elif prev_tail is not None:
                        deps.append(prev_tail.get(c - 2 + DC))
                    mms.append(p.emit("pe",
                        lambda e, c=c, dc=dc, wT=wT: e.matmul(
                            ps_g[c % 2][:, :],
                            wT[:, (c * DC + dc) * 128:(c * DC + dc + 1) * 128],
                            xnT[:, dc * S:(dc + 1) * S],
                            start=(dc == 0), stop=(dc == DC - 1)),
                        deps=deps))
                evs[c] = p.emit("act", lambda e, c=c, bias_t=bias_t,
                                th_dst=th_dst: e.activation(
                    th_dst[:, c * S:(c + 1) * S], ps_g[c % 2][:, :], AF.Tanh,
                    bias=bias_t[:, c:c + 1], scale=0.5),
                    deps=[mms[-1], th_war])
                mm_last = mms[-1]
            return evs, mm_last

        ev_rg, mm_l1 = proj(wgT, th_rg, bg2, war["proj_tail"], war["th_rg"])

        t_bh = {}
        t_al = {}
        t_g = {}
        ta2 = {}
        for c in range(DC):
            t_g[c] = p.emit("pool", lambda e, c=c: e.tensor_scalar(
                g_t[:, c * S:(c + 1) * S], th_rg[:, c * S:(c + 1) * S],
                1.0, k2[:, c:c + 1], ALU.add, ALU.mult),
                deps=[ev_rg[c], war["g"].get(c)])
        for c in range(DC):
            t_al[c] = p.emit("act", lambda e, c=c: e.activation(
                al_t[:, c * S:(c + 1) * S], g_t[:, c * S:(c + 1) * S],
                AF.Exp, scale=-1.0), deps=[t_g[c], war["al"].get(c)])
        for c in range(DC):
            ta2[c] = p.emit("act", lambda e, c=c: e.activation(
                g_t[:, c * S:(c + 1) * S], g_t[:, c * S:(c + 1) * S],
                AF.Exp, scale=-2.0), deps=[t_al[c]])
        for c in range(DC):
            t_bh[c] = p.emit("act", lambda e, c=c: e.activation(
                g_t[:, c * S:(c + 1) * S], g_t[:, c * S:(c + 1) * S],
                AF.Sqrt, scale=-0.25, bias=cst[:, 2:3]),
                deps=[ta2[c], war["xb"]])
        war["th_rg"] = t_g[DC - 1]

        if pend_tail is not None:
            emit_tail(*pend_tail)
        pend_tail = None

        ev_ig, mm_l2 = proj(winT, th_ig, bin2, ev_rg, war["th_ig"])
        war["proj_tail"] = ev_ig
        war["xnT"] = mm_l2

        hc = h_t[st % 2]
        hp = h_t[(st - 1) % 2]
        new_scan = {}
        t_u_last = None
        for c in range(DC):
            tu = p.emit("dve", lambda e, c=c: e.scalar_tensor_tensor(
                u1_t[:, c * S:(c + 1) * S], th_ig[:, c * S:(c + 1) * S],
                1.0, xnT[:, c * S:(c + 1) * S], ALU.add, ALU.mult),
                deps=[ev_ig[c], t_xnT[c]])
            tx = p.emit("dve", lambda e, c=c: e.tensor_tensor(
                xb_t[:, c * S:(c + 1) * S], u1_t[:, c * S:(c + 1) * S],
                g_t[:, c * S:(c + 1) * S], ALU.mult), deps=[tu, t_bh[c]])
            if st == 0:
                init = 0.0
                dep_init = []
            else:
                init = hp[:, c * S + S - 1:c * S + S]
                dep_init = [t_scan[c]]
            new_scan[c] = p.emit("dve", lambda e, c=c, init=init, hc=hc:
                e.tensor_tensor_scan(
                    hc[:, c * S:(c + 1) * S], al_t[:, c * S:(c + 1) * S],
                    xb_t[:, c * S:(c + 1) * S], init, ALU.mult, ALU.add),
                deps=[tx, t_al[c], war["h_tr"][st % 2]] + dep_init)
            war["g"][c] = new_scan[c]
            war["al"][c] = new_scan[c]
            t_u_last = tu
        t_scan = new_scan
        war["th_ig"] = t_u_last
        war["xnT_dve"] = t_u_last
        war["xb"] = new_scan[DC - 1]

        pend_tail = (st, new_scan, hc)
        if st >= 1 and st + 1 < NST:
            for tt in range(NTT):
                x_dma[(st + 1, tt)] = p.dma(
                    "sp", "spq", lambda e, st=st, tt=tt: e.dma_start(
                        out=xs[((st + 1) % 2) * NTT + tt][:, :],
                        in_=x_h[((st + 1) * S + tt * 128):
                                ((st + 1) * S + tt * 128 + 128), :]),
                    deps=[t_x_read.get((st - 1, tt))])

    if pend_tail is not None:
        emit_tail(*pend_tail)
    flushP1(None)
    t_fold0 = fold_ring("m0", 0)
    t_fm0 = max_finalize(0, 0, ps_bc, [t_fold0])
    b0 = barrier(0, 0, 1, [("s0", "i0")], ps_bc, [t_fm0])
    t_s0b = p.emit("dve", lambda e: e.tensor_scalar(
        scol("s0b"), scol("s0"), 65536.0, None, ALU.mult), deps=b0)
    b0 = b0 + [t_s0b]

    ps1.close()
    ctx1.close()
    ctx_w.close()
    p.fence()

    ctx_wu1 = ExitStack()
    wupT = ctx_wu1.enter_context(nc.sbuf_tensor("wupT", [128, IC * DC * 128], F16))
    ctx_wu2 = ExitStack()
    wgfT = ctx_wu2.enter_context(nc.sbuf_tensor("wgfT", [128, IC * DC * 128], F16))

    ctx2 = ExitStack()
    wst2 = {k: ctx2.enter_context(nc.sbuf_tensor(f"q_{k}", [128, n], dt))
            for k, n, dt in [("w32", D, F32), ("w16", D, F16), ("z", D, F32),
                             ("bmax", D // 32, F32), ("sw", D // 32, F32),
                             ("isw", D // 32, F32)]}
    RBD = 6
    xn2_sb = [ctx2.enter_context(nc.sbuf_tensor(f"xn2_sb{i}", [128, NTT * D],
                                                F16)) for i in range(2)]
    t0_sb = [ctx2.enter_context(nc.sbuf_tensor(f"t0_sb{i}", [128, D], F32))
             for i in range(3)]
    xq_sb = [ctx2.enter_context(nc.sbuf_tensor(f"xq_sb{i}", [128, NTT * D],
                                               F16)) for i in range(2)]
    xqT = [ctx2.enter_context(nc.sbuf_tensor(f"xqT2_{i}", [128, DC * S], F16))
           for i in range(2)]
    gw = [ctx2.enter_context(nc.sbuf_tensor(f"gw{i}", [128, S], F16))
          for i in range(RBD)]

    ps2 = ExitStack()
    ps_tr2 = [ps2.enter_context(nc.psum_tensor(f"ps2_tr{i}", [128, 512], F16))
              for i in range(2)]
    ps_tr2w = [ps2.enter_context(nc.psum_tensor(f"ps2_tw{i}", [128, 512], F16))
               for i in range(2)]
    ps_g2 = [ps2.enter_context(nc.psum_tensor(f"ps2_g{i}", [128, 512], F32))
             for i in range(2)]
    ps_bc2 = ps2.enter_context(nc.psum_tensor("ps_bc2", [128, 128], F32))

    t_wgf = {}
    t_wup = {}
    prev_ev_track = []
    prev_dep = b0
    for ic_ in range(IC):
        ev, _, new_evs = prep_lpbq_chunk(wgf_h, wgfT, wst2, ps_tr2w, ic_,
                                         prev_dep, D, prev_ev_track,
                                         post_scale=scol("s0b"))
        t_wgf[ic_] = ev
        prev_ev_track = (prev_ev_track + new_evs)[-2:]
        prev_dep = [ev]

    t_gdma = {}
    t_gev = {}
    up_sched = {}
    nup = 0
    for st in range(NST):
        k = 2 if NST >= IC // 2 else (IC + NST - 1) // NST
        for _ in range(k):
            if nup < IC:
                up_sched.setdefault(st, []).append(nup)
                nup += 1
    if nup < IC:
        up_sched.setdefault(NST - 1, []).extend(range(nup, IC))

    war2 = {"t0": [None] * 3}
    lastrd_xn2 = {}
    lastrd_xq = {}
    mml2 = {}
    gdma_f = {}
    t_xq_dma = {}
    push2a, flush2a = mk_defer(2)
    for st in range(NST):
        par = st % 2
        dx = {}
        for tt in range(NTT):
            dx[tt] = p.dma("sp", "spq",
                lambda e, st=st, tt=tt, par=par: e.dma_start(
                    out=xn2_sb[par][:, tt * D:(tt + 1) * D],
                    in_=xn2_d[(st * S + tt * 128):(st * S + tt * 128 + 128), :]),
                deps=[xn2_dma[(st, tt)], lastrd_xn2.get(st - 2)] + b0)
        t_xq = {}
        last_tt0 = None
        for tt in range(NTT):
            kt = (st * NTT + tt) % 3
            tt0 = p.emit("act", lambda e, tt=tt, par=par, kt=kt: e.activation(
                t0_sb[kt][:, :], xn2_sb[par][:, tt * D:(tt + 1) * D], AF.Relu,
                scale=scol("i0"), bias=cst[:, 3:4]),
                deps=[dx[tt], war2["t0"][kt]])
            last_tt0 = tt0
            t_xq[tt] = p.emit("dve", lambda e, tt=tt, par=par, kt=kt:
                e.tensor_scalar(
                    xq_sb[par][:, tt * D:(tt + 1) * D], t0_sb[kt][:, :],
                    -MAGIC, None, ALU.add),
                deps=[tt0, lastrd_xq.get(st - 2)])
            war2["t0"][kt] = t_xq[tt]
        lastrd_xn2[st] = last_tt0
        t_xqT = {}
        ev_list = []
        tr_last = None
        for dc in range(DC):
            half = dc % 2
            trk = []
            for tt in range(NTT):
                trk.append(p.emit("pe",
                    lambda e, dc=dc, tt=tt, half=half, par=par: e.transpose(
                        ps_tr2[half][:, tt * 128:(tt + 1) * 128],
                        xq_sb[par][:, tt * D + dc * 128:tt * D + dc * 128 + 128],
                        id16[:, :]),
                    deps=[t_xq[tt], t_wgf[IC - 1]] +
                         (ev_list[-2:-1] if len(ev_list) >= 2 else [])))
            tr_last = trk[-1]
            ev_list.append(p.emit("act",
                lambda e, dc=dc, half=half, par=par: e.copy(
                    xqT[par][:, dc * S:(dc + 1) * S], ps_tr2[half][:, :]),
                deps=trk + [mml2.get(st - 2), t_xq_dma.get((st - 2, dc))]))
            t_xqT[dc] = ev_list[-1]
            t_xq_dma[(st, dc)] = p.dma("act", "acq",
                lambda e, dc=dc, st=st, par=par: e.dma_start(
                    out=xqt_d[dc * 128:(dc + 1) * 128, st * S:(st + 1) * S],
                    in_=xqT[par][:, dc * S:(dc + 1) * S]), deps=[ev_list[-1]])
        lastrd_xq[st] = tr_last

        mm_last = None
        for ic_ in range(IC):
            icg = st * IC + ic_
            kb = icg % RBD
            flush2a(icg)
            mms = []
            for dc in range(DC):
                deps = [t_xqT[dc], t_wgf[ic_]]
                prev_ev = t_gev.get((st, ic_ - 2), t_gev.get((st - 1, IC - 2 + ic_)))
                if prev_ev is not None:
                    deps.append(prev_ev)
                mms.append(p.emit("pe",
                    lambda e, ic_=ic_, dc=dc, par=par: e.matmul(
                        ps_g2[ic_ % 2][:, :],
                        wgfT[:, (ic_ * DC + dc) * 128:(ic_ * DC + dc + 1) * 128],
                        xqT[par][:, dc * S:(dc + 1) * S],
                        start=(dc == 0), stop=(dc == DC - 1)), deps=deps))
            mm_last = mms[-1]
            tev = p.emit("act", lambda e, ic_=ic_, kb=kb: e.activation(
                gw[kb][:, :], ps_g2[ic_ % 2][:, :], AF.Copy,
                scale=2.0 ** -16),
                deps=[mms[-1], gdma_f.get(icg - RBD)])
            t_gev[(st, ic_)] = tev
            tred = emit_red("m2", gw[kb][:, :], [tev])

            def _sg(ic_=ic_, st=st, kb=kb, icg=icg, tev=tev, tred=tred):
                t_gdma[(st, ic_)] = p.dma("act", "acq",
                    lambda e, ic_=ic_, st=st, kb=kb: e.dma_start(
                        out=gate_d[ic_ * 128:(ic_ + 1) * 128,
                                   st * S:(st + 1) * S],
                        in_=gw[kb][:, :]), deps=[tev, tred])
                gdma_f[icg] = t_gdma[(st, ic_)]
            push2a(icg, _sg)
        mml2[st] = mm_last
        for ic_ in up_sched.get(st, []):
            ev, _, new_evs = prep_lpbq_chunk(
                wup_h, wupT, wst2, ps_tr2w, ic_,
                [t_wgf[IC - 1], t_wup.get(ic_ - 1)] + b0, D, prev_ev_track,
                post_scale=scol("s0b"))
            t_wup[ic_] = ev
            prev_ev_track = (prev_ev_track + new_evs)[-2:]

    flush2a(None)
    t_fold2 = fold_ring("m2", 1)
    t_fm2 = max_finalize(1, 1, ps_bc2, [t_fold2])
    b1 = barrier(1, 1, 1, [("s2", "i2")], ps_bc2, [t_fm2])
    t_hs2 = p.emit("dve", lambda e: e.tensor_scalar(
        scol("hs2"), scol("s2"), 0.5, None, ALU.mult), deps=b1)
    t_nh = p.emit("dve", lambda e: e.tensor_scalar(
        scol("nhs2m"), scol("hs2"), -MAGIC, None, ALU.mult), deps=[t_hs2])
    t_p16 = p.emit("dve", lambda e: e.tensor_scalar(
        scol("s2p16"), scol("s2"), 2.0 ** -16, None, ALU.mult), deps=b1)
    b1_all = b1 + [t_hs2, t_nh, t_p16]

    ps2.close()
    ctx2.close()
    ctx_wu2.close()
    p.fence()

    ctx3 = ExitStack()
    RBD = 6
    RCD = 3
    xqT3 = [ctx3.enter_context(nc.sbuf_tensor(f"xqT3_{i}", [128, DC * S], F16))
            for i in range(2)]
    uw = [ctx3.enter_context(nc.sbuf_tensor(f"uw{i}", [128, S], F16))
          for i in range(RBD)]
    gld = [ctx3.enter_context(nc.sbuf_tensor(f"gld{i}", [128, S], F16))
           for i in range(RBD)]
    t2w = [ctx3.enter_context(nc.sbuf_tensor(f"t2w{i}", [128, S], F32))
           for i in range(RCD)]
    th3w = [ctx3.enter_context(nc.sbuf_tensor(f"th3w{i}", [128, S], F32))
            for i in range(RCD)]
    w1w = [ctx3.enter_context(nc.sbuf_tensor(f"w1w{i}", [128, S], F32))
           for i in range(RCD)]
    sqw = [ctx3.enter_context(nc.sbuf_tensor(f"sqw{i}", [128, S], F32))
           for i in range(RCD)]
    gqw = [ctx3.enter_context(nc.sbuf_tensor(f"gqw{i}", [128, S], F16))
           for i in range(RCD)]
    g2w = [ctx3.enter_context(nc.sbuf_tensor(f"g2w{i}", [128, S], F16))
           for i in range(RBD)]

    ps3 = ExitStack()
    ps_u = [ps3.enter_context(nc.psum_tensor(f"ps_u{i}", [128, 512], F32))
            for i in range(2)]
    ps_bc3 = ps3.enter_context(nc.psum_tensor("ps_bc3", [128, 128], F32))

    t_udma = {}
    t_uev = {}
    t_g2dma = {}
    udma_f = {}
    g2dma_f = {}
    tt2_f = {}
    mml3 = {}
    war3 = {"t2": [None] * RCD, "t2r": [None] * RCD, "th3": [None] * RCD,
            "w1": [None] * RCD, "sq": [None] * RCD, "gq": [None] * RCD}
    push2b, flush2b = mk_defer(2)
    for st in range(NST):
        par = st % 2
        dxq = {}
        for dc in range(DC):
            dxq[dc] = p.dma("sp", "spq",
                lambda e, dc=dc, st=st, par=par: e.dma_start(
                    out=xqT3[par][:, dc * S:(dc + 1) * S],
                    in_=xqt_d[dc * 128:(dc + 1) * 128, st * S:(st + 1) * S]),
                deps=[t_xq_dma[(st, dc)], mml3.get(st - 2)] + b1_all)
        mm_last = None
        for ic_ in range(IC):
            icg = st * IC + ic_
            kb = icg % RBD
            kc = icg % RCD
            flush2b(icg)
            mms = []
            for dc in range(DC):
                deps = [dxq[dc], t_wup[ic_]]
                prev_ev = t_uev.get((st, ic_ - 2), t_uev.get((st - 1, IC - 2 + ic_)))
                if prev_ev is not None:
                    deps.append(prev_ev)
                mms.append(p.emit("pe",
                    lambda e, ic_=ic_, dc=dc, par=par: e.matmul(
                        ps_u[ic_ % 2][:, :],
                        wupT[:, (ic_ * DC + dc) * 128:(ic_ * DC + dc + 1) * 128],
                        xqT3[par][:, dc * S:(dc + 1) * S],
                        start=(dc == 0), stop=(dc == DC - 1)), deps=deps))
            mm_last = mms[-1]
            tev = p.emit("act", lambda e, ic_=ic_, kb=kb: e.activation(
                uw[kb][:, :], ps_u[ic_ % 2][:, :], AF.Copy,
                scale=2.0 ** -16),
                deps=[mms[-1], udma_f.get(icg - RBD)])
            t_uev[(st, ic_)] = tev
            tred1 = emit_red("m1", uw[kb][:, :], [tev])

            def _su(ic_=ic_, st=st, kb=kb, icg=icg, tev=tev, tred1=tred1):
                t_udma[(st, ic_)] = p.dma("act", "acq",
                    lambda e, ic_=ic_, st=st, kb=kb: e.dma_start(
                        out=up_d[ic_ * 128:(ic_ + 1) * 128,
                                 st * S:(st + 1) * S],
                        in_=uw[kb][:, :]), deps=[tev, tred1])
                udma_f[icg] = t_udma[(st, ic_)]
            push2b(icg, _su)

            dg = p.dma("sp", "spq", lambda e, ic_=ic_, st=st, kb=kb: e.dma_start(
                out=gld[kb][:, :],
                in_=gate_d[ic_ * 128:(ic_ + 1) * 128, st * S:(st + 1) * S]),
                deps=[t_gdma[(st, ic_)], tt2_f.get(icg - RBD)])
            tt2 = p.emit("pool", lambda e, kb=kb, kc=kc: e.tensor_scalar(
                t2w[kc][:, :], gld[kb][:, :], scol("i2"), MAGIC,
                ALU.mult, ALU.add),
                deps=[dg, war3["t2"][kc], war3["t2r"][kc]])
            tt2_f[icg] = tt2
            tth3 = p.emit("act", lambda e, kc=kc: e.activation(
                th3w[kc][:, :], t2w[kc][:, :], AF.Tanh, bias=scol("nhs2m"),
                scale=scol("hs2")), deps=[tt2, war3["th3"][kc]])
            war3["t2r"][kc] = tth3
            tw1_ = p.emit("act", lambda e, kc=kc: e.activation(
                w1w[kc][:, :], th3w[kc][:, :], AF.Relu, scale=32768.0,
                bias=cst[:, 4:5]), deps=[tth3, war3["w1"][kc]])
            war3["th3"][kc] = tw1_
            tsq = p.emit("dve", lambda e, kc=kc: e.tensor_scalar(
                sqw[kc][:, :], w1w[kc][:, :], -MAGIC, 65535.0,
                ALU.add, ALU.min), deps=[tw1_, war3["sq"][kc]])
            war3["w1"][kc] = tsq
            tm2 = p.emit("pool", lambda e, kc=kc: e.tensor_scalar(
                gqw[kc][:, :], t2w[kc][:, :], 1.0, -MAGIC,
                ALU.mult, ALU.add),
                deps=[tt2, war3["gq"][kc]])
            war3["t2"][kc] = tm2
            tg2 = p.emit("dve", lambda e, kb=kb, kc=kc: e.scalar_tensor_tensor(
                g2w[kb][:, :], sqw[kc][:, :], scol("s2p16"), gqw[kc][:, :],
                ALU.mult, ALU.mult),
                deps=[tsq, tm2, g2dma_f.get(icg - RBD)])
            war3["sq"][kc] = tg2
            war3["gq"][kc] = tg2
            tred3 = emit_red("m3", g2w[kb][:, :], [tg2])

            def _sg2(ic_=ic_, st=st, kb=kb, icg=icg, tg2=tg2, tred3=tred3):
                dg2 = p.dma("act", "acq",
                    lambda e, ic_=ic_, st=st, kb=kb: e.dma_start(
                        out=g2_d[ic_ * 128:(ic_ + 1) * 128,
                                 st * S:(st + 1) * S],
                        in_=g2w[kb][:, :]), deps=[tg2, tred3])
                t_g2dma[(st, ic_)] = dg2
                g2dma_f[icg] = dg2
            push2b(icg, _sg2)
        mml3[st] = mm_last

    flush2b(None)
    t_fold1 = fold_ring("m1", 2)
    t_fm1 = max_finalize(2, 2, ps_bc3, [t_fold1])
    t_fold3 = fold_ring("m3", 3)
    t_fm3 = max_finalize(3, 3, ps_bc3, [t_fold3, t_fm1])
    b2 = barrier(2, 2, 2, [("s1", "i1"), ("s3", "i3")], ps_bc3, [t_fm3])

    ps3.close()
    ctx3.close()
    ctx_wu1.close()
    p.fence()

    ctx_wd = ExitStack()
    wdT = ctx_wd.enter_context(nc.sbuf_tensor("wdT", [128, DC * IC * 128], F16))
    ctx4 = ExitStack()
    wst4 = {k: ctx4.enter_context(nc.sbuf_tensor(f"r_{k}", [128, n], dt))
            for k, n, dt in [("w32", I, F32), ("w16", I, F16), ("z", I, F32),
                             ("bmax", I // 32, F32), ("sw", I // 32, F32),
                             ("isw", I // 32, F32)]}
    R4 = 10
    R3 = 3
    RP = 8
    u4 = [ctx4.enter_context(nc.sbuf_tensor(f"u4_{i}", [128, S], F16))
          for i in range(R4)]
    g4 = [ctx4.enter_context(nc.sbuf_tensor(f"g4_{i}", [128, S], F16))
          for i in range(R4)]
    t4w = [ctx4.enter_context(nc.sbuf_tensor(f"t4w{i}", [128, S], F32))
           for i in range(R3)]
    gate2w = [ctx4.enter_context(nc.sbuf_tensor(f"gate2w{i}", [128, S], F16))
              for i in range(R3)]
    tuw = [ctx4.enter_context(nc.sbuf_tensor(f"tuw{i}", [128, S], F32))
           for i in range(R3)]
    upqw = [ctx4.enter_context(nc.sbuf_tensor(f"upqw{i}", [128, S], F16))
            for i in range(R3)]
    pw = [ctx4.enter_context(nc.sbuf_tensor(f"pw{i}", [128, S], F16))
          for i in range(RP)]

    ps4 = ExitStack()
    ps_tr4 = [ps4.enter_context(nc.psum_tensor(f"ps4_tr{i}", [128, 512], F16))
              for i in range(2)]
    ps_bc4 = ps4.enter_context(nc.psum_tensor("ps_bc4", [128, 128], F32))

    t_wd = {}
    prev_dep = b2
    prev_ev_track = []
    for dpc in range(DC):
        ev, _, new_evs = prep_lpbq_chunk(wd_h, wdT, wst4, ps_tr4, dpc,
                                         prev_dep, I, prev_ev_track)
        t_wd[dpc] = ev
        prev_ev_track = (prev_ev_track + new_evs)[-2:]
        prev_dep = [ev]

    t_pdma = {}
    war4 = {"u4": [None] * R4, "g4": [None] * R4, "t4": [None] * R3,
            "gate2": [None] * R3, "tu": [None] * R3, "upq": [None] * R3,
            "pw": [None] * RP}
    push4, flush4 = mk_defer(2)
    for st in range(NST):
        for ic_ in range(IC):
            k4 = (st * IC + ic_) % R4
            k3 = (st * IC + ic_) % R3
            kp = (st * IC + ic_) % RP
            flush4(st * IC + ic_)
            du = p.dma("sp", "spq", lambda e, ic_=ic_, st=st, k4=k4: e.dma_start(
                out=u4[k4][:, :],
                in_=up_d[ic_ * 128:(ic_ + 1) * 128, st * S:(st + 1) * S]),
                deps=[t_udma[(st, ic_)], t_wd[DC - 1], war4["u4"][k4]])
            dg = p.dma("sp", "spq", lambda e, ic_=ic_, st=st, k4=k4: e.dma_start(
                out=g4[k4][:, :],
                in_=g2_d[ic_ * 128:(ic_ + 1) * 128, st * S:(st + 1) * S]),
                deps=[t_g2dma[(st, ic_)], war4["g4"][k4]])
            tt4 = p.emit("act", lambda e, k4=k4, k3=k3: e.activation(
                t4w[k3][:, :], g4[k4][:, :], AF.Relu,
                scale=scol("i3"), bias=cst[:, 3:4]), deps=[dg, war4["t4"][k3]])
            war4["g4"][k4] = tt4
            tg2_ = p.emit("pool", lambda e, k3=k3: e.tensor_scalar(
                gate2w[k3][:, :], t4w[k3][:, :], -MAGIC, scol("s3"),
                ALU.add, ALU.mult), deps=[tt4, war4["gate2"][k3]])
            war4["t4"][k3] = tg2_
            ttu = p.emit("act", lambda e, k4=k4, k3=k3: e.activation(
                tuw[k3][:, :], u4[k4][:, :], AF.Relu,
                scale=scol("i1"), bias=cst[:, 3:4]), deps=[du, war4["tu"][k3]])
            war4["u4"][k4] = ttu
            tuq = p.emit("pool", lambda e, k3=k3: e.tensor_scalar(
                upqw[k3][:, :], tuw[k3][:, :], -MAGIC, scol("s1"),
                ALU.add, ALU.mult), deps=[ttu, war4["upq"][k3]])
            war4["tu"][k3] = tuq
            tp = p.emit("dve", lambda e, k3=k3, kp=kp: e.tensor_tensor(
                pw[kp][:, :], gate2w[k3][:, :], upqw[k3][:, :], ALU.mult),
                deps=[tg2_, tuq, war4["pw"][kp]])
            war4["gate2"][k3] = tp
            war4["upq"][k3] = tp
            tred4 = emit_red("m4", pw[kp][:, :], [tp])

            def _sp4(ic_=ic_, st=st, kp=kp, tp=tp, tred4=tred4):
                t_pdma[(st, ic_)] = p.dma("act", "acq",
                    lambda e, ic_=ic_, st=st, kp=kp: e.dma_start(
                        out=p_d[ic_ * 128:(ic_ + 1) * 128,
                                st * S:(st + 1) * S],
                        in_=pw[kp][:, :]), deps=[tp, tred4])
                war4["pw"][kp] = t_pdma[(st, ic_)]
            push4(st * IC + ic_, _sp4)

    flush4(None)
    t_fold4 = fold_ring("m4", 4)
    t_fm4 = max_finalize(4, 4, ps_bc4, [t_fold4])
    b3 = barrier(3, 4, 1, [("s4", "i4")], ps_bc4, [t_fm4])
    t_e4 = p.emit("dve", lambda e: e.tensor_scalar(
        scol("eps4"), scol("i4"), scol("i4"), 1e-5, ALU.mult, ALU.mult),
        deps=b3)
    b3 = b3 + [t_e4]

    ps4.close()
    ctx4.close()
    p.fence()

    ctx5 = ExitStack()
    o_t = [ctx5.enter_context(nc.sbuf_tensor(f"o_t{i}", [128, IC * S], F16))
           for i in range(2)]
    t5w = [ctx5.enter_context(nc.sbuf_tensor(f"t5w{i}", [128, S], F32))
           for i in range(R3)]
    x2l = [ctx5.enter_context(nc.sbuf_tensor(f"x2l{i}", [128, D], F32))
           for i in range(2)]
    xml = ctx5.enter_context(nc.sbuf_tensor("xml", [128, D], F32))
    outw = [ctx5.enter_context(nc.sbuf_tensor(f"outw{i}", [128, D], F32))
            for i in range(2)]

    ps5 = ExitStack()
    ps_y = [ps5.enter_context(nc.psum_tensor(f"ps_y{i}", [128, 1024], F32))
            for i in range(2)]

    t_out_dma = {}
    war5 = {"o": [{} for _ in range(2)], "rd": [{} for _ in range(2)],
            "t5": [None] * R3, "x2l": [None, None],
            "xml": None, "outw": [None, None], "psy": [None, None]}
    push5, flush5 = mk_defer(2)
    for st in range(NST):
        par = st % 2
        t_o = {}
        for ic_ in range(IC):
            k3 = (st * IC + ic_) % R3
            dp = p.dma("sp", "spq", lambda e, ic_=ic_, st=st, par=par: e.dma_start(
                out=o_t[par][:, ic_ * S:(ic_ + 1) * S],
                in_=p_d[ic_ * 128:(ic_ + 1) * 128, st * S:(st + 1) * S]),
                deps=[t_pdma[(st, ic_)], war5["rd"][par].get(ic_),
                      war5["o"][par].get(ic_)] + b3)
            tt5 = p.emit("act", lambda e, ic_=ic_, par=par, k3=k3: e.activation(
                t5w[k3][:, :], o_t[par][:, ic_ * S:(ic_ + 1) * S], AF.Relu,
                scale=scol("i4"), bias=cst[:, 3:4]),
                deps=[dp, war5["t5"][k3]])
            t_o[ic_] = p.emit("dve", lambda e, ic_=ic_, par=par, k3=k3:
                e.tensor_scalar(
                    o_t[par][:, ic_ * S:(ic_ + 1) * S], t5w[k3][:, :], -MAGIC,
                    None, ALU.add), deps=[tt5])
            war5["t5"][k3] = t_o[ic_]
            war5["o"][par][ic_] = t_o[ic_]
        for tt in range(NTT):
            flush5(st * NTT + tt)
            dx2 = p.dma("sp", "spq", lambda e, st=st, tt=tt: e.dma_start(
                out=x2l[tt % 2][:, :],
                in_=x2_d[(st * S + tt * 128):(st * S + tt * 128 + 128), :]),
                deps=[x2_dma[(st, tt)], war5["x2l"][tt % 2]])
            mms_stop = []
            wdap = wdT.ap().rearrange("p (d i c) -> p d i c", i=IC, c=128)
            for half in range(2):
                for ic_ in range(IC):
                    deps = [t_o[ic_]]
                    if war5["psy"][tt % 2] is not None:
                        deps.append(war5["psy"][tt % 2])
                    rh = wdap[:, half * 4:(half + 1) * 4, ic_, :]
                    mm = p.emit("pe",
                        lambda e, ic_=ic_, half=half, tt=tt, rh=rh, par=par:
                        e.matmul(
                            ps_y[tt % 2][:, half * 512:(half + 1) * 512],
                            o_t[par][:,
                                     ic_ * S + tt * 128:ic_ * S + tt * 128 + 128],
                            rh,
                            start=(ic_ == 0), stop=(ic_ == IC - 1)),
                        deps=deps)
                    war5["rd"][par][ic_] = mm
                mms_stop.append(mm)
            lt4 = ln_stats(lambda lo, tt=tt: ps_y[tt % 2][:, lo:lo + 512],
                           CB["negm2"], CB["r2"], mms_stop,
                           extra_negmr=CB["negm2r2"], eps_ap=scol("eps4"))
            txm = p.emit("dve", lambda e, tt=tt: e.tensor_scalar(
                xml[:, :], x2l[tt % 2][:, :],
                colbuf[:, CB["negm2r2"]:CB["negm2r2"] + 1], None, ALU.add),
                deps=lt4 + [dx2, war5["xml"]])
            war5["x2l"][tt % 2] = txm
            tout = p.emit("dve", lambda e, tt=tt: e.scalar_tensor_tensor(
                outw[tt % 2][:, :], ps_y[tt % 2][:, :],
                colbuf[:, CB["r2"]:CB["r2"] + 1], xml[:, :],
                ALU.mult, ALU.add),
                deps=[txm] + lt4 + [war5["outw"][tt % 2]])
            use_cols(tout, [CB["negm2"], CB["r2"], CB["negm2r2"]])
            war5["xml"] = tout
            war5["psy"][tt % 2] = tout

            def _so5(st=st, tt=tt, tout=tout):
                t_out_dma[(st, tt)] = p.dma("act", "acq",
                    lambda e, st=st, tt=tt: e.dma_start(
                        out=out_h[(st * S + tt * 128):
                                  (st * S + tt * 128 + 128), :],
                        in_=outw[tt % 2][:, :]), deps=[tout])
                war5["outw"][tt % 2] = t_out_dma[(st, tt)]
            push5(st * NTT + tt, _so5)

    flush5(None)
    finals = [(sq, v) for sq, v in p.dcnt.items() if v > 0]
    for e in P.ENGINES:
        p.water[e] = {}
        p.wait(e, finals)

    with nc.Block() as block:
        @block.tensor
        def _(eng):
            for op in p.ops["pe"]:
                op(eng)

        @block.scalar
        def _(eng):
            for op in p.ops["act"]:
                op(eng)

        @block.vector
        def _(eng):
            for op in p.ops["dve"]:
                op(eng)

        @block.gpsimd
        def _(eng):
            for op in p.ops["pool"]:
                op(eng)

        @block.sync
        def _(eng):
            for op in p.ops["sp"]:
                op(eng)

    es.pop_all()
    return nc


def _make_inputs(x_core, W_in, b_in, W_gate, b_gate, lam, w_gf, w_up, w_down):
    return {
        "x": np.ascontiguousarray(x_core, dtype=np.float32),
        "w_in": np.ascontiguousarray(W_in, dtype=np.float32),
        "w_gate": np.ascontiguousarray(W_gate, dtype=np.float32),
        "b_in": np.ascontiguousarray(np.reshape(b_in, (1, -1)), dtype=np.float32),
        "b_gate": np.ascontiguousarray(np.reshape(b_gate, (1, -1)), dtype=np.float32),
        "lam": np.ascontiguousarray(np.reshape(lam, (1, -1)), dtype=np.float32),
        "w_up": np.ascontiguousarray(w_up, dtype=np.float32),
        "w_gf": np.ascontiguousarray(w_gf, dtype=np.float32),
        "w_down": np.ascontiguousarray(w_down, dtype=np.float32),
        "id16_h": np.eye(128, dtype=np.float16),
        "id32_h": np.eye(128, dtype=np.float32),
    }


def kernel(L0_attn_x, L0_attn_W_in, L0_attn_b_in, L0_attn_W_gate,
           L0_attn_b_gate, L0_attn_forget_lambda, L0_ffn_w_gate,
           L0_ffn_w_up, L0_ffn_w_down):
    x = np.ascontiguousarray(np.asarray(L0_attn_x, dtype=np.float32))
    B, T, _ = x.shape
    in_maps = [
        _make_inputs(x[c], L0_attn_W_in, L0_attn_b_in, L0_attn_W_gate,
                     L0_attn_b_gate, L0_attn_forget_lambda, L0_ffn_w_gate,
                     L0_ffn_w_up, L0_ffn_w_down)
        for c in range(B)
    ]
    import os
    axon = (bool(os.environ.get("AXON_TERMINAL_JOB_NAME"))
            or os.environ.get("AXON_H4_ENABLED") == "1")
    use_cc = (os.environ.get("BASS_SIM_FP64") == "1") or not axon
    nc = build_program(T=T, use_cc=use_cc, n_cores=B)
    res = run_bass_kernel_spmd(nc, in_maps, core_ids=list(range(B)))
    return np.stack([np.asarray(res.results[c]["out"], dtype=np.float32)
                     for c in range(B)], axis=0)


# version 17
# speedup vs baseline: 2.5691x; 1.0169x over previous
import numpy as np
import concourse.bass as bass
import concourse.mybir as mybir
from concourse.bass_utils import run_bass_kernel_spmd
from contextlib import ExitStack

F32 = mybir.dt.float32
F16 = mybir.dt.float16
AF = mybir.ActivationFunctionType
ALU = mybir.AluOpType
AX = mybir.AxisListType.X

MAGIC = 12582912.0
QMAX = 32767.0
CLIP16 = 65535.0 / 65536.0
D = 1024
I = 3072
DC = D // 128
IC = I // 128
S = 512
NTT = S // 128


class P:
    ENGINES = ("pe", "act", "dve", "pool", "sp")
    DQUEUES = ("spq", "gpq", "acq", "ccq")
    NPOOL = 8

    def __init__(self, nc):
        self.nc = nc
        self.ops = {e: [] for e in self.ENGINES}
        self.cnt = {e: 0 for e in self.ENGINES}
        self.dcnt = {}
        for q in self.DQUEUES:
            for i in range(self.NPOOL):
                self.dcnt[f"{q}{i}"] = 0
        self.didx = {q: 0 for q in self.DQUEUES}
        self.sems = {}
        self.water = {e: {} for e in self.ENGINES}

    TRACE = []

    def wait(self, e, deps):
        for d in deps:
            if d is None:
                continue
            sname, val = d
            if sname == e:
                if val >= self.cnt[e]:
                    val = self.cnt[e]
            w = self.water[e]
            if w.get(sname, 0) >= val:
                continue
            w[sname] = val

            def op(eng, sem_name=sname, val=val, self=self):
                eng.wait_ge(self.sems[sem_name], val)
            self.ops[e].append(op)
            P.TRACE.append(("wait", e, sname, val))

    def emit(self, e, builder, deps=()):
        self.wait(e, deps)
        self.cnt[e] += 1
        val = self.cnt[e]

        def op(eng, builder=builder, e=e, self=self):
            builder(eng).then_inc(self.sems[e], 1)
        self.ops[e].append(op)
        import traceback
        P.TRACE.append(("emit", e, val, traceback.extract_stack()[-2].lineno))
        return (e, val)

    def dma(self, e, q, builder, deps=()):
        sq = f"{q}{self.didx[q] % self.NPOOL}"
        self.didx[q] += 1
        prev = getattr(self, "dlast", None)
        if prev is None:
            self.dlast = prev = {}
        self.wait(e, list(deps) + [prev.get(sq)])
        self.dcnt[sq] += 16
        val = self.dcnt[sq]
        prev[sq] = (sq, val)

        def op(eng, builder=builder, sq=sq, self=self):
            builder(eng).then_inc(self.sems[sq], 16)
        self.ops[e].append(op)
        import traceback
        P.TRACE.append(("dma", e, sq, val, traceback.extract_stack()[-2].lineno))
        return (sq, val)

    def fence(self):
        toks = [(e, self.cnt[e]) for e in self.ENGINES]
        toks += [(sq, self.dcnt[sq]) for sq in self.dcnt]
        for e in self.ENGINES:
            self.wait(e, toks)


def mk_defer(lag):
    q = []

    def push(idx, fn):
        q.append((idx, fn))

    def flush(now=None):
        while q and (now is None or q[0][0] + lag <= now):
            q.pop(0)[1]()
    return push, flush


def build_program(T=8192, use_cc=True, n_cores=8, dbg=False):
    NST = T // S
    nc = bass.Bass("TRN2", target_bir_lowering=False,
                   num_devices=(n_cores if use_cc else None))

    x_h = nc.dram_tensor("x", [T, D], F32, kind="ExternalInput").ap()
    win_h = nc.dram_tensor("w_in", [D, D], F32, kind="ExternalInput").ap()
    wg_h = nc.dram_tensor("w_gate", [D, D], F32, kind="ExternalInput").ap()
    bin_h = nc.dram_tensor("b_in", [1, D], F32, kind="ExternalInput").ap()
    bg_h = nc.dram_tensor("b_gate", [1, D], F32, kind="ExternalInput").ap()
    lam_h = nc.dram_tensor("lam", [1, D], F32, kind="ExternalInput").ap()
    wup_h = nc.dram_tensor("w_up", [I, D], F32, kind="ExternalInput").ap()
    wgf_h = nc.dram_tensor("w_gf", [I, D], F32, kind="ExternalInput").ap()
    wd_h = nc.dram_tensor("w_down", [D, I], F32, kind="ExternalInput").ap()
    id16_h = nc.dram_tensor("id16_h", [128, 128], F16, kind="ExternalInput").ap()
    id32_h = nc.dram_tensor("id32_h", [128, 128], F32, kind="ExternalInput").ap()
    out_h = nc.dram_tensor("out", [T, D], F32, kind="ExternalOutput").ap()

    kd = "ExternalOutput" if dbg else "Internal"
    x2_d = nc.dram_tensor("x2_d", [T, D], F32, kind=kd).ap()
    xn2_d = nc.dram_tensor("xn2_d", [T, D], F16, kind=kd).ap()
    xqt_d = nc.dram_tensor("xqt_d", [D, T], F16, kind=kd).ap()
    gate_d = nc.dram_tensor("gate_d", [I, T], F16, kind=kd).ap()
    up_d = nc.dram_tensor("up_d", [I, T], F16, kind=kd).ap()
    g2_d = nc.dram_tensor("g2_d", [I, T], F16, kind=kd).ap()
    p_d = nc.dram_tensor("p_d", [I, T], F16, kind=kd).ap()
    ccin_d = [nc.dram_tensor(f"ccin{k}", [1, 2], F32).ap() for k in range(4)]
    ccout_d = [nc.dram_tensor(f"ccout{k}", [1, 2], F32).ap() for k in range(4)]

    p = P(nc)
    es = ExitStack()
    for sname in list(p.cnt) + list(p.dcnt):
        p.sems[sname] = es.enter_context(nc.semaphore(f"s_{sname}"))

    A = nc.alloc_sbuf_tensor
    id16 = A("id16", [128, 128], F16)
    id32 = A("id32", [128, 128], F32)
    ones128 = A("ones128", [1, 128], F32)
    k2 = A("k2", [128, DC], F32)
    bin2 = A("bin2", [128, DC], F32)
    bg2 = A("bg2", [128, DC], F32)
    scal = A("scal", [128, 16], F32)
    srow = A("srow", [1, 16], F32)
    mcol = A("mcol", [128, 8], F32)
    mrow = A("mrow", [1, 128], F32)
    mcell = A("mcell", [1, 8], F32)
    lamt = A("lamt", [128, DC], F32)
    btmp = A("btmp", [128, DC], F32)
    bnbuf = A("bnbuf", [128, 12], F32)
    stats = A("stats", [128, 2], F32)
    colbuf = A("colbuf", [128, 16], F32)
    redcol = A("redcol", [128, 1], F32)
    redring = A("redring", [128, 40], F32)
    cst = A("cst", [128, 8], F32)

    SC = {"s0": 0, "i0": 1, "s2": 2, "i2": 3, "hs2": 4, "nhs2m": 5,
          "s1": 6, "i1": 7, "s3": 8, "i3": 9, "s4": 10, "i4": 11,
          "s2p16": 12, "eps4": 13, "s0b": 14}

    def scol(name):
        return scal[:, SC[name]:SC[name] + 1]

    d_id16 = p.dma("sp", "spq", lambda e: e.dma_start(out=id16[:, :], in_=id16_h))
    d_id32 = p.dma("sp", "spq", lambda e: e.dma_start(out=id32[:, :], in_=id32_h))
    d_lam = p.dma("sp", "spq", lambda e: e.dma_start(
        out=lamt[:, :], in_=lam_h.rearrange("o (c p) -> (o p) c", p=128),
        allow_slow_non_contiguous=True))
    d_bin = p.dma("sp", "spq", lambda e: e.dma_start(
        out=btmp[:, :], in_=bin_h.rearrange("o (c p) -> (o p) c", p=128),
        allow_slow_non_contiguous=True))
    t_ones = p.emit("dve", lambda e: e.memset(ones128[:, :], 1.0))
    t_c0 = p.emit("dve", lambda e: e.memset(cst[:, 0:1], 1e-5))
    t_c1 = p.emit("dve", lambda e: e.memset(cst[:, 1:2], 1.0))
    t_c2 = p.emit("dve", lambda e: e.memset(cst[:, 2:3], 0.25 * (1 + 1e-6)))
    t_c3 = p.emit("dve", lambda e: e.memset(cst[:, 3:4], MAGIC))
    t_c4 = p.emit("dve", lambda e: e.memset(cst[:, 4:5], 32768.0 + MAGIC))
    t_srz = p.emit("dve", lambda e: e.memset(srow[:, :], 0.0))
    t_mz = p.emit("dve", lambda e: e.memset(mcol[:, :], 0.0))
    t_e = p.emit("act", lambda e: e.activation(lamt[:, :], lamt[:, :], AF.Exp),
                 deps=[d_lam])
    t_l = p.emit("act", lambda e: e.activation(lamt[:, :], lamt[:, :], AF.Ln,
                                               bias=cst[:, 1:2]), deps=[t_e, t_c1])
    t_k2 = p.emit("dve", lambda e: e.tensor_scalar(
        k2[:, :], lamt[:, :], 4.0, None, ALU.mult), deps=[t_l])
    t_b1 = p.emit("dve", lambda e: e.tensor_scalar(
        bin2[:, :], btmp[:, :], 0.5, None, ALU.mult), deps=[d_bin])
    d_bg = p.dma("sp", "spq", lambda e: e.dma_start(
        out=btmp[:, :], in_=bg_h.rearrange("o (c p) -> (o p) c", p=128),
        allow_slow_non_contiguous=True), deps=[t_b1])
    t_b2 = p.emit("dve", lambda e: e.tensor_scalar(
        bg2[:, :], btmp[:, :], 0.5, None, ALU.mult), deps=[d_bg])
    pro_toks = [t_k2, t_b2, t_ones, t_mz, d_id16, d_id32, t_c0, t_c2, t_c3, t_c4]

    ln_prev = {"t5": None, "stats_r": None}
    reds = {"m0": [], "m2": [], "m1": [], "m3": [], "m4": []}
    RB = {"m0": 0, "m2": 8, "m1": 16, "m3": 24, "m4": 32}
    MC = {"m0": 0, "m2": 1, "m1": 2, "m3": 3, "m4": 4}
    last_fold = {}

    def _fold8(stream, nret):
        b = RB[stream]
        c = MC[stream]
        tr = p.emit("dve", lambda e, b=b, nret=nret: e.tensor_reduce(
            redcol[:, :], redring[:, b:b + nret], AX, ALU.max),
            deps=[reds[stream][-1], last_fold.get(stream, t_mz)])
        tm = p.emit("dve", lambda e, c=c: e.tensor_tensor(
            mcol[:, c:c + 1], mcol[:, c:c + 1], redcol[:, :], ALU.max),
            deps=[tr, t_mz])
        last_fold[stream] = tm
        return tm

    def emit_red(stream, src_ap, deps):
        k = RB[stream] + (len(reds[stream]) % 8)
        t = p.emit("dve", lambda e, k=k, src_ap=src_ap: e.tensor_reduce(
            redring[:, k:k + 1], src_ap, AX, ALU.max,
            apply_absolute_value=True),
            deps=list(deps) + [last_fold.get(stream)
                               if len(reds[stream]) % 8 == 0 else None])
        reds[stream].append(t)
        if len(reds[stream]) % 8 == 0:
            _fold8(stream, 8)
        return t

    def fold_ring(stream, col_idx):
        rem = len(reds[stream]) % 8
        if rem:
            _fold8(stream, rem)
        return last_fold[stream]
    colw = {}

    def ln_stats(src_ap_fn, negm_col, r_col, deps, extra_negmr=None,
                 eps_ap=None):
        war = [colw.get(r_col), ln_prev["t5"]]
        t1 = p.emit("dve", lambda e: e.bn_stats(bnbuf[:, 0:6], src_ap_fn(0)),
                    deps=list(deps) + [ln_prev["stats_r"]])
        t2 = p.emit("dve", lambda e: e.bn_stats(bnbuf[:, 6:12], src_ap_fn(512)),
                    deps=list(deps))
        t3 = p.emit("dve", lambda e: e.bn_aggr(stats[:, 0:2], bnbuf[:, 0:12]),
                    deps=[t1, t2])
        t4 = p.emit("dve", lambda e: e.tensor_scalar(
            colbuf[:, negm_col:negm_col + 1], stats[:, 0:1], -1.0, None,
            ALU.mult), deps=[t3])
        if eps_ap is None:
            eps_ap = cst[:, 0:1]
        t5 = p.emit("act", lambda e, eps_ap=eps_ap: e.activation(
            colbuf[:, r_col:r_col + 1], stats[:, 1:2], AF.Sqrt,
            bias=eps_ap), deps=[t3] + war)
        t6 = p.emit("dve", lambda e: e.reciprocal(
            colbuf[:, r_col:r_col + 1], colbuf[:, r_col:r_col + 1]), deps=[t5])
        ln_prev["t5"] = t5
        ln_prev["stats_r"] = t5
        toks = [t4, t6]
        if extra_negmr is not None:
            toks.append(p.emit("dve", lambda e: e.tensor_tensor(
                colbuf[:, extra_negmr:extra_negmr + 1],
                colbuf[:, negm_col:negm_col + 1],
                colbuf[:, r_col:r_col + 1], ALU.mult), deps=[t4, t6]))
        return toks

    def use_cols(tok, cols):
        for c in cols:
            colw[c] = tok

    def max_finalize(col_idx, cell_idx, ps_bc, deps):
        ttr = p.emit("pe", lambda e: e.transpose(
            ps_bc[0:1, 0:128], mcol[:, col_idx:col_idx + 1], id32[:, :]),
            deps=deps)
        tv = p.emit("dve", lambda e: e.tensor_copy(
            mrow[:, :], ps_bc[0:1, 0:128]), deps=[ttr])
        tr = p.emit("dve", lambda e: e.tensor_reduce(
            mcell[:, cell_idx:cell_idx + 1], mrow[:, :], AX, ALU.max),
            deps=[tv])
        return tr

    def barrier(kidx, cell0, n, scols, ps_bc, deps):
        toks = list(deps)
        if use_cc:
            d1 = p.dma("pool", "gpq", lambda e: e.dma_start(
                out=ccin_d[kidx][:, 0:n], in_=mcell[:, cell0:cell0 + n]),
                deps=toks)
            tcc = p.dma("pool", "ccq", lambda e: e.collective_compute(
                "AllReduce", ALU.max, replica_groups=[list(range(n_cores))],
                ins=[ccin_d[kidx][:, 0:n].opt()],
                outs=[ccout_d[kidx][:, 0:n].opt()]), deps=[d1])
            d2 = p.dma("pool", "gpq", lambda e: e.dma_start(
                out=mcell[:, cell0:cell0 + n], in_=ccout_d[kidx][:, 0:n]),
                deps=[tcc])
            toks = [d2]
        stoks = []
        for j in range(n):
            ts_ = p.emit("dve", lambda e, j=j: e.tensor_scalar(
                srow[:, 2 * j:2 * j + 1], mcell[:, cell0 + j:cell0 + j + 1],
                1.0 / QMAX, 1e-12, ALU.mult, ALU.max), deps=toks)
            ti_ = p.emit("dve", lambda e, j=j: e.reciprocal(
                srow[:, 2 * j + 1:2 * j + 2], srow[:, 2 * j:2 * j + 1]),
                deps=[ts_])
            stoks.append(ti_)
        tm = p.emit("pe", lambda e: e.matmul(
            ps_bc[0:128, 0:16], ones128[:, :], srow[:, :],
            start=True, stop=True), deps=stoks + [t_srz])
        evs = []
        for j, (sc_, ic_) in enumerate(scols):
            evs.append(p.emit("dve", lambda e, sc_=sc_, j=j: e.tensor_copy(
                scal[:, SC[sc_]:SC[sc_] + 1], ps_bc[0:128, 2 * j:2 * j + 1]),
                deps=[tm]))
            evs.append(p.emit("dve", lambda e, ic_=ic_, j=j: e.tensor_copy(
                scal[:, SC[ic_]:SC[ic_] + 1],
                ps_bc[0:128, 2 * j + 1:2 * j + 2]), deps=[tm]))
        return evs

    def prep_lpbq_chunk(w_hbm, dst, st_, ps_tr, ic, deps, ncols, prev_ev,
                        post_scale=None):
        ncc = ncols // 128
        nb = ncols // 32
        dld = p.dma("sp", "spq", lambda e: e.dma_start(
            out=st_["w32"][:, :ncols], in_=w_hbm[ic * 128:(ic + 1) * 128, :]),
            deps=deps)
        w3 = st_["w32"].ap()[:, :ncols].rearrange("p (b c) -> p b c", c=32)
        t1 = p.emit("dve", lambda e: e.tensor_reduce(
            st_["bmax"][:, :nb], w3, AX, ALU.max, apply_absolute_value=True),
            deps=[dld])
        t2 = p.emit("dve", lambda e: e.tensor_scalar(
            st_["sw"][:, :nb], st_["bmax"][:, :nb], 1.0 / 7.0, 1e-12,
            ALU.mult, ALU.max), deps=[t1])
        t3 = p.emit("dve", lambda e: e.reciprocal(
            st_["isw"][:, :nb], st_["sw"][:, :nb]), deps=[t2])
        if post_scale is not None:
            t2b = p.emit("dve", lambda e: e.tensor_scalar(
                st_["sw"][:, :nb], st_["sw"][:, :nb], post_scale, None,
                ALU.mult), deps=[t3])
            t3 = t2b
        isw3 = st_["isw"].ap()[:, :nb].rearrange("p (b o) -> p b o", o=1)
        sw3 = st_["sw"].ap()[:, :nb].rearrange("p (b o) -> p b o", o=1)
        z3 = st_["z"].ap()[:, :ncols].rearrange("p (b c) -> p b c", c=32)
        wq3 = st_["w16"].ap()[:, :ncols].rearrange("p (b c) -> p b c", c=32)
        a_, b_ = bass.broadcast_tensor_aps(w3, isw3)
        t4 = p.emit("dve", lambda e, a_=a_, b_=b_: e.tensor_tensor(
            z3, a_, b_, ALU.mult), deps=[t3])
        t5 = p.emit("dve", lambda e: e.tensor_scalar(
            st_["z"][:, :ncols], st_["z"][:, :ncols], MAGIC, -MAGIC,
            ALU.add, ALU.add), deps=[t4])
        a2_, b2_ = bass.broadcast_tensor_aps(z3, sw3)
        t6 = p.emit("dve", lambda e, a2_=a2_, b2_=b2_: e.tensor_tensor(
            wq3, a2_, b2_, ALU.mult), deps=[t5])
        ev_toks = list(prev_ev)
        tr_toks = []
        for dc in range(ncc):
            half = (dc // 4) % 2
            tr_toks.append(p.emit("pe", lambda e, dc=dc, half=half:
                e.transpose(
                    ps_tr[half][:, (dc % 4) * 128:(dc % 4 + 1) * 128],
                    st_["w16"][:, dc * 128:(dc + 1) * 128], id16[:, :]),
                deps=[t6] + (ev_toks[-2:-1] if len(ev_toks) >= 2
                             else ev_toks)))
            if dc % 4 == 3:
                ev_toks.append(p.emit("act", lambda e, dc=dc, half=half,
                                      ic=ic, ncc=ncc: e.copy(
                    dst[:, (ic * ncc + dc - 3) * 128:
                        (ic * ncc + dc + 1) * 128],
                    ps_tr[half][:, :]), deps=tr_toks[-4:]))
        return ev_toks[-1], t6, ev_toks[len(prev_ev):]

    ctx_w = ExitStack()
    winT = ctx_w.enter_context(nc.sbuf_tensor("winT", [128, DC * DC * 128], F16))
    wgT = ctx_w.enter_context(nc.sbuf_tensor("wgT", [128, DC * DC * 128], F16))

    ctx1 = ExitStack()
    wst = {k: ctx1.enter_context(nc.sbuf_tensor(f"p_{k}", [128, n], dt))
           for k, n, dt in [("w32", D, F32), ("w16", D, F16), ("z", D, F32),
                            ("bmax", D // 32, F32), ("sw", D // 32, F32),
                            ("isw", D // 32, F32)]}
    xs = [ctx1.enter_context(nc.sbuf_tensor(f"xs{i}", [128, D], F32))
          for i in range(2 * NTT)]
    xn_st = ctx1.enter_context(nc.sbuf_tensor("xn_st", [128, NTT * D], F16))
    xnT = ctx1.enter_context(nc.sbuf_tensor("xnT", [128, DC * S], F16))
    th_rg = ctx1.enter_context(nc.sbuf_tensor("th_rg", [128, DC * S], F16))
    th_ig = ctx1.enter_context(nc.sbuf_tensor("th_ig", [128, DC * S], F16))
    g_t = ctx1.enter_context(nc.sbuf_tensor("g_t", [128, DC * S], F32))
    al_t = ctx1.enter_context(nc.sbuf_tensor("al_t", [128, DC * S], F32))
    u1_t = ctx1.enter_context(nc.sbuf_tensor("u1_t", [128, DC * S], F16))
    xb_t = ctx1.enter_context(nc.sbuf_tensor("xb_t", [128, DC * S], F16))
    h_t = [ctx1.enter_context(nc.sbuf_tensor(f"h{i}", [128, DC * S], F32))
           for i in range(2)]
    x2_tile = [ctx1.enter_context(nc.sbuf_tensor(f"x2_tile{i}", [128, D], F32))
               for i in range(2)]
    xm_tile = ctx1.enter_context(nc.sbuf_tensor("xm_tile", [128, D], F32))
    xn2_tile = [ctx1.enter_context(nc.sbuf_tensor(f"xn2_t{i}", [128, D], F16))
                for i in range(2)]

    ps1 = ExitStack()
    ps_tr = [ps1.enter_context(nc.psum_tensor(f"ps_tr{i}", [128, 512], F16))
             for i in range(2)]
    ps_g = [ps1.enter_context(nc.psum_tensor(f"ps_g{i}", [128, 512], F32))
            for i in range(2)]
    ps_h = [ps1.enter_context(nc.psum_tensor(f"ps_h{i}", [128, 1024], F32))
            for i in range(2)]
    ps_bc = ps_g[0]

    def prep_square(w_hbm, dst, deps):
        toks = []
        last = None
        for dpc in range(DC):
            dld = p.dma("sp", "spq", lambda e, dpc=dpc: e.dma_start(
                out=wst["w32"][:, :], in_=w_hbm[dpc * 128:(dpc + 1) * 128, :]),
                deps=deps + ([last] if last else []))
            tc = p.emit("dve", lambda e: e.tensor_copy(
                wst["w16"][:, :], wst["w32"][:, :]), deps=[dld])
            last = tc
            tr_toks = []
            for dc in range(DC):
                half = (dc // 4) % 2
                tr_toks.append(p.emit("pe", lambda e, dc=dc, half=half:
                    e.transpose(
                        ps_tr[half][:, (dc % 4) * 128:(dc % 4 + 1) * 128],
                        wst["w16"][:, dc * 128:(dc + 1) * 128], id16[:, :]),
                    deps=[tc] + (toks[-2:-1] if len(toks) >= 2 else [])))
                if dc % 4 == 3:
                    toks.append(p.emit("act", lambda e, dc=dc, half=half,
                                       dpc=dpc: e.copy(
                        dst[:, (dpc * DC + dc - 3) * 128:
                            (dpc * DC + dc + 1) * 128],
                        ps_tr[half][:, :]), deps=tr_toks[-4:]))
            last = toks[-1]
        return toks[-1]

    tw1 = prep_square(win_h, winT, pro_toks)
    tw2 = prep_square(wg_h, wgT, [tw1])

    CB = {"negm1": 0, "r1": 4, "negm2": 8, "r2": 9, "negm2r2": 10,
          "negm3": 11, "r3": 12}

    x_dma = {}
    x2_dma = {}
    xn2_dma = {}
    t_scan = {}
    t_x_read = {}
    war = {
        "x2t_dma": [None, None],
        "xn_st": None,
        "xnT": None,
        "xnT_dve": None,
        "th_rg": None, "th_ig": None,
        "g": {}, "al": {},
        "xb": None, "xm": None,
        "h_tr": [None, None],
        "ps_h_rd": [None, None],
        "x2t": [None, None], "xn2t": [None, None],
        "xn2t_pool": [None, None],
        "proj_tail": None,
    }

    for st0 in range(min(2, NST)):
        for tt in range(NTT):
            x_dma[(st0, tt)] = p.dma(
                "sp", "spq", lambda e, st0=st0, tt=tt: e.dma_start(
                    out=xs[(st0 % 2) * NTT + tt][:, :],
                    in_=x_h[(st0 * S + tt * 128):(st0 * S + tt * 128 + 128), :]))

    pushP1, flushP1 = mk_defer(2)

    def emit_tail(st2, scan_toks, hc2):
        t_ht_last = None
        for tt in range(NTT):
            pp = tt % 2
            flushP1(st2 * NTT + tt)
            trh = []
            for dc in range(DC):
                trh.append(p.emit("pe", lambda e, dc=dc, tt=tt, hc2=hc2,
                                  pp=pp: e.transpose(
                        ps_h[pp][:, dc * 128:(dc + 1) * 128],
                        hc2[:, dc * S + tt * 128:dc * S + tt * 128 + 128],
                        id32[:, :]),
                    deps=[scan_toks[dc], war["ps_h_rd"][pp]]))
            lt2 = ln_stats(lambda lo, pp=pp: ps_h[pp][:, lo:lo + 512],
                           CB["negm2"], CB["r2"], trh,
                           extra_negmr=CB["negm2r2"])
            xb_ = xs[(st2 % 2) * NTT + tt]
            txm = p.emit("pool", lambda e, xb_=xb_: e.tensor_scalar(
                xm_tile[:, :], xb_[:, :], 1.0,
                colbuf[:, CB["negm2r2"]:CB["negm2r2"] + 1],
                ALU.mult, ALU.add),
                deps=lt2 + [war["xm"]])
            use_cols(txm, [CB["negm2r2"]])
            t_x_read[(st2, tt)] = txm
            tx2 = p.emit("dve", lambda e, pp=pp: e.scalar_tensor_tensor(
                x2_tile[pp][:, :], ps_h[pp][:, :],
                colbuf[:, CB["r2"]:CB["r2"] + 1], xm_tile[:, :],
                ALU.mult, ALU.add),
                deps=[txm] + lt2 + [war["x2t"][pp], war["x2t_dma"][pp]])
            use_cols(tx2, [CB["negm2"], CB["r2"], CB["negm2r2"]])
            war["xm"] = tx2
            war["ps_h_rd"][pp] = tx2

            def _sx2(st2=st2, tt=tt, pp=pp, tx2=tx2):
                x2_dma[(st2, tt)] = p.dma("act", "acq",
                    lambda e, st2=st2, tt=tt, pp=pp: e.dma_start(
                        out=x2_d[(st2 * S + tt * 128):
                                 (st2 * S + tt * 128 + 128), :],
                        in_=x2_tile[pp][:, :]), deps=[tx2])
                war["x2t_dma"][pp] = x2_dma[(st2, tt)]
            pushP1(st2 * NTT + tt, _sx2)
            lt3 = ln_stats(lambda lo, pp=pp: x2_tile[pp][:, lo:lo + 512],
                           CB["negm3"], CB["r3"], [tx2])
            txn2 = p.emit("pool", lambda e, pp=pp: e.tensor_scalar(
                xn2_tile[pp][:, :], x2_tile[pp][:, :],
                colbuf[:, CB["negm3"]:CB["negm3"] + 1],
                colbuf[:, CB["r3"]:CB["r3"] + 1], ALU.add, ALU.mult),
                deps=lt3 + [war["xn2t"][pp], war["xn2t_pool"][pp]])
            use_cols(txn2, [CB["negm3"], CB["r3"]])
            war["x2t"][pp] = txn2
            tred = emit_red("m0", xn2_tile[pp][:, :], [txn2])

            def _sxn2(st2=st2, tt=tt, pp=pp, txn2=txn2, tred=tred):
                xn2_dma[(st2, tt)] = p.dma("act", "acq",
                    lambda e, st2=st2, tt=tt, pp=pp: e.dma_start(
                        out=xn2_d[(st2 * S + tt * 128):
                                  (st2 * S + tt * 128 + 128), :],
                        in_=xn2_tile[pp][:, :]), deps=[txn2, tred])
                war["xn2t"][pp] = xn2_dma[(st2, tt)]
                war["xn2t_pool"][pp] = tred
            pushP1(st2 * NTT + tt, _sxn2)
            t_ht_last = trh[-1]
        war["h_tr"][st2 % 2] = t_ht_last

    pend_tail = None
    for st in range(NST):
        t_xn = {}
        for tt in range(NTT):
            xb_ = xs[(st % 2) * NTT + tt]
            lt = ln_stats(lambda lo, xb_=xb_: xb_[:, lo:lo + 512],
                          CB["negm1"] + tt, CB["r1"] + tt,
                          [x_dma[(st, tt)]])
            t_xn[tt] = p.emit("dve", lambda e, tt=tt, xb_=xb_: e.tensor_scalar(
                xn_st[:, tt * D:(tt + 1) * D], xb_[:, :],
                colbuf[:, CB["negm1"] + tt:CB["negm1"] + tt + 1],
                colbuf[:, CB["r1"] + tt:CB["r1"] + tt + 1],
                ALU.add, ALU.mult), deps=lt + [war["xn_st"]])
            use_cols(t_xn[tt], [CB["negm1"] + tt, CB["r1"] + tt])

        t_xnT = {}
        ev_list = []
        tr_last = None
        for dc in range(DC):
            half = dc % 2
            trk = []
            for tt in range(NTT):
                trk.append(p.emit("pe", lambda e, dc=dc, tt=tt, half=half:
                    e.transpose(
                        ps_tr[half][:, tt * 128:(tt + 1) * 128],
                        xn_st[:, tt * D + dc * 128:tt * D + dc * 128 + 128],
                        id16[:, :]),
                    deps=[t_xn[tt], tw2] +
                         (ev_list[-2:-1] if len(ev_list) >= 2 else [])))
            tr_last = trk[-1]
            ev_list.append(p.emit("act", lambda e, dc=dc, half=half: e.copy(
                xnT[:, dc * S:(dc + 1) * S], ps_tr[half][:, :]),
                deps=trk + [war["xnT"], war["xnT_dve"]]))
            t_xnT[dc] = ev_list[-1]
        war["xn_st"] = tr_last

        def proj(wT, th_dst, bias_t, prev_tail, th_war):
            evs = {}
            mm_last = None
            for c in range(DC):
                mms = []
                for dc in range(DC):
                    deps = [t_xnT[dc]]
                    if c >= 2:
                        deps.append(evs[c - 2])
                    elif prev_tail is not None:
                        deps.append(prev_tail.get(c - 2 + DC))
                    mms.append(p.emit("pe",
                        lambda e, c=c, dc=dc, wT=wT: e.matmul(
                            ps_g[c % 2][:, :],
                            wT[:, (c * DC + dc) * 128:(c * DC + dc + 1) * 128],
                            xnT[:, dc * S:(dc + 1) * S],
                            start=(dc == 0), stop=(dc == DC - 1)),
                        deps=deps))
                evs[c] = p.emit("act", lambda e, c=c, bias_t=bias_t,
                                th_dst=th_dst: e.activation(
                    th_dst[:, c * S:(c + 1) * S], ps_g[c % 2][:, :], AF.Tanh,
                    bias=bias_t[:, c:c + 1], scale=0.5),
                    deps=[mms[-1], th_war])
                mm_last = mms[-1]
            return evs, mm_last

        ev_rg, mm_l1 = proj(wgT, th_rg, bg2, war["proj_tail"], war["th_rg"])

        t_bh = {}
        t_al = {}
        t_g = {}
        ta2 = {}
        for c in range(DC):
            t_g[c] = p.emit("pool", lambda e, c=c: e.tensor_scalar(
                g_t[:, c * S:(c + 1) * S], th_rg[:, c * S:(c + 1) * S],
                1.0, k2[:, c:c + 1], ALU.add, ALU.mult),
                deps=[ev_rg[c], war["g"].get(c)])
        for c in range(DC):
            t_al[c] = p.emit("act", lambda e, c=c: e.activation(
                al_t[:, c * S:(c + 1) * S], g_t[:, c * S:(c + 1) * S],
                AF.Exp, scale=-1.0), deps=[t_g[c], war["al"].get(c)])
        for c in range(DC):
            ta2[c] = p.emit("act", lambda e, c=c: e.activation(
                g_t[:, c * S:(c + 1) * S], g_t[:, c * S:(c + 1) * S],
                AF.Exp, scale=-2.0), deps=[t_al[c]])
        for c in range(DC):
            t_bh[c] = p.emit("act", lambda e, c=c: e.activation(
                g_t[:, c * S:(c + 1) * S], g_t[:, c * S:(c + 1) * S],
                AF.Sqrt, scale=-0.25, bias=cst[:, 2:3]),
                deps=[ta2[c], war["xb"]])
        war["th_rg"] = t_g[DC - 1]

        if pend_tail is not None:
            emit_tail(*pend_tail)
        pend_tail = None

        ev_ig, mm_l2 = proj(winT, th_ig, bin2, ev_rg, war["th_ig"])
        war["proj_tail"] = ev_ig
        war["xnT"] = mm_l2

        hc = h_t[st % 2]
        hp = h_t[(st - 1) % 2]
        new_scan = {}
        t_u_last = None
        for c in range(DC):
            tu = p.emit("dve", lambda e, c=c: e.scalar_tensor_tensor(
                u1_t[:, c * S:(c + 1) * S], th_ig[:, c * S:(c + 1) * S],
                1.0, xnT[:, c * S:(c + 1) * S], ALU.add, ALU.mult),
                deps=[ev_ig[c], t_xnT[c]])
            tx = p.emit("dve", lambda e, c=c: e.tensor_tensor(
                xb_t[:, c * S:(c + 1) * S], u1_t[:, c * S:(c + 1) * S],
                g_t[:, c * S:(c + 1) * S], ALU.mult), deps=[tu, t_bh[c]])
            if st == 0:
                init = 0.0
                dep_init = []
            else:
                init = hp[:, c * S + S - 1:c * S + S]
                dep_init = [t_scan[c]]
            new_scan[c] = p.emit("dve", lambda e, c=c, init=init, hc=hc:
                e.tensor_tensor_scan(
                    hc[:, c * S:(c + 1) * S], al_t[:, c * S:(c + 1) * S],
                    xb_t[:, c * S:(c + 1) * S], init, ALU.mult, ALU.add),
                deps=[tx, t_al[c], war["h_tr"][st % 2]] + dep_init)
            war["g"][c] = new_scan[c]
            war["al"][c] = new_scan[c]
            t_u_last = tu
        t_scan = new_scan
        war["th_ig"] = t_u_last
        war["xnT_dve"] = t_u_last
        war["xb"] = new_scan[DC - 1]

        pend_tail = (st, new_scan, hc)
        if st >= 1 and st + 1 < NST:
            for tt in range(NTT):
                x_dma[(st + 1, tt)] = p.dma(
                    "sp", "spq", lambda e, st=st, tt=tt: e.dma_start(
                        out=xs[((st + 1) % 2) * NTT + tt][:, :],
                        in_=x_h[((st + 1) * S + tt * 128):
                                ((st + 1) * S + tt * 128 + 128), :]),
                    deps=[t_x_read.get((st - 1, tt))])

    if pend_tail is not None:
        emit_tail(*pend_tail)
    flushP1(None)
    t_fold0 = fold_ring("m0", 0)
    t_fm0 = max_finalize(0, 0, ps_bc, [t_fold0])
    b0 = barrier(0, 0, 1, [("s0", "i0")], ps_bc, [t_fm0])
    t_s0b = p.emit("dve", lambda e: e.tensor_scalar(
        scol("s0b"), scol("s0"), 65536.0, None, ALU.mult), deps=b0)
    b0 = b0 + [t_s0b]

    ps1.close()
    ctx1.close()
    ctx_w.close()
    p.fence()

    ctx_wu1 = ExitStack()
    wupT = ctx_wu1.enter_context(nc.sbuf_tensor("wupT", [128, IC * DC * 128], F16))
    ctx_wu2 = ExitStack()
    wgfT = ctx_wu2.enter_context(nc.sbuf_tensor("wgfT", [128, IC * DC * 128], F16))

    ctx2 = ExitStack()
    wst2 = {k: ctx2.enter_context(nc.sbuf_tensor(f"q_{k}", [128, n], dt))
            for k, n, dt in [("w32", D, F32), ("w16", D, F16), ("z", D, F32),
                             ("bmax", D // 32, F32), ("sw", D // 32, F32),
                             ("isw", D // 32, F32)]}
    RBD = 6
    xn2_sb = [ctx2.enter_context(nc.sbuf_tensor(f"xn2_sb{i}", [128, NTT * D],
                                                F16)) for i in range(2)]
    t0_sb = [ctx2.enter_context(nc.sbuf_tensor(f"t0_sb{i}", [128, D], F32))
             for i in range(3)]
    xq_sb = [ctx2.enter_context(nc.sbuf_tensor(f"xq_sb{i}", [128, NTT * D],
                                               F16)) for i in range(2)]
    xqT = [ctx2.enter_context(nc.sbuf_tensor(f"xqT2_{i}", [128, DC * S], F16))
           for i in range(2)]
    gw = [ctx2.enter_context(nc.sbuf_tensor(f"gw{i}", [128, S], F16))
          for i in range(RBD)]

    ps2 = ExitStack()
    ps_tr2 = [ps2.enter_context(nc.psum_tensor(f"ps2_tr{i}", [128, 512], F16))
              for i in range(2)]
    ps_tr2w = [ps2.enter_context(nc.psum_tensor(f"ps2_tw{i}", [128, 512], F16))
               for i in range(2)]
    ps_g2 = [ps2.enter_context(nc.psum_tensor(f"ps2_g{i}", [128, 512], F32))
             for i in range(3)]
    ps_bc2 = ps_g2[0]

    t_wgf = {}
    t_wup = {}
    prev_ev_track = []
    prev_dep = b0
    for ic_ in range(IC):
        ev, _, new_evs = prep_lpbq_chunk(wgf_h, wgfT, wst2, ps_tr2w, ic_,
                                         prev_dep, D, prev_ev_track,
                                         post_scale=scol("s0b"))
        t_wgf[ic_] = ev
        prev_ev_track = (prev_ev_track + new_evs)[-2:]
        prev_dep = [ev]

    t_gdma = {}
    t_gev = {}
    up_sched = {}
    nup = 0
    for st in range(NST):
        k = 2 if NST >= IC // 2 else (IC + NST - 1) // NST
        for _ in range(k):
            if nup < IC:
                up_sched.setdefault(st, []).append(nup)
                nup += 1
    if nup < IC:
        up_sched.setdefault(NST - 1, []).extend(range(nup, IC))

    war2 = {"t0": [None] * 3}
    lastrd_xn2 = {}
    lastrd_xq = {}
    mml2 = {}
    gdma_f = {}
    t_xq_dma = {}
    push2a, flush2a = mk_defer(2)
    for st in range(NST):
        par = st % 2
        dx = {}
        for tt in range(NTT):
            dx[tt] = p.dma("sp", "spq",
                lambda e, st=st, tt=tt, par=par: e.dma_start(
                    out=xn2_sb[par][:, tt * D:(tt + 1) * D],
                    in_=xn2_d[(st * S + tt * 128):(st * S + tt * 128 + 128), :]),
                deps=[xn2_dma[(st, tt)], lastrd_xn2.get(st - 2)] + b0)
        t_xq = {}
        last_tt0 = None
        for tt in range(NTT):
            kt = (st * NTT + tt) % 3
            tt0 = p.emit("act", lambda e, tt=tt, par=par, kt=kt: e.activation(
                t0_sb[kt][:, :], xn2_sb[par][:, tt * D:(tt + 1) * D], AF.Relu,
                scale=scol("i0"), bias=cst[:, 3:4]),
                deps=[dx[tt], war2["t0"][kt]])
            last_tt0 = tt0
            t_xq[tt] = p.emit("dve", lambda e, tt=tt, par=par, kt=kt:
                e.tensor_scalar(
                    xq_sb[par][:, tt * D:(tt + 1) * D], t0_sb[kt][:, :],
                    -MAGIC, None, ALU.add),
                deps=[tt0, lastrd_xq.get(st - 2)])
            war2["t0"][kt] = t_xq[tt]
        lastrd_xn2[st] = last_tt0
        t_xqT = {}
        ev_list = []
        tr_last = None
        for dc in range(DC):
            half = dc % 2
            trk = []
            for tt in range(NTT):
                trk.append(p.emit("pe",
                    lambda e, dc=dc, tt=tt, half=half, par=par: e.transpose(
                        ps_tr2[half][:, tt * 128:(tt + 1) * 128],
                        xq_sb[par][:, tt * D + dc * 128:tt * D + dc * 128 + 128],
                        id16[:, :]),
                    deps=[t_xq[tt], t_wgf[IC - 1]] +
                         (ev_list[-2:-1] if len(ev_list) >= 2 else [])))
            tr_last = trk[-1]
            ev_list.append(p.emit("act",
                lambda e, dc=dc, half=half, par=par: e.copy(
                    xqT[par][:, dc * S:(dc + 1) * S], ps_tr2[half][:, :]),
                deps=trk + [mml2.get(st - 2), t_xq_dma.get((st - 2, dc))]))
            t_xqT[dc] = ev_list[-1]
            t_xq_dma[(st, dc)] = p.dma("act", "acq",
                lambda e, dc=dc, st=st, par=par: e.dma_start(
                    out=xqt_d[dc * 128:(dc + 1) * 128, st * S:(st + 1) * S],
                    in_=xqT[par][:, dc * S:(dc + 1) * S]), deps=[ev_list[-1]])
        lastrd_xq[st] = tr_last

        mm_last = None
        for ic_ in range(IC):
            icg = st * IC + ic_
            kb = icg % RBD
            flush2a(icg)
            mms = []
            for dc in range(DC):
                deps = [t_xqT[dc], t_wgf[ic_]]
                prev_ev = t_gev.get((st, ic_ - 3), t_gev.get((st - 1, IC - 3 + ic_)))
                if prev_ev is not None:
                    deps.append(prev_ev)
                mms.append(p.emit("pe",
                    lambda e, ic_=ic_, dc=dc, par=par: e.matmul(
                        ps_g2[ic_ % 3][:, :],
                        wgfT[:, (ic_ * DC + dc) * 128:(ic_ * DC + dc + 1) * 128],
                        xqT[par][:, dc * S:(dc + 1) * S],
                        start=(dc == 0), stop=(dc == DC - 1)), deps=deps))
            mm_last = mms[-1]
            tev = p.emit("act", lambda e, ic_=ic_, kb=kb: e.activation(
                gw[kb][:, :], ps_g2[ic_ % 3][:, :], AF.Copy,
                scale=2.0 ** -16),
                deps=[mms[-1], gdma_f.get(icg - RBD)])
            t_gev[(st, ic_)] = tev
            tred = emit_red("m2", gw[kb][:, :], [tev])

            def _sg(ic_=ic_, st=st, kb=kb, icg=icg, tev=tev, tred=tred):
                t_gdma[(st, ic_)] = p.dma("act", "acq",
                    lambda e, ic_=ic_, st=st, kb=kb: e.dma_start(
                        out=gate_d[ic_ * 128:(ic_ + 1) * 128,
                                   st * S:(st + 1) * S],
                        in_=gw[kb][:, :]), deps=[tev, tred])
                gdma_f[icg] = t_gdma[(st, ic_)]
            push2a(icg, _sg)
        mml2[st] = mm_last
        for ic_ in up_sched.get(st, []):
            ev, _, new_evs = prep_lpbq_chunk(
                wup_h, wupT, wst2, ps_tr2w, ic_,
                [t_wgf[IC - 1], t_wup.get(ic_ - 1)] + b0, D, prev_ev_track,
                post_scale=scol("s0b"))
            t_wup[ic_] = ev
            prev_ev_track = (prev_ev_track + new_evs)[-2:]

    flush2a(None)
    t_fold2 = fold_ring("m2", 1)
    t_fm2 = max_finalize(1, 1, ps_bc2, [t_fold2])
    b1 = barrier(1, 1, 1, [("s2", "i2")], ps_bc2, [t_fm2])
    t_hs2 = p.emit("dve", lambda e: e.tensor_scalar(
        scol("hs2"), scol("s2"), 0.5, None, ALU.mult), deps=b1)
    t_nh = p.emit("dve", lambda e: e.tensor_scalar(
        scol("nhs2m"), scol("hs2"), -MAGIC, None, ALU.mult), deps=[t_hs2])
    t_p16 = p.emit("dve", lambda e: e.tensor_scalar(
        scol("s2p16"), scol("s2"), 2.0 ** -16, None, ALU.mult), deps=b1)
    b1_all = b1 + [t_hs2, t_nh, t_p16]

    ps2.close()
    ctx2.close()
    ctx_wu2.close()
    p.fence()

    ctx3 = ExitStack()
    RBD = 6
    RCD = 3
    xqT3 = [ctx3.enter_context(nc.sbuf_tensor(f"xqT3_{i}", [128, DC * S], F16))
            for i in range(2)]
    uw = [ctx3.enter_context(nc.sbuf_tensor(f"uw{i}", [128, S], F16))
          for i in range(RBD)]
    gld = [ctx3.enter_context(nc.sbuf_tensor(f"gld{i}", [128, S], F16))
           for i in range(RBD)]
    t2w = [ctx3.enter_context(nc.sbuf_tensor(f"t2w{i}", [128, S], F32))
           for i in range(RCD)]
    th3w = [ctx3.enter_context(nc.sbuf_tensor(f"th3w{i}", [128, S], F32))
            for i in range(RCD)]
    w1w = [ctx3.enter_context(nc.sbuf_tensor(f"w1w{i}", [128, S], F32))
           for i in range(RCD)]
    sqw = [ctx3.enter_context(nc.sbuf_tensor(f"sqw{i}", [128, S], F32))
           for i in range(RCD)]
    gqw = [ctx3.enter_context(nc.sbuf_tensor(f"gqw{i}", [128, S], F16))
           for i in range(RCD)]
    g2w = [ctx3.enter_context(nc.sbuf_tensor(f"g2w{i}", [128, S], F16))
           for i in range(RBD)]

    ps3 = ExitStack()
    ps_u = [ps3.enter_context(nc.psum_tensor(f"ps_u{i}", [128, 512], F32))
            for i in range(3)]
    ps_bc3 = ps_u[0]

    t_udma = {}
    t_uev = {}
    t_g2dma = {}
    udma_f = {}
    g2dma_f = {}
    tt2_f = {}
    mml3 = {}
    war3 = {"t2": [None] * RCD, "t2r": [None] * RCD, "th3": [None] * RCD,
            "w1": [None] * RCD, "sq": [None] * RCD, "gq": [None] * RCD}
    push2b, flush2b = mk_defer(2)
    for st in range(NST):
        par = st % 2
        dxq = {}
        for dc in range(DC):
            dxq[dc] = p.dma("sp", "spq",
                lambda e, dc=dc, st=st, par=par: e.dma_start(
                    out=xqT3[par][:, dc * S:(dc + 1) * S],
                    in_=xqt_d[dc * 128:(dc + 1) * 128, st * S:(st + 1) * S]),
                deps=[t_xq_dma[(st, dc)], mml3.get(st - 2)] + b1_all)
        mm_last = None
        for ic_ in range(IC):
            icg = st * IC + ic_
            kb = icg % RBD
            kc = icg % RCD
            flush2b(icg)
            mms = []
            for dc in range(DC):
                deps = [dxq[dc], t_wup[ic_]]
                prev_ev = t_uev.get((st, ic_ - 3), t_uev.get((st - 1, IC - 3 + ic_)))
                if prev_ev is not None:
                    deps.append(prev_ev)
                mms.append(p.emit("pe",
                    lambda e, ic_=ic_, dc=dc, par=par: e.matmul(
                        ps_u[ic_ % 3][:, :],
                        wupT[:, (ic_ * DC + dc) * 128:(ic_ * DC + dc + 1) * 128],
                        xqT3[par][:, dc * S:(dc + 1) * S],
                        start=(dc == 0), stop=(dc == DC - 1)), deps=deps))
            mm_last = mms[-1]
            tev = p.emit("act", lambda e, ic_=ic_, kb=kb: e.activation(
                uw[kb][:, :], ps_u[ic_ % 3][:, :], AF.Copy,
                scale=2.0 ** -16),
                deps=[mms[-1], udma_f.get(icg - RBD)])
            t_uev[(st, ic_)] = tev
            tred1 = emit_red("m1", uw[kb][:, :], [tev])

            def _su(ic_=ic_, st=st, kb=kb, icg=icg, tev=tev, tred1=tred1):
                t_udma[(st, ic_)] = p.dma("act", "acq",
                    lambda e, ic_=ic_, st=st, kb=kb: e.dma_start(
                        out=up_d[ic_ * 128:(ic_ + 1) * 128,
                                 st * S:(st + 1) * S],
                        in_=uw[kb][:, :]), deps=[tev, tred1])
                udma_f[icg] = t_udma[(st, ic_)]
            push2b(icg, _su)

            dg = p.dma("sp", "spq", lambda e, ic_=ic_, st=st, kb=kb: e.dma_start(
                out=gld[kb][:, :],
                in_=gate_d[ic_ * 128:(ic_ + 1) * 128, st * S:(st + 1) * S]),
                deps=[t_gdma[(st, ic_)], tt2_f.get(icg - RBD)])
            tt2 = p.emit("pool", lambda e, kb=kb, kc=kc: e.tensor_scalar(
                t2w[kc][:, :], gld[kb][:, :], scol("i2"), MAGIC,
                ALU.mult, ALU.add),
                deps=[dg, war3["t2"][kc], war3["t2r"][kc]])
            tt2_f[icg] = tt2
            tth3 = p.emit("act", lambda e, kc=kc: e.activation(
                th3w[kc][:, :], t2w[kc][:, :], AF.Tanh, bias=scol("nhs2m"),
                scale=scol("hs2")), deps=[tt2, war3["th3"][kc]])
            war3["t2r"][kc] = tth3
            tw1_ = p.emit("act", lambda e, kc=kc: e.activation(
                w1w[kc][:, :], th3w[kc][:, :], AF.Relu, scale=32768.0,
                bias=cst[:, 4:5]), deps=[tth3, war3["w1"][kc]])
            war3["th3"][kc] = tw1_
            tsq = p.emit("dve", lambda e, kc=kc: e.tensor_scalar(
                sqw[kc][:, :], w1w[kc][:, :], -MAGIC, 65535.0,
                ALU.add, ALU.min), deps=[tw1_, war3["sq"][kc]])
            war3["w1"][kc] = tsq
            tm2 = p.emit("pool", lambda e, kc=kc: e.tensor_scalar(
                gqw[kc][:, :], t2w[kc][:, :], 1.0, -MAGIC,
                ALU.mult, ALU.add),
                deps=[tt2, war3["gq"][kc]])
            war3["t2"][kc] = tm2
            tg2 = p.emit("dve", lambda e, kb=kb, kc=kc: e.scalar_tensor_tensor(
                g2w[kb][:, :], sqw[kc][:, :], scol("s2p16"), gqw[kc][:, :],
                ALU.mult, ALU.mult),
                deps=[tsq, tm2, g2dma_f.get(icg - RBD)])
            war3["sq"][kc] = tg2
            war3["gq"][kc] = tg2
            tred3 = emit_red("m3", g2w[kb][:, :], [tg2])

            def _sg2(ic_=ic_, st=st, kb=kb, icg=icg, tg2=tg2, tred3=tred3):
                dg2 = p.dma("act", "acq",
                    lambda e, ic_=ic_, st=st, kb=kb: e.dma_start(
                        out=g2_d[ic_ * 128:(ic_ + 1) * 128,
                                 st * S:(st + 1) * S],
                        in_=g2w[kb][:, :]), deps=[tg2, tred3])
                t_g2dma[(st, ic_)] = dg2
                g2dma_f[icg] = dg2
            push2b(icg, _sg2)
        mml3[st] = mm_last

    flush2b(None)
    ctx3.close()
    ctx_wu1.close()

    ctx_wd = ExitStack()
    wdT = ctx_wd.enter_context(nc.sbuf_tensor("wdT", [128, DC * IC * 128], F16))
    wst4 = {k: ctx_wd.enter_context(nc.sbuf_tensor(f"r_{k}", [128, n], dt))
            for k, n, dt in [("w32", I, F32), ("w16", I, F16), ("z", I, F32),
                             ("bmax", I // 32, F32), ("sw", I // 32, F32),
                             ("isw", I // 32, F32)]}
    ps_wd = ExitStack()
    ps_tr4 = [ps_wd.enter_context(nc.psum_tensor(f"ps4_tr{i}", [128, 512], F16))
              for i in range(2)]
    t_wd = {}
    nlast = NST * IC
    prev_dep = ([mml3[NST - 1]] +
                [udma_f[i] for i in range(nlast - RBD, nlast)] +
                [g2dma_f[i] for i in range(nlast - RBD, nlast)])
    prev_ev_track = []
    for dpc in range(DC):
        ev, _, new_evs = prep_lpbq_chunk(wd_h, wdT, wst4, ps_tr4, dpc,
                                         prev_dep, I, prev_ev_track)
        t_wd[dpc] = ev
        prev_ev_track = (prev_ev_track + new_evs)[-2:]
        prev_dep = [ev]
    ps_wd.close()

    t_fold1 = fold_ring("m1", 2)
    t_fm1 = max_finalize(2, 2, ps_bc3, [t_fold1])
    t_fold3 = fold_ring("m3", 3)
    t_fm3 = max_finalize(3, 3, ps_bc3, [t_fold3, t_fm1])
    b2 = barrier(2, 2, 2, [("s1", "i1"), ("s3", "i3")], ps_bc3, [t_fm3])

    ps3.close()
    p.fence()

    ctx4 = ExitStack()
    R4 = 10
    R3 = 3
    RP = 8
    u4 = [ctx4.enter_context(nc.sbuf_tensor(f"u4_{i}", [128, S], F16))
          for i in range(R4)]
    g4 = [ctx4.enter_context(nc.sbuf_tensor(f"g4_{i}", [128, S], F16))
          for i in range(R4)]
    t4w = [ctx4.enter_context(nc.sbuf_tensor(f"t4w{i}", [128, S], F32))
           for i in range(R3)]
    gate2w = [ctx4.enter_context(nc.sbuf_tensor(f"gate2w{i}", [128, S], F16))
              for i in range(R3)]
    tuw = [ctx4.enter_context(nc.sbuf_tensor(f"tuw{i}", [128, S], F32))
           for i in range(R3)]
    upqw = [ctx4.enter_context(nc.sbuf_tensor(f"upqw{i}", [128, S], F16))
            for i in range(R3)]
    pw = [ctx4.enter_context(nc.sbuf_tensor(f"pw{i}", [128, S], F16))
          for i in range(RP)]

    ps4 = ExitStack()
    ps_bc4 = ps4.enter_context(nc.psum_tensor("ps_bc4", [128, 128], F32))

    t_pdma = {}
    war4 = {"u4": [None] * R4, "g4": [None] * R4, "t4": [None] * R3,
            "gate2": [None] * R3, "tu": [None] * R3, "upq": [None] * R3,
            "pw": [None] * RP}
    push4, flush4 = mk_defer(2)
    for st in range(NST):
        for ic_ in range(IC):
            k4 = (st * IC + ic_) % R4
            k3 = (st * IC + ic_) % R3
            kp = (st * IC + ic_) % RP
            flush4(st * IC + ic_)
            du = p.dma("sp", "spq", lambda e, ic_=ic_, st=st, k4=k4: e.dma_start(
                out=u4[k4][:, :],
                in_=up_d[ic_ * 128:(ic_ + 1) * 128, st * S:(st + 1) * S]),
                deps=[t_udma[(st, ic_)], t_wd[DC - 1], war4["u4"][k4]] + b2)
            dg = p.dma("sp", "spq", lambda e, ic_=ic_, st=st, k4=k4: e.dma_start(
                out=g4[k4][:, :],
                in_=g2_d[ic_ * 128:(ic_ + 1) * 128, st * S:(st + 1) * S]),
                deps=[t_g2dma[(st, ic_)], war4["g4"][k4]] + b2)
            tt4 = p.emit("pool", lambda e, k4=k4, k3=k3: e.tensor_scalar(
                t4w[k3][:, :], g4[k4][:, :], scol("i3"), MAGIC,
                ALU.mult, ALU.add), deps=[dg, war4["t4"][k3]])
            war4["g4"][k4] = tt4
            tg2_ = p.emit("dve", lambda e, k3=k3: e.tensor_scalar(
                gate2w[k3][:, :], t4w[k3][:, :], -MAGIC, scol("s3"),
                ALU.add, ALU.mult), deps=[tt4, war4["gate2"][k3]])
            war4["t4"][k3] = tg2_
            ttu = p.emit("act", lambda e, k4=k4, k3=k3: e.activation(
                tuw[k3][:, :], u4[k4][:, :], AF.Relu,
                scale=scol("i1"), bias=cst[:, 3:4]), deps=[du, war4["tu"][k3]])
            war4["u4"][k4] = ttu
            tuq = p.emit("pool", lambda e, k3=k3: e.tensor_scalar(
                upqw[k3][:, :], tuw[k3][:, :], -MAGIC, scol("s1"),
                ALU.add, ALU.mult), deps=[ttu, war4["upq"][k3]])
            war4["tu"][k3] = tuq
            tp = p.emit("dve", lambda e, k3=k3, kp=kp: e.tensor_tensor(
                pw[kp][:, :], gate2w[k3][:, :], upqw[k3][:, :], ALU.mult),
                deps=[tg2_, tuq, war4["pw"][kp]])
            war4["gate2"][k3] = tp
            war4["upq"][k3] = tp
            tred4 = emit_red("m4", pw[kp][:, :], [tp])

            def _sp4(ic_=ic_, st=st, kp=kp, tp=tp, tred4=tred4):
                t_pdma[(st, ic_)] = p.dma("act", "acq",
                    lambda e, ic_=ic_, st=st, kp=kp: e.dma_start(
                        out=p_d[ic_ * 128:(ic_ + 1) * 128,
                                st * S:(st + 1) * S],
                        in_=pw[kp][:, :]), deps=[tp, tred4])
                war4["pw"][kp] = t_pdma[(st, ic_)]
            push4(st * IC + ic_, _sp4)

    flush4(None)
    t_fold4 = fold_ring("m4", 4)
    t_fm4 = max_finalize(4, 4, ps_bc4, [t_fold4])
    b3 = barrier(3, 4, 1, [("s4", "i4")], ps_bc4, [t_fm4])
    t_e4 = p.emit("dve", lambda e: e.tensor_scalar(
        scol("eps4"), scol("i4"), scol("i4"), 1e-5, ALU.mult, ALU.mult),
        deps=b3)
    b3 = b3 + [t_e4]

    ps4.close()
    ctx4.close()
    p.fence()

    ctx5 = ExitStack()
    o_t = [ctx5.enter_context(nc.sbuf_tensor(f"o_t{i}", [128, IC * S], F16))
           for i in range(2)]
    t5w = [ctx5.enter_context(nc.sbuf_tensor(f"t5w{i}", [128, S], F32))
           for i in range(R3)]
    x2l = [ctx5.enter_context(nc.sbuf_tensor(f"x2l{i}", [128, D], F32))
           for i in range(2)]
    xml = ctx5.enter_context(nc.sbuf_tensor("xml", [128, D], F32))
    outw = [ctx5.enter_context(nc.sbuf_tensor(f"outw{i}", [128, D], F32))
            for i in range(2)]

    ps5 = ExitStack()
    ps_y = [ps5.enter_context(nc.psum_tensor(f"ps_y{i}", [128, 1024], F32))
            for i in range(2)]

    t_out_dma = {}
    war5 = {"o": [{} for _ in range(2)], "rd": [{} for _ in range(2)],
            "t5": [None] * R3, "x2l": [None, None],
            "xml": None, "outw": [None, None], "psy": [None, None]}
    push5, flush5 = mk_defer(2)
    for st in range(NST):
        par = st % 2
        t_o = {}
        for ic_ in range(IC):
            k3 = (st * IC + ic_) % R3
            dp = p.dma("sp", "spq", lambda e, ic_=ic_, st=st, par=par: e.dma_start(
                out=o_t[par][:, ic_ * S:(ic_ + 1) * S],
                in_=p_d[ic_ * 128:(ic_ + 1) * 128, st * S:(st + 1) * S]),
                deps=[t_pdma[(st, ic_)], war5["rd"][par].get(ic_),
                      war5["o"][par].get(ic_)] + b3)
            tt5 = p.emit("act", lambda e, ic_=ic_, par=par, k3=k3: e.activation(
                t5w[k3][:, :], o_t[par][:, ic_ * S:(ic_ + 1) * S], AF.Relu,
                scale=scol("i4"), bias=cst[:, 3:4]),
                deps=[dp, war5["t5"][k3]])
            t_o[ic_] = p.emit("dve", lambda e, ic_=ic_, par=par, k3=k3:
                e.tensor_scalar(
                    o_t[par][:, ic_ * S:(ic_ + 1) * S], t5w[k3][:, :], -MAGIC,
                    None, ALU.add), deps=[tt5])
            war5["t5"][k3] = t_o[ic_]
            war5["o"][par][ic_] = t_o[ic_]
        for tt in range(NTT):
            flush5(st * NTT + tt)
            dx2 = p.dma("sp", "spq", lambda e, st=st, tt=tt: e.dma_start(
                out=x2l[tt % 2][:, :],
                in_=x2_d[(st * S + tt * 128):(st * S + tt * 128 + 128), :]),
                deps=[x2_dma[(st, tt)], war5["x2l"][tt % 2]])
            mms_stop = []
            wdap = wdT.ap().rearrange("p (d i c) -> p d i c", i=IC, c=128)
            for half in range(2):
                for ic_ in range(IC):
                    deps = [t_o[ic_]]
                    if war5["psy"][tt % 2] is not None:
                        deps.append(war5["psy"][tt % 2])
                    rh = wdap[:, half * 4:(half + 1) * 4, ic_, :]
                    mm = p.emit("pe",
                        lambda e, ic_=ic_, half=half, tt=tt, rh=rh, par=par:
                        e.matmul(
                            ps_y[tt % 2][:, half * 512:(half + 1) * 512],
                            o_t[par][:,
                                     ic_ * S + tt * 128:ic_ * S + tt * 128 + 128],
                            rh,
                            start=(ic_ == 0), stop=(ic_ == IC - 1)),
                        deps=deps)
                    war5["rd"][par][ic_] = mm
                mms_stop.append(mm)
            lt4 = ln_stats(lambda lo, tt=tt: ps_y[tt % 2][:, lo:lo + 512],
                           CB["negm2"], CB["r2"], mms_stop,
                           extra_negmr=CB["negm2r2"], eps_ap=scol("eps4"))
            txm = p.emit("dve", lambda e, tt=tt: e.tensor_scalar(
                xml[:, :], x2l[tt % 2][:, :],
                colbuf[:, CB["negm2r2"]:CB["negm2r2"] + 1], None, ALU.add),
                deps=lt4 + [dx2, war5["xml"]])
            war5["x2l"][tt % 2] = txm
            tout = p.emit("dve", lambda e, tt=tt: e.scalar_tensor_tensor(
                outw[tt % 2][:, :], ps_y[tt % 2][:, :],
                colbuf[:, CB["r2"]:CB["r2"] + 1], xml[:, :],
                ALU.mult, ALU.add),
                deps=[txm] + lt4 + [war5["outw"][tt % 2]])
            use_cols(tout, [CB["negm2"], CB["r2"], CB["negm2r2"]])
            war5["xml"] = tout
            war5["psy"][tt % 2] = tout

            def _so5(st=st, tt=tt, tout=tout):
                t_out_dma[(st, tt)] = p.dma("act", "acq",
                    lambda e, st=st, tt=tt: e.dma_start(
                        out=out_h[(st * S + tt * 128):
                                  (st * S + tt * 128 + 128), :],
                        in_=outw[tt % 2][:, :]), deps=[tout])
                war5["outw"][tt % 2] = t_out_dma[(st, tt)]
            push5(st * NTT + tt, _so5)

    flush5(None)
    finals = [(sq, v) for sq, v in p.dcnt.items() if v > 0]
    for e in P.ENGINES:
        p.water[e] = {}
        p.wait(e, finals)

    with nc.Block() as block:
        @block.tensor
        def _(eng):
            for op in p.ops["pe"]:
                op(eng)

        @block.scalar
        def _(eng):
            for op in p.ops["act"]:
                op(eng)

        @block.vector
        def _(eng):
            for op in p.ops["dve"]:
                op(eng)

        @block.gpsimd
        def _(eng):
            for op in p.ops["pool"]:
                op(eng)

        @block.sync
        def _(eng):
            for op in p.ops["sp"]:
                op(eng)

    es.pop_all()
    return nc


def _make_inputs(x_core, W_in, b_in, W_gate, b_gate, lam, w_gf, w_up, w_down):
    return {
        "x": np.ascontiguousarray(x_core, dtype=np.float32),
        "w_in": np.ascontiguousarray(W_in, dtype=np.float32),
        "w_gate": np.ascontiguousarray(W_gate, dtype=np.float32),
        "b_in": np.ascontiguousarray(np.reshape(b_in, (1, -1)), dtype=np.float32),
        "b_gate": np.ascontiguousarray(np.reshape(b_gate, (1, -1)), dtype=np.float32),
        "lam": np.ascontiguousarray(np.reshape(lam, (1, -1)), dtype=np.float32),
        "w_up": np.ascontiguousarray(w_up, dtype=np.float32),
        "w_gf": np.ascontiguousarray(w_gf, dtype=np.float32),
        "w_down": np.ascontiguousarray(w_down, dtype=np.float32),
        "id16_h": np.eye(128, dtype=np.float16),
        "id32_h": np.eye(128, dtype=np.float32),
    }


def kernel(L0_attn_x, L0_attn_W_in, L0_attn_b_in, L0_attn_W_gate,
           L0_attn_b_gate, L0_attn_forget_lambda, L0_ffn_w_gate,
           L0_ffn_w_up, L0_ffn_w_down):
    x = np.ascontiguousarray(np.asarray(L0_attn_x, dtype=np.float32))
    B, T, _ = x.shape
    in_maps = [
        _make_inputs(x[c], L0_attn_W_in, L0_attn_b_in, L0_attn_W_gate,
                     L0_attn_b_gate, L0_attn_forget_lambda, L0_ffn_w_gate,
                     L0_ffn_w_up, L0_ffn_w_down)
        for c in range(B)
    ]
    import os
    axon = (bool(os.environ.get("AXON_TERMINAL_JOB_NAME"))
            or os.environ.get("AXON_H4_ENABLED") == "1")
    use_cc = (os.environ.get("BASS_SIM_FP64") == "1") or not axon
    nc = build_program(T=T, use_cc=use_cc, n_cores=B)
    res = run_bass_kernel_spmd(nc, in_maps, core_ids=list(range(B)))
    return np.stack([np.asarray(res.results[c]["out"], dtype=np.float32)
                     for c in range(B)], axis=0)


# version 18
# speedup vs baseline: 2.6383x; 1.0269x over previous
import numpy as np
import concourse.bass as bass
import concourse.mybir as mybir
from concourse.bass_utils import run_bass_kernel_spmd
from contextlib import ExitStack

F32 = mybir.dt.float32
F16 = mybir.dt.float16
AF = mybir.ActivationFunctionType
ALU = mybir.AluOpType
AX = mybir.AxisListType.X

MAGIC = 12582912.0
QMAX = 32767.0
CLIP16 = 65535.0 / 65536.0
D = 1024
I = 3072
DC = D // 128
IC = I // 128
S = 512
NTT = S // 128


class P:
    ENGINES = ("pe", "act", "dve", "pool", "sp")
    DQUEUES = ("spq", "gpq", "acq", "ccq")
    NPOOL = 8

    def __init__(self, nc):
        self.nc = nc
        self.ops = {e: [] for e in self.ENGINES}
        self.cnt = {e: 0 for e in self.ENGINES}
        self.dcnt = {}
        for q in self.DQUEUES:
            for i in range(self.NPOOL):
                self.dcnt[f"{q}{i}"] = 0
        self.didx = {q: 0 for q in self.DQUEUES}
        self.sems = {}
        self.water = {e: {} for e in self.ENGINES}

    TRACE = []

    def wait(self, e, deps):
        for d in deps:
            if d is None:
                continue
            sname, val = d
            if sname == e:
                if val >= self.cnt[e]:
                    val = self.cnt[e]
            w = self.water[e]
            if w.get(sname, 0) >= val:
                continue
            w[sname] = val

            def op(eng, sem_name=sname, val=val, self=self):
                eng.wait_ge(self.sems[sem_name], val)
            self.ops[e].append(op)
            P.TRACE.append(("wait", e, sname, val))

    def emit(self, e, builder, deps=()):
        self.wait(e, deps)
        self.cnt[e] += 1
        val = self.cnt[e]

        def op(eng, builder=builder, e=e, self=self):
            builder(eng).then_inc(self.sems[e], 1)
        self.ops[e].append(op)
        import traceback
        P.TRACE.append(("emit", e, val, traceback.extract_stack()[-2].lineno))
        return (e, val)

    def dma(self, e, q, builder, deps=()):
        sq = f"{q}{self.didx[q] % self.NPOOL}"
        self.didx[q] += 1
        prev = getattr(self, "dlast", None)
        if prev is None:
            self.dlast = prev = {}
        self.wait(e, list(deps) + [prev.get(sq)])
        self.dcnt[sq] += 16
        val = self.dcnt[sq]
        prev[sq] = (sq, val)

        def op(eng, builder=builder, sq=sq, self=self):
            builder(eng).then_inc(self.sems[sq], 16)
        self.ops[e].append(op)
        import traceback
        P.TRACE.append(("dma", e, sq, val, traceback.extract_stack()[-2].lineno))
        return (sq, val)

    def fence(self):
        toks = [(e, self.cnt[e]) for e in self.ENGINES]
        toks += [(sq, self.dcnt[sq]) for sq in self.dcnt]
        for e in self.ENGINES:
            self.wait(e, toks)


def mk_defer(lag):
    q = []

    def push(idx, fn):
        q.append((idx, fn))

    def flush(now=None):
        while q and (now is None or q[0][0] + lag <= now):
            q.pop(0)[1]()
    return push, flush


def build_program(T=8192, use_cc=True, n_cores=8, dbg=False):
    NST = T // S
    nc = bass.Bass("TRN2", target_bir_lowering=False,
                   num_devices=(n_cores if use_cc else None))

    x_h = nc.dram_tensor("x", [T, D], F32, kind="ExternalInput").ap()
    win_h = nc.dram_tensor("w_in", [D, D], F32, kind="ExternalInput").ap()
    wg_h = nc.dram_tensor("w_gate", [D, D], F32, kind="ExternalInput").ap()
    bin_h = nc.dram_tensor("b_in", [1, D], F32, kind="ExternalInput").ap()
    bg_h = nc.dram_tensor("b_gate", [1, D], F32, kind="ExternalInput").ap()
    lam_h = nc.dram_tensor("lam", [1, D], F32, kind="ExternalInput").ap()
    wup_h = nc.dram_tensor("w_up", [I, D], F32, kind="ExternalInput").ap()
    wgf_h = nc.dram_tensor("w_gf", [I, D], F32, kind="ExternalInput").ap()
    wd_h = nc.dram_tensor("w_down", [D, I], F32, kind="ExternalInput").ap()
    id16_h = nc.dram_tensor("id16_h", [128, 128], F16, kind="ExternalInput").ap()
    id32_h = nc.dram_tensor("id32_h", [128, 128], F32, kind="ExternalInput").ap()
    out_h = nc.dram_tensor("out", [T, D], F32, kind="ExternalOutput").ap()

    kd = "ExternalOutput" if dbg else "Internal"
    x2_d = nc.dram_tensor("x2_d", [T, D], F32, kind=kd).ap()
    xn2_d = nc.dram_tensor("xn2_d", [T, D], F16, kind=kd).ap()
    xqt_d = nc.dram_tensor("xqt_d", [D, T], F16, kind=kd).ap()
    gate_d = nc.dram_tensor("gate_d", [I, T], F16, kind=kd).ap()
    up_d = nc.dram_tensor("up_d", [I, T], F16, kind=kd).ap()
    g2_d = nc.dram_tensor("g2_d", [I, T], F16, kind=kd).ap()
    p_d = nc.dram_tensor("p_d", [I, T], F16, kind=kd).ap()
    ccin_d = [nc.dram_tensor(f"ccin{k}", [1, 2], F32).ap() for k in range(4)]
    ccout_d = [nc.dram_tensor(f"ccout{k}", [1, 2], F32).ap() for k in range(4)]

    p = P(nc)
    es = ExitStack()
    for sname in list(p.cnt) + list(p.dcnt):
        p.sems[sname] = es.enter_context(nc.semaphore(f"s_{sname}"))

    A = nc.alloc_sbuf_tensor
    id16 = A("id16", [128, 128], F16)
    id32 = A("id32", [128, 128], F32)
    ones128 = A("ones128", [1, 128], F32)
    k2 = A("k2", [128, DC], F32)
    bin2 = A("bin2", [128, DC], F32)
    bg2 = A("bg2", [128, DC], F32)
    scal = A("scal", [128, 16], F32)
    srow = A("srow", [1, 16], F32)
    mcol = A("mcol", [128, 8], F32)
    mrow = A("mrow", [1, 128], F32)
    mcell = A("mcell", [1, 8], F32)
    lamt = A("lamt", [128, DC], F32)
    btmp = A("btmp", [128, DC], F32)
    bnbuf = A("bnbuf", [128, 12], F32)
    stats = A("stats", [128, 2], F32)
    colbuf = A("colbuf", [128, 16], F32)
    redcol = A("redcol", [128, 1], F32)
    redring = A("redring", [128, 40], F32)
    cst = A("cst", [128, 8], F32)

    SC = {"s0": 0, "i0": 1, "s2": 2, "i2": 3, "hs2": 4, "nhs2m": 5,
          "s1": 6, "i1": 7, "s3": 8, "i3": 9, "s4": 10, "i4": 11,
          "s2p16": 12, "eps4": 13, "s0b": 14}

    def scol(name):
        return scal[:, SC[name]:SC[name] + 1]

    d_id16 = p.dma("sp", "spq", lambda e: e.dma_start(out=id16[:, :], in_=id16_h))
    d_id32 = p.dma("sp", "spq", lambda e: e.dma_start(out=id32[:, :], in_=id32_h))
    d_lam = p.dma("sp", "spq", lambda e: e.dma_start(
        out=lamt[:, :], in_=lam_h.rearrange("o (c p) -> (o p) c", p=128),
        allow_slow_non_contiguous=True))
    d_bin = p.dma("sp", "spq", lambda e: e.dma_start(
        out=btmp[:, :], in_=bin_h.rearrange("o (c p) -> (o p) c", p=128),
        allow_slow_non_contiguous=True))
    t_ones = p.emit("dve", lambda e: e.memset(ones128[:, :], 1.0))
    t_c0 = p.emit("dve", lambda e: e.memset(cst[:, 0:1], 1e-5))
    t_c1 = p.emit("dve", lambda e: e.memset(cst[:, 1:2], 1.0))
    t_c2 = p.emit("dve", lambda e: e.memset(cst[:, 2:3], 0.25 * (1 + 1e-6)))
    t_c3 = p.emit("dve", lambda e: e.memset(cst[:, 3:4], MAGIC))
    t_c4 = p.emit("dve", lambda e: e.memset(cst[:, 4:5], 32768.0 + MAGIC))
    t_srz = p.emit("dve", lambda e: e.memset(srow[:, :], 0.0))
    t_mz = p.emit("dve", lambda e: e.memset(mcol[:, :], 0.0))
    t_e = p.emit("act", lambda e: e.activation(lamt[:, :], lamt[:, :], AF.Exp),
                 deps=[d_lam])
    t_l = p.emit("act", lambda e: e.activation(lamt[:, :], lamt[:, :], AF.Ln,
                                               bias=cst[:, 1:2]), deps=[t_e, t_c1])
    t_k2 = p.emit("dve", lambda e: e.tensor_scalar(
        k2[:, :], lamt[:, :], 4.0, None, ALU.mult), deps=[t_l])
    t_b1 = p.emit("dve", lambda e: e.tensor_scalar(
        bin2[:, :], btmp[:, :], 0.5, None, ALU.mult), deps=[d_bin])
    d_bg = p.dma("sp", "spq", lambda e: e.dma_start(
        out=btmp[:, :], in_=bg_h.rearrange("o (c p) -> (o p) c", p=128),
        allow_slow_non_contiguous=True), deps=[t_b1])
    t_b2 = p.emit("dve", lambda e: e.tensor_scalar(
        bg2[:, :], btmp[:, :], 0.5, None, ALU.mult), deps=[d_bg])
    pro_toks = [t_k2, t_b2, t_ones, t_mz, d_id16, d_id32, t_c0, t_c2, t_c3, t_c4]

    ln_prev = {"t5": None, "stats_r": None}
    reds = {"m0": [], "m2": [], "m1": [], "m3": [], "m4": []}
    RB = {"m0": 0, "m2": 8, "m1": 16, "m3": 24, "m4": 32}
    MC = {"m0": 0, "m2": 1, "m1": 2, "m3": 3, "m4": 4}
    last_fold = {}

    def _fold8(stream, nret):
        b = RB[stream]
        c = MC[stream]
        tr = p.emit("dve", lambda e, b=b, nret=nret: e.tensor_reduce(
            redcol[:, :], redring[:, b:b + nret], AX, ALU.max),
            deps=[reds[stream][-1], last_fold.get(stream, t_mz)])
        tm = p.emit("dve", lambda e, c=c: e.tensor_tensor(
            mcol[:, c:c + 1], mcol[:, c:c + 1], redcol[:, :], ALU.max),
            deps=[tr, t_mz])
        last_fold[stream] = tm
        return tm

    def emit_red(stream, src_ap, deps):
        k = RB[stream] + (len(reds[stream]) % 8)
        t = p.emit("dve", lambda e, k=k, src_ap=src_ap: e.tensor_reduce(
            redring[:, k:k + 1], src_ap, AX, ALU.max,
            apply_absolute_value=True),
            deps=list(deps) + [last_fold.get(stream)
                               if len(reds[stream]) % 8 == 0 else None])
        reds[stream].append(t)
        if len(reds[stream]) % 8 == 0:
            _fold8(stream, 8)
        return t

    def fold_ring(stream, col_idx):
        rem = len(reds[stream]) % 8
        if rem:
            _fold8(stream, rem)
        return last_fold[stream]
    colw = {}

    def ln_stats(src_ap_fn, negm_col, r_col, deps, extra_negmr=None,
                 eps_ap=None):
        war = [colw.get(r_col), ln_prev["t5"]]
        t1 = p.emit("dve", lambda e: e.bn_stats(bnbuf[:, 0:6], src_ap_fn(0)),
                    deps=list(deps) + [ln_prev["stats_r"]])
        t2 = p.emit("dve", lambda e: e.bn_stats(bnbuf[:, 6:12], src_ap_fn(512)),
                    deps=list(deps))
        t3 = p.emit("dve", lambda e: e.bn_aggr(stats[:, 0:2], bnbuf[:, 0:12]),
                    deps=[t1, t2])
        t4 = p.emit("dve", lambda e: e.tensor_scalar(
            colbuf[:, negm_col:negm_col + 1], stats[:, 0:1], -1.0, None,
            ALU.mult), deps=[t3])
        if eps_ap is None:
            eps_ap = cst[:, 0:1]
        t5 = p.emit("act", lambda e, eps_ap=eps_ap: e.activation(
            colbuf[:, r_col:r_col + 1], stats[:, 1:2], AF.Sqrt,
            bias=eps_ap), deps=[t3] + war)
        t6 = p.emit("dve", lambda e: e.reciprocal(
            colbuf[:, r_col:r_col + 1], colbuf[:, r_col:r_col + 1]), deps=[t5])
        ln_prev["t5"] = t5
        ln_prev["stats_r"] = t5
        toks = [t4, t6]
        if extra_negmr is not None:
            toks.append(p.emit("dve", lambda e: e.tensor_tensor(
                colbuf[:, extra_negmr:extra_negmr + 1],
                colbuf[:, negm_col:negm_col + 1],
                colbuf[:, r_col:r_col + 1], ALU.mult), deps=[t4, t6]))
        return toks

    def use_cols(tok, cols):
        for c in cols:
            colw[c] = tok

    def max_finalize(col_idx, cell_idx, ps_bc, deps):
        ttr = p.emit("pe", lambda e: e.transpose(
            ps_bc[0:1, 0:128], mcol[:, col_idx:col_idx + 1], id32[:, :]),
            deps=deps)
        tv = p.emit("dve", lambda e: e.tensor_copy(
            mrow[:, :], ps_bc[0:1, 0:128]), deps=[ttr])
        tr = p.emit("dve", lambda e: e.tensor_reduce(
            mcell[:, cell_idx:cell_idx + 1], mrow[:, :], AX, ALU.max),
            deps=[tv])
        return tr

    def barrier(kidx, cell0, n, scols, ps_bc, deps):
        toks = list(deps)
        if use_cc:
            d1 = p.dma("pool", "gpq", lambda e: e.dma_start(
                out=ccin_d[kidx][:, 0:n], in_=mcell[:, cell0:cell0 + n]),
                deps=toks)
            tcc = p.dma("pool", "ccq", lambda e: e.collective_compute(
                "AllReduce", ALU.max, replica_groups=[list(range(n_cores))],
                ins=[ccin_d[kidx][:, 0:n].opt()],
                outs=[ccout_d[kidx][:, 0:n].opt()]), deps=[d1])
            d2 = p.dma("pool", "gpq", lambda e: e.dma_start(
                out=mcell[:, cell0:cell0 + n], in_=ccout_d[kidx][:, 0:n]),
                deps=[tcc])
            toks = [d2]
        stoks = []
        for j in range(n):
            ts_ = p.emit("dve", lambda e, j=j: e.tensor_scalar(
                srow[:, 2 * j:2 * j + 1], mcell[:, cell0 + j:cell0 + j + 1],
                1.0 / QMAX, 1e-12, ALU.mult, ALU.max), deps=toks)
            ti_ = p.emit("dve", lambda e, j=j: e.reciprocal(
                srow[:, 2 * j + 1:2 * j + 2], srow[:, 2 * j:2 * j + 1]),
                deps=[ts_])
            stoks.append(ti_)
        tm = p.emit("pe", lambda e: e.matmul(
            ps_bc[0:128, 0:16], ones128[:, :], srow[:, :],
            start=True, stop=True), deps=stoks + [t_srz])
        evs = []
        for j, (sc_, ic_) in enumerate(scols):
            evs.append(p.emit("dve", lambda e, sc_=sc_, j=j: e.tensor_copy(
                scal[:, SC[sc_]:SC[sc_] + 1], ps_bc[0:128, 2 * j:2 * j + 1]),
                deps=[tm]))
            evs.append(p.emit("dve", lambda e, ic_=ic_, j=j: e.tensor_copy(
                scal[:, SC[ic_]:SC[ic_] + 1],
                ps_bc[0:128, 2 * j + 1:2 * j + 2]), deps=[tm]))
        return evs

    def prep_lpbq_chunk(w_hbm, dst, st_, ps_tr, ic, deps, ncols, prev_ev,
                        post_scale=None):
        ncc = ncols // 128
        nb = ncols // 32
        dld = p.dma("sp", "spq", lambda e: e.dma_start(
            out=st_["w32"][:, :ncols], in_=w_hbm[ic * 128:(ic + 1) * 128, :]),
            deps=deps)
        w3 = st_["w32"].ap()[:, :ncols].rearrange("p (b c) -> p b c", c=32)
        t1 = p.emit("dve", lambda e: e.tensor_reduce(
            st_["bmax"][:, :nb], w3, AX, ALU.max, apply_absolute_value=True),
            deps=[dld])
        t2 = p.emit("dve", lambda e: e.tensor_scalar(
            st_["sw"][:, :nb], st_["bmax"][:, :nb], 1.0 / 7.0, 1e-12,
            ALU.mult, ALU.max), deps=[t1])
        t3 = p.emit("dve", lambda e: e.reciprocal(
            st_["isw"][:, :nb], st_["sw"][:, :nb]), deps=[t2])
        if post_scale is not None:
            t2b = p.emit("dve", lambda e: e.tensor_scalar(
                st_["sw"][:, :nb], st_["sw"][:, :nb], post_scale, None,
                ALU.mult), deps=[t3])
            t3 = t2b
        isw3 = st_["isw"].ap()[:, :nb].rearrange("p (b o) -> p b o", o=1)
        sw3 = st_["sw"].ap()[:, :nb].rearrange("p (b o) -> p b o", o=1)
        z3 = st_["z"].ap()[:, :ncols].rearrange("p (b c) -> p b c", c=32)
        wq3 = st_["w16"].ap()[:, :ncols].rearrange("p (b c) -> p b c", c=32)
        a_, b_ = bass.broadcast_tensor_aps(w3, isw3)
        t4 = p.emit("dve", lambda e, a_=a_, b_=b_: e.tensor_tensor(
            z3, a_, b_, ALU.mult), deps=[t3])
        t5 = p.emit("dve", lambda e: e.tensor_scalar(
            st_["z"][:, :ncols], st_["z"][:, :ncols], MAGIC, -MAGIC,
            ALU.add, ALU.add), deps=[t4])
        a2_, b2_ = bass.broadcast_tensor_aps(z3, sw3)
        t6 = p.emit("dve", lambda e, a2_=a2_, b2_=b2_: e.tensor_tensor(
            wq3, a2_, b2_, ALU.mult), deps=[t5])
        ev_toks = list(prev_ev)
        tr_toks = []
        for dc in range(ncc):
            half = (dc // 4) % 2
            tr_toks.append(p.emit("pe", lambda e, dc=dc, half=half:
                e.transpose(
                    ps_tr[half][:, (dc % 4) * 128:(dc % 4 + 1) * 128],
                    st_["w16"][:, dc * 128:(dc + 1) * 128], id16[:, :]),
                deps=[t6] + (ev_toks[-2:-1] if len(ev_toks) >= 2
                             else ev_toks)))
            if dc % 4 == 3:
                ev_toks.append(p.emit("act", lambda e, dc=dc, half=half,
                                      ic=ic, ncc=ncc: e.copy(
                    dst[:, (ic * ncc + dc - 3) * 128:
                        (ic * ncc + dc + 1) * 128],
                    ps_tr[half][:, :]), deps=tr_toks[-4:]))
        return ev_toks[-1], t6, ev_toks[len(prev_ev):]

    ctx_w = ExitStack()
    winT = ctx_w.enter_context(nc.sbuf_tensor("winT", [128, DC * DC * 128], F16))
    wgT = ctx_w.enter_context(nc.sbuf_tensor("wgT", [128, DC * DC * 128], F16))

    ctx1 = ExitStack()
    wst = {k: ctx1.enter_context(nc.sbuf_tensor(f"p_{k}", [128, n], dt))
           for k, n, dt in [("w32", D, F32), ("w16", D, F16), ("z", D, F32),
                            ("bmax", D // 32, F32), ("sw", D // 32, F32),
                            ("isw", D // 32, F32)]}
    xs = [ctx1.enter_context(nc.sbuf_tensor(f"xs{i}", [128, D], F32))
          for i in range(2 * NTT)]
    xn_st = ctx1.enter_context(nc.sbuf_tensor("xn_st", [128, NTT * D], F16))
    xnT = ctx1.enter_context(nc.sbuf_tensor("xnT", [128, DC * S], F16))
    th_rg = ctx1.enter_context(nc.sbuf_tensor("th_rg", [128, DC * S], F16))
    th_ig = ctx1.enter_context(nc.sbuf_tensor("th_ig", [128, DC * S], F16))
    g_t = ctx1.enter_context(nc.sbuf_tensor("g_t", [128, DC * S], F32))
    al_t = ctx1.enter_context(nc.sbuf_tensor("al_t", [128, DC * S], F32))
    u1_t = ctx1.enter_context(nc.sbuf_tensor("u1_t", [128, DC * S], F16))
    xb_t = ctx1.enter_context(nc.sbuf_tensor("xb_t", [128, DC * S], F16))
    h_t = [ctx1.enter_context(nc.sbuf_tensor(f"h{i}", [128, DC * S], F32))
           for i in range(2)]
    x2_tile = [ctx1.enter_context(nc.sbuf_tensor(f"x2_tile{i}", [128, D], F32))
               for i in range(2)]
    xm_tile = ctx1.enter_context(nc.sbuf_tensor("xm_tile", [128, D], F32))
    xn2_tile = [ctx1.enter_context(nc.sbuf_tensor(f"xn2_t{i}", [128, D], F16))
                for i in range(2)]

    ps1 = ExitStack()
    ps_tr = [ps1.enter_context(nc.psum_tensor(f"ps_tr{i}", [128, 512], F16))
             for i in range(2)]
    ps_g = [ps1.enter_context(nc.psum_tensor(f"ps_g{i}", [128, 512], F32))
            for i in range(2)]
    ps_h = [ps1.enter_context(nc.psum_tensor(f"ps_h{i}", [128, 1024], F32))
            for i in range(2)]
    ps_bc = ps_g[0]

    def prep_square(w_hbm, dst, deps):
        toks = []
        last = None
        for dpc in range(DC):
            dld = p.dma("sp", "spq", lambda e, dpc=dpc: e.dma_start(
                out=wst["w32"][:, :], in_=w_hbm[dpc * 128:(dpc + 1) * 128, :]),
                deps=deps + ([last] if last else []))
            tc = p.emit("dve", lambda e: e.tensor_copy(
                wst["w16"][:, :], wst["w32"][:, :]), deps=[dld])
            last = tc
            tr_toks = []
            for dc in range(DC):
                half = (dc // 4) % 2
                tr_toks.append(p.emit("pe", lambda e, dc=dc, half=half:
                    e.transpose(
                        ps_tr[half][:, (dc % 4) * 128:(dc % 4 + 1) * 128],
                        wst["w16"][:, dc * 128:(dc + 1) * 128], id16[:, :]),
                    deps=[tc] + (toks[-2:-1] if len(toks) >= 2 else [])))
                if dc % 4 == 3:
                    toks.append(p.emit("act", lambda e, dc=dc, half=half,
                                       dpc=dpc: e.copy(
                        dst[:, (dpc * DC + dc - 3) * 128:
                            (dpc * DC + dc + 1) * 128],
                        ps_tr[half][:, :]), deps=tr_toks[-4:]))
            last = toks[-1]
        return toks[-1]

    tw1 = prep_square(win_h, winT, pro_toks)
    tw2 = prep_square(wg_h, wgT, [tw1])

    CB = {"negm1": 0, "r1": 4, "negm2": 8, "r2": 9, "negm2r2": 10,
          "negm3": 11, "r3": 12}

    x_dma = {}
    x2_dma = {}
    xn2_dma = {}
    t_scan = {}
    t_x_read = {}
    war = {
        "x2t_dma": [None, None],
        "xn_st": None,
        "xnT": None,
        "xnT_dve": None,
        "th_rg": None, "th_ig": None,
        "g": {}, "al": {},
        "xb": None, "xm": None,
        "h_tr": [None, None],
        "ps_h_rd": [None, None],
        "x2t": [None, None], "xn2t": [None, None],
        "xn2t_pool": [None, None],
        "proj_tail": None,
    }

    for st0 in range(min(2, NST)):
        for tt in range(NTT):
            x_dma[(st0, tt)] = p.dma(
                "sp", "spq", lambda e, st0=st0, tt=tt: e.dma_start(
                    out=xs[(st0 % 2) * NTT + tt][:, :],
                    in_=x_h[(st0 * S + tt * 128):(st0 * S + tt * 128 + 128), :]))

    pushP1, flushP1 = mk_defer(2)

    def emit_tail(st2, scan_toks, hc2):
        t_ht_last = None
        for tt in range(NTT):
            pp = tt % 2
            flushP1(st2 * NTT + tt)
            trh = []
            for dc in range(DC):
                trh.append(p.emit("pe", lambda e, dc=dc, tt=tt, hc2=hc2,
                                  pp=pp: e.transpose(
                        ps_h[pp][:, dc * 128:(dc + 1) * 128],
                        hc2[:, dc * S + tt * 128:dc * S + tt * 128 + 128],
                        id32[:, :]),
                    deps=[scan_toks[dc], war["ps_h_rd"][pp]]))
            lt2 = ln_stats(lambda lo, pp=pp: ps_h[pp][:, lo:lo + 512],
                           CB["negm2"], CB["r2"], trh,
                           extra_negmr=CB["negm2r2"])
            xb_ = xs[(st2 % 2) * NTT + tt]
            txm = p.emit("pool", lambda e, xb_=xb_: e.tensor_scalar(
                xm_tile[:, :], xb_[:, :], 1.0,
                colbuf[:, CB["negm2r2"]:CB["negm2r2"] + 1],
                ALU.mult, ALU.add),
                deps=lt2 + [war["xm"]])
            use_cols(txm, [CB["negm2r2"]])
            t_x_read[(st2, tt)] = txm
            tx2 = p.emit("dve", lambda e, pp=pp: e.scalar_tensor_tensor(
                x2_tile[pp][:, :], ps_h[pp][:, :],
                colbuf[:, CB["r2"]:CB["r2"] + 1], xm_tile[:, :],
                ALU.mult, ALU.add),
                deps=[txm] + lt2 + [war["x2t"][pp], war["x2t_dma"][pp]])
            use_cols(tx2, [CB["negm2"], CB["r2"], CB["negm2r2"]])
            war["xm"] = tx2
            war["ps_h_rd"][pp] = tx2

            def _sx2(st2=st2, tt=tt, pp=pp, tx2=tx2):
                x2_dma[(st2, tt)] = p.dma("act", "acq",
                    lambda e, st2=st2, tt=tt, pp=pp: e.dma_start(
                        out=x2_d[(st2 * S + tt * 128):
                                 (st2 * S + tt * 128 + 128), :],
                        in_=x2_tile[pp][:, :]), deps=[tx2])
                war["x2t_dma"][pp] = x2_dma[(st2, tt)]
            pushP1(st2 * NTT + tt, _sx2)
            lt3 = ln_stats(lambda lo, pp=pp: x2_tile[pp][:, lo:lo + 512],
                           CB["negm3"], CB["r3"], [tx2])
            txn2 = p.emit("pool", lambda e, pp=pp: e.tensor_scalar(
                xn2_tile[pp][:, :], x2_tile[pp][:, :],
                colbuf[:, CB["negm3"]:CB["negm3"] + 1],
                colbuf[:, CB["r3"]:CB["r3"] + 1], ALU.add, ALU.mult),
                deps=lt3 + [war["xn2t"][pp], war["xn2t_pool"][pp]])
            use_cols(txn2, [CB["negm3"], CB["r3"]])
            war["x2t"][pp] = txn2
            tred = emit_red("m0", xn2_tile[pp][:, :], [txn2])

            def _sxn2(st2=st2, tt=tt, pp=pp, txn2=txn2, tred=tred):
                xn2_dma[(st2, tt)] = p.dma("act", "acq",
                    lambda e, st2=st2, tt=tt, pp=pp: e.dma_start(
                        out=xn2_d[(st2 * S + tt * 128):
                                  (st2 * S + tt * 128 + 128), :],
                        in_=xn2_tile[pp][:, :]), deps=[txn2, tred])
                war["xn2t"][pp] = xn2_dma[(st2, tt)]
                war["xn2t_pool"][pp] = tred
            pushP1(st2 * NTT + tt, _sxn2)
            t_ht_last = trh[-1]
        war["h_tr"][st2 % 2] = t_ht_last

    pend_tail = None
    for st in range(NST):
        t_xn = {}
        for tt in range(NTT):
            xb_ = xs[(st % 2) * NTT + tt]
            lt = ln_stats(lambda lo, xb_=xb_: xb_[:, lo:lo + 512],
                          CB["negm1"] + tt, CB["r1"] + tt,
                          [x_dma[(st, tt)]])
            t_xn[tt] = p.emit("dve", lambda e, tt=tt, xb_=xb_: e.tensor_scalar(
                xn_st[:, tt * D:(tt + 1) * D], xb_[:, :],
                colbuf[:, CB["negm1"] + tt:CB["negm1"] + tt + 1],
                colbuf[:, CB["r1"] + tt:CB["r1"] + tt + 1],
                ALU.add, ALU.mult), deps=lt + [war["xn_st"]])
            use_cols(t_xn[tt], [CB["negm1"] + tt, CB["r1"] + tt])

        t_xnT = {}
        ev_list = []
        tr_last = None
        for dc in range(DC):
            half = dc % 2
            trk = []
            for tt in range(NTT):
                trk.append(p.emit("pe", lambda e, dc=dc, tt=tt, half=half:
                    e.transpose(
                        ps_tr[half][:, tt * 128:(tt + 1) * 128],
                        xn_st[:, tt * D + dc * 128:tt * D + dc * 128 + 128],
                        id16[:, :]),
                    deps=[t_xn[tt], tw2] +
                         (ev_list[-2:-1] if len(ev_list) >= 2 else [])))
            tr_last = trk[-1]
            ev_list.append(p.emit("act", lambda e, dc=dc, half=half: e.copy(
                xnT[:, dc * S:(dc + 1) * S], ps_tr[half][:, :]),
                deps=trk + [war["xnT"], war["xnT_dve"]]))
            t_xnT[dc] = ev_list[-1]
        war["xn_st"] = tr_last

        def proj(wT, th_dst, bias_t, prev_tail, th_war):
            evs = {}
            mm_last = None
            for c in range(DC):
                mms = []
                for dc in range(DC):
                    deps = [t_xnT[dc]]
                    if c >= 2:
                        deps.append(evs[c - 2])
                    elif prev_tail is not None:
                        deps.append(prev_tail.get(c - 2 + DC))
                    mms.append(p.emit("pe",
                        lambda e, c=c, dc=dc, wT=wT: e.matmul(
                            ps_g[c % 2][:, :],
                            wT[:, (c * DC + dc) * 128:(c * DC + dc + 1) * 128],
                            xnT[:, dc * S:(dc + 1) * S],
                            start=(dc == 0), stop=(dc == DC - 1)),
                        deps=deps))
                evs[c] = p.emit("act", lambda e, c=c, bias_t=bias_t,
                                th_dst=th_dst: e.activation(
                    th_dst[:, c * S:(c + 1) * S], ps_g[c % 2][:, :], AF.Tanh,
                    bias=bias_t[:, c:c + 1], scale=0.5),
                    deps=[mms[-1], th_war])
                mm_last = mms[-1]
            return evs, mm_last

        ev_rg, mm_l1 = proj(wgT, th_rg, bg2, war["proj_tail"], war["th_rg"])

        t_bh = {}
        t_al = {}
        t_g = {}
        ta2 = {}
        for c in range(DC):
            t_g[c] = p.emit("pool", lambda e, c=c: e.tensor_scalar(
                g_t[:, c * S:(c + 1) * S], th_rg[:, c * S:(c + 1) * S],
                1.0, k2[:, c:c + 1], ALU.add, ALU.mult),
                deps=[ev_rg[c], war["g"].get(c)])
        for c in range(DC):
            t_al[c] = p.emit("act", lambda e, c=c: e.activation(
                al_t[:, c * S:(c + 1) * S], g_t[:, c * S:(c + 1) * S],
                AF.Exp, scale=-1.0), deps=[t_g[c], war["al"].get(c)])
        for c in range(DC):
            ta2[c] = p.emit("act", lambda e, c=c: e.activation(
                g_t[:, c * S:(c + 1) * S], g_t[:, c * S:(c + 1) * S],
                AF.Exp, scale=-2.0), deps=[t_al[c]])
        for c in range(DC):
            t_bh[c] = p.emit("act", lambda e, c=c: e.activation(
                g_t[:, c * S:(c + 1) * S], g_t[:, c * S:(c + 1) * S],
                AF.Sqrt, scale=-0.25, bias=cst[:, 2:3]),
                deps=[ta2[c], war["xb"]])
        war["th_rg"] = t_g[DC - 1]

        if pend_tail is not None:
            emit_tail(*pend_tail)
        pend_tail = None

        ev_ig, mm_l2 = proj(winT, th_ig, bin2, ev_rg, war["th_ig"])
        war["proj_tail"] = ev_ig
        war["xnT"] = mm_l2

        hc = h_t[st % 2]
        hp = h_t[(st - 1) % 2]
        new_scan = {}
        t_u_last = None
        for c in range(DC):
            tu = p.emit("dve", lambda e, c=c: e.scalar_tensor_tensor(
                u1_t[:, c * S:(c + 1) * S], th_ig[:, c * S:(c + 1) * S],
                1.0, xnT[:, c * S:(c + 1) * S], ALU.add, ALU.mult),
                deps=[ev_ig[c], t_xnT[c]])
            tx = p.emit("dve", lambda e, c=c: e.tensor_tensor(
                xb_t[:, c * S:(c + 1) * S], u1_t[:, c * S:(c + 1) * S],
                g_t[:, c * S:(c + 1) * S], ALU.mult), deps=[tu, t_bh[c]])
            if st == 0:
                init = 0.0
                dep_init = []
            else:
                init = hp[:, c * S + S - 1:c * S + S]
                dep_init = [t_scan[c]]
            new_scan[c] = p.emit("dve", lambda e, c=c, init=init, hc=hc:
                e.tensor_tensor_scan(
                    hc[:, c * S:(c + 1) * S], al_t[:, c * S:(c + 1) * S],
                    xb_t[:, c * S:(c + 1) * S], init, ALU.mult, ALU.add),
                deps=[tx, t_al[c], war["h_tr"][st % 2]] + dep_init)
            war["g"][c] = new_scan[c]
            war["al"][c] = new_scan[c]
            t_u_last = tu
        t_scan = new_scan
        war["th_ig"] = t_u_last
        war["xnT_dve"] = t_u_last
        war["xb"] = new_scan[DC - 1]

        pend_tail = (st, new_scan, hc)
        if st >= 1 and st + 1 < NST:
            for tt in range(NTT):
                x_dma[(st + 1, tt)] = p.dma(
                    "sp", "spq", lambda e, st=st, tt=tt: e.dma_start(
                        out=xs[((st + 1) % 2) * NTT + tt][:, :],
                        in_=x_h[((st + 1) * S + tt * 128):
                                ((st + 1) * S + tt * 128 + 128), :]),
                    deps=[t_x_read.get((st - 1, tt))])

    if pend_tail is not None:
        emit_tail(*pend_tail)
    flushP1(None)
    t_fold0 = fold_ring("m0", 0)
    t_fm0 = max_finalize(0, 0, ps_bc, [t_fold0])
    b0 = barrier(0, 0, 1, [("s0", "i0")], ps_bc, [t_fm0])
    t_s0b = p.emit("dve", lambda e: e.tensor_scalar(
        scol("s0b"), scol("s0"), 65536.0, None, ALU.mult), deps=b0)
    b0 = b0 + [t_s0b]

    ps1.close()
    ctx1.close()
    ctx_w.close()
    p.fence()

    ctx_wu1 = ExitStack()
    wupT = ctx_wu1.enter_context(nc.sbuf_tensor("wupT", [128, IC * DC * 128], F16))
    ctx_wu2 = ExitStack()
    wgfT = ctx_wu2.enter_context(nc.sbuf_tensor("wgfT", [128, IC * DC * 128], F16))

    ctx2 = ExitStack()
    wst2 = {k: ctx2.enter_context(nc.sbuf_tensor(f"q_{k}", [128, n], dt))
            for k, n, dt in [("w32", D, F32), ("w16", D, F16), ("z", D, F32),
                             ("bmax", D // 32, F32), ("sw", D // 32, F32),
                             ("isw", D // 32, F32)]}
    RBD = 6
    xn2_sb = [ctx2.enter_context(nc.sbuf_tensor(f"xn2_sb{i}", [128, NTT * D],
                                                F16)) for i in range(2)]
    t0_sb = [ctx2.enter_context(nc.sbuf_tensor(f"t0_sb{i}", [128, D], F32))
             for i in range(3)]
    xq_sb = [ctx2.enter_context(nc.sbuf_tensor(f"xq_sb{i}", [128, NTT * D],
                                               F16)) for i in range(2)]
    xqT = [ctx2.enter_context(nc.sbuf_tensor(f"xqT2_{i}", [128, DC * S], F16))
           for i in range(2)]
    gw = [ctx2.enter_context(nc.sbuf_tensor(f"gw{i}", [128, S], F16))
          for i in range(RBD)]

    ps2 = ExitStack()
    ps_tr2 = [ps2.enter_context(nc.psum_tensor(f"ps2_tr{i}", [128, 512], F16))
              for i in range(2)]
    ps_tr2w = [ps2.enter_context(nc.psum_tensor(f"ps2_tw{i}", [128, 512], F16))
               for i in range(2)]
    ps_g2 = [ps2.enter_context(nc.psum_tensor(f"ps2_g{i}", [128, 512], F32))
             for i in range(3)]
    ps_bc2 = ps_g2[0]

    t_wgf = {}
    t_wup = {}
    prev_ev_track = []
    prev_dep = b0
    for ic_ in range(IC):
        ev, _, new_evs = prep_lpbq_chunk(wgf_h, wgfT, wst2, ps_tr2w, ic_,
                                         prev_dep, D, prev_ev_track,
                                         post_scale=scol("s0b"))
        t_wgf[ic_] = ev
        prev_ev_track = (prev_ev_track + new_evs)[-2:]
        prev_dep = [ev]

    t_gdma = {}
    t_gev = {}
    up_sched = {}
    nup = 0
    for st in range(NST):
        k = 2 if NST >= IC // 2 else (IC + NST - 1) // NST
        for _ in range(k):
            if nup < IC:
                up_sched.setdefault(st, []).append(nup)
                nup += 1
    if nup < IC:
        up_sched.setdefault(NST - 1, []).extend(range(nup, IC))

    war2 = {"t0": [None] * 3}
    lastrd_xn2 = {}
    lastrd_xq = {}
    mml2 = {}
    gdma_f = {}
    t_xq_dma = {}
    push2a, flush2a = mk_defer(2)
    for st in range(NST):
        par = st % 2
        dx = {}
        for tt in range(NTT):
            dx[tt] = p.dma("sp", "spq",
                lambda e, st=st, tt=tt, par=par: e.dma_start(
                    out=xn2_sb[par][:, tt * D:(tt + 1) * D],
                    in_=xn2_d[(st * S + tt * 128):(st * S + tt * 128 + 128), :]),
                deps=[xn2_dma[(st, tt)], lastrd_xn2.get(st - 2)] + b0)
        t_xq = {}
        last_tt0 = None
        for tt in range(NTT):
            kt = (st * NTT + tt) % 3
            tt0 = p.emit("act", lambda e, tt=tt, par=par, kt=kt: e.activation(
                t0_sb[kt][:, :], xn2_sb[par][:, tt * D:(tt + 1) * D], AF.Relu,
                scale=scol("i0"), bias=cst[:, 3:4]),
                deps=[dx[tt], war2["t0"][kt]])
            last_tt0 = tt0
            t_xq[tt] = p.emit("dve", lambda e, tt=tt, par=par, kt=kt:
                e.tensor_scalar(
                    xq_sb[par][:, tt * D:(tt + 1) * D], t0_sb[kt][:, :],
                    -MAGIC, None, ALU.add),
                deps=[tt0, lastrd_xq.get(st - 2)])
            war2["t0"][kt] = t_xq[tt]
        lastrd_xn2[st] = last_tt0
        t_xqT = {}
        ev_list = []
        tr_last = None
        for dc in range(DC):
            half = dc % 2
            trk = []
            for tt in range(NTT):
                trk.append(p.emit("pe",
                    lambda e, dc=dc, tt=tt, half=half, par=par: e.transpose(
                        ps_tr2[half][:, tt * 128:(tt + 1) * 128],
                        xq_sb[par][:, tt * D + dc * 128:tt * D + dc * 128 + 128],
                        id16[:, :]),
                    deps=[t_xq[tt], t_wgf[IC - 1]] +
                         (ev_list[-2:-1] if len(ev_list) >= 2 else [])))
            tr_last = trk[-1]
            ev_list.append(p.emit("act",
                lambda e, dc=dc, half=half, par=par: e.copy(
                    xqT[par][:, dc * S:(dc + 1) * S], ps_tr2[half][:, :]),
                deps=trk + [mml2.get(st - 2), t_xq_dma.get((st - 2, dc))]))
            t_xqT[dc] = ev_list[-1]
            t_xq_dma[(st, dc)] = p.dma("act", "acq",
                lambda e, dc=dc, st=st, par=par: e.dma_start(
                    out=xqt_d[dc * 128:(dc + 1) * 128, st * S:(st + 1) * S],
                    in_=xqT[par][:, dc * S:(dc + 1) * S]), deps=[ev_list[-1]])
        lastrd_xq[st] = tr_last

        mm_last = None
        for ic_ in range(IC):
            icg = st * IC + ic_
            kb = icg % RBD
            flush2a(icg)
            mms = []
            for dc in range(DC):
                deps = [t_xqT[dc], t_wgf[ic_]]
                prev_ev = t_gev.get((st, ic_ - 3), t_gev.get((st - 1, IC - 3 + ic_)))
                if prev_ev is not None:
                    deps.append(prev_ev)
                mms.append(p.emit("pe",
                    lambda e, ic_=ic_, dc=dc, par=par: e.matmul(
                        ps_g2[ic_ % 3][:, :],
                        wgfT[:, (ic_ * DC + dc) * 128:(ic_ * DC + dc + 1) * 128],
                        xqT[par][:, dc * S:(dc + 1) * S],
                        start=(dc == 0), stop=(dc == DC - 1)), deps=deps))
            mm_last = mms[-1]
            tev = p.emit("act", lambda e, ic_=ic_, kb=kb: e.activation(
                gw[kb][:, :], ps_g2[ic_ % 3][:, :], AF.Copy,
                scale=2.0 ** -16),
                deps=[mms[-1], gdma_f.get(icg - RBD)])
            t_gev[(st, ic_)] = tev
            tred = emit_red("m2", gw[kb][:, :], [tev])

            def _sg(ic_=ic_, st=st, kb=kb, icg=icg, tev=tev, tred=tred):
                t_gdma[(st, ic_)] = p.dma("act", "acq",
                    lambda e, ic_=ic_, st=st, kb=kb: e.dma_start(
                        out=gate_d[ic_ * 128:(ic_ + 1) * 128,
                                   st * S:(st + 1) * S],
                        in_=gw[kb][:, :]), deps=[tev, tred])
                gdma_f[icg] = t_gdma[(st, ic_)]
            push2a(icg, _sg)
        mml2[st] = mm_last
        for ic_ in up_sched.get(st, []):
            ev, _, new_evs = prep_lpbq_chunk(
                wup_h, wupT, wst2, ps_tr2w, ic_,
                [t_wgf[IC - 1], t_wup.get(ic_ - 1)] + b0, D, prev_ev_track,
                post_scale=scol("s0b"))
            t_wup[ic_] = ev
            prev_ev_track = (prev_ev_track + new_evs)[-2:]

    flush2a(None)
    t_fold2 = fold_ring("m2", 1)
    t_fm2 = max_finalize(1, 1, ps_bc2, [t_fold2])
    b1 = barrier(1, 1, 1, [("s2", "i2")], ps_bc2, [t_fm2])
    t_hs2 = p.emit("dve", lambda e: e.tensor_scalar(
        scol("hs2"), scol("s2"), 0.5, None, ALU.mult), deps=b1)
    t_nh = p.emit("dve", lambda e: e.tensor_scalar(
        scol("nhs2m"), scol("hs2"), -MAGIC, None, ALU.mult), deps=[t_hs2])
    t_p16 = p.emit("dve", lambda e: e.tensor_scalar(
        scol("s2p16"), scol("s2"), 2.0 ** -16, None, ALU.mult), deps=b1)
    b1_all = b1 + [t_hs2, t_nh, t_p16]

    ps2.close()
    ctx2.close()
    ctx_wu2.close()
    p.fence()

    ctx3 = ExitStack()
    RBD = 6
    RCD = 3
    xqT3 = [ctx3.enter_context(nc.sbuf_tensor(f"xqT3_{i}", [128, DC * S], F16))
            for i in range(2)]
    uw = [ctx3.enter_context(nc.sbuf_tensor(f"uw{i}", [128, S], F16))
          for i in range(RBD)]
    gld = [ctx3.enter_context(nc.sbuf_tensor(f"gld{i}", [128, S], F16))
           for i in range(RBD)]
    t2w = [ctx3.enter_context(nc.sbuf_tensor(f"t2w{i}", [128, S], F32))
           for i in range(RCD)]
    th3w = [ctx3.enter_context(nc.sbuf_tensor(f"th3w{i}", [128, S], F32))
            for i in range(RCD)]
    w1w = [ctx3.enter_context(nc.sbuf_tensor(f"w1w{i}", [128, S], F32))
           for i in range(RCD)]
    sqw = [ctx3.enter_context(nc.sbuf_tensor(f"sqw{i}", [128, S], F32))
           for i in range(RCD)]
    gqw = [ctx3.enter_context(nc.sbuf_tensor(f"gqw{i}", [128, S], F16))
           for i in range(RCD)]
    g2w = [ctx3.enter_context(nc.sbuf_tensor(f"g2w{i}", [128, S], F16))
           for i in range(RBD)]

    ps3 = ExitStack()
    ps_u = [ps3.enter_context(nc.psum_tensor(f"ps_u{i}", [128, 512], F32))
            for i in range(3)]
    ps_bc3 = ps_u[0]

    t_udma = {}
    t_uev = {}
    t_g2dma = {}
    udma_f = {}
    g2dma_f = {}
    tt2_f = {}
    mml3 = {}
    war3 = {"t2": [None] * RCD, "t2r": [None] * RCD, "th3": [None] * RCD,
            "w1": [None] * RCD, "sq": [None] * RCD, "gq": [None] * RCD}
    push2b, flush2b = mk_defer(2)
    for st in range(NST):
        par = st % 2
        dxq = {}
        for dc in range(DC):
            dxq[dc] = p.dma("sp", "spq",
                lambda e, dc=dc, st=st, par=par: e.dma_start(
                    out=xqT3[par][:, dc * S:(dc + 1) * S],
                    in_=xqt_d[dc * 128:(dc + 1) * 128, st * S:(st + 1) * S]),
                deps=[t_xq_dma[(st, dc)], mml3.get(st - 2)] + b1_all)
        mm_last = None
        for ic_ in range(IC):
            icg = st * IC + ic_
            kb = icg % RBD
            kc = icg % RCD
            flush2b(icg)
            mms = []
            for dc in range(DC):
                deps = [dxq[dc], t_wup[ic_]]
                prev_ev = t_uev.get((st, ic_ - 3), t_uev.get((st - 1, IC - 3 + ic_)))
                if prev_ev is not None:
                    deps.append(prev_ev)
                mms.append(p.emit("pe",
                    lambda e, ic_=ic_, dc=dc, par=par: e.matmul(
                        ps_u[ic_ % 3][:, :],
                        wupT[:, (ic_ * DC + dc) * 128:(ic_ * DC + dc + 1) * 128],
                        xqT3[par][:, dc * S:(dc + 1) * S],
                        start=(dc == 0), stop=(dc == DC - 1)), deps=deps))
            mm_last = mms[-1]
            tev = p.emit("act", lambda e, ic_=ic_, kb=kb: e.activation(
                uw[kb][:, :], ps_u[ic_ % 3][:, :], AF.Copy,
                scale=2.0 ** -16),
                deps=[mms[-1], udma_f.get(icg - RBD)])
            t_uev[(st, ic_)] = tev
            tred1 = emit_red("m1", uw[kb][:, :], [tev])

            def _su(ic_=ic_, st=st, kb=kb, icg=icg, tev=tev, tred1=tred1):
                t_udma[(st, ic_)] = p.dma("act", "acq",
                    lambda e, ic_=ic_, st=st, kb=kb: e.dma_start(
                        out=up_d[ic_ * 128:(ic_ + 1) * 128,
                                 st * S:(st + 1) * S],
                        in_=uw[kb][:, :]), deps=[tev, tred1])
                udma_f[icg] = t_udma[(st, ic_)]
            push2b(icg, _su)

            dg = p.dma("sp", "spq", lambda e, ic_=ic_, st=st, kb=kb: e.dma_start(
                out=gld[kb][:, :],
                in_=gate_d[ic_ * 128:(ic_ + 1) * 128, st * S:(st + 1) * S]),
                deps=[t_gdma[(st, ic_)], tt2_f.get(icg - RBD)])
            tt2 = p.emit("pool", lambda e, kb=kb, kc=kc: e.tensor_scalar(
                t2w[kc][:, :], gld[kb][:, :], scol("i2"), MAGIC,
                ALU.mult, ALU.add),
                deps=[dg, war3["t2"][kc], war3["t2r"][kc]])
            tt2_f[icg] = tt2
            tth3 = p.emit("act", lambda e, kc=kc: e.activation(
                th3w[kc][:, :], t2w[kc][:, :], AF.Tanh, bias=scol("nhs2m"),
                scale=scol("hs2")), deps=[tt2, war3["th3"][kc]])
            war3["t2r"][kc] = tth3
            tw1_ = p.emit("act", lambda e, kc=kc: e.activation(
                w1w[kc][:, :], th3w[kc][:, :], AF.Relu, scale=32768.0,
                bias=cst[:, 4:5]), deps=[tth3, war3["w1"][kc]])
            war3["th3"][kc] = tw1_
            tsq = p.emit("dve", lambda e, kc=kc: e.tensor_scalar(
                sqw[kc][:, :], w1w[kc][:, :], -MAGIC, 65535.0,
                ALU.add, ALU.min), deps=[tw1_, war3["sq"][kc]])
            war3["w1"][kc] = tsq
            tm2 = p.emit("pool", lambda e, kc=kc: e.tensor_scalar(
                gqw[kc][:, :], t2w[kc][:, :], 1.0, -MAGIC,
                ALU.mult, ALU.add),
                deps=[tt2, war3["gq"][kc]])
            war3["t2"][kc] = tm2
            tg2 = p.emit("dve", lambda e, kb=kb, kc=kc: e.scalar_tensor_tensor(
                g2w[kb][:, :], sqw[kc][:, :], scol("s2p16"), gqw[kc][:, :],
                ALU.mult, ALU.mult),
                deps=[tsq, tm2, g2dma_f.get(icg - RBD)])
            war3["sq"][kc] = tg2
            war3["gq"][kc] = tg2
            tred3 = emit_red("m3", g2w[kb][:, :], [tg2])

            def _sg2(ic_=ic_, st=st, kb=kb, icg=icg, tg2=tg2, tred3=tred3):
                dg2 = p.dma("act", "acq",
                    lambda e, ic_=ic_, st=st, kb=kb: e.dma_start(
                        out=g2_d[ic_ * 128:(ic_ + 1) * 128,
                                 st * S:(st + 1) * S],
                        in_=g2w[kb][:, :]), deps=[tg2, tred3])
                t_g2dma[(st, ic_)] = dg2
                g2dma_f[icg] = dg2
            push2b(icg, _sg2)
        mml3[st] = mm_last

    flush2b(None)
    ctx3.close()
    ctx_wu1.close()

    ctx_wd = ExitStack()
    wdT = ctx_wd.enter_context(nc.sbuf_tensor("wdT", [128, DC * IC * 128], F16))
    wst4 = {k: ctx_wd.enter_context(nc.sbuf_tensor(f"r_{k}", [128, n], dt))
            for k, n, dt in [("w32", I, F32), ("w16", I, F16), ("z", I, F32),
                             ("bmax", I // 32, F32), ("sw", I // 32, F32),
                             ("isw", I // 32, F32)]}
    ps_wd = ExitStack()
    ps_tr4 = [ps_wd.enter_context(nc.psum_tensor(f"ps4_tr{i}", [128, 512], F16))
              for i in range(2)]
    t_wd = {}
    nlast = NST * IC
    prev_dep = ([mml3[NST - 1]] +
                [udma_f[i] for i in range(nlast - RBD, nlast)] +
                [g2dma_f[i] for i in range(nlast - RBD, nlast)])
    prev_ev_track = []
    for dpc in range(DC):
        ev, _, new_evs = prep_lpbq_chunk(wd_h, wdT, wst4, ps_tr4, dpc,
                                         prev_dep, I, prev_ev_track)
        t_wd[dpc] = ev
        prev_ev_track = (prev_ev_track + new_evs)[-2:]
        prev_dep = [ev]
    ps_wd.close()

    t_fold1 = fold_ring("m1", 2)
    t_fm1 = max_finalize(2, 2, ps_bc3, [t_fold1])
    t_fold3 = fold_ring("m3", 3)
    t_fm3 = max_finalize(3, 3, ps_bc3, [t_fold3, t_fm1])
    b2 = barrier(2, 2, 2, [("s1", "i1"), ("s3", "i3")], ps_bc3, [t_fm3])

    ps3.close()
    p.fence()

    ctx4 = ExitStack()
    S4 = 1024
    NST4 = T // S4
    R4 = 8
    R3 = 3
    RP = 6
    u4 = [ctx4.enter_context(nc.sbuf_tensor(f"u4_{i}", [128, S4], F16))
          for i in range(R4)]
    g4 = [ctx4.enter_context(nc.sbuf_tensor(f"g4_{i}", [128, S4], F16))
          for i in range(R4)]
    t4w = [ctx4.enter_context(nc.sbuf_tensor(f"t4w{i}", [128, S4], F32))
           for i in range(R3)]
    gate2w = [ctx4.enter_context(nc.sbuf_tensor(f"gate2w{i}", [128, S4], F16))
              for i in range(R3)]
    tuw = [ctx4.enter_context(nc.sbuf_tensor(f"tuw{i}", [128, S4], F32))
           for i in range(R3)]
    upqw = [ctx4.enter_context(nc.sbuf_tensor(f"upqw{i}", [128, S4], F16))
            for i in range(R3)]
    pw = [ctx4.enter_context(nc.sbuf_tensor(f"pw{i}", [128, S4], F16))
          for i in range(RP)]

    ps4 = ExitStack()
    ps_bc4 = ps4.enter_context(nc.psum_tensor("ps_bc4", [128, 128], F32))

    t_pdma = {}
    war4 = {"u4": [None] * R4, "g4": [None] * R4, "t4": [None] * R3,
            "gate2": [None] * R3, "tu": [None] * R3, "upq": [None] * R3,
            "pw": [None] * RP}
    push4, flush4 = mk_defer(2)
    for st in range(NST4):
        for ic_ in range(IC):
            k4 = (st * IC + ic_) % R4
            k3 = (st * IC + ic_) % R3
            kp = (st * IC + ic_) % RP
            flush4(st * IC + ic_)
            du = p.dma("sp", "spq", lambda e, ic_=ic_, st=st, k4=k4: e.dma_start(
                out=u4[k4][:, :],
                in_=up_d[ic_ * 128:(ic_ + 1) * 128, st * S4:(st + 1) * S4]),
                deps=[t_udma[(2 * st, ic_)], t_udma[(2 * st + 1, ic_)],
                      t_wd[DC - 1], war4["u4"][k4]] + b2)
            dg = p.dma("sp", "spq", lambda e, ic_=ic_, st=st, k4=k4: e.dma_start(
                out=g4[k4][:, :],
                in_=g2_d[ic_ * 128:(ic_ + 1) * 128, st * S4:(st + 1) * S4]),
                deps=[t_g2dma[(2 * st, ic_)], t_g2dma[(2 * st + 1, ic_)],
                      war4["g4"][k4]] + b2)
            tt4 = p.emit("pool", lambda e, k4=k4, k3=k3: e.tensor_scalar(
                t4w[k3][:, :], g4[k4][:, :], scol("i3"), MAGIC,
                ALU.mult, ALU.add), deps=[dg, war4["t4"][k3]])
            war4["g4"][k4] = tt4
            tg2_ = p.emit("dve", lambda e, k3=k3: e.tensor_scalar(
                gate2w[k3][:, :], t4w[k3][:, :], -MAGIC, scol("s3"),
                ALU.add, ALU.mult), deps=[tt4, war4["gate2"][k3]])
            war4["t4"][k3] = tg2_
            ttu = p.emit("act", lambda e, k4=k4, k3=k3: e.activation(
                tuw[k3][:, :], u4[k4][:, :], AF.Relu,
                scale=scol("i1"), bias=cst[:, 3:4]), deps=[du, war4["tu"][k3]])
            war4["u4"][k4] = ttu
            tuq = p.emit("pool", lambda e, k3=k3: e.tensor_scalar(
                upqw[k3][:, :], tuw[k3][:, :], -MAGIC, scol("s1"),
                ALU.add, ALU.mult), deps=[ttu, war4["upq"][k3]])
            war4["tu"][k3] = tuq
            tp = p.emit("dve", lambda e, k3=k3, kp=kp: e.tensor_tensor(
                pw[kp][:, :], gate2w[k3][:, :], upqw[k3][:, :], ALU.mult),
                deps=[tg2_, tuq, war4["pw"][kp]])
            war4["gate2"][k3] = tp
            war4["upq"][k3] = tp
            tred4 = emit_red("m4", pw[kp][:, :], [tp])

            def _sp4(ic_=ic_, st=st, kp=kp, tp=tp, tred4=tred4):
                t_pdma[(st, ic_)] = p.dma("act", "acq",
                    lambda e, ic_=ic_, st=st, kp=kp: e.dma_start(
                        out=p_d[ic_ * 128:(ic_ + 1) * 128,
                                st * S4:(st + 1) * S4],
                        in_=pw[kp][:, :]), deps=[tp, tred4])
                war4["pw"][kp] = t_pdma[(st, ic_)]
            push4(st * IC + ic_, _sp4)

    flush4(None)
    t_fold4 = fold_ring("m4", 4)
    t_fm4 = max_finalize(4, 4, ps_bc4, [t_fold4])
    b3 = barrier(3, 4, 1, [("s4", "i4")], ps_bc4, [t_fm4])
    t_e4 = p.emit("dve", lambda e: e.tensor_scalar(
        scol("eps4"), scol("i4"), scol("i4"), 1e-5, ALU.mult, ALU.mult),
        deps=b3)
    b3 = b3 + [t_e4]

    ps4.close()
    ctx4.close()
    p.fence()

    ctx5 = ExitStack()
    o_t = [ctx5.enter_context(nc.sbuf_tensor(f"o_t{i}", [128, IC * S], F16))
           for i in range(2)]
    t5w = [ctx5.enter_context(nc.sbuf_tensor(f"t5w{i}", [128, S], F32))
           for i in range(R3)]
    x2l = [ctx5.enter_context(nc.sbuf_tensor(f"x2l{i}", [128, D], F32))
           for i in range(2)]
    xml = ctx5.enter_context(nc.sbuf_tensor("xml", [128, D], F32))
    outw = [ctx5.enter_context(nc.sbuf_tensor(f"outw{i}", [128, D], F32))
            for i in range(2)]

    ps5 = ExitStack()
    ps_y = [ps5.enter_context(nc.psum_tensor(f"ps_y{i}", [128, 1024], F32))
            for i in range(2)]

    t_out_dma = {}
    war5 = {"o": [{} for _ in range(2)], "rd": [{} for _ in range(2)],
            "t5": [None] * R3, "x2l": [None, None],
            "xml": None, "outw": [None, None], "psy": [None, None]}
    push5, flush5 = mk_defer(2)
    for st in range(NST):
        par = st % 2
        t_o = {}
        for ic_ in range(IC):
            k3 = (st * IC + ic_) % R3
            dp = p.dma("sp", "spq", lambda e, ic_=ic_, st=st, par=par: e.dma_start(
                out=o_t[par][:, ic_ * S:(ic_ + 1) * S],
                in_=p_d[ic_ * 128:(ic_ + 1) * 128, st * S:(st + 1) * S]),
                deps=[t_pdma[(st // 2, ic_)], war5["rd"][par].get(ic_),
                      war5["o"][par].get(ic_)] + b3)
            tt5 = p.emit("act", lambda e, ic_=ic_, par=par, k3=k3: e.activation(
                t5w[k3][:, :], o_t[par][:, ic_ * S:(ic_ + 1) * S], AF.Relu,
                scale=scol("i4"), bias=cst[:, 3:4]),
                deps=[dp, war5["t5"][k3]])
            t_o[ic_] = p.emit("dve", lambda e, ic_=ic_, par=par, k3=k3:
                e.tensor_scalar(
                    o_t[par][:, ic_ * S:(ic_ + 1) * S], t5w[k3][:, :], -MAGIC,
                    None, ALU.add), deps=[tt5])
            war5["t5"][k3] = t_o[ic_]
            war5["o"][par][ic_] = t_o[ic_]
        for tt in range(NTT):
            flush5(st * NTT + tt)
            dx2 = p.dma("sp", "spq", lambda e, st=st, tt=tt: e.dma_start(
                out=x2l[tt % 2][:, :],
                in_=x2_d[(st * S + tt * 128):(st * S + tt * 128 + 128), :]),
                deps=[x2_dma[(st, tt)], war5["x2l"][tt % 2]])
            mms_stop = []
            wdap = wdT.ap().rearrange("p (d i c) -> p d i c", i=IC, c=128)
            for half in range(2):
                for ic_ in range(IC):
                    deps = [t_o[ic_]]
                    if war5["psy"][tt % 2] is not None:
                        deps.append(war5["psy"][tt % 2])
                    rh = wdap[:, half * 4:(half + 1) * 4, ic_, :]
                    mm = p.emit("pe",
                        lambda e, ic_=ic_, half=half, tt=tt, rh=rh, par=par:
                        e.matmul(
                            ps_y[tt % 2][:, half * 512:(half + 1) * 512],
                            o_t[par][:,
                                     ic_ * S + tt * 128:ic_ * S + tt * 128 + 128],
                            rh,
                            start=(ic_ == 0), stop=(ic_ == IC - 1)),
                        deps=deps)
                    war5["rd"][par][ic_] = mm
                mms_stop.append(mm)
            lt4 = ln_stats(lambda lo, tt=tt: ps_y[tt % 2][:, lo:lo + 512],
                           CB["negm2"], CB["r2"], mms_stop,
                           extra_negmr=CB["negm2r2"], eps_ap=scol("eps4"))
            txm = p.emit("dve", lambda e, tt=tt: e.tensor_scalar(
                xml[:, :], x2l[tt % 2][:, :],
                colbuf[:, CB["negm2r2"]:CB["negm2r2"] + 1], None, ALU.add),
                deps=lt4 + [dx2, war5["xml"]])
            war5["x2l"][tt % 2] = txm
            tout = p.emit("dve", lambda e, tt=tt: e.scalar_tensor_tensor(
                outw[tt % 2][:, :], ps_y[tt % 2][:, :],
                colbuf[:, CB["r2"]:CB["r2"] + 1], xml[:, :],
                ALU.mult, ALU.add),
                deps=[txm] + lt4 + [war5["outw"][tt % 2]])
            use_cols(tout, [CB["negm2"], CB["r2"], CB["negm2r2"]])
            war5["xml"] = tout
            war5["psy"][tt % 2] = tout

            def _so5(st=st, tt=tt, tout=tout):
                t_out_dma[(st, tt)] = p.dma("act", "acq",
                    lambda e, st=st, tt=tt: e.dma_start(
                        out=out_h[(st * S + tt * 128):
                                  (st * S + tt * 128 + 128), :],
                        in_=outw[tt % 2][:, :]), deps=[tout])
                war5["outw"][tt % 2] = t_out_dma[(st, tt)]
            push5(st * NTT + tt, _so5)

    flush5(None)
    finals = [(sq, v) for sq, v in p.dcnt.items() if v > 0]
    for e in P.ENGINES:
        p.water[e] = {}
        p.wait(e, finals)

    with nc.Block() as block:
        @block.tensor
        def _(eng):
            for op in p.ops["pe"]:
                op(eng)

        @block.scalar
        def _(eng):
            for op in p.ops["act"]:
                op(eng)

        @block.vector
        def _(eng):
            for op in p.ops["dve"]:
                op(eng)

        @block.gpsimd
        def _(eng):
            for op in p.ops["pool"]:
                op(eng)

        @block.sync
        def _(eng):
            for op in p.ops["sp"]:
                op(eng)

    es.pop_all()
    return nc


def _make_inputs(x_core, W_in, b_in, W_gate, b_gate, lam, w_gf, w_up, w_down):
    return {
        "x": np.ascontiguousarray(x_core, dtype=np.float32),
        "w_in": np.ascontiguousarray(W_in, dtype=np.float32),
        "w_gate": np.ascontiguousarray(W_gate, dtype=np.float32),
        "b_in": np.ascontiguousarray(np.reshape(b_in, (1, -1)), dtype=np.float32),
        "b_gate": np.ascontiguousarray(np.reshape(b_gate, (1, -1)), dtype=np.float32),
        "lam": np.ascontiguousarray(np.reshape(lam, (1, -1)), dtype=np.float32),
        "w_up": np.ascontiguousarray(w_up, dtype=np.float32),
        "w_gf": np.ascontiguousarray(w_gf, dtype=np.float32),
        "w_down": np.ascontiguousarray(w_down, dtype=np.float32),
        "id16_h": np.eye(128, dtype=np.float16),
        "id32_h": np.eye(128, dtype=np.float32),
    }


def kernel(L0_attn_x, L0_attn_W_in, L0_attn_b_in, L0_attn_W_gate,
           L0_attn_b_gate, L0_attn_forget_lambda, L0_ffn_w_gate,
           L0_ffn_w_up, L0_ffn_w_down):
    x = np.ascontiguousarray(np.asarray(L0_attn_x, dtype=np.float32))
    B, T, _ = x.shape
    in_maps = [
        _make_inputs(x[c], L0_attn_W_in, L0_attn_b_in, L0_attn_W_gate,
                     L0_attn_b_gate, L0_attn_forget_lambda, L0_ffn_w_gate,
                     L0_ffn_w_up, L0_ffn_w_down)
        for c in range(B)
    ]
    import os
    axon = (bool(os.environ.get("AXON_TERMINAL_JOB_NAME"))
            or os.environ.get("AXON_H4_ENABLED") == "1")
    use_cc = (os.environ.get("BASS_SIM_FP64") == "1") or not axon
    nc = build_program(T=T, use_cc=use_cc, n_cores=B)
    res = run_bass_kernel_spmd(nc, in_maps, core_ids=list(range(B)))
    return np.stack([np.asarray(res.results[c]["out"], dtype=np.float32)
                     for c in range(B)], axis=0)
